# version 07
# baseline (speedup 1.0000x reference)
import numpy as np
import ml_dtypes
import concourse.bass as bass
import concourse.mybir as mybir
from concourse.bass_utils import run_bass_kernel_spmd

F32 = mybir.dt.float32
BF16 = mybir.dt.bfloat16
AF = mybir.ActivationFunctionType
ALU = mybir.AluOpType
AX = mybir.AxisListType

D_MODEL = 1024
NCHUNK = 8
BATCH = 4
SEQ = 8192
HSEQ = SEQ // 2
TT = 512
NTILE = HSEQ // TT

bfdt = ml_dtypes.bfloat16


class _DummyInstr:
    def then_inc(self, *a, **k):
        return self


class _CountProxy:

    def __init__(self):
        self.n = 0

    def __getattr__(self, name):
        if name.startswith("_"):
            raise AttributeError(name)
        d = _DummyInstr()

        def f(*a, **k):
            self.n += 1
            return d

        return f


ENG_IDS = {"sync": 0, "scalar": 1, "vector": 2, "tensor": 3, "gpsimd": 4}
SNAP_DMA = 5
SNAP_CC = 9
NSNAP = 10


class Chain:

    ENGINES = ("sync", "scalar", "vector", "tensor", "gpsimd")

    def __init__(self):
        self.q = {e: [] for e in self.ENGINES}
        self.cnt = [0] * NSNAP
        self.dma_phase = {"sync": 0, "gpsimd": 0}

    def snap(self):
        return tuple(self.cnt)

    @staticmethod
    def dep_max(*deps):
        return tuple(max(d[i] for d in deps) for i in range(NSNAP))

    def _emit_waits(self, p, sems, dep, eng_i, include_same=True):
        for e in range(5):
            v = dep[e]
            if v <= 0:
                continue
            if e == eng_i and not include_same:
                continue
            p.wait_ge(sems[e], v)
        for k in range(4):
            if dep[SNAP_DMA + k] > 0:
                p.wait_ge(sems[5 + k], 16 * dep[SNAP_DMA + k])
        if dep[SNAP_CC] > 0:
            p.wait_ge(sems[9], dep[SNAP_CC])

    def comp(self, eng, emit, relax=False, dep=None):
        cp = _CountProxy()
        emit(cp)
        n = cp.n
        assert n > 0
        ei = ENG_IDS[eng]
        st0 = tuple(dep) if dep is not None else self.snap()
        base_e = self.cnt[ei]
        self.cnt[ei] += 1 if relax else n
        ch = self

        def run(p, sems):
            insts = []

            class _Cap:
                def __getattr__(self, name):
                    if name.startswith("_"):
                        raise AttributeError(name)
                    meth = getattr(p, name)

                    def f(*a, **k):
                        if not relax:
                            if insts:
                                p.wait_ge(sems[ei], base_e + len(insts))
                            elif base_e > 0:
                                p.wait_ge(sems[ei], base_e)
                        elif not insts and base_e > 0:
                            p.wait_ge(sems[ei], base_e)
                        if not insts:
                            ch._emit_waits(p, sems, st0, ei,
                                           include_same=False)
                        inst = meth(*a, **k)
                        if not relax:
                            inst.then_inc(sems[ei], 1)
                        insts.append(inst)
                        return inst

                    return f

            emit(_Cap())
            assert len(insts) == n
            if relax:
                insts[-1].then_inc(sems[ei], 1)

        self.q[eng].append(run)

    def dma_n(self, eng, n, emit_list, dep=None):
        ei = ENG_IDS[eng]
        st0 = tuple(dep) if dep is not None else self.snap()
        qn = "gpsimd" if eng == "gpsimd" else "sync"
        par = self.dma_phase[qn] % 2
        self.dma_phase[qn] += 1
        didx = 5 + (2 if qn == "gpsimd" else 0) + par
        prior_same = self.cnt[SNAP_DMA + (2 if qn == "gpsimd" else 0) + par]
        self.cnt[SNAP_DMA + (2 if qn == "gpsimd" else 0) + par] += n
        ch = self

        def run(p, sems):
            if prior_same > 0:
                p.wait_ge(sems[didx], 16 * prior_same)
            ch._emit_waits(p, sems, st0, ei, include_same=True)
            insts = emit_list(p)
            assert len(insts) == n, f"declared {n} dmas, emitted {len(insts)}"
            for i in insts:
                i.then_inc(sems[didx], 16)

        self.q[eng].append(run)

    def collective(self, emit):
        st0 = self.snap()
        self.cnt[SNAP_CC] += 1
        ch = self

        def run(p, sems):
            ch._emit_waits(p, sems, st0, ENG_IDS["gpsimd"],
                           include_same=True)
            inst = emit(p)
            inst.then_inc(sems[9], 1)

        self.q["gpsimd"].append(run)

    def build_with_clear(self, nc):
        totals = self.snap()
        fin_engs = [e for e in ("sync", "scalar", "vector", "tensor")
                    if self.q[e]]
        nfin = len(fin_engs)
        ch = self

        def fin(eng):
            ei = ENG_IDS[eng]

            def run(p, sems):
                ch._emit_waits(p, sems, totals, ei, include_same=True)
                p.nop().then_inc(sems[ei], 1)

            self.q[eng].append(run)

        for e in fin_engs:
            fin(e)

        def gfin(p, sems):
            for e in ("sync", "scalar", "vector", "tensor"):
                ei = ENG_IDS[e]
                v = totals[ei] + (1 if e in fin_engs else 0)
                if v > 0:
                    p.wait_ge(sems[ei], v)
            for k in range(4):
                if totals[SNAP_DMA + k] > 0:
                    p.wait_ge(sems[5 + k], 16 * totals[SNAP_DMA + k])
            if totals[SNAP_CC] > 0:
                p.wait_ge(sems[9], totals[SNAP_CC])
            if totals[ENG_IDS["gpsimd"]] > 0:
                p.wait_ge(sems[ENG_IDS["gpsimd"]],
                          totals[ENG_IDS["gpsimd"]])
            nums = sorted(s.num for s in sems)
            p.dma_reset(range(nums[0], nums[-1] + 1))
            for s in sems:
                p.sem_clear(s)

        self.q["gpsimd"].append(gfin)
        self.build(nc)

    def build(self, nc):
        with (
            nc.semaphore("c_sy") as s0,
            nc.semaphore("c_sc") as s1,
            nc.semaphore("c_ve") as s2,
            nc.semaphore("c_te") as s3,
            nc.semaphore("c_gp") as s4,
            nc.semaphore("dsyA") as s5,
            nc.semaphore("dsyB") as s6,
            nc.semaphore("dgpA") as s7,
            nc.semaphore("dgpB") as s8,
            nc.semaphore("dcc") as s9,
            nc.Block() as block,
        ):
            ch = self
            sems = (s0, s1, s2, s3, s4, s5, s6, s7, s8, s9)

            @block.sync
            def _(p):
                for fn in ch.q["sync"]:
                    fn(p, sems)

            @block.scalar
            def _(p):
                for fn in ch.q["scalar"]:
                    fn(p, sems)

            @block.vector
            def _(p):
                for fn in ch.q["vector"]:
                    fn(p, sems)

            @block.tensor
            def _(p):
                for fn in ch.q["tensor"]:
                    fn(p, sems)

            @block.gpsimd
            def _(p):
                for fn in ch.q["gpsimd"]:
                    fn(p, sems)


class Ctx:

    def __init__(self, nc, ch):
        self.nc = nc
        self.ch = ch
        self.exits = []
        self.prefix = ""

    def sbuf(self, name, shape, dtype):
        t = self.nc.sbuf_tensor(self.prefix + name, shape, dtype)
        h = t.__enter__()
        self.exits.append(t)
        return h

    def psum(self, name, shape, dtype=F32):
        t = self.nc.psum_tensor(self.prefix + name, shape, dtype)
        h = t.__enter__()
        self.exits.append(t)
        return h

    def mark(self):
        return len(self.exits)

    def release(self, mark):
        while len(self.exits) > mark:
            self.exits.pop().__exit__(None, None, None)

    def close(self):
        self.release(0)


def make_common_consts(cx):
    cx.ones_col_bf = cx.sbuf("ones_col_bf", [128, 1], BF16)
    cx.ones_row_f = cx.sbuf("ones_row_f", [1, 128], F32)

    def emit(p):
        p.memset(cx.ones_col_bf[:], 1.0)
        return p.memset(cx.ones_row_f[:], 1.0)

    cx.ch.comp("vector", emit)


def fmview(ap, nch=NCHUNK):
    return ap.rearrange("(c p) t -> p c t", p=128)


def emit_ln(cx, xt, T, sq, xb, psA, psB, psA2, psB2, rows, need_xb=True):
    ch = cx.ch
    ch.comp("scalar",
            lambda p: p.activation(sq[:, :, 0:T], xt[:, :, 0:T], AF.Square))
    if need_xb:
        ch.comp("vector",
                lambda p: p.tensor_copy(xb[:, :, 0:T], xt[:, :, 0:T]))

    def e_mm(p):
        for c in range(NCHUNK):
            p.matmul(psA[0:1, 0:T], cx.ones_col_bf[:], xb[:, c, 0:T],
                     start=(c == 0), stop=(c == NCHUNK - 1))
        last = None
        for c in range(NCHUNK):
            last = p.matmul(psB[0:1, 0:T], cx.ones_col_bf[:], sq[:, c, 0:T],
                            start=(c == 0), stop=(c == NCHUNK - 1))
        return last

    ch.comp("tensor", e_mm, relax=True)

    mean = rows[0:1, 0:T]
    m2 = rows[0:1, T:2 * T]
    varp = rows[0:1, 2 * T:3 * T]
    amb = rows[0:1, 3 * T:4 * T]

    def e_rows(p):
        p.tensor_scalar_mul(mean, psA[0:1, 0:T], 1.0 / D_MODEL)
        p.tensor_tensor(m2, mean, mean, ALU.mult)
        p.tensor_scalar_mul(varp, psB[0:1, 0:T], 1.0 / D_MODEL)
        p.tensor_tensor(varp, varp, m2, ALU.subtract)
        p.tensor_scalar_add(varp, varp, 1e-5)
        return p.reciprocal(varp, varp)

    ch.comp("vector", e_rows)
    ch.comp("scalar", lambda p: p.activation(varp, varp, AF.Sqrt))
    ch.comp("vector",
            lambda p: p.scalar_tensor_tensor(amb, mean, -1.0, varp, ALU.mult,
                                             ALU.mult))

    def e_bc(p):
        p.matmul(psA2[:, 0:T], cx.ones_row_f[:], varp, start=True, stop=True)
        return p.matmul(psB2[:, 0:T], cx.ones_row_f[:], amb, start=True,
                        stop=True)

    ch.comp("tensor", e_bc, relax=True)


def emit_ln_apply(cx, xt, T, psA2, psB2, out, tmp=None, add_to=None):
    ch = cx.ch
    if add_to is None:
        ch.comp("vector", lambda p: [p.tensor_tensor(
            out[:, c, 0:T], xt[:, c, 0:T], psA2[:, 0:T], ALU.mult)
            for c in range(NCHUNK)], relax=True)
        ch.comp("vector", lambda p: [p.scalar_tensor_tensor(
            out[:, c, 0:T], out[:, c, 0:T], 1.0, psB2[:, 0:T], ALU.mult,
            ALU.add) for c in range(NCHUNK)], relax=True)
    else:
        ch.comp("vector", lambda p: [p.tensor_tensor(
            out[:, c, 0:T], xt[:, c, 0:T], psA2[:, 0:T], ALU.mult)
            for c in range(NCHUNK)], relax=True)
        ch.comp("vector", lambda p: [p.scalar_tensor_tensor(
            out[:, c, 0:T], out[:, c, 0:T], 1.0, psB2[:, 0:T], ALU.mult,
            ALU.add) for c in range(NCHUNK)], relax=True)
        ch.comp("vector", lambda p: [p.tensor_tensor(
            out[:, c, 0:T], out[:, c, 0:T], add_to[:, c, 0:T], ALU.add)
            for c in range(NCHUNK)], relax=True)


def build_n1():
    nc = bass.Bass("TRN2", target_bir_lowering=False, debug=False)
    x = nc.dram_tensor("x", [D_MODEL, HSEQ], F32, kind="ExternalInput").ap()
    w1t = nc.dram_tensor("w1t", [D_MODEL, 4096], F32, kind="ExternalInput").ap()
    w2t = nc.dram_tensor("w2t", [D_MODEL, 4096], F32, kind="ExternalInput").ap()
    s1 = nc.dram_tensor("s1", [128, 32], F32, kind="ExternalInput").ap()
    s2 = nc.dram_tensor("s2", [128, 32], F32, kind="ExternalInput").ap()
    h = nc.dram_tensor("h", [4096, HSEQ], BF16, kind="ExternalOutput").ap()

    ch = Chain()
    cx = Ctx(nc, ch)
    xt = cx.sbuf("xt", [128, NCHUNK, TT], F32)
    sq = cx.sbuf("sq", [128, NCHUNK, TT], BF16)
    xb = cx.sbuf("xb", [128, NCHUNK, TT], BF16)
    ub = cx.sbuf("ub", [128, NCHUNK, TT], BF16)
    w1s = cx.sbuf("w1s", [128, NCHUNK, 4096], BF16)
    w2s = cx.sbuf("w2s", [128, NCHUNK, 4096], BF16)
    s1s = cx.sbuf("s1s", [128, 32], F32)
    s2s = cx.sbuf("s2s", [128, 32], F32)
    rows = cx.sbuf("rows", [1, 4 * TT], F32)
    sa = cx.sbuf("sa", [128, TT], F32)
    sb2 = cx.sbuf("sb2", [128, TT], F32)
    hb = cx.sbuf("hb", [128, TT], BF16)
    psA = cx.psum("psA", [128, TT])
    psB = cx.psum("psB", [128, TT])
    psA2 = cx.psum("psA2", [128, TT])
    psB2 = cx.psum("psB2", [128, TT])
    psm = cx.psum("psm", [128, TT])
    psm2 = cx.psum("psm2", [128, TT])
    make_common_consts(cx)

    ch.dma_n("gpsimd", 4, lambda p: [
        p.dma_start(out=w1s[:], in_=fmview(w1t)),
        p.dma_start(out=w2s[:], in_=fmview(w2t)),
        p.dma_start(out=s1s[:], in_=s1[:, :]),
        p.dma_start(out=s2s[:], in_=s2[:, :]),
    ])

    for t in range(NTILE):
        t0 = t * TT
        ch.dma_n("sync", 1, lambda p, t0=t0: [
            p.dma_start(out=xt[:], in_=fmview(x)[:, :, t0:t0 + TT])])
        emit_ln(cx, xt, TT, sq, xb, psA, psB, psA2, psB2, rows)
        emit_ln_apply(cx, xt, TT, psA2, psB2, ub)
        for o in range(32):
            def e_mma(p, o=o):
                last = None
                for c in range(NCHUNK):
                    last = p.matmul(psm[:, :],
                                    w1s[:, c, o * 128:(o + 1) * 128],
                                    ub[:, c, :], start=(c == 0),
                                    stop=(c == NCHUNK - 1))
                return last

            ch.comp("tensor", e_mma, relax=True)
            ch.comp("scalar", lambda p, o=o: p.activation(
                sa[:], psm[:], AF.Silu, scale=s1s[:, o:o + 1]))

            def e_mmb(p, o=o):
                last = None
                for c in range(NCHUNK):
                    last = p.matmul(psm2[:, :],
                                    w2s[:, c, o * 128:(o + 1) * 128],
                                    ub[:, c, :], start=(c == 0),
                                    stop=(c == NCHUNK - 1))
                return last

            ch.comp("tensor", e_mmb, relax=True)

            def e_mul(p, o=o):
                p.tensor_scalar_mul(sb2[:], psm2[:], s2s[:, o:o + 1])
                return p.tensor_tensor(hb[:], sa[:], sb2[:], ALU.mult)

            ch.comp("vector", e_mul)
            ch.dma_n("sync", 1, lambda p, o=o, t0=t0: [
                p.dma_start(out=h[o * 128:(o + 1) * 128, t0:t0 + TT],
                            in_=hb[:])])

    ch.build_with_clear(nc)
    cx.close()
    return nc


def build_nh():
    nc = bass.Bass("TRN2", target_bir_lowering=False, debug=False)
    hi = nc.dram_tensor("hi", [4096, HSEQ], BF16, kind="ExternalInput").ap()
    wt = nc.dram_tensor("wt", [32, 4096, 128], F32, kind="ExternalInput").ap()
    ss = nc.dram_tensor("ss", [128, 32], F32, kind="ExternalInput").ap()
    ho = nc.dram_tensor("ho", [4096, HSEQ], BF16, kind="ExternalOutput").ap()

    ch = Chain()
    cx = Ctx(nc, ch)
    ht = cx.sbuf("ht", [128, 32, TT], BF16)
    ws = [cx.sbuf(f"ws{i}", [128, 32, 128], BF16) for i in range(2)]
    sss = cx.sbuf("sss", [128, 32], F32)
    ob = cx.sbuf("ob", [128, TT], BF16)
    psm = cx.psum("psm", [128, TT])
    psm2 = cx.psum("psm2", [128, TT])

    ch.dma_n("gpsimd", 1, lambda p: [p.dma_start(out=sss[:], in_=ss[:, :])])

    for t in range(NTILE):
        t0 = t * TT
        ch.dma_n("sync", 1, lambda p, t0=t0: [
            p.dma_start(out=ht[:], in_=fmview(hi, 32)[:, :, t0:t0 + TT])])
        for o in range(32):
            wb = ws[o % 2]
            ch.dma_n("gpsimd", 1, lambda p, o=o, wb=wb: [
                p.dma_start(out=wb[:],
                            in_=wt[o].rearrange("(c p) m -> p c m", p=128))])
            ps = psm if o % 2 == 0 else psm2

            def e_mm(p, wb=wb, ps=ps):
                last = None
                for c in range(32):
                    last = p.matmul(ps[:, :], wb[:, c, :], ht[:, c, :],
                                    start=(c == 0), stop=(c == 31))
                return last

            ch.comp("tensor", e_mm, relax=True)
            ch.comp("scalar", lambda p, o=o, ps=ps: p.activation(
                ob[:], ps[:], AF.Gelu, scale=sss[:, o:o + 1]))
            ch.dma_n("sync", 1, lambda p, o=o, t0=t0: [
                p.dma_start(out=ho[o * 128:(o + 1) * 128, t0:t0 + TT],
                            in_=ob[:])])

    ch.build_with_clear(nc)
    cx.close()
    return nc


def build_n4():
    nc = bass.Bass("TRN2", target_bir_lowering=False, debug=False)
    hi = nc.dram_tensor("hi", [4096, HSEQ], BF16, kind="ExternalInput").ap()
    w3t = nc.dram_tensor("w3t", [4096, D_MODEL], F32,
                         kind="ExternalInput").ap()
    s3 = nc.dram_tensor("s3", [128, 8], F32, kind="ExternalInput").ap()
    x3 = nc.dram_tensor("x3", [D_MODEL, HSEQ], F32, kind="ExternalInput").ap()
    y = nc.dram_tensor("y", [D_MODEL, HSEQ], F32, kind="ExternalOutput").ap()

    ch = Chain()
    cx = Ctx(nc, ch)
    ht = cx.sbuf("ht", [128, 32, TT], BF16)
    x3t = cx.sbuf("x3t", [128, NCHUNK, TT], F32)
    w3s = cx.sbuf("w3s", [128, 32, D_MODEL], BF16)
    s3s = cx.sbuf("s3s", [128, 8], F32)
    yt = cx.sbuf("yt", [128, NCHUNK, TT], F32)
    yb = cx.sbuf("yb", [128, NCHUNK, TT], BF16)
    sq = cx.sbuf("sq", [128, NCHUNK, TT], BF16)
    rows = cx.sbuf("rows", [1, 4 * TT], F32)
    outt = cx.sbuf("outt", [128, NCHUNK, TT], F32)
    psm = cx.psum("psm", [128, TT])
    psA = cx.psum("psA", [128, TT])
    psB = cx.psum("psB", [128, TT])
    psA2 = cx.psum("psA2", [128, TT])
    psB2 = cx.psum("psB2", [128, TT])
    make_common_consts(cx)

    ch.dma_n("gpsimd", 2, lambda p: [
        p.dma_start(out=w3s[:], in_=fmview(w3t, 32)),
        p.dma_start(out=s3s[:], in_=s3[:, :]),
    ])

    for t in range(NTILE):
        t0 = t * TT
        ch.dma_n("sync", 2, lambda p, t0=t0: [
            p.dma_start(out=ht[:], in_=fmview(hi, 32)[:, :, t0:t0 + TT]),
            p.dma_start(out=x3t[:], in_=fmview(x3)[:, :, t0:t0 + TT]),
        ])
        for o in range(NCHUNK):
            def e_mm(p, o=o):
                last = None
                for c in range(32):
                    last = p.matmul(psm[:, :],
                                    w3s[:, c, o * 128:(o + 1) * 128],
                                    ht[:, c, :], start=(c == 0), stop=(c == 31))
                return last

            ch.comp("tensor", e_mm, relax=True)

            def e_ev(p, o=o):
                p.tensor_scalar_mul(yt[:, o, :], psm[:], s3s[:, o:o + 1])
                return p.tensor_copy(yb[:, o, :], yt[:, o, :])

            ch.comp("vector", e_ev)
        emit_ln(cx, yt, TT, sq, yb, psA, psB, psA2, psB2, rows, need_xb=False)
        emit_ln_apply(cx, yt, TT, psA2, psB2, outt, add_to=x3t)
        ch.dma_n("sync", 1, lambda p, t0=t0: [
            p.dma_start(out=fmview(y)[:, :, t0:t0 + TT], in_=outt[:])])

    ch.build_with_clear(nc)
    cx.close()
    return nc


def _c(a):
    return np.ascontiguousarray(a)


def _scale_tiles(s):
    n = s.shape[0]
    return _c(s.reshape(n // 128, 128).T.astype(np.float32))


def _run_l1_ffn(x3_fm, w):
    nc1 = build_n1()
    w1t = _c(w["L1_ffn_w1_w"].T.astype(np.float32))
    w2t = _c(w["L1_ffn_w2_w"].T.astype(np.float32))
    s1 = _scale_tiles(w["L1_ffn_w1_s"])
    s2 = _scale_tiles(w["L1_ffn_w2_s"])
    maps = []
    for core in range(8):
        b, hh = core // 2, core % 2
        xs = _c(x3_fm[b, :, hh * HSEQ:(hh + 1) * HSEQ])
        maps.append({"x": xs, "w1t": w1t, "w2t": w2t, "s1": s1, "s2": s2})
    res = run_bass_kernel_spmd(nc1, maps, core_ids=list(range(8)))
    hs = [res.results[c]["h"] for c in range(8)]

    nch = build_nh()
    for wname, sname in (("L1_ffn_h1_w", "L1_ffn_h1_s"),
                         ("L1_ffn_h2_w", "L1_ffn_h2_s")):
        wt = w[wname].T.astype(np.float32)
        wtt = _c(wt.reshape(4096, 32, 128).transpose(1, 0, 2))
        ss = _scale_tiles(w[sname])
        maps = [{"hi": _c(hs[c]), "wt": wtt, "ss": ss} for c in range(8)]
        res = run_bass_kernel_spmd(nch, maps, core_ids=list(range(8)))
        hs = [res.results[c]["ho"] for c in range(8)]

    nc4 = build_n4()
    w3t = _c(w["L1_ffn_w3_w"].T.astype(np.float32))
    s3 = _scale_tiles(w["L1_ffn_w3_s"])
    maps = []
    for core in range(8):
        b, hh = core // 2, core % 2
        maps.append({"hi": _c(hs[core]), "w3t": w3t, "s3": s3,
                     "x3": _c(x3_fm[b, :, hh * HSEQ:(hh + 1) * HSEQ])})
    res = run_bass_kernel_spmd(nc4, maps, core_ids=list(range(8)))
    out = np.empty_like(x3_fm)
    for core in range(8):
        b, hh = core // 2, core % 2
        out[b, :, hh * HSEQ:(hh + 1) * HSEQ] = res.results[core]["y"]
    return out


def kernel(**inp):
    raise NotImplementedError("assembled later")


def build_r1():
    nc = bass.Bass("TRN2", target_bir_lowering=False, debug=False)
    x = nc.dram_tensor("x", [D_MODEL, 1 + HSEQ], F32, kind="ExternalInput").ap()
    hkeep = nc.dram_tensor("hkeep", [128, 1], F32, kind="ExternalInput").ap()
    wkt = nc.dram_tensor("wkt", [D_MODEL, D_MODEL], F32,
                         kind="ExternalInput").ap()
    wvt = nc.dram_tensor("wvt", [D_MODEL, D_MODEL], F32,
                         kind="ExternalInput").ap()
    wrt = nc.dram_tensor("wrt", [D_MODEL, D_MODEL], F32,
                         kind="ExternalInput").ap()
    mixk = nc.dram_tensor("mixk", [128, NCHUNK], F32, kind="ExternalInput").ap()
    mixv = nc.dram_tensor("mixv", [128, NCHUNK], F32, kind="ExternalInput").ap()
    mixr = nc.dram_tensor("mixr", [128, NCHUNK], F32, kind="ExternalInput").ap()
    ko = nc.dram_tensor("k", [D_MODEL, HSEQ], F32, kind="ExternalOutput").ap()
    vo = nc.dram_tensor("v", [D_MODEL, HSEQ], F32, kind="ExternalOutput").ap()
    sro = nc.dram_tensor("sr", [D_MODEL, HSEQ], F32, kind="ExternalOutput").ap()

    ch = Chain()
    cx = Ctx(nc, ch)
    xt = cx.sbuf("xt", [128, NCHUNK, TT], F32)
    xh = cx.sbuf("xh", [128, NCHUNK, 1], F32)
    ut = cx.sbuf("ut", [128, NCHUNK, TT], F32)
    ucol = cx.sbuf("ucol", [128, NCHUNK, 1], F32)
    dcol = cx.sbuf("dcol", [128, NCHUNK, TT], F32)
    sq = cx.sbuf("sq", [128, NCHUNK, TT], BF16)
    xb = cx.sbuf("xb", [128, NCHUNK, TT], BF16)
    mixs = [cx.sbuf(f"mix{i}", [128, NCHUNK], F32) for i in range(3)]
    hkp = cx.sbuf("hkp", [128, 1], F32)
    wss = [cx.sbuf(f"w{i}", [128, NCHUNK, D_MODEL], BF16) for i in range(3)]
    mxb = cx.sbuf("mxb", [128, NCHUNK, TT], BF16)
    evt = cx.sbuf("evt", [128, TT], F32)
    rows = cx.sbuf("rows", [1, 4 * TT], F32)
    psA = cx.psum("psA", [128, TT])
    psB = cx.psum("psB", [128, TT])
    psA2 = cx.psum("psA2", [128, TT])
    psB2 = cx.psum("psB2", [128, TT])
    psm = cx.psum("psm", [128, TT])
    make_common_consts(cx)

    ch.dma_n("gpsimd", 7, lambda p: [
        p.dma_start(out=wss[0][:], in_=fmview(wkt)),
        p.dma_start(out=wss[1][:], in_=fmview(wvt)),
        p.dma_start(out=wss[2][:], in_=fmview(wrt)),
        p.dma_start(out=mixs[0][:], in_=mixk[:, :]),
        p.dma_start(out=mixs[1][:], in_=mixv[:, :]),
        p.dma_start(out=mixs[2][:], in_=mixr[:, :]),
        p.dma_start(out=hkp[:], in_=hkeep[:, :]),
    ])
    ch.dma_n("sync", 1,
             lambda p: [p.dma_start(out=xh[:], in_=fmview(x)[:, :, 0:1],
                                    allow_slow_non_contiguous=True)])
    emit_ln(cx, xh, 1, sq, xb, psA, psB, psA2, psB2, rows)
    emit_ln_apply(cx, xh, 1, psA2, psB2, ucol)
    ch.comp("vector", lambda p: p.tensor_scalar_mul(
        ucol[:, :, 0:1].rearrange("p c one -> p (c one)"),
        ucol[:, :, 0:1].rearrange("p c one -> p (c one)"), hkp[:]))

    outs = (ko, vo, sro)
    for t in range(NTILE):
        t0 = t * TT
        ch.dma_n("sync", 1, lambda p, t0=t0: [
            p.dma_start(out=xt[:], in_=fmview(x)[:, :, 1 + t0:1 + t0 + TT])])
        emit_ln(cx, xt, TT, sq, xb, psA, psB, psA2, psB2, rows)
        emit_ln_apply(cx, xt, TT, psA2, psB2, ut)

        def e_d(p):
            last = None
            for c in range(NCHUNK):
                p.tensor_tensor(dcol[:, c, 0:1], ut[:, c, 0:1], ucol[:, c, 0:1],
                                ALU.subtract)
                last = p.tensor_tensor(dcol[:, c, 1:TT], ut[:, c, 1:TT],
                                       ut[:, c, 0:TT - 1], ALU.subtract)
            return last

        ch.comp("vector", e_d, relax=True)

        for mi in range(3):
            def e_mix(p, mi=mi):
                last = None
                for c in range(NCHUNK):
                    p.scalar_tensor_tensor(mxb[:, c, 0:1], dcol[:, c, 0:1],
                                           mixs[mi][:, c:c + 1],
                                           ucol[:, c, 0:1], ALU.mult, ALU.add)
                    last = p.scalar_tensor_tensor(
                        mxb[:, c, 1:TT], dcol[:, c, 1:TT],
                        mixs[mi][:, c:c + 1], ut[:, c, 0:TT - 1], ALU.mult,
                        ALU.add)
                return last

            ch.comp("vector", e_mix, relax=True)
            for o in range(NCHUNK):
                def e_mm(p, mi=mi, o=o):
                    last = None
                    for c in range(NCHUNK):
                        last = p.matmul(psm[:, :],
                                        wss[mi][:, c, o * 128:(o + 1) * 128],
                                        mxb[:, c, :], start=(c == 0),
                                        stop=(c == NCHUNK - 1))
                    return last

                ch.comp("tensor", e_mm, relax=True)
                if mi < 2:
                    ch.comp("vector",
                            lambda p: p.tensor_copy(evt[:], psm[:]))
                else:
                    ch.comp("scalar", lambda p: p.activation(evt[:], psm[:],
                                                             AF.Sigmoid))
                ch.dma_n("sync", 1, lambda p, mi=mi, o=o, t0=t0: [
                    p.dma_start(out=outs[mi][o * 128:(o + 1) * 128,
                                t0:t0 + TT], in_=evt[:])])
        def e_carry(p):
            return p.tensor_copy(ucol[:, :, 0:1], ut[:, :, TT - 1:TT])

        ch.comp("vector", e_carry)

    ch.build_with_clear(nc)
    cx.close()
    return nc


def build_wkv():
    HS = 4096
    nc = bass.Bass("TRN2", target_bir_lowering=False, debug=False)
    k = nc.dram_tensor("k", [BATCH, 128, SEQ], F32, kind="ExternalInput").ap()
    v = nc.dram_tensor("v", [BATCH, 128, SEQ], F32, kind="ExternalInput").ap()
    w = nc.dram_tensor("w", [128, 1], F32, kind="ExternalInput").ap()
    u = nc.dram_tensor("u", [128, 1], F32, kind="ExternalInput").ap()
    o = nc.dram_tensor("o", [BATCH, 128, SEQ], F32, kind="ExternalOutput").ap()

    ch = Chain()
    cx = Ctx(nc, ch)
    kb = cx.sbuf("kb", [128, HS], F32)
    vb = cx.sbuf("vb", [128, HS], F32)
    ek = cx.sbuf("ek", [128, HS], F32)
    ekv = cx.sbuf("ekv", [128, HS], F32)
    dbc = cx.sbuf("dbc", [128, HS], F32)
    ns = cx.sbuf("ns", [128, HS], F32)
    ds = cx.sbuf("ds", [128, HS], F32)
    wd = cx.sbuf("wd", [128, 1], F32)
    eu = cx.sbuf("eu", [128, 1], F32)
    cn = cx.sbuf("cn", [128, 1], F32)
    cd = cx.sbuf("cd", [128, 1], F32)

    ch.dma_n("sync", 2, lambda p: [
        p.dma_start(out=wd[:], in_=w[:, :]),
        p.dma_start(out=eu[:], in_=u[:, :]),
    ])
    ch.comp("scalar", lambda p: p.activation(wd[:], wd[:], AF.Exp))
    ch.comp("vector", lambda p: p.tensor_scalar_mul(wd[:], wd[:], -1.0))
    ch.comp("scalar", lambda p: p.activation(wd[:], wd[:], AF.Exp))
    ch.comp("scalar", lambda p: p.activation(eu[:], eu[:], AF.Exp))
    ch.comp("vector", lambda p: p.memset(dbc[:], 1.0))
    ch.comp("vector", lambda p: p.tensor_scalar_mul(dbc[:], dbc[:], wd[:]))

    for b in range(BATCH):
        for hh in range(2):
            s0 = hh * HS
            ch.dma_n("sync", 2, lambda p, b=b, s0=s0: [
                p.dma_start(out=kb[:], in_=k[b, :, s0:s0 + HS]),
                p.dma_start(out=vb[:], in_=v[b, :, s0:s0 + HS]),
            ])
            ch.comp("scalar", lambda p: p.activation(ek[:], kb[:], AF.Exp))
            ch.comp("vector",
                    lambda p: p.tensor_tensor(ekv[:], ek[:], vb[:], ALU.mult))

            def e_scan(p, hh=hh):
                ini_n = 0.0 if hh == 0 else cn[:]
                ini_d = 0.0 if hh == 0 else cd[:]
                p.tensor_tensor_scan(ns[:], dbc[:], ekv[:], ini_n, ALU.mult,
                                     ALU.add)
                return p.tensor_tensor_scan(ds[:], dbc[:], ek[:], ini_d,
                                            ALU.mult, ALU.add)

            ch.comp("vector", e_scan)

            def e_fin(p, hh=hh):
                if hh == 0:
                    p.tensor_scalar_mul(kb[:, 0:1], ekv[:, 0:1], eu[:])
                    p.tensor_scalar_mul(vb[:, 0:1], ek[:, 0:1], eu[:])
                else:
                    p.scalar_tensor_tensor(kb[:, 0:1], ekv[:, 0:1], eu[:],
                                           cn[:], ALU.mult, ALU.add)
                    p.scalar_tensor_tensor(vb[:, 0:1], ek[:, 0:1], eu[:],
                                           cd[:], ALU.mult, ALU.add)
                p.scalar_tensor_tensor(kb[:, 1:HS], ekv[:, 1:HS], eu[:],
                                       ns[:, 0:HS - 1], ALU.mult, ALU.add)
                p.scalar_tensor_tensor(vb[:, 1:HS], ek[:, 1:HS], eu[:],
                                       ds[:, 0:HS - 1], ALU.mult, ALU.add)
                p.tensor_copy(cn[:], ns[:, HS - 1:HS])
                p.tensor_copy(cd[:], ds[:, HS - 1:HS])
                p.reciprocal(vb[:], vb[:])
                return p.tensor_tensor(ek[:], kb[:], vb[:], ALU.mult)

            ch.comp("vector", e_fin)
            ch.dma_n("sync", 1, lambda p, b=b, s0=s0: [
                p.dma_start(out=o[b, :, s0:s0 + HS], in_=ek[:])])

    ch.build_with_clear(nc)
    cx.close()
    return nc


def build_r3():
    nc = bass.Bass("TRN2", target_bir_lowering=False, debug=False)
    wk = nc.dram_tensor("wkv", [D_MODEL, HSEQ], F32, kind="ExternalInput").ap()
    sr = nc.dram_tensor("sr", [D_MODEL, HSEQ], F32, kind="ExternalInput").ap()
    x2 = nc.dram_tensor("x2", [D_MODEL, HSEQ], F32, kind="ExternalInput").ap()
    wot = nc.dram_tensor("wot", [D_MODEL, D_MODEL], F32,
                         kind="ExternalInput").ap()
    y = nc.dram_tensor("y", [D_MODEL, HSEQ], F32, kind="ExternalOutput").ap()

    ch = Chain()
    cx = Ctx(nc, ch)
    wt = cx.sbuf("wt", [128, NCHUNK, TT], F32)
    st = cx.sbuf("st", [128, NCHUNK, TT], F32)
    x2t = cx.sbuf("x2t", [128, NCHUNK, TT], F32)
    rb = cx.sbuf("rb", [128, NCHUNK, TT], BF16)
    wos = cx.sbuf("wos", [128, NCHUNK, D_MODEL], BF16)
    yt = cx.sbuf("yt", [128, NCHUNK, TT], F32)
    yb = cx.sbuf("yb", [128, NCHUNK, TT], BF16)
    sq = cx.sbuf("sq", [128, NCHUNK, TT], BF16)
    rows = cx.sbuf("rows", [1, 4 * TT], F32)
    outt = cx.sbuf("outt", [128, NCHUNK, TT], F32)
    psm = cx.psum("psm", [128, TT])
    psA = cx.psum("psA", [128, TT])
    psB = cx.psum("psB", [128, TT])
    psA2 = cx.psum("psA2", [128, TT])
    psB2 = cx.psum("psB2", [128, TT])
    make_common_consts(cx)

    ch.dma_n("gpsimd", 1, lambda p: [p.dma_start(out=wos[:], in_=fmview(wot))])

    for t in range(NTILE):
        t0 = t * TT
        ch.dma_n("sync", 3, lambda p, t0=t0: [
            p.dma_start(out=wt[:], in_=fmview(wk)[:, :, t0:t0 + TT]),
            p.dma_start(out=st[:], in_=fmview(sr)[:, :, t0:t0 + TT]),
            p.dma_start(out=x2t[:], in_=fmview(x2)[:, :, t0:t0 + TT]),
        ])
        ch.comp("vector", lambda p: p.tensor_tensor(
            rb[:, :, :], wt[:, :, :], st[:, :, :], ALU.mult))
        for o in range(NCHUNK):
            def e_mm(p, o=o):
                last = None
                for c in range(NCHUNK):
                    last = p.matmul(psm[:, :],
                                    wos[:, c, o * 128:(o + 1) * 128],
                                    rb[:, c, :], start=(c == 0),
                                    stop=(c == NCHUNK - 1))
                return last

            ch.comp("tensor", e_mm, relax=True)

            def e_ev(p, o=o):
                p.tensor_copy(yt[:, o, :], psm[:])
                return p.tensor_copy(yb[:, o, :], yt[:, o, :])

            ch.comp("vector", e_ev)
        emit_ln(cx, yt, TT, sq, yb, psA, psB, psA2, psB2, rows, need_xb=False)
        emit_ln_apply(cx, yt, TT, psA2, psB2, outt, add_to=x2t)
        ch.dma_n("sync", 1, lambda p, t0=t0: [
            p.dma_start(out=fmview(y)[:, :, t0:t0 + TT], in_=outt[:])])

    ch.build_with_clear(nc)
    cx.close()
    return nc


def _run_l1_attn(x2_fm, w):
    nc1 = build_r1()
    wkt = _c(w["L1_attn_Wk"].T.astype(np.float32))
    wvt = _c(w["L1_attn_Wv"].T.astype(np.float32))
    wrt = _c(w["L1_attn_Wr"].T.astype(np.float32))
    mk = _scale_tiles(w["L1_attn_mix_k"])
    mv = _scale_tiles(w["L1_attn_mix_v"])
    mr = _scale_tiles(w["L1_attn_mix_r"])
    maps = []
    for core in range(8):
        b, hh = core // 2, core % 2
        xs = np.zeros((D_MODEL, 1 + HSEQ), np.float32)
        xs[:, 1:] = x2_fm[b, :, hh * HSEQ:(hh + 1) * HSEQ]
        if hh == 1:
            xs[:, 0] = x2_fm[b, :, HSEQ - 1]
        hkeep = np.full((128, 1), float(hh), np.float32)
        maps.append({"x": _c(xs), "hkeep": hkeep, "wkt": wkt, "wvt": wvt,
                     "wrt": wrt, "mixk": mk, "mixv": mv, "mixr": mr})
    res = run_bass_kernel_spmd(nc1, maps, core_ids=list(range(8)))

    kf = np.empty((BATCH, D_MODEL, SEQ), np.float32)
    vf = np.empty_like(kf)
    srf = np.empty_like(kf)
    for core in range(8):
        b, hh = core // 2, core % 2
        sl = slice(hh * HSEQ, (hh + 1) * HSEQ)
        kf[b, :, sl] = res.results[core]["k"]
        vf[b, :, sl] = res.results[core]["v"]
        srf[b, :, sl] = res.results[core]["sr"]

    nc2 = build_wkv()
    maps = []
    for core in range(8):
        cs = slice(core * 128, (core + 1) * 128)
        maps.append({
            "k": _c(kf[:, cs, :]), "v": _c(vf[:, cs, :]),
            "w": _c(w["L1_attn_w"][cs].reshape(128, 1).astype(np.float32)),
            "u": _c(w["L1_attn_u"][cs].reshape(128, 1).astype(np.float32)),
        })
    res = run_bass_kernel_spmd(nc2, maps, core_ids=list(range(8)))
    wkvf = np.empty((BATCH, D_MODEL, SEQ), np.float32)
    for core in range(8):
        cs = slice(core * 128, (core + 1) * 128)
        wkvf[:, cs, :] = res.results[core]["o"]

    nc3 = build_r3()
    wot = _c(w["L1_attn_Wo"].T.astype(np.float32))
    maps = []
    for core in range(8):
        b, hh = core // 2, core % 2
        sl = slice(hh * HSEQ, (hh + 1) * HSEQ)
        maps.append({"wkv": _c(wkvf[b, :, sl]), "sr": _c(srf[b, :, sl]),
                     "x2": _c(x2_fm[b, :, sl]), "wot": wot})
    res = run_bass_kernel_spmd(nc3, maps, core_ids=list(range(8)))
    out = np.empty_like(x2_fm)
    for core in range(8):
        b, hh = core // 2, core % 2
        out[b, :, hh * HSEQ:(hh + 1) * HSEQ] = res.results[core]["y"]
    return out


def build_a1():
    nc = bass.Bass("TRN2", target_bir_lowering=False, debug=False)
    x = nc.dram_tensor("x", [D_MODEL, HSEQ], F32, kind="ExternalInput").ap()
    wqt = nc.dram_tensor("wqt", [D_MODEL, 3072], F32, kind="ExternalInput").ap()
    qo = nc.dram_tensor("q", [D_MODEL, HSEQ], F32, kind="ExternalOutput").ap()
    ko = nc.dram_tensor("k", [D_MODEL, HSEQ], F32, kind="ExternalOutput").ap()
    vo = nc.dram_tensor("v", [D_MODEL, HSEQ], F32, kind="ExternalOutput").ap()

    ch = Chain()
    cx = Ctx(nc, ch)
    xt = cx.sbuf("xt", [128, NCHUNK, TT], F32)
    sq = cx.sbuf("sq", [128, NCHUNK, TT], BF16)
    xb = cx.sbuf("xb", [128, NCHUNK, TT], BF16)
    ub = cx.sbuf("ub", [128, NCHUNK, TT], BF16)
    wqs = cx.sbuf("wqs", [128, NCHUNK, 3072], BF16)
    rows = cx.sbuf("rows", [1, 4 * TT], F32)
    t0b = cx.sbuf("t0b", [128, TT], F32)
    t1b = cx.sbuf("t1b", [128, TT], F32)
    evt = cx.sbuf("evt", [128, TT], F32)
    psA = cx.psum("psA", [128, TT])
    psB = cx.psum("psB", [128, TT])
    psA2 = cx.psum("psA2", [128, TT])
    psB2 = cx.psum("psB2", [128, TT])
    psm = cx.psum("psm", [128, TT])
    make_common_consts(cx)

    ch.dma_n("gpsimd", 1, lambda p: [p.dma_start(out=wqs[:], in_=fmview(wqt))])
    outs = (qo, ko, vo)
    for t in range(NTILE):
        t0 = t * TT
        ch.dma_n("sync", 1, lambda p, t0=t0: [
            p.dma_start(out=xt[:], in_=fmview(x)[:, :, t0:t0 + TT])])
        emit_ln(cx, xt, TT, sq, xb, psA, psB, psA2, psB2, rows)
        emit_ln_apply(cx, xt, TT, psA2, psB2, ub)
        for o in range(24):
            def e_mm(p, o=o):
                for c in range(NCHUNK):
                    p.matmul(psm[:, :], wqs[:, c, o * 128:(o + 1) * 128],
                             ub[:, c, :], start=(c == 0), stop=(c == NCHUNK - 1))

            ch.comp("tensor", e_mm, relax=True)
            kind = o // 8
            if kind < 2:
                def e_elu(p, kind=kind):
                    p.tensor_scalar_min(t0b[:], psm[:], 0.0)
                    p.tensor_scalar_max(t1b[:], psm[:], 0.0)

                ch.comp("vector", e_elu, relax=True)
                ch.comp("scalar", lambda p: p.activation(t0b[:], t0b[:],
                                                         AF.Exp))

                def e_sum(p, kind=kind):
                    p.tensor_tensor(evt[:], t0b[:], t1b[:], ALU.add)
                    if kind == 0:
                        p.tensor_scalar_mul(evt[:], evt[:], 0.125)

                ch.comp("vector", e_sum)
            else:
                ch.comp("vector", lambda p: p.tensor_copy(evt[:], psm[:]))
            ch.dma_n("sync", 1, lambda p, o=o, t0=t0, kind=kind: [
                p.dma_start(out=outs[kind][(o % 8) * 128:(o % 8 + 1) * 128,
                            t0:t0 + TT], in_=evt[:])])

    ch.build_with_clear(nc)
    cx.close()
    return nc


def build_a2():
    CH = 256
    NCH = SEQ // CH
    nc = bass.Bass("TRN2", target_bir_lowering=False, debug=False)
    qf = nc.dram_tensor("qf", [512, SEQ], F32, kind="ExternalInput").ap()
    kf = nc.dram_tensor("kf", [512, SEQ], F32, kind="ExternalInput").ap()
    vt = nc.dram_tensor("vt", [8, SEQ, 64], F32, kind="ExternalInput").ap()
    kt = nc.dram_tensor("kt", [8, SEQ, 64], F32, kind="ExternalInput").ap()
    oo = nc.dram_tensor("o", [512, SEQ], F32, kind="ExternalOutput").ap()

    ch = Chain()
    cx = Ctx(nc, ch)
    qh = cx.sbuf("qh", [64, SEQ], BF16)
    kh = cx.sbuf("kh", [64, SEQ], BF16)
    vtp = [cx.sbuf(f"vtp{i}", [128, 65], BF16) for i in range(2)]
    ktm = [cx.sbuf(f"ktm{i}", [128, 64], BF16) for i in range(2)]
    atT = [cx.sbuf(f"atT{i}", [128, CH], BF16) for i in range(2)]
    maskf = [cx.sbuf(f"maskf{i}", [128, CH], F32) for i in range(2)]
    maskb = [cx.sbuf(f"maskb{i}", [128, CH], BF16) for i in range(2)]
    S_f = cx.sbuf("S_f", [64, 65], F32)
    S_b = cx.sbuf("S_b", [64, 65], BF16)
    dd = cx.sbuf("dd", [1, CH], F32)
    bc = cx.sbuf("bc", [64, CH], F32)
    osb = cx.sbuf("osb", [64, SEQ], F32)
    ps_t = [cx.psum(f"ps_t{i}", [128, CH]) for i in range(2)]
    ps_num = cx.psum("ps_num", [128, 512])
    ps_bc = cx.psum("ps_bc", [64, CH])
    ps_S = cx.psum("ps_S", [64, 65])
    make_common_consts(cx)
    cx.ones_row64 = cx.sbuf("ones_row64", [1, 64], F32)

    def e_masks(p):
        p.memset(cx.ones_row64[:], 1.0)
        p.memset(maskf[0][:], 0.0)
        p.memset(maskf[1][:], 0.0)
        p.memset(vtp[0][:, 64:65], 1.0)
        p.memset(vtp[1][:, 64:65], 1.0)

    ch.comp("vector", e_masks)

    def e_sel(p):
        p.affine_select(maskf[0][:], maskf[0][:], [[-1, CH]], ALU.is_ge, 1.0,
                        base=-1, channel_multiplier=1)
        p.affine_select(maskf[1][:], maskf[1][:], [[-1, CH]], ALU.is_ge, 1.0,
                        base=127, channel_multiplier=1)

    ch.comp("gpsimd", e_sel)

    def e_mb(p):
        p.tensor_copy(maskb[0][:], maskf[0][:])
        p.tensor_copy(maskb[1][:], maskf[1][:])

    ch.comp("vector", e_mb)

    for h in range(8):
        ch.dma_n("gpsimd", 2, lambda p, h=h: [
            p.dma_start(out=qh[:], in_=qf[h * 64:(h + 1) * 64, :]),
            p.dma_start(out=kh[:], in_=kf[h * 64:(h + 1) * 64, :]),
        ])

        def e_S0(p):
            p.memset(S_f[:], 0.0)
            p.memset(S_b[:], 0.0)

        ch.comp("vector", e_S0, relax=True)

        for i in range(NCH):
            i0 = i * CH
            ch.dma_n("gpsimd", 4, lambda p, h=h, i0=i0: [
                p.dma_start(out=vtp[0][:, 0:64], in_=vt[h, i0:i0 + 128, :]),
                p.dma_start(out=vtp[1][:, 0:64],
                            in_=vt[h, i0 + 128:i0 + 256, :]),
                p.dma_start(out=ktm[0][:], in_=kt[h, i0:i0 + 128, :]),
                p.dma_start(out=ktm[1][:], in_=kt[h, i0 + 128:i0 + 256, :]),
            ])

            def e_attnT(p, i0=i0):
                p.matmul(ps_t[0][:], kh[:, i0:i0 + 128], qh[:, i0:i0 + CH],
                         start=True, stop=True)
                p.matmul(ps_t[1][:], kh[:, i0 + 128:i0 + 256],
                         qh[:, i0:i0 + CH], start=True, stop=True)

            ch.comp("tensor", e_attnT, relax=True)

            def e_mask(p):
                p.tensor_tensor(atT[0][:], ps_t[0][:], maskb[0][:], ALU.mult)
                p.tensor_tensor(atT[1][:], ps_t[1][:], maskb[1][:], ALU.mult)

            ch.comp("vector", e_mask, relax=True)

            def e_num(p, i0=i0):
                p.matmul(ps_num[0:65, 0:CH], vtp[0][:], atT[0][:],
                         start=True, stop=False)
                p.matmul(ps_num[0:65, 0:CH], vtp[1][:], atT[1][:],
                         start=False, stop=False)
                p.matmul(ps_num[0:65, 0:CH], S_b[:], qh[:, i0:i0 + CH],
                         start=False, stop=True)

            ch.comp("tensor", e_num, relax=True)

            def e_div(p):
                p.tensor_scalar_max(dd[:], ps_num[64:65, 0:CH], 1e-6)
                p.reciprocal(dd[:], dd[:])

            ch.comp("vector", e_div)

            ch.comp("tensor", lambda p: p.matmul(
                ps_bc[:, :], cx.ones_row64[:], dd[:], start=True, stop=True), relax=True)

            def e_out(p, i0=i0):
                p.tensor_copy(bc[:], ps_bc[:, :])
                p.tensor_tensor(osb[:, i0:i0 + CH], ps_num[0:64, 0:CH], bc[:],
                                ALU.mult)

            ch.comp("vector", e_out)

            def e_Supd(p):
                p.matmul(ps_S[:, :], ktm[0][:], vtp[0][:], start=True,
                         stop=False)
                p.matmul(ps_S[:, :], ktm[1][:], vtp[1][:], start=False,
                         stop=True)

            ch.comp("tensor", e_Supd, relax=True)

            def e_Sacc(p):
                p.tensor_tensor(S_f[:], S_f[:], ps_S[:, :], ALU.add)
                p.tensor_copy(S_b[:], S_f[:])

            ch.comp("vector", e_Sacc)

        ch.dma_n("sync", 1, lambda p, h=h: [
            p.dma_start(out=oo[h * 64:(h + 1) * 64, :], in_=osb[:])])

    ch.build_with_clear(nc)
    cx.close()
    return nc


def build_proj_ln_add():
    nc = bass.Bass("TRN2", target_bir_lowering=False, debug=False)
    ii = nc.dram_tensor("inp", [D_MODEL, HSEQ], F32, kind="ExternalInput").ap()
    xx = nc.dram_tensor("x", [D_MODEL, HSEQ], F32, kind="ExternalInput").ap()
    wt = nc.dram_tensor("wt", [D_MODEL, D_MODEL], F32,
                        kind="ExternalInput").ap()
    y = nc.dram_tensor("y", [D_MODEL, HSEQ], F32, kind="ExternalOutput").ap()

    ch = Chain()
    cx = Ctx(nc, ch)
    it = cx.sbuf("it", [128, NCHUNK, TT], BF16)
    xt = cx.sbuf("xt", [128, NCHUNK, TT], F32)
    ws = cx.sbuf("ws", [128, NCHUNK, D_MODEL], BF16)
    yt = cx.sbuf("yt", [128, NCHUNK, TT], F32)
    yb = cx.sbuf("yb", [128, NCHUNK, TT], BF16)
    sq = cx.sbuf("sq", [128, NCHUNK, TT], BF16)
    rows = cx.sbuf("rows", [1, 4 * TT], F32)
    outt = cx.sbuf("outt", [128, NCHUNK, TT], F32)
    psm = cx.psum("psm", [128, TT])
    psA = cx.psum("psA", [128, TT])
    psB = cx.psum("psB", [128, TT])
    psA2 = cx.psum("psA2", [128, TT])
    psB2 = cx.psum("psB2", [128, TT])
    make_common_consts(cx)

    ch.dma_n("gpsimd", 1, lambda p: [p.dma_start(out=ws[:], in_=fmview(wt))])
    for t in range(NTILE):
        t0 = t * TT
        ch.dma_n("gpsimd", 1, lambda p, t0=t0: [
            p.dma_start(out=it[:], in_=fmview(ii)[:, :, t0:t0 + TT])])
        ch.dma_n("sync", 1, lambda p, t0=t0: [
            p.dma_start(out=xt[:], in_=fmview(xx)[:, :, t0:t0 + TT])])
        for o in range(NCHUNK):
            def e_mm(p, o=o):
                for c in range(NCHUNK):
                    p.matmul(psm[:, :], ws[:, c, o * 128:(o + 1) * 128],
                             it[:, c, :], start=(c == 0),
                             stop=(c == NCHUNK - 1))

            ch.comp("tensor", e_mm, relax=True)

            def e_ev(p, o=o):
                p.tensor_copy(yt[:, o, :], psm[:])
                p.tensor_copy(yb[:, o, :], psm[:])

            ch.comp("vector", e_ev, relax=True)
        emit_ln(cx, yt, TT, sq, yb, psA, psB, psA2, psB2, rows, need_xb=False)
        emit_ln_apply(cx, yt, TT, psA2, psB2, outt, add_to=xt)
        ch.dma_n("sync", 1, lambda p, t0=t0: [
            p.dma_start(out=fmview(y)[:, :, t0:t0 + TT], in_=outt[:])])

    ch.build_with_clear(nc)
    cx.close()
    return nc


def _run_l0_attn(x_fm, w):
    nc1 = build_a1()
    wqt = _c(w["L0_attn_W_qkv"].T.astype(np.float32))
    maps = []
    for core in range(8):
        b, hh = core // 2, core % 2
        maps.append({"x": _c(x_fm[b, :, hh * HSEQ:(hh + 1) * HSEQ]),
                     "wqt": wqt})
    res = run_bass_kernel_spmd(nc1, maps, core_ids=list(range(8)))
    qg = np.empty((BATCH, D_MODEL, SEQ), np.float32)
    kg = np.empty_like(qg)
    vg = np.empty_like(qg)
    for core in range(8):
        b, hh = core // 2, core % 2
        sl = slice(hh * HSEQ, (hh + 1) * HSEQ)
        qg[b, :, sl] = res.results[core]["q"]
        kg[b, :, sl] = res.results[core]["k"]
        vg[b, :, sl] = res.results[core]["v"]

    nc2 = build_a2()
    maps = []
    for core in range(8):
        b, hg = core // 2, core % 2
        rs = slice(hg * 512, (hg + 1) * 512)
        vt = _c(vg[b, rs, :].reshape(8, 64, SEQ).transpose(0, 2, 1))
        ktm = _c(kg[b, rs, :].reshape(8, 64, SEQ).transpose(0, 2, 1))
        maps.append({"qf": _c(qg[b, rs, :]), "kf": _c(kg[b, rs, :]),
                     "vt": vt, "kt": ktm})
    res = run_bass_kernel_spmd(nc2, maps, core_ids=list(range(8)))
    og = np.empty((BATCH, D_MODEL, SEQ), np.float32)
    for core in range(8):
        b, hg = core // 2, core % 2
        og[b, hg * 512:(hg + 1) * 512, :] = res.results[core]["o"]

    nc3 = build_proj_ln_add()
    wot = _c(w["L0_attn_W_out"].T.astype(np.float32))
    maps = []
    for core in range(8):
        b, hh = core // 2, core % 2
        sl = slice(hh * HSEQ, (hh + 1) * HSEQ)
        maps.append({"inp": _c(og[b, :, sl]), "x": _c(x_fm[b, :, sl]),
                     "wt": wot})
    res = run_bass_kernel_spmd(nc3, maps, core_ids=list(range(8)))
    out = np.empty_like(x_fm)
    for core in range(8):
        b, hh = core // 2, core % 2
        out[b, :, hh * HSEQ:(hh + 1) * HSEQ] = res.results[core]["y"]
    return out


L0F_H = 1025
L0F_W = 512
L0F_NW = 10


def build_l0f(ln_in, fin_out):
    W = L0F_W
    nc = bass.Bass("TRN2", target_bir_lowering=False, debug=False)
    xs = nc.dram_tensor("xs", [D_MODEL, L0F_H + HSEQ], F32,
                        kind="ExternalInput").ap()
    wqkv = nc.dram_tensor("wqkv", [12, 128, NCHUNK, 128], F32,
                          kind="ExternalInput").ap()
    qsc = nc.dram_tensor("qsc", [128, 1], F32, kind="ExternalInput").ap()
    ksc = nc.dram_tensor("ksc", [128, 1], F32, kind="ExternalInput").ap()
    wo = nc.dram_tensor("wo", [512, D_MODEL], F32, kind="ExternalInput").ap()
    lg = nc.dram_tensor("lg", [128, NCHUNK], F32, kind="ExternalInput").ap()
    lb = nc.dram_tensor("lb", [128, NCHUNK], F32, kind="ExternalInput").ap()
    w1pt = nc.dram_tensor("w1pt", [44, 128, NCHUNK, 128], F32,
                          kind="ExternalInput").ap()
    b1p = nc.dram_tensor("b1p", [128, 44], F32, kind="ExternalInput").ap()
    w2pt = nc.dram_tensor("w2pt", [8, 128, 22, 128], F32,
                          kind="ExternalInput").ap()
    b2 = nc.dram_tensor("b2", [128, NCHUNK], F32, kind="ExternalInput").ap()
    prot = nc.dram_tensor("prot", [128, 128], F32, kind="ExternalInput").ap()
    tqc = nc.dram_tensor("tqc", [128, L0F_NW * W], F32,
                         kind="ExternalInput").ap()
    tqs = nc.dram_tensor("tqs", [128, L0F_NW * W], F32,
                         kind="ExternalInput").ap()
    tkc = nc.dram_tensor("tkc", [128, L0F_NW * W], F32,
                         kind="ExternalInput").ap()
    tks = nc.dram_tensor("tks", [128, L0F_NW * W], F32,
                         kind="ExternalInput").ap()
    amask = nc.dram_tensor("amask", [2, 4, 128, 1024], F32,
                           kind="ExternalInput").ap()
    bmask = nc.dram_tensor("bmask", [2, 8, 128, W], F32,
                           kind="ExternalInput").ap()
    ffnkeep = nc.dram_tensor("ffnkeep", [128, 1], F32,
                             kind="ExternalInput").ap()
    if fin_out:
        xo = nc.dram_tensor("xo", [D_MODEL, HSEQ], F32,
                            kind="ExternalInput").ap()
    y = nc.dram_tensor("y", [D_MODEL, HSEQ], F32, kind="ExternalOutput").ap()

    ch = Chain()
    cx = Ctx(nc, ch)
    xt = cx.sbuf("xt", [128, NCHUNK, W], F32)
    ut = cx.sbuf("ut", [128, NCHUNK, W], F32)
    xm = cx.sbuf("xm", [128, NCHUNK, W], F32)
    hs = cx.sbuf("hs", [128, NCHUNK, W], F32)
    ucar = cx.sbuf("ucar", [128, NCHUNK, 1], F32)
    xmcar = cx.sbuf("xmcar", [128, NCHUNK, 1], F32)
    xh1 = cx.sbuf("xh1", [128, NCHUNK, 1], F32)
    rows = cx.sbuf("rows", [1, 4 * W], F32)
    sq = cx.sbuf("sq", [128, NCHUNK, W], BF16)
    xb = cx.sbuf("xb", [128, NCHUNK, W], BF16)
    wqs = [cx.sbuf(f"wqs{i}", [128, NCHUNK, 128], BF16) for i in range(2)]
    wos = cx.sbuf("wos", [128, 4, D_MODEL], BF16)
    w1s = [cx.sbuf(f"w1s{i}", [128, NCHUNK, 128], BF16) for i in range(2)]
    w2s = cx.sbuf("w2s", [128, 22, 128], BF16)
    qscs = cx.sbuf("qscs", [128, 1], F32)
    kscs = cx.sbuf("kscs", [128, 1], F32)
    lgs = cx.sbuf("lgs", [128, NCHUNK], F32)
    lbs = cx.sbuf("lbs", [128, NCHUNK], F32)
    b1s = cx.sbuf("b1s", [128, 44], F32)
    b2s = cx.sbuf("b2s", [128, NCHUNK], F32)
    fks = cx.sbuf("fks", [128, 1], F32)
    t4 = [cx.sbuf(f"t4_{i}", [128, W], F32) for i in range(4)]
    ams = cx.sbuf("ams", [128, 4, 1024], BF16)
    bms = cx.sbuf("bms", [128, 16, W], BF16)
    ident = cx.sbuf("ident", [128, 128], F32)
    blkdiag = cx.sbuf("blkdiag", [128, 128], BF16)
    blkrot = cx.sbuf("blkrot", [128, 128], BF16)
    onesq = cx.sbuf("onesq", [128, 128], BF16)
    ones64 = cx.sbuf("ones64", [1, 64], F32)
    qr = cx.sbuf("qr", [128, 4, W], BF16)
    kr = cx.sbuf("kr", [128, 4, 2 * W], BF16)
    vat = cx.sbuf("vat", [128, 8, 8, 65], BF16)
    evq = cx.sbuf("evq", [128, W], F32)
    qhat = cx.sbuf("qhat", [128, W], F32)
    rstb = cx.sbuf("rstb", [128, W], F32)
    rt1 = cx.sbuf("rt1", [128, W], F32)
    sqs = cx.sbuf("sqs", [128, W], BF16)
    nmrow = cx.sbuf("nmrow", [128, W], BF16)
    nmcol = cx.sbuf("nmcol", [128, 1], F32)
    n0 = cx.sbuf("n0", [128, 1], F32)
    atT = cx.sbuf("atT", [128, W], BF16)
    dd = cx.sbuf("dd", [1, W], F32)
    bc = cx.sbuf("bc", [64, W], F32)
    ow = cx.sbuf("ow", [128, 4, W], BF16)
    hgl = cx.sbuf("hgl", [128, 22, W], BF16)
    sa = evq
    gg = qhat
    P0 = cx.psum("P0", [128, 1024])
    P1 = cx.psum("P1", [128, W])
    P2 = cx.psum("P2", [128, W])
    P3 = cx.psum("P3", [128, W])
    P4 = cx.psum("P4", [128, W])
    P5 = cx.psum("P5", [128, W])
    make_common_consts(cx)

    ch.dma_n("gpsimd", 6, lambda p: [
        p.dma_start(out=wos[:], in_=wo.rearrange("(c p) o -> p c o", p=128)),
        p.dma_start(out=ams[:], in_=amask.rearrange("s i p j -> p (s i) j")[:, 4:8, :]),
        p.dma_start(out=bms[:], in_=bmask.rearrange("s r p j -> p (s r) j")),
        p.dma_start(out=qscs[:], in_=qsc[:, :]),
        p.dma_start(out=kscs[:], in_=ksc[:, :]),
        p.dma_start(out=blkrot[:], in_=prot[:, :]),
    ])
    ch.dma_n("sync", 6, lambda p: [
        p.dma_start(out=lgs[:], in_=lg[:, :]),
        p.dma_start(out=lbs[:], in_=lb[:, :]),
        p.dma_start(out=b1s[:], in_=b1p[:, :]),
        p.dma_start(out=b2s[:], in_=b2[:, :]),
        p.dma_start(out=fks[:], in_=ffnkeep[:, :]),
        p.dma_start(out=xh1[:], in_=fmview(xs)[:, :, 0:1],
                    allow_slow_non_contiguous=True),
    ])

    def e_consts(p):
        p.memset(ident[:], 0.0)
        p.memset(blkdiag[:], 0.0)
        p.memset(blkdiag[0:64, 0:64], 1.0)
        p.memset(blkdiag[64:128, 64:128], 1.0)
        p.memset(onesq[:], 1.0)
        p.memset(ones64[:], 1.0)
        p.memset(kr[:], 0.0)
        p.memset(vat[:], 0.0)

    ch.comp("vector", e_consts)
    ch.comp("gpsimd", lambda p: p.affine_select(
        ident[:], ident[:], [[-1, 128]], ALU.not_equal, 1.0, base=0,
        channel_multiplier=1))

    if ln_in:
        emit_ln(cx, xh1, 1, sq, xb, P2, P3, P1, P5, rows)
        emit_ln_apply(cx, xh1, 1, P1, P5, ucar)
    else:
        ch.comp("vector", lambda p: p.tensor_copy(ucar[:], xh1[:]))

    for w in range(L0F_NW):
        c0 = 1 + w * W
        ch.dma_n("sync", 1, lambda p, c0=c0: [
            p.dma_start(out=xt[:], in_=fmview(xs)[:, :, c0:c0 + W])])
        if ln_in:
            emit_ln(cx, xt, W, sq, xb, P2, P3, P1, P5, rows)
            emit_ln_apply(cx, xt, W, P1, P5, ut)
        else:
            ch.comp("vector", lambda p: p.tensor_copy(ut[:], xt[:]))

        def e_shift(p):
            for c in range(NCHUNK):
                if c < 4:
                    p.tensor_copy(xb[:, c, :], ut[:, c, :])
                else:
                    p.tensor_copy(xb[:, c, 0:1], ucar[:, c, 0:1])
                    p.tensor_copy(xb[:, c, 1:W], ut[:, c, 0:W - 1])

        ch.comp("vector", e_shift, relax=True)
        ch.comp("vector",
                lambda p: p.tensor_copy(ucar[:], ut[:, :, W - 1:W]))

        def e_ring(p):
            p.tensor_copy(kr[:, :, 0:W], kr[:, :, W:2 * W])
            p.tensor_copy(vat[:, 0:4, :, :], vat[:, 4:8, :, :])

        ch.comp("vector", e_ring, relax=True)
        ch.dma_n("sync", 4, lambda p, w=w: [
            p.dma_start(out=t4[0][:], in_=tqc[:, w * W:(w + 1) * W]),
            p.dma_start(out=t4[1][:], in_=tqs[:, w * W:(w + 1) * W]),
            p.dma_start(out=t4[2][:], in_=tkc[:, w * W:(w + 1) * W]),
            p.dma_start(out=t4[3][:], in_=tks[:, w * W:(w + 1) * W]),
        ])

        for o in range(8):
            kind = o // 4
            pt = o % 4
            wb = wqs[o % 2]
            ch.dma_n("gpsimd", 1,
                     lambda p, o=o, wb=wb: [p.dma_start(out=wb[:],
                                                        in_=wqkv[o])])

            def e_mm(p, wb=wb):
                for c in range(NCHUNK):
                    p.matmul(P4[:, :], wb[:, c, :], xb[:, c, :],
                             start=(c == 0), stop=(c == NCHUNK - 1))

            ch.comp("tensor", e_mm, relax=True)
            ch.comp("vector", lambda p: p.tensor_copy(evq[:], P4[:, :]))
            ch.comp("scalar",
                    lambda p: p.activation(sqs[:], evq[:], AF.Square))
            ch.comp("tensor", lambda p: p.matmul(
                P1[:, :], blkdiag[:], sqs[:], start=True, stop=True), relax=True)

            def e_rst(p):
                p.tensor_scalar_max(rstb[:], P1[:, :], 1e-24)
                p.reciprocal(rstb[:], rstb[:])

            ch.comp("vector", e_rst)
            ch.comp("scalar",
                    lambda p: p.activation(rstb[:], rstb[:], AF.Sqrt))

            def e_hat(p, kind=kind):
                scs = qscs if kind == 0 else kscs
                p.tensor_tensor(qhat[:], evq[:], rstb[:], ALU.mult)
                p.tensor_scalar_mul(qhat[:], qhat[:], scs[:])
                p.tensor_copy(sqs[:], qhat[:])

            ch.comp("vector", e_hat)
            ch.comp("tensor", lambda p: p.matmul(
                P2[:, :], blkrot[:], sqs[:], start=True, stop=True), relax=True)

            def e_rot(p, kind=kind, pt=pt):
                tc, ts_ = (t4[0], t4[1]) if kind == 0 else (t4[2], t4[3])
                p.tensor_tensor(rt1[:], qhat[:], tc[:], ALU.mult)
                p.tensor_tensor(qhat[:], P2[:, :], ts_[:], ALU.mult)
                if kind == 0:
                    p.tensor_tensor(qr[:, pt, :], rt1[:], qhat[:], ALU.add)
                else:
                    p.tensor_tensor(kr[:, pt, W:2 * W], rt1[:], qhat[:],
                                    ALU.add)

            ch.comp("vector", e_rot)

        for o in range(8, 12):
            wb = wqs[o % 2]
            ch.dma_n("gpsimd", 1,
                     lambda p, o=o, wb=wb: [p.dma_start(out=wb[:],
                                                        in_=wqkv[o])])

            def e_mm(p, wb=wb):
                for c in range(NCHUNK):
                    p.matmul(P4[:, :], wb[:, c, :], xb[:, c, :],
                             start=(c == 0), stop=(c == NCHUNK - 1))

            ch.comp("tensor", e_mm, relax=True)
            ch.comp("vector", lambda p: p.tensor_copy(evq[:], P4[:, :]))
            for hh in range(2):
                h = (o - 8) * 2 + hh
                rb = hh * 64
                for jj in range(4):
                    def e_tr(p, rb=rb, jj=jj):
                        return p.transpose(
                            P3[0:128, 0:64],
                            evq[rb:rb + 64, jj * 128:(jj + 1) * 128],
                            ident[rb:rb + 64, rb:rb + 64])

                    ch.comp("tensor", e_tr, relax=True)

                    def e_vat(p, h=h, jj=jj):
                        p.tensor_copy(vat[:, 4 + jj, h, 0:64],
                                      P3[0:128, 0:64])
                        p.memset(vat[:, 4 + jj, h, 64:65], 1.0)

                    ch.comp("vector", e_vat, relax=True)

        if w == 0:
            continue

        mset = 0 if w == 2 else 1
        if w == 2:
            ch.dma_n("gpsimd", 1, lambda p: [p.dma_start(
                out=ams[:],
                in_=amask.rearrange("s i p j -> p (s i) j")[:, 0:4, :])])
        if w == 3:
            ch.dma_n("gpsimd", 1, lambda p: [p.dma_start(
                out=ams[:],
                in_=amask.rearrange("s i p j -> p (s i) j")[:, 4:8, :])])
        for h in range(8):
            pt, rb = h // 2, (h % 2) * 64
            for it in range(4):
                def e_sim(p, pt=pt, rb=rb, it=it):
                    p.matmul(
                        P0[:, 0:W],
                        qr[rb:rb + 64, pt, it * 128:(it + 1) * 128],
                        kr[rb:rb + 64, pt, 0:W], start=True, stop=True)
                    return p.matmul(
                        P0[:, W:2 * W],
                        qr[rb:rb + 64, pt, it * 128:(it + 1) * 128],
                        kr[rb:rb + 64, pt, W:2 * W], start=True, stop=True)

                ch.comp("tensor", e_sim, relax=True)

                def e_am(p, it=it, mset=mset):
                    p.tensor_tensor(rt1[:], P0[:, 0:W],
                                    ams[:, it, 0:W], ALU.add)
                    p.reduce_max(n0[:], rt1[:], axis=AX.X, negate=True)
                    p.tensor_tensor(rt1[:], P0[:, W:2 * W],
                                    ams[:, it, W:2 * W], ALU.add)
                    p.reduce_max(nmcol[:], rt1[:], axis=AX.X, negate=True)
                    p.tensor_tensor(nmcol[:], nmcol[:], n0[:], ALU.min)

                ch.comp("vector", e_am)
                ch.comp("tensor", lambda p: p.matmul(
                    P3[0:1, 0:128], nmcol[:], ident[:], start=True,
                    stop=True), relax=True)

                def e_nm(p, it=it):
                    p.tensor_copy(nmrow[0:1, it * 128:(it + 1) * 128],
                                  P3[0:1, 0:128])
                    p.tensor_copy(nmrow[64:65, it * 128:(it + 1) * 128],
                                  P3[0:1, 0:128])

                ch.comp("vector", e_nm, relax=True)

            for js in range(8):
                def e_simT(p, pt=pt, rb=rb, js=js):
                    p.matmul(P1[:, :],
                             kr[rb:rb + 64, pt, js * 128:(js + 1) * 128],
                             qr[rb:rb + 64, pt, :], start=True, stop=False)
                    p.matmul(P1[:, :], onesq[rb:rb + 1, :],
                             nmrow[rb:rb + 1, :], start=False, stop=True)

                ch.comp("tensor", e_simT, relax=True)
                ch.comp("scalar",
                        lambda p: p.activation(atT[:], P1[:, :], AF.Exp))

                def e_bm(p, js=js, mset=mset):
                    r, jj = js // 4, js % 4
                    p.tensor_tensor(atT[:], atT[:],
                                    bms[:, mset * 8 + r * 4 + jj, :],
                                    ALU.mult)

                ch.comp("vector", e_bm)

                def e_av(p, h=h, js=js):
                    return p.matmul(P2[0:65, :], vat[:, js, h, :], atT[:],
                                    start=(js == 0), stop=(js == 7))

                ch.comp("tensor", e_av, relax=True)

            ch.comp("vector", lambda p: p.reciprocal(dd[:], P2[64:65, :]))
            ch.comp("tensor", lambda p: p.matmul(
                P5[0:64, :], ones64[:], dd[:], start=True, stop=True), relax=True)

            def e_o(p, h=h):
                p.tensor_copy(bc[:], P5[0:64, :])
                p.tensor_tensor(ow[(h % 2) * 64:(h % 2) * 64 + 64,
                                h // 2, :], P2[0:64, :], bc[:], ALU.mult)

            ch.comp("vector", e_o)

        for o in range(NCHUNK):
            def e_wo(p, o=o):
                for c in range(4):
                    p.matmul(P4[:, :], wos[:, c, o * 128:(o + 1) * 128],
                             ow[:, c, :], start=(c == 0), stop=(c == 3))

            ch.comp("tensor", e_wo, relax=True)
            ch.comp("vector", lambda p, o=o: p.tensor_tensor(
                xm[:, o, :], ut[:, o, :], P4[:, :], ALU.add))

        if w == 1:
            ch.comp("vector",
                    lambda p: p.tensor_copy(xmcar[:], xm[:, :, W - 1:W]))
            continue

        if w == 2:
            def e_fk(p):
                for c in range(NCHUNK):
                    p.tensor_scalar_mul(xmcar[:, c, 0:1], xmcar[:, c, 0:1],
                                        fks[:])

            ch.comp("vector", e_fk, relax=True)

        def e_shift2(p):
            for c in range(NCHUNK):
                if c < 4:
                    p.tensor_copy(hs[:, c, :], xm[:, c, :])
                else:
                    p.tensor_copy(hs[:, c, 0:1], xmcar[:, c, 0:1])
                    p.tensor_copy(hs[:, c, 1:W], xm[:, c, 0:W - 1])

        ch.comp("vector", e_shift2, relax=True)
        ch.comp("vector",
                lambda p: p.tensor_copy(xmcar[:], xm[:, :, W - 1:W]))
        emit_ln(cx, hs, W, sq, xb, P2, P3, P1, P5, rows)

        def e_ln2(p):
            for c in range(NCHUNK):
                p.tensor_tensor(rt1[:], hs[:, c, :], P1[:, :], ALU.mult)
                p.scalar_tensor_tensor(rt1[:], rt1[:], 1.0, P5[:, :],
                                       ALU.mult, ALU.add)
                p.tensor_scalar(sq[:, c, :], rt1[:], lgs[:, c:c + 1],
                                lbs[:, c:c + 1], ALU.mult, ALU.add)

        ch.comp("vector", e_ln2)

        for j in range(22):
            ch.dma_n("gpsimd", 2, lambda p, j=j: [
                p.dma_start(out=w1s[0][:], in_=w1pt[j]),
                p.dma_start(out=w1s[1][:], in_=w1pt[22 + j]),
            ])

            def e_w1a(p):
                for c in range(NCHUNK):
                    p.matmul(P4[:, :], w1s[0][:, c, :], sq[:, c, :],
                             start=(c == 0), stop=(c == NCHUNK - 1))

            ch.comp("tensor", e_w1a, relax=True)
            ch.comp("vector", lambda p, j=j: p.tensor_scalar_add(
                sa[:], P4[:, :], b1s[:, j:j + 1]))

            def e_w1g(p):
                for c in range(NCHUNK):
                    p.matmul(P5[:, :], w1s[1][:, c, :], sq[:, c, :],
                             start=(c == 0), stop=(c == NCHUNK - 1))

            ch.comp("tensor", e_w1g, relax=True)
            ch.comp("scalar", lambda p, j=j: p.activation(
                gg[:], P5[:, :], AF.Gelu, bias=b1s[:, 22 + j:23 + j],
                scale=1.0))
            ch.comp("vector", lambda p, j=j: p.tensor_tensor(
                hgl[:, j, :], sa[:], gg[:], ALU.mult))

        for o in range(NCHUNK):
            ch.dma_n("gpsimd", 1,
                     lambda p, o=o: [p.dma_start(out=w2s[:], in_=w2pt[o])])

            def e_w2(p):
                for c in range(22):
                    p.matmul(P4[:, :], w2s[:, c, :], hgl[:, c, :],
                             start=(c == 0), stop=(c == 21))

            ch.comp("tensor", e_w2, relax=True)

            def e_res(p, o=o):
                p.tensor_scalar_add(rt1[:], P4[:, :], b2s[:, o:o + 1])
                p.tensor_tensor(xm[:, o, :], xm[:, o, :], rt1[:], ALU.add)

            ch.comp("vector", e_res)

        t0 = (w - 2) * W
        if fin_out:
            ch.dma_n("sync", 1, lambda p, t0=t0: [
                p.dma_start(out=hs[:], in_=fmview(xo)[:, :, t0:t0 + W])])
            emit_ln(cx, xm, W, sq, xb, P2, P3, P1, P5, rows)
            emit_ln_apply(cx, xm, W, P1, P5, xm, tmp=rt1, add_to=hs)

        ch.dma_n("sync", 1, lambda p, t0=t0: [
            p.dma_start(out=fmview(y)[:, :, t0:t0 + W], in_=xm[:])])

    ch.build_with_clear(nc)
    cx.close()
    return nc


def _l0f_tile_w(wmat, nout):
    Kd = wmat.shape[0]
    return _c(wmat.reshape(Kd // 128, 128, nout, 128).transpose(2, 1, 0, 3)
              .astype(np.float32))


def _l0f_consts(hh):
    W, DH = 512, 64
    t0g = hh * HSEQ
    pos = np.maximum(np.arange(t0g - 1024, t0g + HSEQ), 0).astype(np.float64)
    half_idx = np.arange(0, DH, 2, dtype=np.float64)
    inv_freq = 1.0 / 10000.0 ** (half_idx / DH)
    freqs = pos[:, None] * inv_freq[None, :]
    freqs = np.concatenate([freqs, freqs], axis=-1)
    base = (half_idx + 0.4 * DH) / (1.4 * DH)
    power = (pos - SEQ // 2) / 256.0
    scale = base[None, :] ** power[:, None]
    scale = np.concatenate([scale, scale], axis=-1)
    cos, sin = np.cos(freqs), np.sin(freqs)
    tqc = (cos * scale).T
    tqs = (sin * scale).T
    tkc = (cos / scale).T
    tks = (sin / scale).T

    def tile2(tt):
        return _c(np.concatenate([tt, tt], axis=0).astype(np.float32))

    iL = np.arange(W)[None, :]
    pj = np.arange(128)[:, None]
    am = np.zeros((2, 4, 128, 1024), np.float32)
    bm = np.zeros((2, 8, 128, W), np.float32)
    for it in range(4):
        iloc = it * 128 + np.arange(128)[:, None]
        jprev = np.arange(W)[None, :]
        jown = np.arange(W)[None, :]
        am[1, it, :, 0:W] = np.where(iloc <= jprev, 0.0, -1e30)
        am[1, it, :, W:] = np.where(iloc >= jown, 0.0, -1e30)
    for jj in range(4):
        jloc = jj * 128 + pj
        bm[1, jj] = (iL <= jloc).astype(np.float32)
        bm[1, 4 + jj] = (iL >= jloc).astype(np.float32)
    am[0] = am[1]
    bm[0] = bm[1]
    if hh == 0:
        am[0, :, :, 0:W] = -1e30
        bm[0, 0:4] = 0.0
    P = np.zeros((64, 64), np.float64)
    for d in range(32):
        P[d, d + 32] = -1.0
        P[d + 32, d] = 1.0
    PT = P.T
    prot = np.zeros((128, 128), np.float32)
    prot[0:64, 0:64] = PT
    prot[64:128, 64:128] = PT
    return (tile2(tqc), tile2(tqs), tile2(tkc), tile2(tks), _c(am), _c(bm),
            prot)


def _run_l0_ffn(x_fm, w):
    H = L0F_H
    QK = 8.0
    consts = [_l0f_consts(0), _l0f_consts(1)]
    layer_inp = {}
    for li in range(2):
        wqkv = w["L0_ffn_wqkv"][li].astype(np.float64)
        w1 = w["L0_ffn_w1"][li].astype(np.float64)
        b1 = w["L0_ffn_b1"][li].astype(np.float64)
        w2 = w["L0_ffn_w2"][li].astype(np.float64)
        w1p = np.zeros((1024, 5632))
        w1p[:, 0:2730] = w1[:, 0:2730]
        w1p[:, 2816:5546] = w1[:, 2730:]
        b1p = np.zeros(5632)
        b1p[0:2730] = b1[0:2730]
        b1p[2816:5546] = b1[2730:]
        w2p = np.zeros((2816, 1024))
        w2p[0:2730] = w2
        layer_inp[li] = {
            "wqkv": _l0f_tile_w(wqkv, 12),
            "qsc": _c(np.tile(w["L0_ffn_q_scale"][li] * QK, 2)
                      .reshape(128, 1).astype(np.float32)),
            "ksc": _c(np.tile(w["L0_ffn_k_scale"][li], 2)
                      .reshape(128, 1).astype(np.float32)),
            "wo": _c(w["L0_ffn_wo"][li].astype(np.float32)),
            "lg": _scale_tiles(w["L0_ffn_ln_g"][li]),
            "lb": _scale_tiles(w["L0_ffn_ln_b"][li]),
            "w1pt": _l0f_tile_w(w1p, 44),
            "b1p": _c(b1p.reshape(44, 128).T.astype(np.float32)),
            "w2pt": _l0f_tile_w(w2p, 8),
            "b2": _scale_tiles(w["L0_ffn_b2"][li]),
        }

    def halo_slices(src_fm):
        out = []
        for core in range(8):
            b, hh = core // 2, core % 2
            t0g = hh * HSEQ
            xs = np.zeros((D_MODEL, H + HSEQ), np.float32)
            lo = t0g - H
            if lo < 0:
                xs[:, -lo:] = src_fm[b, :, 0:t0g + HSEQ]
            else:
                xs[:] = src_fm[b, :, lo:t0g + HSEQ]
            out.append(_c(xs))
        return out

    cur = x_fm
    for li in range(2):
        ln_in = li == 0
        fin = li == 1
        ncl = build_l0f(ln_in, fin)
        xss = halo_slices(cur)
        maps = []
        for core in range(8):
            b, hh = core // 2, core % 2
            tqc, tqs, tkc, tks, am, bm, prot = consts[hh]
            m = dict(layer_inp[li])
            m["xs"] = xss[core]
            m["tqc"] = tqc
            m["tqs"] = tqs
            m["tkc"] = tkc
            m["tks"] = tks
            m["amask"] = am
            m["bmask"] = bm
            m["prot"] = prot
            m["ffnkeep"] = np.full((128, 1), 0.0 if hh == 0 else 1.0,
                                   np.float32)
            if fin:
                m["xo"] = _c(x_fm[b, :, hh * HSEQ:(hh + 1) * HSEQ])
            maps.append(m)
        res = run_bass_kernel_spmd(ncl, maps, core_ids=list(range(8)))
        nxt = np.empty((BATCH, D_MODEL, SEQ), np.float32)
        for core in range(8):
            b, hh = core // 2, core % 2
            nxt[b, :, hh * HSEQ:(hh + 1) * HSEQ] = res.results[core]["y"]
        cur = nxt
    return cur


FUSED = True


def kernel(**inp):
    if FUSED:
        return kernel_fused(inp)
    x = np.asarray(inp["L0_attn_x"], np.float32)
    x_fm = _c(x.transpose(0, 2, 1))
    x_fm = _run_l0_attn(x_fm, inp)
    x_fm = _run_l0_ffn(x_fm, inp)
    x_fm = _run_l1_attn(x_fm, inp)
    x_fm = _run_l1_ffn(x_fm, inp)
    return _c(x_fm.transpose(0, 2, 1))


REPL_PAIRS = [[0, 1], [2, 3], [4, 5], [6, 7]]


def emit_pair_exchange(cx, pref, src_ap, agout_ap, rows, cols, selA, selB,
                       write_fn, tile_cols=None):
    ch = cx.ch
    nc = cx.nc
    mk = cx.mark()
    tc = cols if tile_cols is None else tile_cols
    blkA = cx.sbuf(pref + "blkA", [128, tc], F32)
    blkB = cx.sbuf(pref + "blkB", [128, tc], F32)
    cmb = cx.sbuf(pref + "cmb", [128, tc], F32)
    ch.collective(lambda p: p.collective_compute(
        "AllGather", ALU.bypass, replica_groups=REPL_PAIRS,
        ins=[src_ap], outs=[agout_ap]))
    slow = cols < 64
    rg = min(128, rows)
    for r0 in range(0, rows, rg):
        for c0 in range(0, cols, tc):
            cw = min(tc, cols - c0)
            ch.dma_n("sync", 2, lambda p, r0=r0, c0=c0, cw=cw: [
                p.dma_start(out=blkA[0:rg, 0:cw],
                            in_=agout_ap[r0:r0 + rg, c0:c0 + cw],
                            allow_slow_non_contiguous=slow),
                p.dma_start(out=blkB[0:rg, 0:cw],
                            in_=agout_ap[rows + r0:rows + r0 + rg,
                                         c0:c0 + cw],
                            allow_slow_non_contiguous=slow),
            ])

            def e_cmb(p, cw=cw):
                p.tensor_scalar_mul(cmb[0:rg, 0:cw], blkA[0:rg, 0:cw],
                                    selA[0:rg, 0:1])
                p.scalar_tensor_tensor(cmb[0:rg, 0:cw], blkB[0:rg, 0:cw],
                                       selB[0:rg, 0:1], cmb[0:rg, 0:cw],
                                       ALU.mult, ALU.add)

            ch.comp("vector", e_cmb)
            write_fn(ch, cmb, r0, c0, cw, slow)
    cx.release(mk)


def emit_halo_fill(cx, pref, stream_ap, halo_ap, hcols, selA, selB, tmp_ag):
    nc = cx.nc
    src = nc.dram_tensor(pref + "agsrc", [D_MODEL, hcols], F32,
                         kind="Internal").ap()
    ch = cx.ch
    mk = cx.mark()
    bt = cx.sbuf(pref + "bt", [128, NCHUNK, hcols], F32)
    slow = hcols < 64
    ch.dma_n("sync", 1, lambda p: [
        p.dma_start(out=bt[:], in_=fmview(stream_ap)[:, :,
                    HSEQ - hcols:HSEQ], allow_slow_non_contiguous=slow)])
    ch.dma_n("sync", 1, lambda p: [
        p.dma_start(out=fmview(src)[:, :, 0:hcols], in_=bt[:],
                    allow_slow_non_contiguous=slow)])
    cx.release(mk)

    def wr(ch2, cmb, r0, c0, cw, slow2):
        ch2.dma_n("sync", 1, lambda p: [
            p.dma_start(out=halo_ap[r0:r0 + 128, c0:c0 + cw],
                        in_=cmb[:, 0:cw], allow_slow_non_contiguous=slow2)])

    emit_pair_exchange(cx, pref, src, tmp_ag, D_MODEL, hcols, selA, selB, wr,
                       tile_cols=min(hcols, 1025))


def emit_a1f(cx, T):
    ch = cx.ch
    mk = cx.mark()
    x, wqt = T["x"], T["wqt"]
    qo, ko, vo, kvtm = T["q"], T["k"], T["v"], T["kvtm"]
    xt = cx.sbuf("xt", [128, NCHUNK, TT], F32)
    sq = cx.sbuf("sq", [128, NCHUNK, TT], BF16)
    xb = cx.sbuf("xb", [128, NCHUNK, TT], BF16)
    ub = cx.sbuf("ub", [128, NCHUNK, TT], BF16)
    wqs = cx.sbuf("wqs", [128, NCHUNK, 3072], BF16)
    rows = cx.sbuf("rows", [1, 4 * TT], F32)
    t0b = cx.sbuf("t0b", [128, TT], F32)
    t1b = cx.sbuf("t1b", [128, TT], F32)
    evt = cx.sbuf("evt", [128, TT], F32)
    evb = cx.sbuf("evb", [128, TT], BF16)
    psA = cx.psum("psA", [128, TT])
    psB = cx.psum("psB", [128, TT])
    psA2 = cx.psum("psA2", [128, TT])
    psB2 = cx.psum("psB2", [128, TT])
    psm = cx.psum("psm", [128, TT])

    ch.dma_n("gpsimd", 1, lambda p: [p.dma_start(out=wqs[:], in_=fmview(wqt))])
    outs = (qo, ko, vo)
    for t in range(NTILE):
        t0 = t * TT
        ch.dma_n("sync", 1, lambda p, t0=t0: [
            p.dma_start(out=xt[:], in_=fmview(x)[:, :, t0:t0 + TT])])
        emit_ln(cx, xt, TT, sq, xb, psA, psB, psA2, psB2, rows)
        emit_ln_apply(cx, xt, TT, psA2, psB2, ub)
        for o in range(24):
            def e_mm(p, o=o):
                for c in range(NCHUNK):
                    p.matmul(psm[:, :], wqs[:, c, o * 128:(o + 1) * 128],
                             ub[:, c, :], start=(c == 0),
                             stop=(c == NCHUNK - 1))

            ch.comp("tensor", e_mm, relax=True)
            kind = o // 8
            if kind < 2:
                def e_elu(p):
                    p.tensor_scalar_min(t0b[:], psm[:], 0.0)
                    p.tensor_scalar_max(t1b[:], psm[:], 0.0)

                ch.comp("vector", e_elu, relax=True)
                ch.comp("scalar",
                        lambda p: p.activation(t0b[:], t0b[:], AF.Exp))

                def e_sum(p, kind=kind):
                    p.tensor_tensor(evt[:], t0b[:], t1b[:], ALU.add)
                    if kind == 0:
                        p.tensor_scalar_mul(evt[:], evt[:], 0.125)

                ch.comp("vector", e_sum)
            else:
                ch.comp("vector", lambda p: p.tensor_copy(evt[:], psm[:]))
            ch.dma_n("sync", 1, lambda p, o=o, t0=t0, kind=kind: [
                p.dma_start(out=outs[kind][(o % 8) * 128:(o % 8 + 1) * 128,
                            t0:t0 + TT], in_=evt[:])])
        for s in range(4):
            stile = t * 4 + s
            for cb in range(4):
                def e_kv(p, s=s, cb=cb):
                    for c in range(NCHUNK):
                        p.matmul(psm[:, :], ub[:, c, s * 128:(s + 1) * 128],
                                 wqs[:, c, 1024 + cb * 512:1024 +
                                     (cb + 1) * 512],
                                 start=(c == 0), stop=(c == NCHUNK - 1))

                ch.comp("tensor", e_kv, relax=True)
                if cb < 2:
                    def e_elu2(p):
                        p.tensor_scalar_min(t0b[:], psm[:], 0.0)
                        p.tensor_scalar_max(t1b[:], psm[:], 0.0)

                    ch.comp("vector", e_elu2, relax=True)
                    ch.comp("scalar",
                            lambda p: p.activation(t0b[:], t0b[:], AF.Exp))
                    ch.comp("vector", lambda p: p.tensor_tensor(
                        evb[:], t0b[:], t1b[:], ALU.add))
                else:
                    ch.comp("vector", lambda p: p.tensor_copy(evb[:], psm[:]))
                ch.dma_n("sync", 1, lambda p, stile=stile, cb=cb: [
                    p.dma_start(out=kvtm[stile, :, cb * 512:(cb + 1) * 512],
                                in_=evb[:])])
    cx.release(mk)


def emit_a2f(cx, T):
    CH = 256
    NCHL = HSEQ // CH
    ch = cx.ch
    nc = cx.nc
    mk = cx.mark()
    qd, kd, kvtm, oo = T["q"], T["k"], T["kvtm"], T["o"]
    selA, selB = T["selA"], T["selB"]
    agS_in = nc.dram_tensor("a2_agin", [64, 16 * 65], F32,
                            kind="Internal").ap()
    agS_out = nc.dram_tensor("a2_agout", [128, 16 * 65], F32,
                             kind="Internal").ap()
    qhd = qd.rearrange("(h d) t -> d h t", d=64)
    khd = kd.rearrange("(h d) t -> d h t", d=64)
    ohd = oo.rearrange("(h d) t -> d h t", d=64)
    qc = cx.sbuf("qc", [64, 16, CH], BF16)
    kc = cx.sbuf("kc", [64, 16, CH], BF16)
    kvsb = cx.sbuf("kvsb", [128, 2, 2048], BF16)
    atT = [cx.sbuf(f"atT{i}", [128, CH], BF16) for i in range(2)]
    maskb = [cx.sbuf(f"maskb{i}", [128, CH], BF16) for i in range(2)]
    maskf = cx.sbuf("maskf", [128, CH], F32)
    Schk = cx.sbuf("Schk", [64, 16, NCHL, 65], F32)
    Stot = cx.sbuf("Stot", [64, 16, 65], F32)
    Srun = cx.sbuf("Srun", [64, 16, 65], F32)
    Srb = cx.sbuf("Srb", [64, 16, 65], BF16)
    dd = cx.sbuf("dd", [1, CH], F32)
    bc = cx.sbuf("bc", [64, CH], F32)
    carc = cx.sbuf("carc", [64, CH], F32)
    carD = cx.sbuf("carD", [1, CH], F32)
    numt = cx.sbuf("numt", [64, CH], F32)
    ofm = cx.sbuf("ofm", [64, 16, CH], F32)
    ones_row64 = cx.sbuf("ones_row64", [1, 64], F32)
    ps_t = [cx.psum(f"ps_t{i}", [128, CH]) for i in range(2)]
    ps_num = cx.psum("ps_num", [64, CH])
    ps_den = cx.psum("ps_den", [1, CH])
    ps_bc = cx.psum("ps_bc", [64, CH])
    ps_S = cx.psum("ps_S", [64, 128])
    ps_car = cx.psum("ps_car", [64, CH])
    ps_carD = cx.psum("ps_carD", [1, CH])

    def e_consts(p):
        p.memset(ones_row64[:], 1.0)
        p.memset(maskf[:], 0.0)
        p.memset(Stot[:], 0.0)

    ch.comp("vector", e_consts)
    ch.comp("gpsimd", lambda p: p.affine_select(
        maskf[:], maskf[:], [[-1, CH]], ALU.is_ge, 1.0, base=-1,
        channel_multiplier=1))

    def e_mb(p):
        p.tensor_copy(maskb[0][:], maskf[:])
        p.memset(maskf[:], 0.0)

    ch.comp("vector", e_mb)
    ch.comp("gpsimd", lambda p: p.affine_select(
        maskf[:], maskf[:], [[-1, CH]], ALU.is_ge, 1.0, base=127,
        channel_multiplier=1))
    ch.comp("vector", lambda p: p.tensor_copy(maskb[1][:], maskf[:]))

    for i in range(NCHL):
        ch.dma_n("sync", 2, lambda p, i=i: [
            p.dma_start(out=kvsb[:, 0, :], in_=kvtm[2 * i]),
            p.dma_start(out=kvsb[:, 1, :], in_=kvtm[2 * i + 1]),
        ])
        for h in range(16):
            def e_S(p, h=h):
                for m in range(2):
                    p.matmul(ps_S[0:64, 0:64], kvsb[:, m, h * 64:h * 64 + 64],
                             kvsb[:, m, 1024 + h * 64:1024 + h * 64 + 64],
                             start=(m == 0), stop=(m == 1))
                for m in range(2):
                    p.matmul(ps_S[0:64, 64:65],
                             kvsb[:, m, h * 64:h * 64 + 64],
                             cx.ones_col_bf[:], start=(m == 0), stop=(m == 1))

            ch.comp("tensor", e_S, relax=True)

            def e_acc(p, i=i, h=h):
                p.tensor_copy(Schk[:, h, i, :], ps_S[0:64, 0:65])
                p.tensor_tensor(Stot[:, h, :], Stot[:, h, :],
                                ps_S[0:64, 0:65], ALU.add)

            ch.comp("vector", e_acc, relax=True)

    mkx = cx.mark()
    sbt = cx.sbuf("sbt", [64, 16 * 65], F32)
    ch.comp("vector", lambda p: p.tensor_copy(
        sbt[:], Stot[:].rearrange("p c e -> p (c e)")))
    ch.dma_n("sync", 1, lambda p: [p.dma_start(out=agS_in[:, :], in_=sbt[:])])

    def wrS(ch2, cmb, r0, c0, cw, slow2):
        def e(p):
            p.tensor_copy(Srun[:].rearrange("p c e -> p (c e)")[:,
                          c0:c0 + cw], cmb[0:64, 0:cw])

        ch2.comp("vector", e)

    emit_pair_exchange(cx, "a2x", agS_in, agS_out, 64, 16 * 65, selA, selB,
                       wrS)
    cx.release(mkx)
    ch.comp("vector", lambda p: p.tensor_copy(Srb[:], Srun[:]))

    for i in range(NCHL):
        i0 = i * CH
        ch.dma_n("sync", 2, lambda p, i=i: [
            p.dma_start(out=kvsb[:, 0, :], in_=kvtm[2 * i]),
            p.dma_start(out=kvsb[:, 1, :], in_=kvtm[2 * i + 1]),
        ])
        ch.dma_n("gpsimd", 2, lambda p, i0=i0: [
            p.dma_start(out=qc[:], in_=qhd[:, :, i0:i0 + CH]),
            p.dma_start(out=kc[:], in_=khd[:, :, i0:i0 + CH]),
        ])
        for h in range(16):
            def e_attnT(p, h=h):
                p.matmul(ps_t[0][:], kc[:, h, 0:128], qc[:, h, :],
                         start=True, stop=True)
                p.matmul(ps_t[1][:], kc[:, h, 128:256], qc[:, h, :],
                         start=True, stop=True)

            ch.comp("tensor", e_attnT, relax=True)

            def e_mask(p):
                p.tensor_tensor(atT[0][:], ps_t[0][:], maskb[0][:], ALU.mult)
                p.tensor_tensor(atT[1][:], ps_t[1][:], maskb[1][:], ALU.mult)

            ch.comp("vector", e_mask, relax=True)

            def e_num(p, h=h):
                for m in range(2):
                    p.matmul(ps_num[:, :],
                             kvsb[:, m, 1024 + h * 64:1024 + h * 64 + 64],
                             atT[m][:], start=(m == 0), stop=(m == 1))
                for m in range(2):
                    p.matmul(ps_den[:, :], cx.ones_col_bf[:], atT[m][:],
                             start=(m == 0), stop=(m == 1))
                p.matmul(ps_car[:, :], Srb[:, h, 0:64], qc[:, h, :],
                         start=True, stop=True)
                p.matmul(ps_carD[:, :], Srb[:, h, 64:65], qc[:, h, :],
                         start=True, stop=True)

            ch.comp("tensor", e_num, relax=True)

            def e_div(p):
                p.tensor_copy(carc[:, :], ps_car[:, :])
                p.tensor_copy(carD[:, :], ps_carD[:, :])
                p.tensor_tensor(dd[:], ps_den[:, :], carD[:, :], ALU.add)
                p.tensor_scalar_max(dd[:], dd[:], 1e-6)
                p.reciprocal(dd[:], dd[:])
                p.tensor_tensor(numt[:], ps_num[:, :], carc[:, :], ALU.add)

            ch.comp("vector", e_div)
            ch.comp("tensor", lambda p: p.matmul(
                ps_bc[:, :], ones_row64[:], dd[:], start=True, stop=True), relax=True)

            def e_out(p, h=h):
                p.tensor_copy(bc[:], ps_bc[:, :])
                p.tensor_tensor(ofm[:, h, :], numt[:], bc[:], ALU.mult)

            ch.comp("vector", e_out)

        def e_Supd(p, i=i):
            p.tensor_tensor(Srun[:, :, :], Srun[:, :, :], Schk[:, :, i, :],
                            ALU.add)
            p.tensor_copy(Srb[:], Srun[:])

        ch.comp("vector", e_Supd)
        ch.dma_n("sync", 1, lambda p, i0=i0: [
            p.dma_start(out=ohd[:, :, i0:i0 + CH], in_=ofm[:])])
    cx.release(mk)


def emit_projf(cx, T):
    ch = cx.ch
    mk = cx.mark()
    ii, xx, wt, y = T["inp"], T["x"], T["wt"], T["y"]
    it = cx.sbuf("it", [128, NCHUNK, TT], BF16)
    xt = cx.sbuf("xt", [128, NCHUNK, TT], F32)
    ws = cx.sbuf("ws", [128, NCHUNK, D_MODEL], BF16)
    yt = cx.sbuf("yt", [128, NCHUNK, TT], F32)
    yb = cx.sbuf("yb", [128, NCHUNK, TT], BF16)
    sq = cx.sbuf("sq", [128, NCHUNK, TT], BF16)
    rows = cx.sbuf("rows", [1, 4 * TT], F32)
    outt = cx.sbuf("outt", [128, NCHUNK, TT], F32)
    psm = cx.psum("psm", [128, TT])
    psA = cx.psum("psA", [128, TT])
    psB = cx.psum("psB", [128, TT])
    psA2 = cx.psum("psA2", [128, TT])
    psB2 = cx.psum("psB2", [128, TT])

    ch.dma_n("gpsimd", 1, lambda p: [p.dma_start(out=ws[:], in_=fmview(wt))])
    for t in range(NTILE):
        t0 = t * TT
        ch.dma_n("gpsimd", 1, lambda p, t0=t0: [
            p.dma_start(out=it[:], in_=fmview(ii)[:, :, t0:t0 + TT])])
        ch.dma_n("sync", 1, lambda p, t0=t0: [
            p.dma_start(out=xt[:], in_=fmview(xx)[:, :, t0:t0 + TT])])
        for o in range(NCHUNK):
            def e_mm(p, o=o):
                for c in range(NCHUNK):
                    p.matmul(psm[:, :], ws[:, c, o * 128:(o + 1) * 128],
                             it[:, c, :], start=(c == 0),
                             stop=(c == NCHUNK - 1))

            ch.comp("tensor", e_mm, relax=True)

            def e_ev(p, o=o):
                p.tensor_copy(yt[:, o, :], psm[:])
                p.tensor_copy(yb[:, o, :], psm[:])

            ch.comp("vector", e_ev, relax=True)
        emit_ln(cx, yt, TT, sq, yb, psA, psB, psA2, psB2, rows, need_xb=False)
        emit_ln_apply(cx, yt, TT, psA2, psB2, outt, add_to=xt)
        ch.dma_n("sync", 1, lambda p, t0=t0: [
            p.dma_start(out=fmview(y)[:, :, t0:t0 + TT], in_=outt[:])])
    cx.release(mk)


def emit_r1f(cx, T):
    ch = cx.ch
    mk = cx.mark()
    x, halo1 = T["x"], T["halo1"]
    ko, vo, sro = T["k"], T["v"], T["sr"]
    wkt, wvt, wrt = T["wkt"], T["wvt"], T["wrt"]
    mixk, mixv, mixr = T["mixk"], T["mixv"], T["mixr"]
    xt = cx.sbuf("xt", [128, NCHUNK, TT], F32)
    xh = cx.sbuf("xh", [128, NCHUNK, 1], F32)
    ut = cx.sbuf("ut", [128, NCHUNK, TT], F32)
    ucol = cx.sbuf("ucol", [128, NCHUNK, 1], F32)
    dcol = cx.sbuf("dcol", [128, NCHUNK, TT], F32)
    sq = cx.sbuf("sq", [128, NCHUNK, TT], BF16)
    xb = cx.sbuf("xb", [128, NCHUNK, TT], BF16)
    mixs = [cx.sbuf(f"mix{i}", [128, NCHUNK], F32) for i in range(3)]
    wss = [cx.sbuf(f"w{i}", [128, NCHUNK, D_MODEL], BF16) for i in range(3)]
    mxb = cx.sbuf("mxb", [128, NCHUNK, TT], BF16)
    evt = cx.sbuf("evt", [128, TT], F32)
    rows = cx.sbuf("rows", [1, 4 * TT], F32)
    psA = cx.psum("psA", [128, TT])
    psB = cx.psum("psB", [128, TT])
    psA2 = cx.psum("psA2", [128, TT])
    psB2 = cx.psum("psB2", [128, TT])
    psm = cx.psum("psm", [128, TT])

    ch.dma_n("gpsimd", 6, lambda p: [
        p.dma_start(out=wss[0][:], in_=fmview(wkt)),
        p.dma_start(out=wss[1][:], in_=fmview(wvt)),
        p.dma_start(out=wss[2][:], in_=fmview(wrt)),
        p.dma_start(out=mixs[0][:], in_=mixk[:, :]),
        p.dma_start(out=mixs[1][:], in_=mixv[:, :]),
        p.dma_start(out=mixs[2][:], in_=mixr[:, :]),
    ])
    ch.dma_n("sync", 1,
             lambda p: [p.dma_start(out=xh[:], in_=fmview(halo1)[:, :, 0:1],
                                    allow_slow_non_contiguous=True)])
    emit_ln(cx, xh, 1, sq, xb, psA, psB, psA2, psB2, rows)
    emit_ln_apply(cx, xh, 1, psA2, psB2, ucol)

    outs = (ko, vo, sro)
    for t in range(NTILE):
        t0 = t * TT
        ch.dma_n("sync", 1, lambda p, t0=t0: [
            p.dma_start(out=xt[:], in_=fmview(x)[:, :, t0:t0 + TT])])
        emit_ln(cx, xt, TT, sq, xb, psA, psB, psA2, psB2, rows)
        emit_ln_apply(cx, xt, TT, psA2, psB2, ut)

        def e_d(p):
            for c in range(NCHUNK):
                p.tensor_tensor(dcol[:, c, 0:1], ut[:, c, 0:1],
                                ucol[:, c, 0:1], ALU.subtract)
                p.tensor_tensor(dcol[:, c, 1:TT], ut[:, c, 1:TT],
                                ut[:, c, 0:TT - 1], ALU.subtract)

        ch.comp("vector", e_d, relax=True)

        for mi in range(3):
            def e_mix(p, mi=mi):
                for c in range(NCHUNK):
                    p.scalar_tensor_tensor(mxb[:, c, 0:1], dcol[:, c, 0:1],
                                           mixs[mi][:, c:c + 1],
                                           ucol[:, c, 0:1], ALU.mult, ALU.add)
                    p.scalar_tensor_tensor(
                        mxb[:, c, 1:TT], dcol[:, c, 1:TT],
                        mixs[mi][:, c:c + 1], ut[:, c, 0:TT - 1], ALU.mult,
                        ALU.add)

            ch.comp("vector", e_mix, relax=True)
            for o in range(NCHUNK):
                def e_mm(p, mi=mi, o=o):
                    for c in range(NCHUNK):
                        p.matmul(psm[:, :],
                                 wss[mi][:, c, o * 128:(o + 1) * 128],
                                 mxb[:, c, :], start=(c == 0),
                                 stop=(c == NCHUNK - 1))

                ch.comp("tensor", e_mm, relax=True)
                if mi < 2:
                    ch.comp("vector",
                            lambda p: p.tensor_copy(evt[:], psm[:]))
                else:
                    ch.comp("scalar", lambda p: p.activation(evt[:], psm[:],
                                                             AF.Sigmoid))
                ch.dma_n("sync", 1, lambda p, mi=mi, o=o, t0=t0: [
                    p.dma_start(out=outs[mi][o * 128:(o + 1) * 128,
                                t0:t0 + TT], in_=evt[:])])

        def e_carry(p):
            return p.tensor_copy(ucol[:, :, 0:1], ut[:, :, TT - 1:TT])

        ch.comp("vector", e_carry)
    cx.release(mk)


def emit_wkvf(cx, T):
    ch = cx.ch
    nc = cx.nc
    mk = cx.mark()
    kd, vd, oo = T["k"], T["v"], T["wkv"]
    wv, uv = T["rw"], T["ru"]
    selA, selB = T["selA"], T["selB"]
    agin = nc.dram_tensor("wkv_agin", [128, 16], F32, kind="Internal").ap()
    agout = nc.dram_tensor("wkv_agout", [256, 16], F32, kind="Internal").ap()
    kb = cx.sbuf("kb", [128, HSEQ], F32)
    vb = cx.sbuf("vb", [128, HSEQ], F32)
    ek = cx.sbuf("ek", [128, HSEQ], F32)
    ekv = cx.sbuf("ekv", [128, HSEQ], F32)
    dbc = cx.sbuf("dbc", [128, HSEQ], F32)
    ns = cx.sbuf("ns", [128, HSEQ], F32)
    ds = cx.sbuf("ds", [128, HSEQ], F32)
    zb = cx.sbuf("zb", [128, HSEQ], F32)
    pw = cx.sbuf("pw", [128, HSEQ], F32)
    wd = cx.sbuf("wd", [128, NCHUNK], F32)
    eu = cx.sbuf("eu", [128, NCHUNK], F32)
    lasts = cx.sbuf("lasts", [128, 16], F32)
    carry = cx.sbuf("carry", [128, 16], F32)

    ch.dma_n("sync", 2, lambda p: [
        p.dma_start(out=wd[:], in_=wv[:, :]),
        p.dma_start(out=eu[:], in_=uv[:, :]),
    ])
    ch.comp("scalar", lambda p: p.activation(wd[:], wd[:], AF.Exp))
    ch.comp("vector", lambda p: p.tensor_scalar_mul(wd[:], wd[:], -1.0))
    ch.comp("scalar", lambda p: p.activation(wd[:], wd[:], AF.Exp))
    ch.comp("scalar", lambda p: p.activation(eu[:], eu[:], AF.Exp))
    ch.comp("vector", lambda p: p.memset(zb[:], 0.0))

    def scan_pass(c, final):
        ch.dma_n("sync", 2, lambda p, c=c: [
            p.dma_start(out=kb[:], in_=fmview(kd)[:, c, :]),
            p.dma_start(out=vb[:], in_=fmview(vd)[:, c, :]),
        ])
        ch.comp("scalar", lambda p: p.activation(ek[:], kb[:], AF.Exp))

        def e_pre(p, c=c):
            p.tensor_tensor(ekv[:], ek[:], vb[:], ALU.mult)
            p.memset(dbc[:], 1.0)
            p.tensor_scalar_mul(dbc[:], dbc[:], wd[:, c:c + 1])
            p.tensor_tensor_scan(ns[:], dbc[:], ekv[:], 0.0, ALU.mult,
                                 ALU.add)
            p.tensor_tensor_scan(ds[:], dbc[:], ek[:], 0.0, ALU.mult,
                                 ALU.add)

        ch.comp("vector", e_pre)
        if not final:
            def e_last(p, c=c):
                p.tensor_copy(lasts[:, 2 * c:2 * c + 1],
                              ns[:, HSEQ - 1:HSEQ])
                p.tensor_copy(lasts[:, 2 * c + 1:2 * c + 2],
                              ds[:, HSEQ - 1:HSEQ])

            ch.comp("vector", e_last, relax=True)
            return

        def e_fin(p, c=c):
            p.tensor_tensor_scan(pw[:], dbc[:], zb[:], 1.0, ALU.mult, ALU.add)
            p.scalar_tensor_tensor(ns[:], pw[:], carry[:, 2 * c:2 * c + 1],
                                   ns[:], ALU.mult, ALU.add)
            p.scalar_tensor_tensor(ds[:], pw[:], carry[:, 2 * c + 1:2 * c + 2],
                                   ds[:], ALU.mult, ALU.add)
            p.scalar_tensor_tensor(kb[:, 0:1], ekv[:, 0:1], eu[:, c:c + 1],
                                   carry[:, 2 * c:2 * c + 1], ALU.mult,
                                   ALU.add)
            p.scalar_tensor_tensor(vb[:, 0:1], ek[:, 0:1], eu[:, c:c + 1],
                                   carry[:, 2 * c + 1:2 * c + 2], ALU.mult,
                                   ALU.add)
            p.scalar_tensor_tensor(kb[:, 1:HSEQ], ekv[:, 1:HSEQ],
                                   eu[:, c:c + 1], ns[:, 0:HSEQ - 1],
                                   ALU.mult, ALU.add)
            p.scalar_tensor_tensor(vb[:, 1:HSEQ], ek[:, 1:HSEQ],
                                   eu[:, c:c + 1], ds[:, 0:HSEQ - 1],
                                   ALU.mult, ALU.add)
            p.reciprocal(vb[:], vb[:])
            p.tensor_tensor(ek[:], kb[:], vb[:], ALU.mult)

        ch.comp("vector", e_fin)
        ch.dma_n("sync", 1, lambda p, c=c: [
            p.dma_start(out=fmview(oo)[:, c, :], in_=ek[:])])

    for c in range(NCHUNK):
        scan_pass(c, False)
    ch.dma_n("sync", 1, lambda p: [p.dma_start(out=agin[:, :], in_=lasts[:])])

    def wrC(ch2, cmb, r0, c0, cw, slow2):
        ch2.comp("vector",
                 lambda p: p.tensor_copy(carry[:, c0:c0 + cw],
                                         cmb[:, 0:cw]))

    emit_pair_exchange(cx, "wkvx", agin, agout, 128, 16, selA, selB, wrC)
    for c in range(NCHUNK):
        scan_pass(c, True)
    cx.release(mk)


def emit_l0ff(cx, T, ln_in, fin_out):
    W = L0F_W
    ch = cx.ch
    nc = cx.nc
    mk = cx.mark()
    xs = T["xs"]
    wqkv = T["wqkv"]
    qsc = T["qsc"]
    ksc = T["ksc"]
    wo = T["wo"]
    lg = T["lg"]
    lb = T["lb"]
    w1pt = T["w1pt"]
    b1p = T["b1p"]
    w2pt = T["w2pt"]
    b2 = T["b2"]
    prot = T["prot"]
    tqc = T["tqc"]
    tqs = T["tqs"]
    tkc = T["tkc"]
    tks = T["tks"]
    amask = T["amask"]
    bmask = T["bmask"]
    ffnkeep = T["ffnkeep"]
    halo = T["halo"]
    if fin_out:
        xo = T["xo"]
    y = T["y"]

    xt = cx.sbuf("xt", [128, NCHUNK, W], F32)
    ut = cx.sbuf("ut", [128, NCHUNK, W], F32)
    xm = cx.sbuf("xm", [128, NCHUNK, W], F32)
    hs = cx.sbuf("hs", [128, NCHUNK, W], F32)
    ucar = cx.sbuf("ucar", [128, NCHUNK, 1], F32)
    xmcar = cx.sbuf("xmcar", [128, NCHUNK, 1], F32)
    xh1 = cx.sbuf("xh1", [128, NCHUNK, 1], F32)
    rows = cx.sbuf("rows", [1, 4 * W], F32)
    sq = cx.sbuf("sq", [128, NCHUNK, W], BF16)
    xb = cx.sbuf("xb", [128, NCHUNK, W], BF16)
    wqs = [cx.sbuf(f"wqs{i}", [128, NCHUNK, 128], BF16) for i in range(2)]
    wos = cx.sbuf("wos", [128, 4, D_MODEL], BF16)
    w1s = [cx.sbuf(f"w1s{i}", [128, NCHUNK, 128], BF16) for i in range(2)]
    w2s = cx.sbuf("w2s", [128, 22, 128], BF16)
    qscs = cx.sbuf("qscs", [128, 1], F32)
    kscs = cx.sbuf("kscs", [128, 1], F32)
    lgs = cx.sbuf("lgs", [128, NCHUNK], F32)
    lbs = cx.sbuf("lbs", [128, NCHUNK], F32)
    b1s = cx.sbuf("b1s", [128, 44], F32)
    b2s = cx.sbuf("b2s", [128, NCHUNK], F32)
    fks = cx.sbuf("fks", [128, 1], F32)
    t4 = [cx.sbuf(f"t4_{i}", [128, W], F32) for i in range(4)]
    ams = cx.sbuf("ams", [128, 4, 1024], BF16)
    bms = cx.sbuf("bms", [128, 16, W], BF16)
    ident = cx.sbuf("ident", [128, 128], F32)
    blkdiag = cx.sbuf("blkdiag", [128, 128], BF16)
    blkrot = cx.sbuf("blkrot", [128, 128], BF16)
    onesq = cx.sbuf("onesq", [128, 128], BF16)
    ones64 = cx.sbuf("ones64", [1, 64], F32)
    qr = cx.sbuf("qr", [128, 4, W], BF16)
    kr = cx.sbuf("kr", [128, 4, 2 * W], BF16)
    vat = cx.sbuf("vat", [128, 8, 8, 65], BF16)
    evq = cx.sbuf("evq", [128, W], F32)
    qhat = cx.sbuf("qhat", [128, W], F32)
    rstb = cx.sbuf("rstb", [128, W], F32)
    rt1 = cx.sbuf("rt1", [128, W], F32)
    sqs = cx.sbuf("sqs", [128, W], BF16)
    nmrow = cx.sbuf("nmrow", [128, W], BF16)
    nmcol = cx.sbuf("nmcol", [128, 1], F32)
    n0 = cx.sbuf("n0", [128, 1], F32)
    atT = cx.sbuf("atT", [128, W], BF16)
    dd = cx.sbuf("dd", [1, W], F32)
    bc = cx.sbuf("bc", [64, W], F32)
    ow = cx.sbuf("ow", [128, 4, W], BF16)
    hgl = cx.sbuf("hgl", [128, 22, W], BF16)
    sa = evq
    gg = qhat
    P0 = cx.psum("P0", [128, 1024])
    P1 = cx.psum("P1", [128, W])
    P2 = cx.psum("P2", [128, W])
    P3 = cx.psum("P3", [128, W])
    P4 = cx.psum("P4", [128, W])
    P5 = cx.psum("P5", [128, W])

    ch.dma_n("gpsimd", 6, lambda p: [
        p.dma_start(out=wos[:], in_=wo.rearrange("(c p) o -> p c o", p=128)),
        p.dma_start(out=ams[:], in_=amask.rearrange("s i p j -> p (s i) j")[:, 4:8, :]),
        p.dma_start(out=bms[:], in_=bmask.rearrange("s r p j -> p (s r) j")),
        p.dma_start(out=qscs[:], in_=qsc[:, :]),
        p.dma_start(out=kscs[:], in_=ksc[:, :]),
        p.dma_start(out=blkrot[:], in_=prot[:, :]),
    ])
    ch.dma_n("sync", 6, lambda p: [
        p.dma_start(out=lgs[:], in_=lg[:, :]),
        p.dma_start(out=lbs[:], in_=lb[:, :]),
        p.dma_start(out=b1s[:], in_=b1p[:, :]),
        p.dma_start(out=b2s[:], in_=b2[:, :]),
        p.dma_start(out=fks[:], in_=ffnkeep[:, :]),
        p.dma_start(out=xh1[:], in_=fmview(halo)[:, :, 0:1],
                    allow_slow_non_contiguous=True),
    ])

    def e_consts(p):
        p.memset(ident[:], 0.0)
        p.memset(blkdiag[:], 0.0)
        p.memset(blkdiag[0:64, 0:64], 1.0)
        p.memset(blkdiag[64:128, 64:128], 1.0)
        p.memset(onesq[:], 1.0)
        p.memset(ones64[:], 1.0)
        p.memset(kr[:], 0.0)
        p.memset(vat[:], 0.0)

    ch.comp("vector", e_consts)
    ch.comp("gpsimd", lambda p: p.affine_select(
        ident[:], ident[:], [[-1, 128]], ALU.not_equal, 1.0, base=0,
        channel_multiplier=1))

    if ln_in:
        emit_ln(cx, xh1, 1, sq, xb, P2, P3, P1, P5, rows)
        emit_ln_apply(cx, xh1, 1, P1, P5, ucar)
    else:
        ch.comp("vector", lambda p: p.tensor_copy(ucar[:], xh1[:]))

    for w in range(L0F_NW):
        if w < 2:
            hsrc = fmview(halo)[:, :, 1 + w * W:1 + (w + 1) * W]
        else:
            hsrc = fmview(xs)[:, :, (w - 2) * W:(w - 1) * W]
        ch.dma_n("sync", 1, lambda p, hsrc=hsrc: [
            p.dma_start(out=xt[:], in_=hsrc)])
        if ln_in:
            emit_ln(cx, xt, W, sq, xb, P2, P3, P1, P5, rows)
            emit_ln_apply(cx, xt, W, P1, P5, ut)
        else:
            ch.comp("vector", lambda p: p.tensor_copy(ut[:], xt[:]))

        def e_shift(p):
            for c in range(NCHUNK):
                if c < 4:
                    p.tensor_copy(xb[:, c, :], ut[:, c, :])
                else:
                    p.tensor_copy(xb[:, c, 0:1], ucar[:, c, 0:1])
                    p.tensor_copy(xb[:, c, 1:W], ut[:, c, 0:W - 1])

        ch.comp("vector", e_shift, relax=True)
        ch.comp("vector",
                lambda p: p.tensor_copy(ucar[:], ut[:, :, W - 1:W]))

        def e_ring(p):
            p.tensor_copy(kr[:, :, 0:W], kr[:, :, W:2 * W])
            p.tensor_copy(vat[:, 0:4, :, :], vat[:, 4:8, :, :])

        ch.comp("vector", e_ring, relax=True)
        ch.dma_n("sync", 4, lambda p, w=w: [
            p.dma_start(out=t4[0][:], in_=tqc[:, w * W:(w + 1) * W]),
            p.dma_start(out=t4[1][:], in_=tqs[:, w * W:(w + 1) * W]),
            p.dma_start(out=t4[2][:], in_=tkc[:, w * W:(w + 1) * W]),
            p.dma_start(out=t4[3][:], in_=tks[:, w * W:(w + 1) * W]),
        ])

        for o in range(8):
            kind = o // 4
            pt = o % 4
            wb = wqs[o % 2]
            ch.dma_n("gpsimd", 1,
                     lambda p, o=o, wb=wb: [p.dma_start(out=wb[:],
                                                        in_=wqkv[o])])

            def e_mm(p, wb=wb):
                for c in range(NCHUNK):
                    p.matmul(P4[:, :], wb[:, c, :], xb[:, c, :],
                             start=(c == 0), stop=(c == NCHUNK - 1))

            ch.comp("tensor", e_mm, relax=True)
            ch.comp("vector", lambda p: p.tensor_copy(evq[:], P4[:, :]))
            ch.comp("scalar",
                    lambda p: p.activation(sqs[:], evq[:], AF.Square))
            ch.comp("tensor", lambda p: p.matmul(
                P1[:, :], blkdiag[:], sqs[:], start=True, stop=True), relax=True)

            def e_rst(p):
                p.tensor_scalar_max(rstb[:], P1[:, :], 1e-24)
                p.reciprocal(rstb[:], rstb[:])

            ch.comp("vector", e_rst)
            ch.comp("scalar",
                    lambda p: p.activation(rstb[:], rstb[:], AF.Sqrt))

            def e_hat(p, kind=kind):
                scs = qscs if kind == 0 else kscs
                p.tensor_tensor(qhat[:], evq[:], rstb[:], ALU.mult)
                p.tensor_scalar_mul(qhat[:], qhat[:], scs[:])
                p.tensor_copy(sqs[:], qhat[:])

            ch.comp("vector", e_hat)
            ch.comp("tensor", lambda p: p.matmul(
                P2[:, :], blkrot[:], sqs[:], start=True, stop=True), relax=True)

            def e_rot(p, kind=kind, pt=pt):
                tc, ts_ = (t4[0], t4[1]) if kind == 0 else (t4[2], t4[3])
                p.tensor_tensor(rt1[:], qhat[:], tc[:], ALU.mult)
                p.tensor_tensor(qhat[:], P2[:, :], ts_[:], ALU.mult)
                if kind == 0:
                    p.tensor_tensor(qr[:, pt, :], rt1[:], qhat[:], ALU.add)
                else:
                    p.tensor_tensor(kr[:, pt, W:2 * W], rt1[:], qhat[:],
                                    ALU.add)

            ch.comp("vector", e_rot)

        for o in range(8, 12):
            wb = wqs[o % 2]
            ch.dma_n("gpsimd", 1,
                     lambda p, o=o, wb=wb: [p.dma_start(out=wb[:],
                                                        in_=wqkv[o])])

            def e_mm(p, wb=wb):
                for c in range(NCHUNK):
                    p.matmul(P4[:, :], wb[:, c, :], xb[:, c, :],
                             start=(c == 0), stop=(c == NCHUNK - 1))

            ch.comp("tensor", e_mm, relax=True)
            ch.comp("vector", lambda p: p.tensor_copy(evq[:], P4[:, :]))
            for hh in range(2):
                h = (o - 8) * 2 + hh
                rb = hh * 64
                for jj in range(4):
                    def e_tr(p, rb=rb, jj=jj):
                        return p.transpose(
                            P3[0:128, 0:64],
                            evq[rb:rb + 64, jj * 128:(jj + 1) * 128],
                            ident[rb:rb + 64, rb:rb + 64])

                    ch.comp("tensor", e_tr, relax=True)

                    def e_vat(p, h=h, jj=jj):
                        p.tensor_copy(vat[:, 4 + jj, h, 0:64],
                                      P3[0:128, 0:64])
                        p.memset(vat[:, 4 + jj, h, 64:65], 1.0)

                    ch.comp("vector", e_vat, relax=True)

        if w == 0:
            continue

        mset = 0 if w == 2 else 1
        if w == 2:
            ch.dma_n("gpsimd", 1, lambda p: [p.dma_start(
                out=ams[:],
                in_=amask.rearrange("s i p j -> p (s i) j")[:, 0:4, :])])
        if w == 3:
            ch.dma_n("gpsimd", 1, lambda p: [p.dma_start(
                out=ams[:],
                in_=amask.rearrange("s i p j -> p (s i) j")[:, 4:8, :])])
        for h in range(8):
            pt, rb = h // 2, (h % 2) * 64
            for it in range(4):
                def e_sim(p, pt=pt, rb=rb, it=it):
                    p.matmul(
                        P0[:, 0:W],
                        qr[rb:rb + 64, pt, it * 128:(it + 1) * 128],
                        kr[rb:rb + 64, pt, 0:W], start=True, stop=True)
                    return p.matmul(
                        P0[:, W:2 * W],
                        qr[rb:rb + 64, pt, it * 128:(it + 1) * 128],
                        kr[rb:rb + 64, pt, W:2 * W], start=True, stop=True)

                ch.comp("tensor", e_sim, relax=True)

                def e_am(p, it=it, mset=mset):
                    p.tensor_tensor(rt1[:], P0[:, 0:W],
                                    ams[:, it, 0:W], ALU.add)
                    p.reduce_max(n0[:], rt1[:], axis=AX.X, negate=True)
                    p.tensor_tensor(rt1[:], P0[:, W:2 * W],
                                    ams[:, it, W:2 * W], ALU.add)
                    p.reduce_max(nmcol[:], rt1[:], axis=AX.X, negate=True)
                    p.tensor_tensor(nmcol[:], nmcol[:], n0[:], ALU.min)

                ch.comp("vector", e_am)
                ch.comp("tensor", lambda p: p.matmul(
                    P3[0:1, 0:128], nmcol[:], ident[:], start=True,
                    stop=True), relax=True)

                def e_nm(p, it=it):
                    p.tensor_copy(nmrow[0:1, it * 128:(it + 1) * 128],
                                  P3[0:1, 0:128])
                    p.tensor_copy(nmrow[64:65, it * 128:(it + 1) * 128],
                                  P3[0:1, 0:128])

                ch.comp("vector", e_nm, relax=True)

            for js in range(8):
                def e_simT(p, pt=pt, rb=rb, js=js):
                    p.matmul(P1[:, :],
                             kr[rb:rb + 64, pt, js * 128:(js + 1) * 128],
                             qr[rb:rb + 64, pt, :], start=True, stop=False)
                    p.matmul(P1[:, :], onesq[rb:rb + 1, :],
                             nmrow[rb:rb + 1, :], start=False, stop=True)

                ch.comp("tensor", e_simT, relax=True)
                ch.comp("scalar",
                        lambda p: p.activation(atT[:], P1[:, :], AF.Exp))

                def e_bm(p, js=js, mset=mset):
                    r, jj = js // 4, js % 4
                    p.tensor_tensor(atT[:], atT[:],
                                    bms[:, mset * 8 + r * 4 + jj, :],
                                    ALU.mult)

                ch.comp("vector", e_bm)

                def e_av(p, h=h, js=js):
                    return p.matmul(P2[0:65, :], vat[:, js, h, :], atT[:],
                                    start=(js == 0), stop=(js == 7))

                ch.comp("tensor", e_av, relax=True)

            ch.comp("vector", lambda p: p.reciprocal(dd[:], P2[64:65, :]))
            ch.comp("tensor", lambda p: p.matmul(
                P5[0:64, :], ones64[:], dd[:], start=True, stop=True), relax=True)

            def e_o(p, h=h):
                p.tensor_copy(bc[:], P5[0:64, :])
                p.tensor_tensor(ow[(h % 2) * 64:(h % 2) * 64 + 64,
                                h // 2, :], P2[0:64, :], bc[:], ALU.mult)

            ch.comp("vector", e_o)

        for o in range(NCHUNK):
            def e_wo(p, o=o):
                for c in range(4):
                    p.matmul(P4[:, :], wos[:, c, o * 128:(o + 1) * 128],
                             ow[:, c, :], start=(c == 0), stop=(c == 3))

            ch.comp("tensor", e_wo, relax=True)
            ch.comp("vector", lambda p, o=o: p.tensor_tensor(
                xm[:, o, :], ut[:, o, :], P4[:, :], ALU.add))

        if w == 1:
            ch.comp("vector",
                    lambda p: p.tensor_copy(xmcar[:], xm[:, :, W - 1:W]))
            continue

        if w == 2:
            def e_fk(p):
                for c in range(NCHUNK):
                    p.tensor_scalar_mul(xmcar[:, c, 0:1], xmcar[:, c, 0:1],
                                        fks[:])

            ch.comp("vector", e_fk, relax=True)

        def e_shift2(p):
            for c in range(NCHUNK):
                if c < 4:
                    p.tensor_copy(hs[:, c, :], xm[:, c, :])
                else:
                    p.tensor_copy(hs[:, c, 0:1], xmcar[:, c, 0:1])
                    p.tensor_copy(hs[:, c, 1:W], xm[:, c, 0:W - 1])

        ch.comp("vector", e_shift2, relax=True)
        ch.comp("vector",
                lambda p: p.tensor_copy(xmcar[:], xm[:, :, W - 1:W]))
        emit_ln(cx, hs, W, sq, xb, P2, P3, P1, P5, rows)

        def e_ln2(p):
            for c in range(NCHUNK):
                p.tensor_tensor(rt1[:], hs[:, c, :], P1[:, :], ALU.mult)
                p.scalar_tensor_tensor(rt1[:], rt1[:], 1.0, P5[:, :],
                                       ALU.mult, ALU.add)
                p.tensor_scalar(sq[:, c, :], rt1[:], lgs[:, c:c + 1],
                                lbs[:, c:c + 1], ALU.mult, ALU.add)

        ch.comp("vector", e_ln2)

        for j in range(22):
            ch.dma_n("gpsimd", 2, lambda p, j=j: [
                p.dma_start(out=w1s[0][:], in_=w1pt[j]),
                p.dma_start(out=w1s[1][:], in_=w1pt[22 + j]),
            ])

            def e_w1a(p):
                for c in range(NCHUNK):
                    p.matmul(P4[:, :], w1s[0][:, c, :], sq[:, c, :],
                             start=(c == 0), stop=(c == NCHUNK - 1))

            ch.comp("tensor", e_w1a, relax=True)
            ch.comp("vector", lambda p, j=j: p.tensor_scalar_add(
                sa[:], P4[:, :], b1s[:, j:j + 1]))

            def e_w1g(p):
                for c in range(NCHUNK):
                    p.matmul(P5[:, :], w1s[1][:, c, :], sq[:, c, :],
                             start=(c == 0), stop=(c == NCHUNK - 1))

            ch.comp("tensor", e_w1g, relax=True)
            ch.comp("scalar", lambda p, j=j: p.activation(
                gg[:], P5[:, :], AF.Gelu, bias=b1s[:, 22 + j:23 + j],
                scale=1.0))
            ch.comp("vector", lambda p, j=j: p.tensor_tensor(
                hgl[:, j, :], sa[:], gg[:], ALU.mult))

        for o in range(NCHUNK):
            ch.dma_n("gpsimd", 1,
                     lambda p, o=o: [p.dma_start(out=w2s[:], in_=w2pt[o])])

            def e_w2(p):
                for c in range(22):
                    p.matmul(P4[:, :], w2s[:, c, :], hgl[:, c, :],
                             start=(c == 0), stop=(c == 21))

            ch.comp("tensor", e_w2, relax=True)

            def e_res(p, o=o):
                p.tensor_scalar_add(rt1[:], P4[:, :], b2s[:, o:o + 1])
                p.tensor_tensor(xm[:, o, :], xm[:, o, :], rt1[:], ALU.add)

            ch.comp("vector", e_res)

        t0 = (w - 2) * W
        if fin_out:
            ch.dma_n("sync", 1, lambda p, t0=t0: [
                p.dma_start(out=hs[:], in_=fmview(xo)[:, :, t0:t0 + W])])
            emit_ln(cx, xm, W, sq, xb, P2, P3, P1, P5, rows)
            emit_ln_apply(cx, xm, W, P1, P5, xm, tmp=rt1, add_to=hs)

        ch.dma_n("sync", 1, lambda p, t0=t0: [
            p.dma_start(out=fmview(y)[:, :, t0:t0 + W], in_=xm[:])])

    cx.release(mk)




def emit_n1f(cx, T):
    ch = cx.ch
    mk = cx.mark()
    x, w1t, w2t, s1, s2, h = (T["x"], T["w1t"], T["w2t"], T["s1"], T["s2"],
                              T["h"])
    xt = cx.sbuf("xt", [128, NCHUNK, TT], F32)
    sq = cx.sbuf("sq", [128, NCHUNK, TT], BF16)
    xb = cx.sbuf("xb", [128, NCHUNK, TT], BF16)
    ub = cx.sbuf("ub", [128, NCHUNK, TT], BF16)
    w1s = cx.sbuf("w1s", [128, NCHUNK, 4096], BF16)
    w2s = cx.sbuf("w2s", [128, NCHUNK, 4096], BF16)
    s1s = cx.sbuf("s1s", [128, 32], F32)
    s2s = cx.sbuf("s2s", [128, 32], F32)
    rows = cx.sbuf("rows", [1, 4 * TT], F32)
    sa = [cx.sbuf(f"sa{i}", [128, TT], F32) for i in range(2)]
    sb2 = [cx.sbuf(f"sb2{i}", [128, TT], F32) for i in range(2)]
    hb = [cx.sbuf(f"hb{i}", [128, TT], BF16) for i in range(2)]
    psA = cx.psum("psA", [128, TT])
    psB = cx.psum("psB", [128, TT])
    psA2 = cx.psum("psA2", [128, TT])
    psB2 = cx.psum("psB2", [128, TT])
    psm = [cx.psum(f"psm{i}", [128, TT]) for i in range(2)]
    psm2 = [cx.psum(f"psm2{i}", [128, TT]) for i in range(2)]

    ch.dma_n("gpsimd", 4, lambda p: [
        p.dma_start(out=w1s[:], in_=fmview(w1t)),
        p.dma_start(out=w2s[:], in_=fmview(w2t)),
        p.dma_start(out=s1s[:], in_=s1[:, :]),
        p.dma_start(out=s2s[:], in_=s2[:, :]),
    ])
    for t in range(NTILE):
        t0 = t * TT
        ch.dma_n("sync", 1, lambda p, t0=t0: [
            p.dma_start(out=xt[:], in_=fmview(x)[:, :, t0:t0 + TT])])
        emit_ln(cx, xt, TT, sq, xb, psA, psB, psA2, psB2, rows)
        emit_ln_apply(cx, xt, TT, psA2, psB2, ub)
        s_ub = ch.snap()
        s_mma, s_mmb, s_act, s_mul, s_st = {}, {}, {}, {}, {}
        for o in range(32):
            j = o % 2

            def e_mma(p, o=o, j=j):
                for c in range(NCHUNK):
                    p.matmul(psm[j][:, :], w1s[:, c, o * 128:(o + 1) * 128],
                             ub[:, c, :], start=(c == 0),
                             stop=(c == NCHUNK - 1))

            ch.comp("tensor", e_mma, relax=True,
                    dep=Chain.dep_max(s_ub, s_act.get(o - 2, s_ub)))
            s_mma[o] = ch.snap()

            def e_mmb(p, o=o, j=j):
                for c in range(NCHUNK):
                    p.matmul(psm2[j][:, :], w2s[:, c, o * 128:(o + 1) * 128],
                             ub[:, c, :], start=(c == 0),
                             stop=(c == NCHUNK - 1))

            ch.comp("tensor", e_mmb, relax=True,
                    dep=Chain.dep_max(s_ub, s_mul.get(o - 2, s_ub)))
            s_mmb[o] = ch.snap()
            ch.comp("scalar", lambda p, o=o, j=j: p.activation(
                sa[j][:], psm[j][:], AF.Silu, scale=s1s[:, o:o + 1]),
                    relax=True,
                    dep=Chain.dep_max(s_mma[o], s_mul.get(o - 2, s_ub)))
            s_act[o] = ch.snap()

            def e_mul(p, o=o, j=j):
                p.tensor_scalar_mul(sb2[j][:], psm2[j][:], s2s[:, o:o + 1])
                p.tensor_tensor(hb[j][:], sa[j][:], sb2[j][:], ALU.mult)

            ch.comp("vector", e_mul,
                    dep=Chain.dep_max(s_mmb[o], s_act[o],
                                      s_st.get(o - 2, s_ub)))
            s_mul[o] = ch.snap()
            ch.dma_n("sync", 1, lambda p, o=o, t0=t0, j=j: [
                p.dma_start(out=h[o * 128:(o + 1) * 128, t0:t0 + TT],
                            in_=hb[j][:])], dep=s_mul[o])
            s_st[o] = ch.snap()
    cx.release(mk)


def emit_nhf(cx, T):
    ch = cx.ch
    mk = cx.mark()
    hi, wt, ss, ho = T["hi"], T["wt"], T["ss"], T["ho"]
    ht = [cx.sbuf(f"ht{i}", [128, 32, TT], BF16) for i in range(2)]
    ws = [cx.sbuf(f"ws{i}", [128, 32, 128], BF16) for i in range(3)]
    sss = cx.sbuf("sss", [128, 32], F32)
    ob = [cx.sbuf(f"ob{i}", [128, TT], BF16) for i in range(2)]
    psm = [cx.psum(f"psm{i}", [128, TT]) for i in range(2)]

    ch.dma_n("gpsimd", 1, lambda p: [p.dma_start(out=sss[:], in_=ss[:, :])])
    z = ch.snap()
    s_ld = {}
    s_wld = {}
    s_mm = {}
    s_act = {}
    s_st = {}
    idx = lambda t, o: t * 32 + o
    for t in range(NTILE):
        t0 = t * TT
        ch.dma_n("sync", 1, lambda p, t0=t0, t=t: [
            p.dma_start(out=ht[t % 2][:],
                        in_=fmview(hi, 32)[:, :, t0:t0 + TT])],
                  dep=s_mm.get(idx(t - 2, 31), z))
        s_ld[t] = ch.snap()
        for o in range(32):
            i = idx(t, o)
            wb = ws[i % 3]
            ch.dma_n("gpsimd", 1, lambda p, o=o, wb=wb: [
                p.dma_start(out=wb[:],
                            in_=wt[o].rearrange("(c p) m -> p c m", p=128))],
                      dep=s_mm.get(i - 3, z))
            s_wld[i] = ch.snap()
            ps = psm[i % 2]

            def e_mm(p, wb=wb, ps=ps, t=t):
                for c in range(32):
                    p.matmul(ps[:, :], wb[:, c, :], ht[t % 2][:, c, :],
                             start=(c == 0), stop=(c == 31))

            ch.comp("tensor", e_mm, relax=True,
                    dep=Chain.dep_max(s_wld[i], s_ld[t],
                                      s_act.get(i - 2, z)))
            s_mm[i] = ch.snap()
            ch.comp("scalar", lambda p, o=o, ps=ps, ob2=ob[i % 2]:
                    p.activation(ob2[:], ps[:], AF.Gelu,
                                 scale=sss[:, o:o + 1]),
                    relax=True,
                    dep=Chain.dep_max(s_mm[i], s_st.get(i - 2, z)))
            s_act[i] = ch.snap()
            ch.dma_n("sync", 1, lambda p, o=o, t0=t0, ob2=ob[i % 2]: [
                p.dma_start(out=ho[o * 128:(o + 1) * 128, t0:t0 + TT],
                            in_=ob2[:])], dep=s_act[i])
            s_st[i] = ch.snap()
    cx.release(mk)


def emit_n4f(cx, T):
    ch = cx.ch
    mk = cx.mark()
    hi, w3t, s3, x3, y = T["hi"], T["w3t"], T["s3"], T["x3"], T["y"]
    ht = cx.sbuf("ht", [128, 32, TT], BF16)
    x3t = cx.sbuf("x3t", [128, NCHUNK, TT], F32)
    w3s = cx.sbuf("w3s", [128, 32, D_MODEL], BF16)
    s3s = cx.sbuf("s3s", [128, 8], F32)
    yt = cx.sbuf("yt", [128, NCHUNK, TT], F32)
    yb = cx.sbuf("yb", [128, NCHUNK, TT], BF16)
    sq = cx.sbuf("sq", [128, NCHUNK, TT], BF16)
    rows = cx.sbuf("rows", [1, 4 * TT], F32)
    outt = cx.sbuf("outt", [128, NCHUNK, TT], F32)
    psm = cx.psum("psm", [128, TT])
    psA = cx.psum("psA", [128, TT])
    psB = cx.psum("psB", [128, TT])
    psA2 = cx.psum("psA2", [128, TT])
    psB2 = cx.psum("psB2", [128, TT])

    ch.dma_n("gpsimd", 2, lambda p: [
        p.dma_start(out=w3s[:], in_=fmview(w3t, 32)),
        p.dma_start(out=s3s[:], in_=s3[:, :]),
    ])
    for t in range(NTILE):
        t0 = t * TT
        ch.dma_n("sync", 2, lambda p, t0=t0: [
            p.dma_start(out=ht[:], in_=fmview(hi, 32)[:, :, t0:t0 + TT]),
            p.dma_start(out=x3t[:], in_=fmview(x3)[:, :, t0:t0 + TT]),
        ])
        for o in range(NCHUNK):
            def e_mm(p, o=o):
                for c in range(32):
                    p.matmul(psm[:, :], w3s[:, c, o * 128:(o + 1) * 128],
                             ht[:, c, :], start=(c == 0), stop=(c == 31))

            ch.comp("tensor", e_mm, relax=True)

            def e_ev(p, o=o):
                p.tensor_scalar_mul(yt[:, o, :], psm[:], s3s[:, o:o + 1])
                p.tensor_scalar_mul(yb[:, o, :], psm[:], s3s[:, o:o + 1])

            ch.comp("vector", e_ev, relax=True)
        emit_ln(cx, yt, TT, sq, yb, psA, psB, psA2, psB2, rows, need_xb=False)
        emit_ln_apply(cx, yt, TT, psA2, psB2, outt, add_to=x3t)
        ch.dma_n("sync", 1, lambda p, t0=t0: [
            p.dma_start(out=fmview(y)[:, :, t0:t0 + TT], in_=outt[:])])
    cx.release(mk)


def build_fused():
    nc = bass.Bass("TRN2", target_bir_lowering=False, debug=False)
    E = "ExternalInput"
    d = nc.dram_tensor
    T = {}

    def ein(name, shape):
        T[name] = d(name, shape, F32, kind=E).ap()

    def intr(name, shape, dt=F32):
        T[name] = d(name, shape, dt, kind="Internal").ap()

    ein("x", [D_MODEL, HSEQ])
    T["y"] = d("y", [D_MODEL, HSEQ], F32, kind="ExternalOutput").ap()
    ein("selA", [128, 1])
    ein("selB", [128, 1])
    ein("ffnkeep", [128, 1])
    ein("a1_wqt", [D_MODEL, 3072])
    ein("a3_wot", [D_MODEL, D_MODEL])
    intr("q", [D_MODEL, HSEQ])
    intr("k", [D_MODEL, HSEQ])
    intr("v", [D_MODEL, HSEQ])
    intr("kvtm", [32, 128, 2048], BF16)
    intr("o_att", [D_MODEL, HSEQ])
    intr("x1", [D_MODEL, HSEQ])
    for li in range(2):
        ein(f"f{li}_wqkv", [12, 128, NCHUNK, 128])
        ein(f"f{li}_qsc", [128, 1])
        ein(f"f{li}_ksc", [128, 1])
        ein(f"f{li}_wo", [512, D_MODEL])
        ein(f"f{li}_lg", [128, NCHUNK])
        ein(f"f{li}_lb", [128, NCHUNK])
        ein(f"f{li}_w1pt", [44, 128, NCHUNK, 128])
        ein(f"f{li}_b1p", [128, 44])
        ein(f"f{li}_w2pt", [8, 128, 22, 128])
        ein(f"f{li}_b2", [128, NCHUNK])
    ein("f_prot", [128, 128])
    ein("f_tqc", [128, L0F_NW * L0F_W])
    ein("f_tqs", [128, L0F_NW * L0F_W])
    ein("f_tkc", [128, L0F_NW * L0F_W])
    ein("f_tks", [128, L0F_NW * L0F_W])
    ein("f_amask", [2, 4, 128, 1024])
    ein("f_bmask", [2, 8, 128, L0F_W])
    intr("halo0", [D_MODEL, 1025])
    intr("inner1", [D_MODEL, HSEQ])
    intr("halo1", [D_MODEL, 1025])
    intr("x2", [D_MODEL, HSEQ])
    intr("agh", [2 * D_MODEL, 1025])
    intr("aghR", [2 * D_MODEL, 1])
    ein("r_wkt", [D_MODEL, D_MODEL])
    ein("r_wvt", [D_MODEL, D_MODEL])
    ein("r_wrt", [D_MODEL, D_MODEL])
    ein("r_wot", [D_MODEL, D_MODEL])
    ein("r_mixk", [128, NCHUNK])
    ein("r_mixv", [128, NCHUNK])
    ein("r_mixr", [128, NCHUNK])
    ein("r_w", [128, NCHUNK])
    ein("r_u", [128, NCHUNK])
    intr("haloR", [D_MODEL, 1])
    intr("kR", [D_MODEL, HSEQ])
    intr("vR", [D_MODEL, HSEQ])
    intr("srR", [D_MODEL, HSEQ])
    intr("wkvo", [D_MODEL, HSEQ])
    intr("x3", [D_MODEL, HSEQ])
    ein("n_w1t", [D_MODEL, 4096])
    ein("n_w2t", [D_MODEL, 4096])
    ein("n_s1", [128, 32])
    ein("n_s2", [128, 32])
    ein("n_h1wt", [32, 4096, 128])
    ein("n_h1s", [128, 32])
    ein("n_h2wt", [32, 4096, 128])
    ein("n_h2s", [128, 32])
    ein("n_w3t", [4096, D_MODEL])
    ein("n_s3", [128, 8])
    intr("hb1", [4096, HSEQ], BF16)
    intr("hb2", [4096, HSEQ], BF16)
    intr("hb3", [4096, HSEQ], BF16)

    ch = Chain()
    cx = Ctx(nc, ch)
    make_common_consts(cx)
    selA = cx.sbuf("g_selA", [128, 1], F32)
    selB = cx.sbuf("g_selB", [128, 1], F32)
    ch.dma_n("sync", 2, lambda p: [
        p.dma_start(out=selA[:], in_=T["selA"][:, :]),
        p.dma_start(out=selB[:], in_=T["selB"][:, :]),
    ])

    cx.prefix = "a1_"
    emit_a1f(cx, {"x": T["x"], "wqt": T["a1_wqt"], "q": T["q"], "k": T["k"],
                  "v": T["v"], "kvtm": T["kvtm"]})
    cx.prefix = "a2_"
    emit_a2f(cx, {"q": T["q"], "k": T["k"], "kvtm": T["kvtm"],
                  "o": T["o_att"], "selA": selA, "selB": selB})
    cx.prefix = "a3_"
    emit_projf(cx, {"inp": T["o_att"], "x": T["x"], "wt": T["a3_wot"],
                    "y": T["x1"]})

    cx.prefix = "hx0_"
    emit_halo_fill(cx, "hx0_", T["x1"], T["halo0"], 1025, selA, selB,
                   T["agh"])
    cx.prefix = "f0_"
    emit_l0ff(cx, {"xs": T["x1"], "halo": T["halo0"],
                   "wqkv": T["f0_wqkv"], "qsc": T["f0_qsc"],
                   "ksc": T["f0_ksc"], "wo": T["f0_wo"], "lg": T["f0_lg"],
                   "lb": T["f0_lb"], "w1pt": T["f0_w1pt"],
                   "b1p": T["f0_b1p"], "w2pt": T["f0_w2pt"],
                   "b2": T["f0_b2"], "prot": T["f_prot"], "tqc": T["f_tqc"],
                   "tqs": T["f_tqs"], "tkc": T["f_tkc"], "tks": T["f_tks"],
                   "amask": T["f_amask"], "bmask": T["f_bmask"],
                   "ffnkeep": T["ffnkeep"], "y": T["inner1"]},
              True, False)
    cx.prefix = "hx1_"
    emit_halo_fill(cx, "hx1_", T["inner1"], T["halo1"], 1025, selA, selB,
                   T["agh"])
    cx.prefix = "f1_"
    emit_l0ff(cx, {"xs": T["inner1"], "halo": T["halo1"],
                   "wqkv": T["f1_wqkv"], "qsc": T["f1_qsc"],
                   "ksc": T["f1_ksc"], "wo": T["f1_wo"], "lg": T["f1_lg"],
                   "lb": T["f1_lb"], "w1pt": T["f1_w1pt"],
                   "b1p": T["f1_b1p"], "w2pt": T["f1_w2pt"],
                   "b2": T["f1_b2"], "prot": T["f_prot"], "tqc": T["f_tqc"],
                   "tqs": T["f_tqs"], "tkc": T["f_tkc"], "tks": T["f_tks"],
                   "amask": T["f_amask"], "bmask": T["f_bmask"],
                   "ffnkeep": T["ffnkeep"], "xo": T["x1"], "y": T["x2"]},
              False, True)

    cx.prefix = "hxr_"
    emit_halo_fill(cx, "hxr_", T["x2"], T["haloR"], 1, selA, selB,
                   T["aghR"])
    cx.prefix = "r1_"
    emit_r1f(cx, {"x": T["x2"], "halo1": T["haloR"], "k": T["kR"],
                  "v": T["vR"], "sr": T["srR"], "wkt": T["r_wkt"],
                  "wvt": T["r_wvt"], "wrt": T["r_wrt"], "mixk": T["r_mixk"],
                  "mixv": T["r_mixv"], "mixr": T["r_mixr"]})
    cx.prefix = "wk_"
    emit_wkvf(cx, {"k": T["kR"], "v": T["vR"], "wkv": T["wkvo"],
                   "rw": T["r_w"], "ru": T["r_u"], "selA": selA,
                   "selB": selB})
    cx.prefix = "r3_"
    emit_r3f(cx, {"wkv": T["wkvo"], "sr": T["srR"], "x2": T["x2"],
                  "wot": T["r_wot"], "y": T["x3"]})

    cx.prefix = "n1_"
    emit_n1f(cx, {"x": T["x3"], "w1t": T["n_w1t"], "w2t": T["n_w2t"],
                  "s1": T["n_s1"], "s2": T["n_s2"], "h": T["hb1"]})
    cx.prefix = "n2_"
    emit_nhf(cx, {"hi": T["hb1"], "wt": T["n_h1wt"], "ss": T["n_h1s"],
                  "ho": T["hb2"]})
    cx.prefix = "n3_"
    emit_nhf(cx, {"hi": T["hb2"], "wt": T["n_h2wt"], "ss": T["n_h2s"],
                  "ho": T["hb3"]})
    cx.prefix = "n4_"
    emit_n4f(cx, {"hi": T["hb3"], "w3t": T["n_w3t"], "s3": T["n_s3"],
                  "x3": T["x3"], "y": T["y"]})

    ch.build_with_clear(nc)
    cx.close()
    return nc


def emit_r3f(cx, T):
    ch = cx.ch
    mk = cx.mark()
    wk, sr, x2, wot, y = T["wkv"], T["sr"], T["x2"], T["wot"], T["y"]
    wt = cx.sbuf("wt", [128, NCHUNK, TT], F32)
    st = cx.sbuf("st", [128, NCHUNK, TT], F32)
    x2t = cx.sbuf("x2t", [128, NCHUNK, TT], F32)
    rb = cx.sbuf("rb", [128, NCHUNK, TT], BF16)
    wos = cx.sbuf("wos", [128, NCHUNK, D_MODEL], BF16)
    yt = cx.sbuf("yt", [128, NCHUNK, TT], F32)
    yb = cx.sbuf("yb", [128, NCHUNK, TT], BF16)
    sq = cx.sbuf("sq", [128, NCHUNK, TT], BF16)
    rows = cx.sbuf("rows", [1, 4 * TT], F32)
    outt = cx.sbuf("outt", [128, NCHUNK, TT], F32)
    psm = cx.psum("psm", [128, TT])
    psA = cx.psum("psA", [128, TT])
    psB = cx.psum("psB", [128, TT])
    psA2 = cx.psum("psA2", [128, TT])
    psB2 = cx.psum("psB2", [128, TT])

    ch.dma_n("gpsimd", 1, lambda p: [p.dma_start(out=wos[:], in_=fmview(wot))])
    for t in range(NTILE):
        t0 = t * TT
        ch.dma_n("sync", 3, lambda p, t0=t0: [
            p.dma_start(out=wt[:], in_=fmview(wk)[:, :, t0:t0 + TT]),
            p.dma_start(out=st[:], in_=fmview(sr)[:, :, t0:t0 + TT]),
            p.dma_start(out=x2t[:], in_=fmview(x2)[:, :, t0:t0 + TT]),
        ])

        def e_rb(p):
            for c in range(NCHUNK):
                p.tensor_tensor(rb[:, c, :], wt[:, c, :], st[:, c, :],
                                ALU.mult)

        ch.comp("vector", e_rb, relax=True)
        for o in range(NCHUNK):
            def e_mm(p, o=o):
                for c in range(NCHUNK):
                    p.matmul(psm[:, :], wos[:, c, o * 128:(o + 1) * 128],
                             rb[:, c, :], start=(c == 0),
                             stop=(c == NCHUNK - 1))

            ch.comp("tensor", e_mm, relax=True)

            def e_ev(p, o=o):
                p.tensor_copy(yt[:, o, :], psm[:])
                p.tensor_copy(yb[:, o, :], psm[:])

            ch.comp("vector", e_ev, relax=True)
        emit_ln(cx, yt, TT, sq, yb, psA, psB, psA2, psB2, rows, need_xb=False)
        emit_ln_apply(cx, yt, TT, psA2, psB2, outt, add_to=x2t)
        ch.dma_n("sync", 1, lambda p, t0=t0: [
            p.dma_start(out=fmview(y)[:, :, t0:t0 + TT], in_=outt[:])])
    cx.release(mk)


def _fused_maps(inp):
    w = inp
    x = np.asarray(inp["L0_attn_x"], np.float32)
    x_fm = _c(x.transpose(0, 2, 1))
    QK = 8.0
    shared = {
        "a1_wqt": _c(w["L0_attn_W_qkv"].T.astype(np.float32)),
        "a3_wot": _c(w["L0_attn_W_out"].T.astype(np.float32)),
        "f_tqc": None, "f_tqs": None, "f_tkc": None, "f_tks": None,
        "r_wkt": _c(w["L1_attn_Wk"].T.astype(np.float32)),
        "r_wvt": _c(w["L1_attn_Wv"].T.astype(np.float32)),
        "r_wrt": _c(w["L1_attn_Wr"].T.astype(np.float32)),
        "r_wot": _c(w["L1_attn_Wo"].T.astype(np.float32)),
        "r_mixk": _scale_tiles(w["L1_attn_mix_k"]),
        "r_mixv": _scale_tiles(w["L1_attn_mix_v"]),
        "r_mixr": _scale_tiles(w["L1_attn_mix_r"]),
        "r_w": _scale_tiles(w["L1_attn_w"]),
        "r_u": _scale_tiles(w["L1_attn_u"]),
        "n_w1t": _c(w["L1_ffn_w1_w"].T.astype(np.float32)),
        "n_w2t": _c(w["L1_ffn_w2_w"].T.astype(np.float32)),
        "n_s1": _scale_tiles(w["L1_ffn_w1_s"]),
        "n_s2": _scale_tiles(w["L1_ffn_w2_s"]),
        "n_h1wt": _c(w["L1_ffn_h1_w"].T.astype(np.float32)
                     .reshape(4096, 32, 128).transpose(1, 0, 2)),
        "n_h1s": _scale_tiles(w["L1_ffn_h1_s"]),
        "n_h2wt": _c(w["L1_ffn_h2_w"].T.astype(np.float32)
                     .reshape(4096, 32, 128).transpose(1, 0, 2)),
        "n_h2s": _scale_tiles(w["L1_ffn_h2_s"]),
        "n_w3t": _c(w["L1_ffn_w3_w"].T.astype(np.float32)),
        "n_s3": _scale_tiles(w["L1_ffn_w3_s"]),
    }
    for li in range(2):
        wqkv = w["L0_ffn_wqkv"][li].astype(np.float64)
        w1 = w["L0_ffn_w1"][li].astype(np.float64)
        b1 = w["L0_ffn_b1"][li].astype(np.float64)
        w2 = w["L0_ffn_w2"][li].astype(np.float64)
        w1p = np.zeros((1024, 5632))
        w1p[:, 0:2730] = w1[:, 0:2730]
        w1p[:, 2816:5546] = w1[:, 2730:]
        b1p = np.zeros(5632)
        b1p[0:2730] = b1[0:2730]
        b1p[2816:5546] = b1[2730:]
        w2p = np.zeros((2816, 1024))
        w2p[0:2730] = w2
        shared[f"f{li}_wqkv"] = _l0f_tile_w(wqkv, 12)
        shared[f"f{li}_qsc"] = _c(np.tile(w["L0_ffn_q_scale"][li] * QK, 2)
                                  .reshape(128, 1).astype(np.float32))
        shared[f"f{li}_ksc"] = _c(np.tile(w["L0_ffn_k_scale"][li], 2)
                                  .reshape(128, 1).astype(np.float32))
        shared[f"f{li}_wo"] = _c(w["L0_ffn_wo"][li].astype(np.float32))
        shared[f"f{li}_lg"] = _scale_tiles(w["L0_ffn_ln_g"][li])
        shared[f"f{li}_lb"] = _scale_tiles(w["L0_ffn_ln_b"][li])
        shared[f"f{li}_w1pt"] = _l0f_tile_w(w1p, 44)
        shared[f"f{li}_b1p"] = _c(b1p.reshape(44, 128).T.astype(np.float32))
        shared[f"f{li}_w2pt"] = _l0f_tile_w(w2p, 8)
        shared[f"f{li}_b2"] = _scale_tiles(w["L0_ffn_b2"][li])
    consts = [_l0f_consts(0), _l0f_consts(1)]
    maps = []
    for core in range(8):
        b, hh = core // 2, core % 2
        tqc, tqs, tkc, tks, am, bm, prot = consts[hh]
        m = dict(shared)
        m["f_tqc"], m["f_tqs"], m["f_tkc"], m["f_tks"] = tqc, tqs, tkc, tks
        m["f_amask"] = am
        m["f_bmask"] = bm
        m["f_prot"] = prot
        m["x"] = _c(x_fm[b, :, hh * HSEQ:(hh + 1) * HSEQ])
        m["selA"] = np.full((128, 1), 1.0 if hh == 1 else 0.0, np.float32)
        m["selB"] = np.zeros((128, 1), np.float32)
        m["ffnkeep"] = np.full((128, 1), 0.0 if hh == 0 else 1.0, np.float32)
        maps.append(m)
    return maps


def kernel_fused(inp):
    nc = build_fused()
    maps = _fused_maps(inp)
    res = run_bass_kernel_spmd(nc, maps, core_ids=list(range(8)))
    x = np.asarray(inp["L0_attn_x"], np.float32)
    out = np.empty((BATCH, D_MODEL, SEQ), np.float32)
    for core in range(8):
        b, hh = core // 2, core % 2
        out[b, :, hh * HSEQ:(hh + 1) * HSEQ] = res.results[core]["y"]
    return _c(out.transpose(0, 2, 1))


# version 09
# speedup vs baseline: 1.0470x; 1.0470x over previous
import numpy as np
import ml_dtypes
import concourse.bass as bass
import concourse.mybir as mybir
from concourse.bass_utils import run_bass_kernel_spmd

F32 = mybir.dt.float32
BF16 = mybir.dt.bfloat16
AF = mybir.ActivationFunctionType
ALU = mybir.AluOpType
AX = mybir.AxisListType

D_MODEL = 1024
NCHUNK = 8
BATCH = 4
SEQ = 8192
HSEQ = SEQ // 2
TT = 512
NTILE = HSEQ // TT

bfdt = ml_dtypes.bfloat16


class _DummyInstr:
    def then_inc(self, *a, **k):
        return self


class _CountProxy:

    def __init__(self):
        self.n = 0

    def __getattr__(self, name):
        if name.startswith("_"):
            raise AttributeError(name)
        d = _DummyInstr()

        def f(*a, **k):
            self.n += 1
            return d

        return f


ENG_IDS = {"sync": 0, "scalar": 1, "vector": 2, "tensor": 3, "gpsimd": 4}
SNAP_DMA = 5
SNAP_CC = 9
NSNAP = 10


class Chain:

    ENGINES = ("sync", "scalar", "vector", "tensor", "gpsimd")

    def __init__(self):
        self.q = {e: [] for e in self.ENGINES}
        self.cnt = [0] * NSNAP
        self.dma_phase = {"sync": 0, "gpsimd": 0}

    def snap(self):
        return tuple(self.cnt)

    @staticmethod
    def dep_max(*deps):
        return tuple(max(d[i] for d in deps) for i in range(NSNAP))

    def _emit_waits(self, p, sems, dep, eng_i, include_same=True):
        for e in range(5):
            v = dep[e]
            if v <= 0:
                continue
            if e == eng_i and not include_same:
                continue
            p.wait_ge(sems[e], v)
        for k in range(4):
            if dep[SNAP_DMA + k] > 0:
                p.wait_ge(sems[5 + k], 16 * dep[SNAP_DMA + k])
        if dep[SNAP_CC] > 0:
            p.wait_ge(sems[9], dep[SNAP_CC])

    def comp(self, eng, emit, relax=False, dep=None):
        cp = _CountProxy()
        emit(cp)
        n = cp.n
        assert n > 0
        ei = ENG_IDS[eng]
        st0 = tuple(dep) if dep is not None else self.snap()
        base_e = self.cnt[ei]
        self.cnt[ei] += 1 if relax else n
        ch = self

        def run(p, sems):
            insts = []

            class _Cap:
                def __getattr__(self, name):
                    if name.startswith("_"):
                        raise AttributeError(name)
                    meth = getattr(p, name)

                    def f(*a, **k):
                        if not relax:
                            if insts:
                                p.wait_ge(sems[ei], base_e + len(insts))
                            elif base_e > 0:
                                p.wait_ge(sems[ei], base_e)
                        elif not insts and base_e > 0:
                            p.wait_ge(sems[ei], base_e)
                        if not insts:
                            ch._emit_waits(p, sems, st0, ei,
                                           include_same=False)
                        inst = meth(*a, **k)
                        if not relax:
                            inst.then_inc(sems[ei], 1)
                        insts.append(inst)
                        return inst

                    return f

            emit(_Cap())
            assert len(insts) == n
            if relax:
                insts[-1].then_inc(sems[ei], 1)

        self.q[eng].append(run)

    def dma_n(self, eng, n, emit_list, dep=None):
        ei = ENG_IDS[eng]
        st0 = tuple(dep) if dep is not None else self.snap()
        qn = "gpsimd" if eng == "gpsimd" else "sync"
        par = self.dma_phase[qn] % 2
        self.dma_phase[qn] += 1
        didx = 5 + (2 if qn == "gpsimd" else 0) + par
        prior_same = self.cnt[SNAP_DMA + (2 if qn == "gpsimd" else 0) + par]
        self.cnt[SNAP_DMA + (2 if qn == "gpsimd" else 0) + par] += n
        ch = self

        def run(p, sems):
            if prior_same > 0:
                p.wait_ge(sems[didx], 16 * prior_same)
            ch._emit_waits(p, sems, st0, ei, include_same=True)
            insts = emit_list(p)
            assert len(insts) == n, f"declared {n} dmas, emitted {len(insts)}"
            for i in insts:
                i.then_inc(sems[didx], 16)

        self.q[eng].append(run)

    def collective(self, emit):
        st0 = self.snap()
        self.cnt[SNAP_CC] += 1
        ch = self

        def run(p, sems):
            ch._emit_waits(p, sems, st0, ENG_IDS["gpsimd"],
                           include_same=True)
            inst = emit(p)
            inst.then_inc(sems[9], 1)

        self.q["gpsimd"].append(run)

    def build_with_clear(self, nc):
        totals = self.snap()
        fin_engs = [e for e in ("sync", "scalar", "vector", "tensor")
                    if self.q[e]]
        nfin = len(fin_engs)
        ch = self

        def fin(eng):
            ei = ENG_IDS[eng]

            def run(p, sems):
                ch._emit_waits(p, sems, totals, ei, include_same=True)
                p.nop().then_inc(sems[ei], 1)

            self.q[eng].append(run)

        for e in fin_engs:
            fin(e)

        def gfin(p, sems):
            for e in ("sync", "scalar", "vector", "tensor"):
                ei = ENG_IDS[e]
                v = totals[ei] + (1 if e in fin_engs else 0)
                if v > 0:
                    p.wait_ge(sems[ei], v)
            for k in range(4):
                if totals[SNAP_DMA + k] > 0:
                    p.wait_ge(sems[5 + k], 16 * totals[SNAP_DMA + k])
            if totals[SNAP_CC] > 0:
                p.wait_ge(sems[9], totals[SNAP_CC])
            if totals[ENG_IDS["gpsimd"]] > 0:
                p.wait_ge(sems[ENG_IDS["gpsimd"]],
                          totals[ENG_IDS["gpsimd"]])
            nums = sorted(s.num for s in sems)
            p.dma_reset(range(nums[0], nums[-1] + 1))
            for s in sems:
                p.sem_clear(s)

        self.q["gpsimd"].append(gfin)
        self.build(nc)

    def build(self, nc):
        with (
            nc.semaphore("c_sy") as s0,
            nc.semaphore("c_sc") as s1,
            nc.semaphore("c_ve") as s2,
            nc.semaphore("c_te") as s3,
            nc.semaphore("c_gp") as s4,
            nc.semaphore("dsyA") as s5,
            nc.semaphore("dsyB") as s6,
            nc.semaphore("dgpA") as s7,
            nc.semaphore("dgpB") as s8,
            nc.semaphore("dcc") as s9,
            nc.Block() as block,
        ):
            ch = self
            sems = (s0, s1, s2, s3, s4, s5, s6, s7, s8, s9)

            @block.sync
            def _(p):
                for fn in ch.q["sync"]:
                    fn(p, sems)

            @block.scalar
            def _(p):
                for fn in ch.q["scalar"]:
                    fn(p, sems)

            @block.vector
            def _(p):
                for fn in ch.q["vector"]:
                    fn(p, sems)

            @block.tensor
            def _(p):
                for fn in ch.q["tensor"]:
                    fn(p, sems)

            @block.gpsimd
            def _(p):
                for fn in ch.q["gpsimd"]:
                    fn(p, sems)


class Ctx:

    def __init__(self, nc, ch):
        self.nc = nc
        self.ch = ch
        self.exits = []
        self.prefix = ""

    def sbuf(self, name, shape, dtype):
        t = self.nc.sbuf_tensor(self.prefix + name, shape, dtype)
        h = t.__enter__()
        self.exits.append(t)
        return h

    def psum(self, name, shape, dtype=F32):
        t = self.nc.psum_tensor(self.prefix + name, shape, dtype)
        h = t.__enter__()
        self.exits.append(t)
        return h

    def mark(self):
        return len(self.exits)

    def release(self, mark):
        while len(self.exits) > mark:
            self.exits.pop().__exit__(None, None, None)

    def close(self):
        self.release(0)


def make_common_consts(cx):
    cx.ones_col_bf = cx.sbuf("ones_col_bf", [128, 1], BF16)
    cx.ones_row_f = cx.sbuf("ones_row_f", [1, 128], F32)

    def emit(p):
        p.memset(cx.ones_col_bf[:], 1.0)
        return p.memset(cx.ones_row_f[:], 1.0)

    cx.ch.comp("vector", emit)


def fmview(ap, nch=NCHUNK):
    return ap.rearrange("(c p) t -> p c t", p=128)


def emit_ln(cx, xt, T, sq, xb, psA, psB, psA2, psB2, rows, need_xb=True):
    ch = cx.ch
    ch.comp("scalar",
            lambda p: p.activation(sq[:, :, 0:T], xt[:, :, 0:T], AF.Square))
    if need_xb:
        ch.comp("vector",
                lambda p: p.tensor_copy(xb[:, :, 0:T], xt[:, :, 0:T]))

    def e_mm(p):
        for c in range(NCHUNK):
            p.matmul(psA[0:1, 0:T], cx.ones_col_bf[:], xb[:, c, 0:T],
                     start=(c == 0), stop=(c == NCHUNK - 1))
        last = None
        for c in range(NCHUNK):
            last = p.matmul(psB[0:1, 0:T], cx.ones_col_bf[:], sq[:, c, 0:T],
                            start=(c == 0), stop=(c == NCHUNK - 1))
        return last

    ch.comp("tensor", e_mm, relax=True)

    mean = rows[0:1, 0:T]
    m2 = rows[0:1, T:2 * T]
    varp = rows[0:1, 2 * T:3 * T]
    amb = rows[0:1, 3 * T:4 * T]

    def e_rows(p):
        p.tensor_scalar_mul(mean, psA[0:1, 0:T], 1.0 / D_MODEL)
        p.tensor_tensor(m2, mean, mean, ALU.mult)
        p.tensor_scalar_mul(varp, psB[0:1, 0:T], 1.0 / D_MODEL)
        p.tensor_tensor(varp, varp, m2, ALU.subtract)
        p.tensor_scalar_add(varp, varp, 1e-5)
        return p.reciprocal(varp, varp)

    ch.comp("vector", e_rows)
    ch.comp("scalar", lambda p: p.activation(varp, varp, AF.Sqrt))
    ch.comp("vector",
            lambda p: p.scalar_tensor_tensor(amb, mean, -1.0, varp, ALU.mult,
                                             ALU.mult))

    def e_bc(p):
        p.matmul(psA2[:, 0:T], cx.ones_row_f[:], varp, start=True, stop=True)
        return p.matmul(psB2[:, 0:T], cx.ones_row_f[:], amb, start=True,
                        stop=True)

    ch.comp("tensor", e_bc, relax=True)


def emit_ln_apply(cx, xt, T, psA2, psB2, out, tmp=None, add_to=None):
    ch = cx.ch
    if add_to is None:
        ch.comp("vector", lambda p: [p.tensor_tensor(
            out[:, c, 0:T], xt[:, c, 0:T], psA2[:, 0:T], ALU.mult)
            for c in range(NCHUNK)], relax=True)
        ch.comp("vector", lambda p: [p.scalar_tensor_tensor(
            out[:, c, 0:T], out[:, c, 0:T], 1.0, psB2[:, 0:T], ALU.mult,
            ALU.add) for c in range(NCHUNK)], relax=True)
    else:
        ch.comp("vector", lambda p: [p.tensor_tensor(
            out[:, c, 0:T], xt[:, c, 0:T], psA2[:, 0:T], ALU.mult)
            for c in range(NCHUNK)], relax=True)
        ch.comp("vector", lambda p: [p.scalar_tensor_tensor(
            out[:, c, 0:T], out[:, c, 0:T], 1.0, psB2[:, 0:T], ALU.mult,
            ALU.add) for c in range(NCHUNK)], relax=True)
        ch.comp("vector", lambda p: [p.tensor_tensor(
            out[:, c, 0:T], out[:, c, 0:T], add_to[:, c, 0:T], ALU.add)
            for c in range(NCHUNK)], relax=True)


def _c(a):
    return np.ascontiguousarray(a)


def _scale_tiles(s):
    n = s.shape[0]
    return _c(s.reshape(n // 128, 128).T.astype(np.float32))


def kernel(**inp):
    raise NotImplementedError("assembled later")


def _l0f_tile_w(wmat, nout):
    Kd = wmat.shape[0]
    return _c(wmat.reshape(Kd // 128, 128, nout, 128).transpose(2, 1, 0, 3)
              .astype(np.float32))


def _l0f_consts(hh):
    W, DH = 512, 64
    t0g = hh * HSEQ
    pos = np.maximum(np.arange(t0g - 1024, t0g + HSEQ), 0).astype(np.float64)
    half_idx = np.arange(0, DH, 2, dtype=np.float64)
    inv_freq = 1.0 / 10000.0 ** (half_idx / DH)
    freqs = pos[:, None] * inv_freq[None, :]
    freqs = np.concatenate([freqs, freqs], axis=-1)
    base = (half_idx + 0.4 * DH) / (1.4 * DH)
    power = (pos - SEQ // 2) / 256.0
    scale = base[None, :] ** power[:, None]
    scale = np.concatenate([scale, scale], axis=-1)
    cos, sin = np.cos(freqs), np.sin(freqs)
    tqc = (cos * scale).T
    tqs = (sin * scale).T
    tkc = (cos / scale).T
    tks = (sin / scale).T

    def tile2(tt):
        return _c(np.concatenate([tt, tt], axis=0).astype(np.float32))

    iL = np.arange(W)[None, :]
    pj = np.arange(128)[:, None]
    am = np.zeros((2, 4, 128, 1024), np.float32)
    bm = np.zeros((2, 8, 128, W), np.float32)
    for it in range(4):
        iloc = it * 128 + np.arange(128)[:, None]
        jprev = np.arange(W)[None, :]
        jown = np.arange(W)[None, :]
        am[1, it, :, 0:W] = np.where(iloc <= jprev, 0.0, -1e30)
        am[1, it, :, W:] = np.where(iloc >= jown, 0.0, -1e30)
    for jj in range(4):
        jloc = jj * 128 + pj
        bm[1, jj] = (iL <= jloc).astype(np.float32)
        bm[1, 4 + jj] = (iL >= jloc).astype(np.float32)
    am[0] = am[1]
    bm[0] = bm[1]
    if hh == 0:
        am[0, :, :, 0:W] = -1e30
        bm[0, 0:4] = 0.0
    P = np.zeros((64, 64), np.float64)
    for d in range(32):
        P[d, d + 32] = -1.0
        P[d + 32, d] = 1.0
    PT = P.T
    prot = np.zeros((128, 128), np.float32)
    prot[0:64, 0:64] = PT
    prot[64:128, 64:128] = PT
    return (tile2(tqc), tile2(tqs), tile2(tkc), tile2(tks), _c(am), _c(bm),
            prot)


def kernel(**inp):
    return kernel_fused(inp)


L0F_H = 1025
L0F_W = 512
L0F_NW = 10


REPL_PAIRS = [[0, 1], [2, 3], [4, 5], [6, 7]]


def emit_pair_exchange(cx, pref, src_ap, agout_ap, rows, cols, selA, selB,
                       write_fn, tile_cols=None):
    ch = cx.ch
    nc = cx.nc
    mk = cx.mark()
    tc = cols if tile_cols is None else tile_cols
    blkA = cx.sbuf(pref + "blkA", [128, tc], F32)
    blkB = cx.sbuf(pref + "blkB", [128, tc], F32)
    cmb = cx.sbuf(pref + "cmb", [128, tc], F32)
    ch.collective(lambda p: p.collective_compute(
        "AllGather", ALU.bypass, replica_groups=REPL_PAIRS,
        ins=[src_ap], outs=[agout_ap]))
    slow = cols < 64
    rg = min(128, rows)
    for r0 in range(0, rows, rg):
        for c0 in range(0, cols, tc):
            cw = min(tc, cols - c0)
            ch.dma_n("sync", 2, lambda p, r0=r0, c0=c0, cw=cw: [
                p.dma_start(out=blkA[0:rg, 0:cw],
                            in_=agout_ap[r0:r0 + rg, c0:c0 + cw],
                            allow_slow_non_contiguous=slow),
                p.dma_start(out=blkB[0:rg, 0:cw],
                            in_=agout_ap[rows + r0:rows + r0 + rg,
                                         c0:c0 + cw],
                            allow_slow_non_contiguous=slow),
            ])

            def e_cmb(p, cw=cw):
                p.tensor_scalar_mul(cmb[0:rg, 0:cw], blkA[0:rg, 0:cw],
                                    selA[0:rg, 0:1])
                p.scalar_tensor_tensor(cmb[0:rg, 0:cw], blkB[0:rg, 0:cw],
                                       selB[0:rg, 0:1], cmb[0:rg, 0:cw],
                                       ALU.mult, ALU.add)

            ch.comp("vector", e_cmb)
            write_fn(ch, cmb, r0, c0, cw, slow)
    cx.release(mk)


def emit_halo_fill(cx, pref, stream_ap, halo_ap, hcols, selA, selB, tmp_ag):
    nc = cx.nc
    src = nc.dram_tensor(pref + "agsrc", [D_MODEL, hcols], F32,
                         kind="Internal").ap()
    ch = cx.ch
    mk = cx.mark()
    bt = cx.sbuf(pref + "bt", [128, NCHUNK, hcols], F32)
    slow = hcols < 64
    ch.dma_n("sync", 1, lambda p: [
        p.dma_start(out=bt[:], in_=fmview(stream_ap)[:, :,
                    HSEQ - hcols:HSEQ], allow_slow_non_contiguous=slow)])
    ch.dma_n("sync", 1, lambda p: [
        p.dma_start(out=fmview(src)[:, :, 0:hcols], in_=bt[:],
                    allow_slow_non_contiguous=slow)])
    cx.release(mk)

    def wr(ch2, cmb, r0, c0, cw, slow2):
        ch2.dma_n("sync", 1, lambda p: [
            p.dma_start(out=halo_ap[r0:r0 + 128, c0:c0 + cw],
                        in_=cmb[:, 0:cw], allow_slow_non_contiguous=slow2)])

    emit_pair_exchange(cx, pref, src, tmp_ag, D_MODEL, hcols, selA, selB, wr,
                       tile_cols=min(hcols, 1025))


def emit_a1f(cx, T):
    ch = cx.ch
    mk = cx.mark()
    x, wqt = T["x"], T["wqt"]
    qo, ko, vo, kvtm = T["q"], T["k"], T["v"], T["kvtm"]
    xt = cx.sbuf("xt", [128, NCHUNK, TT], F32)
    sq = cx.sbuf("sq", [128, NCHUNK, TT], BF16)
    xb = cx.sbuf("xb", [128, NCHUNK, TT], BF16)
    ub = cx.sbuf("ub", [128, NCHUNK, TT], BF16)
    wqs = cx.sbuf("wqs", [128, NCHUNK, 3072], BF16)
    rows = cx.sbuf("rows", [1, 4 * TT], F32)
    t0b = [cx.sbuf(f"t0b{i}", [128, TT], F32) for i in range(2)]
    t1b = [cx.sbuf(f"t1b{i}", [128, TT], F32) for i in range(2)]
    evt = [cx.sbuf(f"evt{i}", [128, TT], F32) for i in range(2)]
    evb = [cx.sbuf(f"evb{i}", [128, TT], BF16) for i in range(2)]
    psA = cx.psum("psA", [128, TT])
    psB = cx.psum("psB", [128, TT])
    psA2 = cx.psum("psA2", [128, TT])
    psB2 = cx.psum("psB2", [128, TT])
    psm = [cx.psum(f"psm{i}", [128, TT]) for i in range(2)]

    ch.dma_n("gpsimd", 1, lambda p: [p.dma_start(out=wqs[:], in_=fmview(wqt))])
    outs = (qo, ko, vo)
    for t in range(NTILE):
        t0 = t * TT
        ch.dma_n("sync", 1, lambda p, t0=t0: [
            p.dma_start(out=xt[:], in_=fmview(x)[:, :, t0:t0 + TT])])
        emit_ln(cx, xt, TT, sq, xb, psA, psB, psA2, psB2, rows)
        emit_ln_apply(cx, xt, TT, psA2, psB2, ub)
        s_in = ch.snap()
        s_mm, s_ev, s_st = {}, {}, {}
        for o in range(24):
            j = o % 2
            kind = o // 8

            def e_mm(p, o=o, j=j):
                for c in range(NCHUNK):
                    p.matmul(psm[j][:, :], wqs[:, c, o * 128:(o + 1) * 128],
                             ub[:, c, :], start=(c == 0),
                             stop=(c == NCHUNK - 1))

            ch.comp("tensor", e_mm, relax=True,
                    dep=Chain.dep_max(s_in, s_ev.get(o - 2, s_in)))
            s_mm[o] = ch.snap()
            if kind < 2:
                def e_elu(p, j=j):
                    p.tensor_scalar_min(t0b[j][:], psm[j][:], 0.0)
                    p.tensor_scalar_max(t1b[j][:], psm[j][:], 0.0)

                ch.comp("vector", e_elu, relax=True,
                        dep=Chain.dep_max(s_mm[o], s_st.get(o - 2, s_in)))
                sv = ch.snap()
                ch.comp("scalar",
                        lambda p, j=j: p.activation(t0b[j][:], t0b[j][:],
                                                    AF.Exp), dep=sv)
                sa_ = ch.snap()

                def e_sum(p, kind=kind, j=j):
                    p.tensor_tensor(evt[j][:], t0b[j][:], t1b[j][:], ALU.add)
                    if kind == 0:
                        p.tensor_scalar_mul(evt[j][:], evt[j][:], 0.125)

                ch.comp("vector", e_sum, dep=sa_)
            else:
                ch.comp("vector",
                        lambda p, j=j: p.tensor_copy(evt[j][:], psm[j][:]),
                        dep=Chain.dep_max(s_mm[o], s_st.get(o - 2, s_in)))
            s_ev[o] = ch.snap()
            ch.dma_n("sync", 1, lambda p, o=o, t0=t0, kind=kind, j=j: [
                p.dma_start(out=outs[kind][(o % 8) * 128:(o % 8 + 1) * 128,
                            t0:t0 + TT], in_=evt[j][:])], dep=s_ev[o])
            s_st[o] = ch.snap()
        s_mmk, s_evk, s_stk = {}, {}, {}
        for s in range(4):
            stile = t * 4 + s
            for cb in range(4):
                i = s * 4 + cb
                j = i % 2

                def e_kv(p, s=s, cb=cb, j=j):
                    for c in range(NCHUNK):
                        p.matmul(psm[j][:, :],
                                 ub[:, c, s * 128:(s + 1) * 128],
                                 wqs[:, c, 1024 + cb * 512:1024 +
                                     (cb + 1) * 512],
                                 start=(c == 0), stop=(c == NCHUNK - 1))

                ch.comp("tensor", e_kv, relax=True,
                        dep=Chain.dep_max(s_in, s_evk.get(i - 2, s_in),
                                          s_ev.get(23, s_in)))
                s_mmk[i] = ch.snap()
                if cb < 2:
                    def e_elu2(p, j=j):
                        p.tensor_scalar_min(t0b[j][:], psm[j][:], 0.0)
                        p.tensor_scalar_max(t1b[j][:], psm[j][:], 0.0)

                    ch.comp("vector", e_elu2, relax=True,
                            dep=Chain.dep_max(s_mmk[i],
                                              s_stk.get(i - 2, s_in)))
                    sv = ch.snap()
                    ch.comp("scalar",
                            lambda p, j=j: p.activation(
                                t0b[j][:], t0b[j][:], AF.Exp), dep=sv)
                    sa_ = ch.snap()
                    ch.comp("vector", lambda p, j=j: p.tensor_tensor(
                        evb[j][:], t0b[j][:], t1b[j][:], ALU.add), dep=sa_)
                else:
                    ch.comp("vector", lambda p, j=j: p.tensor_copy(
                        evb[j][:], psm[j][:]),
                            dep=Chain.dep_max(s_mmk[i],
                                              s_stk.get(i - 2, s_in)))
                s_evk[i] = ch.snap()
                ch.dma_n("sync", 1, lambda p, stile=stile, cb=cb, j=j: [
                    p.dma_start(out=kvtm[stile, :,
                                cb * 512:(cb + 1) * 512],
                                in_=evb[j][:])], dep=s_evk[i])
                s_stk[i] = ch.snap()
    cx.release(mk)


def emit_a2f(cx, T):
    CH = 256
    NCHL = HSEQ // CH
    ch = cx.ch
    nc = cx.nc
    mk = cx.mark()
    qd, kd, kvtm, oo = T["q"], T["k"], T["kvtm"], T["o"]
    selA, selB = T["selA"], T["selB"]
    agS_in = nc.dram_tensor("a2_agin", [64, 16 * 65], F32,
                            kind="Internal").ap()
    agS_out = nc.dram_tensor("a2_agout", [128, 16 * 65], F32,
                             kind="Internal").ap()
    qhd = qd.rearrange("(h d) t -> d h t", d=64)
    khd = kd.rearrange("(h d) t -> d h t", d=64)
    ohd = oo.rearrange("(h d) t -> d h t", d=64)
    qc = cx.sbuf("qc", [64, 16, CH], BF16)
    kc = cx.sbuf("kc", [64, 16, CH], BF16)
    kvsb = cx.sbuf("kvsb", [128, 2, 2048], BF16)
    atT = [cx.sbuf(f"atT{i}", [128, CH], BF16) for i in range(2)]
    maskb = [cx.sbuf(f"maskb{i}", [128, CH], BF16) for i in range(2)]
    maskf = cx.sbuf("maskf", [128, CH], F32)
    Schk = cx.sbuf("Schk", [64, 16, NCHL, 65], F32)
    Stot = cx.sbuf("Stot", [64, 16, 65], F32)
    Srun = cx.sbuf("Srun", [64, 16, 65], F32)
    Srb = cx.sbuf("Srb", [64, 16, 65], BF16)
    dd = cx.sbuf("dd", [1, CH], F32)
    bc = cx.sbuf("bc", [64, CH], F32)
    carc = cx.sbuf("carc", [64, CH], F32)
    carD = cx.sbuf("carD", [1, CH], F32)
    numt = cx.sbuf("numt", [64, CH], F32)
    ofm = cx.sbuf("ofm", [64, 16, CH], F32)
    ones_row64 = cx.sbuf("ones_row64", [1, 64], F32)
    ps_t = [cx.psum(f"ps_t{i}", [128, CH]) for i in range(2)]
    ps_num = cx.psum("ps_num", [64, CH])
    ps_den = cx.psum("ps_den", [1, CH])
    ps_bc = cx.psum("ps_bc", [64, CH])
    ps_S = cx.psum("ps_S", [64, 128])
    ps_car = cx.psum("ps_car", [64, CH])
    ps_carD = cx.psum("ps_carD", [1, CH])

    def e_consts(p):
        p.memset(ones_row64[:], 1.0)
        p.memset(maskf[:], 0.0)
        p.memset(Stot[:], 0.0)

    ch.comp("vector", e_consts)
    ch.comp("gpsimd", lambda p: p.affine_select(
        maskf[:], maskf[:], [[-1, CH]], ALU.is_ge, 1.0, base=-1,
        channel_multiplier=1))

    def e_mb(p):
        p.tensor_copy(maskb[0][:], maskf[:])
        p.memset(maskf[:], 0.0)

    ch.comp("vector", e_mb)
    ch.comp("gpsimd", lambda p: p.affine_select(
        maskf[:], maskf[:], [[-1, CH]], ALU.is_ge, 1.0, base=127,
        channel_multiplier=1))
    ch.comp("vector", lambda p: p.tensor_copy(maskb[1][:], maskf[:]))

    for i in range(NCHL):
        ch.dma_n("sync", 2, lambda p, i=i: [
            p.dma_start(out=kvsb[:, 0, :], in_=kvtm[2 * i]),
            p.dma_start(out=kvsb[:, 1, :], in_=kvtm[2 * i + 1]),
        ])
        for h in range(16):
            def e_S(p, h=h):
                for m in range(2):
                    p.matmul(ps_S[0:64, 0:64], kvsb[:, m, h * 64:h * 64 + 64],
                             kvsb[:, m, 1024 + h * 64:1024 + h * 64 + 64],
                             start=(m == 0), stop=(m == 1))
                for m in range(2):
                    p.matmul(ps_S[0:64, 64:65],
                             kvsb[:, m, h * 64:h * 64 + 64],
                             cx.ones_col_bf[:], start=(m == 0), stop=(m == 1))

            ch.comp("tensor", e_S, relax=True)

            def e_acc(p, i=i, h=h):
                p.tensor_copy(Schk[:, h, i, :], ps_S[0:64, 0:65])
                p.tensor_tensor(Stot[:, h, :], Stot[:, h, :],
                                ps_S[0:64, 0:65], ALU.add)

            ch.comp("vector", e_acc, relax=True)

    mkx = cx.mark()
    sbt = cx.sbuf("sbt", [64, 16 * 65], F32)
    ch.comp("vector", lambda p: p.tensor_copy(
        sbt[:], Stot[:].rearrange("p c e -> p (c e)")))
    ch.dma_n("sync", 1, lambda p: [p.dma_start(out=agS_in[:, :], in_=sbt[:])])

    def wrS(ch2, cmb, r0, c0, cw, slow2):
        def e(p):
            p.tensor_copy(Srun[:].rearrange("p c e -> p (c e)")[:,
                          c0:c0 + cw], cmb[0:64, 0:cw])

        ch2.comp("vector", e)

    emit_pair_exchange(cx, "a2x", agS_in, agS_out, 64, 16 * 65, selA, selB,
                       wrS)
    cx.release(mkx)
    ch.comp("vector", lambda p: p.tensor_copy(Srb[:], Srun[:]))

    for i in range(NCHL):
        i0 = i * CH
        ch.dma_n("sync", 2, lambda p, i=i: [
            p.dma_start(out=kvsb[:, 0, :], in_=kvtm[2 * i]),
            p.dma_start(out=kvsb[:, 1, :], in_=kvtm[2 * i + 1]),
        ])
        ch.dma_n("gpsimd", 2, lambda p, i0=i0: [
            p.dma_start(out=qc[:], in_=qhd[:, :, i0:i0 + CH]),
            p.dma_start(out=kc[:], in_=khd[:, :, i0:i0 + CH]),
        ])
        for h in range(16):
            def e_attnT(p, h=h):
                p.matmul(ps_t[0][:], kc[:, h, 0:128], qc[:, h, :],
                         start=True, stop=True)
                p.matmul(ps_t[1][:], kc[:, h, 128:256], qc[:, h, :],
                         start=True, stop=True)

            ch.comp("tensor", e_attnT, relax=True)

            def e_mask(p):
                p.tensor_tensor(atT[0][:], ps_t[0][:], maskb[0][:], ALU.mult)
                p.tensor_tensor(atT[1][:], ps_t[1][:], maskb[1][:], ALU.mult)

            ch.comp("vector", e_mask, relax=True)

            def e_num(p, h=h):
                for m in range(2):
                    p.matmul(ps_num[:, :],
                             kvsb[:, m, 1024 + h * 64:1024 + h * 64 + 64],
                             atT[m][:], start=(m == 0), stop=(m == 1))
                for m in range(2):
                    p.matmul(ps_den[:, :], cx.ones_col_bf[:], atT[m][:],
                             start=(m == 0), stop=(m == 1))
                p.matmul(ps_car[:, :], Srb[:, h, 0:64], qc[:, h, :],
                         start=True, stop=True)
                p.matmul(ps_carD[:, :], Srb[:, h, 64:65], qc[:, h, :],
                         start=True, stop=True)

            ch.comp("tensor", e_num, relax=True)

            def e_div(p):
                p.tensor_copy(carc[:, :], ps_car[:, :])
                p.tensor_copy(carD[:, :], ps_carD[:, :])
                p.tensor_tensor(dd[:], ps_den[:, :], carD[:, :], ALU.add)
                p.tensor_scalar_max(dd[:], dd[:], 1e-6)
                p.reciprocal(dd[:], dd[:])
                p.tensor_tensor(numt[:], ps_num[:, :], carc[:, :], ALU.add)

            ch.comp("vector", e_div)
            ch.comp("tensor", lambda p: p.matmul(
                ps_bc[:, :], ones_row64[:], dd[:], start=True, stop=True), relax=True)

            def e_out(p, h=h):
                p.tensor_copy(bc[:], ps_bc[:, :])
                p.tensor_tensor(ofm[:, h, :], numt[:], bc[:], ALU.mult)

            ch.comp("vector", e_out)

        def e_Supd(p, i=i):
            p.tensor_tensor(Srun[:, :, :], Srun[:, :, :], Schk[:, :, i, :],
                            ALU.add)
            p.tensor_copy(Srb[:], Srun[:])

        ch.comp("vector", e_Supd)
        ch.dma_n("sync", 1, lambda p, i0=i0: [
            p.dma_start(out=ohd[:, :, i0:i0 + CH], in_=ofm[:])])
    cx.release(mk)


def emit_projf(cx, T):
    ch = cx.ch
    mk = cx.mark()
    ii, xx, wt, y = T["inp"], T["x"], T["wt"], T["y"]
    it = cx.sbuf("it", [128, NCHUNK, TT], BF16)
    xt = cx.sbuf("xt", [128, NCHUNK, TT], F32)
    ws = cx.sbuf("ws", [128, NCHUNK, D_MODEL], BF16)
    yt = cx.sbuf("yt", [128, NCHUNK, TT], F32)
    yb = cx.sbuf("yb", [128, NCHUNK, TT], BF16)
    sq = cx.sbuf("sq", [128, NCHUNK, TT], BF16)
    rows = cx.sbuf("rows", [1, 4 * TT], F32)
    outt = cx.sbuf("outt", [128, NCHUNK, TT], F32)
    psm = [cx.psum(f"psm{i}", [128, TT]) for i in range(2)]
    psA = cx.psum("psA", [128, TT])
    psB = cx.psum("psB", [128, TT])
    psA2 = cx.psum("psA2", [128, TT])
    psB2 = cx.psum("psB2", [128, TT])

    ch.dma_n("gpsimd", 1, lambda p: [p.dma_start(out=ws[:], in_=fmview(wt))])
    for t in range(NTILE):
        t0 = t * TT
        ch.dma_n("gpsimd", 1, lambda p, t0=t0: [
            p.dma_start(out=it[:], in_=fmview(ii)[:, :, t0:t0 + TT])])
        ch.dma_n("sync", 1, lambda p, t0=t0: [
            p.dma_start(out=xt[:], in_=fmview(xx)[:, :, t0:t0 + TT])])
        s_in = ch.snap()
        s_mm, s_ev = {}, {}
        for o in range(NCHUNK):
            j = o % 2

            def e_mm(p, o=o, j=j):
                for c in range(NCHUNK):
                    p.matmul(psm[j][:, :], ws[:, c, o * 128:(o + 1) * 128],
                             it[:, c, :], start=(c == 0),
                             stop=(c == NCHUNK - 1))

            ch.comp("tensor", e_mm, relax=True,
                    dep=Chain.dep_max(s_in, s_ev.get(o - 2, s_in)))
            s_mm[o] = ch.snap()

            def e_ev(p, o=o, j=j):
                p.tensor_copy(yt[:, o, :], psm[j][:])
                p.tensor_copy(yb[:, o, :], psm[j][:])

            ch.comp("vector", e_ev, relax=True, dep=s_mm[o])
            s_ev[o] = ch.snap()
        emit_ln(cx, yt, TT, sq, yb, psA, psB, psA2, psB2, rows, need_xb=False)
        emit_ln_apply(cx, yt, TT, psA2, psB2, outt, add_to=xt)
        ch.dma_n("sync", 1, lambda p, t0=t0: [
            p.dma_start(out=fmview(y)[:, :, t0:t0 + TT], in_=outt[:])])
    cx.release(mk)


def emit_r1f(cx, T):
    ch = cx.ch
    mk = cx.mark()
    x, halo1 = T["x"], T["halo1"]
    ko, vo, sro = T["k"], T["v"], T["sr"]
    wkt, wvt, wrt = T["wkt"], T["wvt"], T["wrt"]
    mixk, mixv, mixr = T["mixk"], T["mixv"], T["mixr"]
    xt = cx.sbuf("xt", [128, NCHUNK, TT], F32)
    xh = cx.sbuf("xh", [128, NCHUNK, 1], F32)
    ut = cx.sbuf("ut", [128, NCHUNK, TT], F32)
    ucol = cx.sbuf("ucol", [128, NCHUNK, 1], F32)
    dcol = cx.sbuf("dcol", [128, NCHUNK, TT], F32)
    sq = cx.sbuf("sq", [128, NCHUNK, TT], BF16)
    xb = cx.sbuf("xb", [128, NCHUNK, TT], BF16)
    mixs = [cx.sbuf(f"mix{i}", [128, NCHUNK], F32) for i in range(3)]
    wss = [cx.sbuf(f"w{i}", [128, NCHUNK, D_MODEL], BF16) for i in range(3)]
    mxb = [cx.sbuf(f"mxb{i}", [128, NCHUNK, TT], BF16) for i in range(3)]
    evt = [cx.sbuf(f"evt{i}", [128, TT], F32) for i in range(2)]
    rows = cx.sbuf("rows", [1, 4 * TT], F32)
    psA = cx.psum("psA", [128, TT])
    psB = cx.psum("psB", [128, TT])
    psA2 = cx.psum("psA2", [128, TT])
    psB2 = cx.psum("psB2", [128, TT])
    psm = [cx.psum(f"psm{i}", [128, TT]) for i in range(2)]

    ch.dma_n("gpsimd", 6, lambda p: [
        p.dma_start(out=wss[0][:], in_=fmview(wkt)),
        p.dma_start(out=wss[1][:], in_=fmview(wvt)),
        p.dma_start(out=wss[2][:], in_=fmview(wrt)),
        p.dma_start(out=mixs[0][:], in_=mixk[:, :]),
        p.dma_start(out=mixs[1][:], in_=mixv[:, :]),
        p.dma_start(out=mixs[2][:], in_=mixr[:, :]),
    ])
    ch.dma_n("sync", 1,
             lambda p: [p.dma_start(out=xh[:], in_=fmview(halo1)[:, :, 0:1],
                                    allow_slow_non_contiguous=True)])
    emit_ln(cx, xh, 1, sq, xb, psA, psB, psA2, psB2, rows)
    emit_ln_apply(cx, xh, 1, psA2, psB2, ucol)

    outs = (ko, vo, sro)
    for t in range(NTILE):
        t0 = t * TT
        ch.dma_n("sync", 1, lambda p, t0=t0: [
            p.dma_start(out=xt[:], in_=fmview(x)[:, :, t0:t0 + TT])])
        emit_ln(cx, xt, TT, sq, xb, psA, psB, psA2, psB2, rows)
        emit_ln_apply(cx, xt, TT, psA2, psB2, ut)

        def e_d(p):
            for c in range(NCHUNK):
                p.tensor_tensor(dcol[:, c, 0:1], ut[:, c, 0:1],
                                ucol[:, c, 0:1], ALU.subtract)
                p.tensor_tensor(dcol[:, c, 1:TT], ut[:, c, 1:TT],
                                ut[:, c, 0:TT - 1], ALU.subtract)

        ch.comp("vector", e_d, relax=True)

        for mi in range(3):
            def e_mix(p, mi=mi):
                for c in range(NCHUNK):
                    p.scalar_tensor_tensor(mxb[mi][:, c, 0:1],
                                           dcol[:, c, 0:1],
                                           mixs[mi][:, c:c + 1],
                                           ucol[:, c, 0:1], ALU.mult, ALU.add)
                    p.scalar_tensor_tensor(
                        mxb[mi][:, c, 1:TT], dcol[:, c, 1:TT],
                        mixs[mi][:, c:c + 1], ut[:, c, 0:TT - 1], ALU.mult,
                        ALU.add)

            ch.comp("vector", e_mix, relax=True)
        s_in = ch.snap()
        s_mm, s_ev, s_st = {}, {}, {}
        for mi in range(3):
            for o in range(NCHUNK):
                i = mi * NCHUNK + o
                j = i % 2

                def e_mm(p, mi=mi, o=o, j=j):
                    for c in range(NCHUNK):
                        p.matmul(psm[j][:, :],
                                 wss[mi][:, c, o * 128:(o + 1) * 128],
                                 mxb[mi][:, c, :], start=(c == 0),
                                 stop=(c == NCHUNK - 1))

                ch.comp("tensor", e_mm, relax=True,
                        dep=Chain.dep_max(s_in, s_ev.get(i - 2, s_in)))
                s_mm[i] = ch.snap()
                dd_ = Chain.dep_max(s_mm[i], s_st.get(i - 2, s_in))
                if mi < 2:
                    ch.comp("vector",
                            lambda p, j=j: p.tensor_copy(evt[j][:],
                                                         psm[j][:]),
                            dep=dd_)
                else:
                    ch.comp("scalar",
                            lambda p, j=j: p.activation(evt[j][:], psm[j][:],
                                                        AF.Sigmoid), dep=dd_)
                s_ev[i] = ch.snap()
                ch.dma_n("sync", 1, lambda p, mi=mi, o=o, t0=t0, j=j: [
                    p.dma_start(out=outs[mi][o * 128:(o + 1) * 128,
                                t0:t0 + TT], in_=evt[j][:])], dep=s_ev[i])
                s_st[i] = ch.snap()

        def e_carry(p):
            return p.tensor_copy(ucol[:, :, 0:1], ut[:, :, TT - 1:TT])

        ch.comp("vector", e_carry)
    cx.release(mk)


def emit_wkvf(cx, T):
    ch = cx.ch
    nc = cx.nc
    mk = cx.mark()
    kd, vd, oo = T["k"], T["v"], T["wkv"]
    wv, uv = T["rw"], T["ru"]
    selA, selB = T["selA"], T["selB"]
    agin = nc.dram_tensor("wkv_agin", [128, 16], F32, kind="Internal").ap()
    agout = nc.dram_tensor("wkv_agout", [256, 16], F32, kind="Internal").ap()
    kb = cx.sbuf("kb", [128, HSEQ], F32)
    vb = cx.sbuf("vb", [128, HSEQ], F32)
    ek = cx.sbuf("ek", [128, HSEQ], F32)
    ekv = cx.sbuf("ekv", [128, HSEQ], F32)
    dbc = cx.sbuf("dbc", [128, HSEQ], F32)
    ns = cx.sbuf("ns", [128, HSEQ], F32)
    ds = cx.sbuf("ds", [128, HSEQ], F32)
    zb = cx.sbuf("zb", [128, HSEQ], F32)
    pw = cx.sbuf("pw", [128, HSEQ], F32)
    wd = cx.sbuf("wd", [128, NCHUNK], F32)
    eu = cx.sbuf("eu", [128, NCHUNK], F32)
    lasts = cx.sbuf("lasts", [128, 16], F32)
    carry = cx.sbuf("carry", [128, 16], F32)

    ch.dma_n("sync", 2, lambda p: [
        p.dma_start(out=wd[:], in_=wv[:, :]),
        p.dma_start(out=eu[:], in_=uv[:, :]),
    ])
    ch.comp("scalar", lambda p: p.activation(wd[:], wd[:], AF.Exp))
    ch.comp("vector", lambda p: p.tensor_scalar_mul(wd[:], wd[:], -1.0))
    ch.comp("scalar", lambda p: p.activation(wd[:], wd[:], AF.Exp))
    ch.comp("scalar", lambda p: p.activation(eu[:], eu[:], AF.Exp))
    ch.comp("vector", lambda p: p.memset(zb[:], 0.0))

    def scan_pass(c, final):
        ch.dma_n("sync", 2, lambda p, c=c: [
            p.dma_start(out=kb[:], in_=fmview(kd)[:, c, :]),
            p.dma_start(out=vb[:], in_=fmview(vd)[:, c, :]),
        ])
        ch.comp("scalar", lambda p: p.activation(ek[:], kb[:], AF.Exp))

        def e_pre(p, c=c):
            p.tensor_tensor(ekv[:], ek[:], vb[:], ALU.mult)
            p.memset(dbc[:], 1.0)
            p.tensor_scalar_mul(dbc[:], dbc[:], wd[:, c:c + 1])
            p.tensor_tensor_scan(ns[:], dbc[:], ekv[:], 0.0, ALU.mult,
                                 ALU.add)
            p.tensor_tensor_scan(ds[:], dbc[:], ek[:], 0.0, ALU.mult,
                                 ALU.add)

        ch.comp("vector", e_pre)
        if not final:
            def e_last(p, c=c):
                p.tensor_copy(lasts[:, 2 * c:2 * c + 1],
                              ns[:, HSEQ - 1:HSEQ])
                p.tensor_copy(lasts[:, 2 * c + 1:2 * c + 2],
                              ds[:, HSEQ - 1:HSEQ])

            ch.comp("vector", e_last, relax=True)
            return

        def e_fin(p, c=c):
            p.tensor_tensor_scan(pw[:], dbc[:], zb[:], 1.0, ALU.mult, ALU.add)
            p.scalar_tensor_tensor(ns[:], pw[:], carry[:, 2 * c:2 * c + 1],
                                   ns[:], ALU.mult, ALU.add)
            p.scalar_tensor_tensor(ds[:], pw[:], carry[:, 2 * c + 1:2 * c + 2],
                                   ds[:], ALU.mult, ALU.add)
            p.scalar_tensor_tensor(kb[:, 0:1], ekv[:, 0:1], eu[:, c:c + 1],
                                   carry[:, 2 * c:2 * c + 1], ALU.mult,
                                   ALU.add)
            p.scalar_tensor_tensor(vb[:, 0:1], ek[:, 0:1], eu[:, c:c + 1],
                                   carry[:, 2 * c + 1:2 * c + 2], ALU.mult,
                                   ALU.add)
            p.scalar_tensor_tensor(kb[:, 1:HSEQ], ekv[:, 1:HSEQ],
                                   eu[:, c:c + 1], ns[:, 0:HSEQ - 1],
                                   ALU.mult, ALU.add)
            p.scalar_tensor_tensor(vb[:, 1:HSEQ], ek[:, 1:HSEQ],
                                   eu[:, c:c + 1], ds[:, 0:HSEQ - 1],
                                   ALU.mult, ALU.add)
            p.reciprocal(vb[:], vb[:])
            p.tensor_tensor(ek[:], kb[:], vb[:], ALU.mult)

        ch.comp("vector", e_fin)
        ch.dma_n("sync", 1, lambda p, c=c: [
            p.dma_start(out=fmview(oo)[:, c, :], in_=ek[:])])

    for c in range(NCHUNK):
        scan_pass(c, False)
    ch.dma_n("sync", 1, lambda p: [p.dma_start(out=agin[:, :], in_=lasts[:])])

    def wrC(ch2, cmb, r0, c0, cw, slow2):
        ch2.comp("vector",
                 lambda p: p.tensor_copy(carry[:, c0:c0 + cw],
                                         cmb[:, 0:cw]))

    emit_pair_exchange(cx, "wkvx", agin, agout, 128, 16, selA, selB, wrC)
    for c in range(NCHUNK):
        scan_pass(c, True)
    cx.release(mk)


def emit_l0ff(cx, T, ln_in, fin_out):
    W = L0F_W
    ch = cx.ch
    nc = cx.nc
    mk = cx.mark()
    xs = T["xs"]
    wqkv = T["wqkv"]
    qsc = T["qsc"]
    ksc = T["ksc"]
    wo = T["wo"]
    lg = T["lg"]
    lb = T["lb"]
    w1pt = T["w1pt"]
    b1p = T["b1p"]
    w2pt = T["w2pt"]
    b2 = T["b2"]
    prot = T["prot"]
    tqc = T["tqc"]
    tqs = T["tqs"]
    tkc = T["tkc"]
    tks = T["tks"]
    amask = T["amask"]
    bmask = T["bmask"]
    ffnkeep = T["ffnkeep"]
    halo = T["halo"]
    if fin_out:
        xo = T["xo"]
    y = T["y"]

    xt = cx.sbuf("xt", [128, NCHUNK, W], F32)
    ut = cx.sbuf("ut", [128, NCHUNK, W], F32)
    xm = cx.sbuf("xm", [128, NCHUNK, W], F32)
    hs = cx.sbuf("hs", [128, NCHUNK, W], F32)
    ucar = cx.sbuf("ucar", [128, NCHUNK, 1], F32)
    xmcar = cx.sbuf("xmcar", [128, NCHUNK, 1], F32)
    xh1 = cx.sbuf("xh1", [128, NCHUNK, 1], F32)
    rows = cx.sbuf("rows", [1, 4 * W], F32)
    sq = cx.sbuf("sq", [128, NCHUNK, W], BF16)
    xb = cx.sbuf("xb", [128, NCHUNK, W], BF16)
    wqs = [cx.sbuf(f"wqs{i}", [128, NCHUNK, 128], BF16) for i in range(2)]
    wos = cx.sbuf("wos", [128, 4, D_MODEL], BF16)
    w1s = [cx.sbuf(f"w1s{i}", [128, NCHUNK, 128], BF16) for i in range(2)]
    w2s = cx.sbuf("w2s", [128, 22, 128], BF16)
    qscs = cx.sbuf("qscs", [128, 1], F32)
    kscs = cx.sbuf("kscs", [128, 1], F32)
    lgs = cx.sbuf("lgs", [128, NCHUNK], F32)
    lbs = cx.sbuf("lbs", [128, NCHUNK], F32)
    b1s = cx.sbuf("b1s", [128, 44], F32)
    b2s = cx.sbuf("b2s", [128, NCHUNK], F32)
    fks = cx.sbuf("fks", [128, 1], F32)
    t4 = [cx.sbuf(f"t4_{i}", [128, W], F32) for i in range(4)]
    ams = cx.sbuf("ams", [128, 4, 1024], BF16)
    bms = cx.sbuf("bms", [128, 16, W], BF16)
    ident = cx.sbuf("ident", [128, 128], F32)
    blkdiag = cx.sbuf("blkdiag", [128, 128], BF16)
    blkrot = cx.sbuf("blkrot", [128, 128], BF16)
    onesq = cx.sbuf("onesq", [128, 128], BF16)
    ones64 = cx.sbuf("ones64", [1, 64], F32)
    qr = cx.sbuf("qr", [128, 4, W], BF16)
    kr = cx.sbuf("kr", [128, 4, 2 * W], BF16)
    vat = cx.sbuf("vat", [128, 8, 8, 65], BF16)
    evq = cx.sbuf("evq", [128, W], F32)
    qhat = cx.sbuf("qhat", [128, W], F32)
    rstb = cx.sbuf("rstb", [128, W], F32)
    rt1 = cx.sbuf("rt1", [128, W], F32)
    sqs = cx.sbuf("sqs", [128, W], BF16)
    nmrow = cx.sbuf("nmrow", [128, W], BF16)
    nmcol = cx.sbuf("nmcol", [128, 1], F32)
    n0 = cx.sbuf("n0", [128, 1], F32)
    atT = cx.sbuf("atT", [128, W], BF16)
    dd = cx.sbuf("dd", [1, W], F32)
    bc = cx.sbuf("bc", [64, W], F32)
    ow = cx.sbuf("ow", [128, 4, W], BF16)
    hgl = cx.sbuf("hgl", [128, 22, W], BF16)
    sa = evq
    gg = qhat
    P0 = cx.psum("P0", [128, 1024])
    P1 = cx.psum("P1", [128, W])
    P2 = cx.psum("P2", [128, W])
    P3 = cx.psum("P3", [128, W])
    P4 = cx.psum("P4", [128, W])
    P5 = cx.psum("P5", [128, W])

    ch.dma_n("gpsimd", 6, lambda p: [
        p.dma_start(out=wos[:], in_=wo.rearrange("(c p) o -> p c o", p=128)),
        p.dma_start(out=ams[:], in_=amask.rearrange("s i p j -> p (s i) j")[:, 4:8, :]),
        p.dma_start(out=bms[:], in_=bmask.rearrange("s r p j -> p (s r) j")),
        p.dma_start(out=qscs[:], in_=qsc[:, :]),
        p.dma_start(out=kscs[:], in_=ksc[:, :]),
        p.dma_start(out=blkrot[:], in_=prot[:, :]),
    ])
    ch.dma_n("sync", 6, lambda p: [
        p.dma_start(out=lgs[:], in_=lg[:, :]),
        p.dma_start(out=lbs[:], in_=lb[:, :]),
        p.dma_start(out=b1s[:], in_=b1p[:, :]),
        p.dma_start(out=b2s[:], in_=b2[:, :]),
        p.dma_start(out=fks[:], in_=ffnkeep[:, :]),
        p.dma_start(out=xh1[:], in_=fmview(halo)[:, :, 0:1],
                    allow_slow_non_contiguous=True),
    ])

    def e_consts(p):
        p.memset(ident[:], 0.0)
        p.memset(blkdiag[:], 0.0)
        p.memset(blkdiag[0:64, 0:64], 1.0)
        p.memset(blkdiag[64:128, 64:128], 1.0)
        p.memset(onesq[:], 1.0)
        p.memset(ones64[:], 1.0)
        p.memset(kr[:], 0.0)
        p.memset(vat[:], 0.0)

    ch.comp("vector", e_consts)
    ch.comp("gpsimd", lambda p: p.affine_select(
        ident[:], ident[:], [[-1, 128]], ALU.not_equal, 1.0, base=0,
        channel_multiplier=1))

    if ln_in:
        emit_ln(cx, xh1, 1, sq, xb, P2, P3, P1, P5, rows)
        emit_ln_apply(cx, xh1, 1, P1, P5, ucar)
    else:
        ch.comp("vector", lambda p: p.tensor_copy(ucar[:], xh1[:]))

    for w in range(L0F_NW):
        if w < 2:
            hsrc = fmview(halo)[:, :, 1 + w * W:1 + (w + 1) * W]
        else:
            hsrc = fmview(xs)[:, :, (w - 2) * W:(w - 1) * W]
        ch.dma_n("sync", 1, lambda p, hsrc=hsrc: [
            p.dma_start(out=xt[:], in_=hsrc)])
        if ln_in:
            emit_ln(cx, xt, W, sq, xb, P2, P3, P1, P5, rows)
            emit_ln_apply(cx, xt, W, P1, P5, ut)
        else:
            ch.comp("vector", lambda p: p.tensor_copy(ut[:], xt[:]))

        def e_shift(p):
            for c in range(NCHUNK):
                if c < 4:
                    p.tensor_copy(xb[:, c, :], ut[:, c, :])
                else:
                    p.tensor_copy(xb[:, c, 0:1], ucar[:, c, 0:1])
                    p.tensor_copy(xb[:, c, 1:W], ut[:, c, 0:W - 1])

        ch.comp("vector", e_shift, relax=True)
        ch.comp("vector",
                lambda p: p.tensor_copy(ucar[:], ut[:, :, W - 1:W]))

        def e_ring(p):
            p.tensor_copy(kr[:, :, 0:W], kr[:, :, W:2 * W])
            p.tensor_copy(vat[:, 0:4, :, :], vat[:, 4:8, :, :])

        ch.comp("vector", e_ring, relax=True)
        ch.dma_n("sync", 4, lambda p, w=w: [
            p.dma_start(out=t4[0][:], in_=tqc[:, w * W:(w + 1) * W]),
            p.dma_start(out=t4[1][:], in_=tqs[:, w * W:(w + 1) * W]),
            p.dma_start(out=t4[2][:], in_=tkc[:, w * W:(w + 1) * W]),
            p.dma_start(out=t4[3][:], in_=tks[:, w * W:(w + 1) * W]),
        ])

        for o in range(8):
            kind = o // 4
            pt = o % 4
            wb = wqs[o % 2]
            ch.dma_n("gpsimd", 1,
                     lambda p, o=o, wb=wb: [p.dma_start(out=wb[:],
                                                        in_=wqkv[o])])

            def e_mm(p, wb=wb):
                for c in range(NCHUNK):
                    p.matmul(P4[:, :], wb[:, c, :], xb[:, c, :],
                             start=(c == 0), stop=(c == NCHUNK - 1))

            ch.comp("tensor", e_mm, relax=True)
            ch.comp("vector", lambda p: p.tensor_copy(evq[:], P4[:, :]))
            ch.comp("scalar",
                    lambda p: p.activation(sqs[:], evq[:], AF.Square))
            ch.comp("tensor", lambda p: p.matmul(
                P1[:, :], blkdiag[:], sqs[:], start=True, stop=True), relax=True)

            def e_rst(p):
                p.tensor_scalar_max(rstb[:], P1[:, :], 1e-24)
                p.reciprocal(rstb[:], rstb[:])

            ch.comp("vector", e_rst)
            ch.comp("scalar",
                    lambda p: p.activation(rstb[:], rstb[:], AF.Sqrt))

            def e_hat(p, kind=kind):
                scs = qscs if kind == 0 else kscs
                p.tensor_tensor(qhat[:], evq[:], rstb[:], ALU.mult)
                p.tensor_scalar_mul(qhat[:], qhat[:], scs[:])
                p.tensor_copy(sqs[:], qhat[:])

            ch.comp("vector", e_hat)
            ch.comp("tensor", lambda p: p.matmul(
                P2[:, :], blkrot[:], sqs[:], start=True, stop=True), relax=True)

            def e_rot(p, kind=kind, pt=pt):
                tc, ts_ = (t4[0], t4[1]) if kind == 0 else (t4[2], t4[3])
                p.tensor_tensor(rt1[:], qhat[:], tc[:], ALU.mult)
                p.tensor_tensor(qhat[:], P2[:, :], ts_[:], ALU.mult)
                if kind == 0:
                    p.tensor_tensor(qr[:, pt, :], rt1[:], qhat[:], ALU.add)
                else:
                    p.tensor_tensor(kr[:, pt, W:2 * W], rt1[:], qhat[:],
                                    ALU.add)

            ch.comp("vector", e_rot)

        for o in range(8, 12):
            wb = wqs[o % 2]
            ch.dma_n("gpsimd", 1,
                     lambda p, o=o, wb=wb: [p.dma_start(out=wb[:],
                                                        in_=wqkv[o])])

            def e_mm(p, wb=wb):
                for c in range(NCHUNK):
                    p.matmul(P4[:, :], wb[:, c, :], xb[:, c, :],
                             start=(c == 0), stop=(c == NCHUNK - 1))

            ch.comp("tensor", e_mm, relax=True)
            ch.comp("vector", lambda p: p.tensor_copy(evq[:], P4[:, :]))
            for hh in range(2):
                h = (o - 8) * 2 + hh
                rb = hh * 64
                for jj in range(4):
                    def e_tr(p, rb=rb, jj=jj):
                        return p.transpose(
                            P3[0:128, 0:64],
                            evq[rb:rb + 64, jj * 128:(jj + 1) * 128],
                            ident[rb:rb + 64, rb:rb + 64])

                    ch.comp("tensor", e_tr, relax=True)

                    def e_vat(p, h=h, jj=jj):
                        p.tensor_copy(vat[:, 4 + jj, h, 0:64],
                                      P3[0:128, 0:64])
                        p.memset(vat[:, 4 + jj, h, 64:65], 1.0)

                    ch.comp("vector", e_vat, relax=True)

        if w == 0:
            continue

        mset = 0 if w == 2 else 1
        if w == 2:
            ch.dma_n("gpsimd", 1, lambda p: [p.dma_start(
                out=ams[:],
                in_=amask.rearrange("s i p j -> p (s i) j")[:, 0:4, :])])
        if w == 3:
            ch.dma_n("gpsimd", 1, lambda p: [p.dma_start(
                out=ams[:],
                in_=amask.rearrange("s i p j -> p (s i) j")[:, 4:8, :])])
        for h in range(8):
            pt, rb = h // 2, (h % 2) * 64
            for it in range(4):
                def e_sim(p, pt=pt, rb=rb, it=it):
                    p.matmul(
                        P0[:, 0:W],
                        qr[rb:rb + 64, pt, it * 128:(it + 1) * 128],
                        kr[rb:rb + 64, pt, 0:W], start=True, stop=True)
                    return p.matmul(
                        P0[:, W:2 * W],
                        qr[rb:rb + 64, pt, it * 128:(it + 1) * 128],
                        kr[rb:rb + 64, pt, W:2 * W], start=True, stop=True)

                ch.comp("tensor", e_sim, relax=True)

                def e_am(p, it=it, mset=mset):
                    p.tensor_tensor(rt1[:], P0[:, 0:W],
                                    ams[:, it, 0:W], ALU.add)
                    p.reduce_max(n0[:], rt1[:], axis=AX.X, negate=True)
                    p.tensor_tensor(rt1[:], P0[:, W:2 * W],
                                    ams[:, it, W:2 * W], ALU.add)
                    p.reduce_max(nmcol[:], rt1[:], axis=AX.X, negate=True)
                    p.tensor_tensor(nmcol[:], nmcol[:], n0[:], ALU.min)

                ch.comp("vector", e_am)
                ch.comp("tensor", lambda p: p.matmul(
                    P3[0:1, 0:128], nmcol[:], ident[:], start=True,
                    stop=True), relax=True)

                def e_nm(p, it=it):
                    p.tensor_copy(nmrow[0:1, it * 128:(it + 1) * 128],
                                  P3[0:1, 0:128])
                    p.tensor_copy(nmrow[64:65, it * 128:(it + 1) * 128],
                                  P3[0:1, 0:128])

                ch.comp("vector", e_nm, relax=True)

            for js in range(8):
                def e_simT(p, pt=pt, rb=rb, js=js):
                    p.matmul(P1[:, :],
                             kr[rb:rb + 64, pt, js * 128:(js + 1) * 128],
                             qr[rb:rb + 64, pt, :], start=True, stop=False)
                    p.matmul(P1[:, :], onesq[rb:rb + 1, :],
                             nmrow[rb:rb + 1, :], start=False, stop=True)

                ch.comp("tensor", e_simT, relax=True)
                ch.comp("scalar",
                        lambda p: p.activation(atT[:], P1[:, :], AF.Exp))

                def e_bm(p, js=js, mset=mset):
                    r, jj = js // 4, js % 4
                    p.tensor_tensor(atT[:], atT[:],
                                    bms[:, mset * 8 + r * 4 + jj, :],
                                    ALU.mult)

                ch.comp("vector", e_bm)

                def e_av(p, h=h, js=js):
                    return p.matmul(P2[0:65, :], vat[:, js, h, :], atT[:],
                                    start=(js == 0), stop=(js == 7))

                ch.comp("tensor", e_av, relax=True)

            ch.comp("vector", lambda p: p.reciprocal(dd[:], P2[64:65, :]))
            ch.comp("tensor", lambda p: p.matmul(
                P5[0:64, :], ones64[:], dd[:], start=True, stop=True), relax=True)

            def e_o(p, h=h):
                p.tensor_copy(bc[:], P5[0:64, :])
                p.tensor_tensor(ow[(h % 2) * 64:(h % 2) * 64 + 64,
                                h // 2, :], P2[0:64, :], bc[:], ALU.mult)

            ch.comp("vector", e_o)

        for o in range(NCHUNK):
            def e_wo(p, o=o):
                for c in range(4):
                    p.matmul(P4[:, :], wos[:, c, o * 128:(o + 1) * 128],
                             ow[:, c, :], start=(c == 0), stop=(c == 3))

            ch.comp("tensor", e_wo, relax=True)
            ch.comp("vector", lambda p, o=o: p.tensor_tensor(
                xm[:, o, :], ut[:, o, :], P4[:, :], ALU.add))

        if w == 1:
            ch.comp("vector",
                    lambda p: p.tensor_copy(xmcar[:], xm[:, :, W - 1:W]))
            continue

        if w == 2:
            def e_fk(p):
                for c in range(NCHUNK):
                    p.tensor_scalar_mul(xmcar[:, c, 0:1], xmcar[:, c, 0:1],
                                        fks[:])

            ch.comp("vector", e_fk, relax=True)

        def e_shift2(p):
            for c in range(NCHUNK):
                if c < 4:
                    p.tensor_copy(hs[:, c, :], xm[:, c, :])
                else:
                    p.tensor_copy(hs[:, c, 0:1], xmcar[:, c, 0:1])
                    p.tensor_copy(hs[:, c, 1:W], xm[:, c, 0:W - 1])

        ch.comp("vector", e_shift2, relax=True)
        ch.comp("vector",
                lambda p: p.tensor_copy(xmcar[:], xm[:, :, W - 1:W]))
        emit_ln(cx, hs, W, sq, xb, P2, P3, P1, P5, rows)

        def e_ln2(p):
            for c in range(NCHUNK):
                p.tensor_tensor(rt1[:], hs[:, c, :], P1[:, :], ALU.mult)
                p.scalar_tensor_tensor(rt1[:], rt1[:], 1.0, P5[:, :],
                                       ALU.mult, ALU.add)
                p.tensor_scalar(sq[:, c, :], rt1[:], lgs[:, c:c + 1],
                                lbs[:, c:c + 1], ALU.mult, ALU.add)

        ch.comp("vector", e_ln2)

        for j in range(22):
            ch.dma_n("gpsimd", 2, lambda p, j=j: [
                p.dma_start(out=w1s[0][:], in_=w1pt[j]),
                p.dma_start(out=w1s[1][:], in_=w1pt[22 + j]),
            ])

            def e_w1a(p):
                for c in range(NCHUNK):
                    p.matmul(P4[:, :], w1s[0][:, c, :], sq[:, c, :],
                             start=(c == 0), stop=(c == NCHUNK - 1))

            ch.comp("tensor", e_w1a, relax=True)
            ch.comp("vector", lambda p, j=j: p.tensor_scalar_add(
                sa[:], P4[:, :], b1s[:, j:j + 1]))

            def e_w1g(p):
                for c in range(NCHUNK):
                    p.matmul(P5[:, :], w1s[1][:, c, :], sq[:, c, :],
                             start=(c == 0), stop=(c == NCHUNK - 1))

            ch.comp("tensor", e_w1g, relax=True)
            ch.comp("scalar", lambda p, j=j: p.activation(
                gg[:], P5[:, :], AF.Gelu, bias=b1s[:, 22 + j:23 + j],
                scale=1.0))
            ch.comp("vector", lambda p, j=j: p.tensor_tensor(
                hgl[:, j, :], sa[:], gg[:], ALU.mult))

        for o in range(NCHUNK):
            ch.dma_n("gpsimd", 1,
                     lambda p, o=o: [p.dma_start(out=w2s[:], in_=w2pt[o])])

            def e_w2(p):
                for c in range(22):
                    p.matmul(P4[:, :], w2s[:, c, :], hgl[:, c, :],
                             start=(c == 0), stop=(c == 21))

            ch.comp("tensor", e_w2, relax=True)

            def e_res(p, o=o):
                p.tensor_scalar_add(rt1[:], P4[:, :], b2s[:, o:o + 1])
                p.tensor_tensor(xm[:, o, :], xm[:, o, :], rt1[:], ALU.add)

            ch.comp("vector", e_res)

        t0 = (w - 2) * W
        if fin_out:
            ch.dma_n("sync", 1, lambda p, t0=t0: [
                p.dma_start(out=hs[:], in_=fmview(xo)[:, :, t0:t0 + W])])
            emit_ln(cx, xm, W, sq, xb, P2, P3, P1, P5, rows)
            emit_ln_apply(cx, xm, W, P1, P5, xm, tmp=rt1, add_to=hs)

        ch.dma_n("sync", 1, lambda p, t0=t0: [
            p.dma_start(out=fmview(y)[:, :, t0:t0 + W], in_=xm[:])])

    cx.release(mk)




def emit_n1f(cx, T):
    ch = cx.ch
    mk = cx.mark()
    x, w1t, w2t, s1, s2, h = (T["x"], T["w1t"], T["w2t"], T["s1"], T["s2"],
                              T["h"])
    xt = cx.sbuf("xt", [128, NCHUNK, TT], F32)
    sq = cx.sbuf("sq", [128, NCHUNK, TT], BF16)
    xb = cx.sbuf("xb", [128, NCHUNK, TT], BF16)
    ub = cx.sbuf("ub", [128, NCHUNK, TT], BF16)
    w1s = cx.sbuf("w1s", [128, NCHUNK, 4096], BF16)
    w2s = cx.sbuf("w2s", [128, NCHUNK, 4096], BF16)
    s1s = cx.sbuf("s1s", [128, 32], F32)
    s2s = cx.sbuf("s2s", [128, 32], F32)
    rows = cx.sbuf("rows", [1, 4 * TT], F32)
    sa = [cx.sbuf(f"sa{i}", [128, TT], F32) for i in range(2)]
    sb2 = [cx.sbuf(f"sb2{i}", [128, TT], F32) for i in range(2)]
    hb = [cx.sbuf(f"hb{i}", [128, TT], BF16) for i in range(2)]
    psA = cx.psum("psA", [128, TT])
    psB = cx.psum("psB", [128, TT])
    psA2 = cx.psum("psA2", [128, TT])
    psB2 = cx.psum("psB2", [128, TT])
    psm = [cx.psum(f"psm{i}", [128, TT]) for i in range(2)]
    psm2 = [cx.psum(f"psm2{i}", [128, TT]) for i in range(2)]

    ch.dma_n("gpsimd", 4, lambda p: [
        p.dma_start(out=w1s[:], in_=fmview(w1t)),
        p.dma_start(out=w2s[:], in_=fmview(w2t)),
        p.dma_start(out=s1s[:], in_=s1[:, :]),
        p.dma_start(out=s2s[:], in_=s2[:, :]),
    ])
    for t in range(NTILE):
        t0 = t * TT
        ch.dma_n("sync", 1, lambda p, t0=t0: [
            p.dma_start(out=xt[:], in_=fmview(x)[:, :, t0:t0 + TT])])
        emit_ln(cx, xt, TT, sq, xb, psA, psB, psA2, psB2, rows)
        emit_ln_apply(cx, xt, TT, psA2, psB2, ub)
        s_ub = ch.snap()
        s_mma, s_mmb, s_act, s_mul, s_st = {}, {}, {}, {}, {}
        for o in range(32):
            j = o % 2

            def e_mma(p, o=o, j=j):
                for c in range(NCHUNK):
                    p.matmul(psm[j][:, :], w1s[:, c, o * 128:(o + 1) * 128],
                             ub[:, c, :], start=(c == 0),
                             stop=(c == NCHUNK - 1))

            ch.comp("tensor", e_mma, relax=True,
                    dep=Chain.dep_max(s_ub, s_act.get(o - 2, s_ub)))
            s_mma[o] = ch.snap()

            def e_mmb(p, o=o, j=j):
                for c in range(NCHUNK):
                    p.matmul(psm2[j][:, :], w2s[:, c, o * 128:(o + 1) * 128],
                             ub[:, c, :], start=(c == 0),
                             stop=(c == NCHUNK - 1))

            ch.comp("tensor", e_mmb, relax=True,
                    dep=Chain.dep_max(s_ub, s_mul.get(o - 2, s_ub)))
            s_mmb[o] = ch.snap()
            ch.comp("scalar", lambda p, o=o, j=j: p.activation(
                sa[j][:], psm[j][:], AF.Silu, scale=s1s[:, o:o + 1]),
                    relax=True,
                    dep=Chain.dep_max(s_mma[o], s_mul.get(o - 2, s_ub)))
            s_act[o] = ch.snap()

            def e_mul(p, o=o, j=j):
                p.tensor_scalar_mul(sb2[j][:], psm2[j][:], s2s[:, o:o + 1])
                p.tensor_tensor(hb[j][:], sa[j][:], sb2[j][:], ALU.mult)

            ch.comp("vector", e_mul,
                    dep=Chain.dep_max(s_mmb[o], s_act[o],
                                      s_st.get(o - 2, s_ub)))
            s_mul[o] = ch.snap()
            ch.dma_n("sync", 1, lambda p, o=o, t0=t0, j=j: [
                p.dma_start(out=h[o * 128:(o + 1) * 128, t0:t0 + TT],
                            in_=hb[j][:])], dep=s_mul[o])
            s_st[o] = ch.snap()
    cx.release(mk)


def emit_nhf(cx, T):
    ch = cx.ch
    mk = cx.mark()
    hi, wt, ss, ho = T["hi"], T["wt"], T["ss"], T["ho"]
    ht = [cx.sbuf(f"ht{i}", [128, 32, TT], BF16) for i in range(2)]
    ws = [cx.sbuf(f"ws{i}", [128, 32, 128], BF16) for i in range(3)]
    sss = cx.sbuf("sss", [128, 32], F32)
    ob = [cx.sbuf(f"ob{i}", [128, TT], BF16) for i in range(2)]
    psm = [cx.psum(f"psm{i}", [128, TT]) for i in range(2)]

    ch.dma_n("gpsimd", 1, lambda p: [p.dma_start(out=sss[:], in_=ss[:, :])])
    z = ch.snap()
    s_ld = {}
    s_wld = {}
    s_mm = {}
    s_act = {}
    s_st = {}
    idx = lambda t, o: t * 32 + o
    for t in range(NTILE):
        t0 = t * TT
        ch.dma_n("sync", 1, lambda p, t0=t0, t=t: [
            p.dma_start(out=ht[t % 2][:],
                        in_=fmview(hi, 32)[:, :, t0:t0 + TT])],
                  dep=s_mm.get(idx(t - 2, 31), z))
        s_ld[t] = ch.snap()
        for o in range(32):
            i = idx(t, o)
            wb = ws[i % 3]
            ch.dma_n("gpsimd", 1, lambda p, o=o, wb=wb: [
                p.dma_start(out=wb[:],
                            in_=wt[o].rearrange("(c p) m -> p c m", p=128))],
                      dep=s_mm.get(i - 3, z))
            s_wld[i] = ch.snap()
            ps = psm[i % 2]

            def e_mm(p, wb=wb, ps=ps, t=t):
                for c in range(32):
                    p.matmul(ps[:, :], wb[:, c, :], ht[t % 2][:, c, :],
                             start=(c == 0), stop=(c == 31))

            ch.comp("tensor", e_mm, relax=True,
                    dep=Chain.dep_max(s_wld[i], s_ld[t],
                                      s_act.get(i - 2, z)))
            s_mm[i] = ch.snap()
            ch.comp("scalar", lambda p, o=o, ps=ps, ob2=ob[i % 2]:
                    p.activation(ob2[:], ps[:], AF.Gelu,
                                 scale=sss[:, o:o + 1]),
                    relax=True,
                    dep=Chain.dep_max(s_mm[i], s_st.get(i - 2, z)))
            s_act[i] = ch.snap()
            ch.dma_n("sync", 1, lambda p, o=o, t0=t0, ob2=ob[i % 2]: [
                p.dma_start(out=ho[o * 128:(o + 1) * 128, t0:t0 + TT],
                            in_=ob2[:])], dep=s_act[i])
            s_st[i] = ch.snap()
    cx.release(mk)


def emit_n4f(cx, T):
    ch = cx.ch
    mk = cx.mark()
    hi, w3t, s3, x3, y = T["hi"], T["w3t"], T["s3"], T["x3"], T["y"]
    ht = cx.sbuf("ht", [128, 32, TT], BF16)
    x3t = cx.sbuf("x3t", [128, NCHUNK, TT], F32)
    w3s = cx.sbuf("w3s", [128, 32, D_MODEL], BF16)
    s3s = cx.sbuf("s3s", [128, 8], F32)
    yt = cx.sbuf("yt", [128, NCHUNK, TT], F32)
    yb = cx.sbuf("yb", [128, NCHUNK, TT], BF16)
    sq = cx.sbuf("sq", [128, NCHUNK, TT], BF16)
    rows = cx.sbuf("rows", [1, 4 * TT], F32)
    outt = cx.sbuf("outt", [128, NCHUNK, TT], F32)
    psm = [cx.psum(f"psm{i}", [128, TT]) for i in range(2)]
    psA = cx.psum("psA", [128, TT])
    psB = cx.psum("psB", [128, TT])
    psA2 = cx.psum("psA2", [128, TT])
    psB2 = cx.psum("psB2", [128, TT])

    ch.dma_n("gpsimd", 2, lambda p: [
        p.dma_start(out=w3s[:], in_=fmview(w3t, 32)),
        p.dma_start(out=s3s[:], in_=s3[:, :]),
    ])
    for t in range(NTILE):
        t0 = t * TT
        ch.dma_n("sync", 2, lambda p, t0=t0: [
            p.dma_start(out=ht[:], in_=fmview(hi, 32)[:, :, t0:t0 + TT]),
            p.dma_start(out=x3t[:], in_=fmview(x3)[:, :, t0:t0 + TT]),
        ])
        s_in = ch.snap()
        s_mm, s_ev = {}, {}
        for o in range(NCHUNK):
            j = o % 2

            def e_mm(p, o=o, j=j):
                for c in range(32):
                    p.matmul(psm[j][:, :], w3s[:, c, o * 128:(o + 1) * 128],
                             ht[:, c, :], start=(c == 0), stop=(c == 31))

            ch.comp("tensor", e_mm, relax=True,
                    dep=Chain.dep_max(s_in, s_ev.get(o - 2, s_in)))
            s_mm[o] = ch.snap()

            def e_ev(p, o=o, j=j):
                p.tensor_scalar_mul(yt[:, o, :], psm[j][:], s3s[:, o:o + 1])
                p.tensor_scalar_mul(yb[:, o, :], psm[j][:], s3s[:, o:o + 1])

            ch.comp("vector", e_ev, relax=True, dep=s_mm[o])
            s_ev[o] = ch.snap()
        emit_ln(cx, yt, TT, sq, yb, psA, psB, psA2, psB2, rows, need_xb=False)
        emit_ln_apply(cx, yt, TT, psA2, psB2, outt, add_to=x3t)
        ch.dma_n("sync", 1, lambda p, t0=t0: [
            p.dma_start(out=fmview(y)[:, :, t0:t0 + TT], in_=outt[:])])
    cx.release(mk)


def build_fused():
    nc = bass.Bass("TRN2", target_bir_lowering=False, debug=False)
    E = "ExternalInput"
    d = nc.dram_tensor
    T = {}

    def ein(name, shape):
        T[name] = d(name, shape, F32, kind=E).ap()

    def intr(name, shape, dt=F32):
        T[name] = d(name, shape, dt, kind="Internal").ap()

    ein("x", [D_MODEL, HSEQ])
    T["y"] = d("y", [D_MODEL, HSEQ], F32, kind="ExternalOutput").ap()
    ein("selA", [128, 1])
    ein("selB", [128, 1])
    ein("ffnkeep", [128, 1])
    ein("a1_wqt", [D_MODEL, 3072])
    ein("a3_wot", [D_MODEL, D_MODEL])
    intr("q", [D_MODEL, HSEQ])
    intr("k", [D_MODEL, HSEQ])
    intr("v", [D_MODEL, HSEQ])
    intr("kvtm", [32, 128, 2048], BF16)
    intr("o_att", [D_MODEL, HSEQ])
    intr("x1", [D_MODEL, HSEQ])
    for li in range(2):
        ein(f"f{li}_wqkv", [12, 128, NCHUNK, 128])
        ein(f"f{li}_qsc", [128, 1])
        ein(f"f{li}_ksc", [128, 1])
        ein(f"f{li}_wo", [512, D_MODEL])
        ein(f"f{li}_lg", [128, NCHUNK])
        ein(f"f{li}_lb", [128, NCHUNK])
        ein(f"f{li}_w1pt", [44, 128, NCHUNK, 128])
        ein(f"f{li}_b1p", [128, 44])
        ein(f"f{li}_w2pt", [8, 128, 22, 128])
        ein(f"f{li}_b2", [128, NCHUNK])
    ein("f_prot", [128, 128])
    ein("f_tqc", [128, L0F_NW * L0F_W])
    ein("f_tqs", [128, L0F_NW * L0F_W])
    ein("f_tkc", [128, L0F_NW * L0F_W])
    ein("f_tks", [128, L0F_NW * L0F_W])
    ein("f_amask", [2, 4, 128, 1024])
    ein("f_bmask", [2, 8, 128, L0F_W])
    intr("halo0", [D_MODEL, 1025])
    intr("inner1", [D_MODEL, HSEQ])
    intr("halo1", [D_MODEL, 1025])
    intr("x2", [D_MODEL, HSEQ])
    intr("agh", [2 * D_MODEL, 1025])
    intr("aghR", [2 * D_MODEL, 1])
    ein("r_wkt", [D_MODEL, D_MODEL])
    ein("r_wvt", [D_MODEL, D_MODEL])
    ein("r_wrt", [D_MODEL, D_MODEL])
    ein("r_wot", [D_MODEL, D_MODEL])
    ein("r_mixk", [128, NCHUNK])
    ein("r_mixv", [128, NCHUNK])
    ein("r_mixr", [128, NCHUNK])
    ein("r_w", [128, NCHUNK])
    ein("r_u", [128, NCHUNK])
    intr("haloR", [D_MODEL, 1])
    intr("kR", [D_MODEL, HSEQ])
    intr("vR", [D_MODEL, HSEQ])
    intr("srR", [D_MODEL, HSEQ])
    intr("wkvo", [D_MODEL, HSEQ])
    intr("x3", [D_MODEL, HSEQ])
    ein("n_w1t", [D_MODEL, 4096])
    ein("n_w2t", [D_MODEL, 4096])
    ein("n_s1", [128, 32])
    ein("n_s2", [128, 32])
    ein("n_h1wt", [32, 4096, 128])
    ein("n_h1s", [128, 32])
    ein("n_h2wt", [32, 4096, 128])
    ein("n_h2s", [128, 32])
    ein("n_w3t", [4096, D_MODEL])
    ein("n_s3", [128, 8])
    intr("hb1", [4096, HSEQ], BF16)
    intr("hb2", [4096, HSEQ], BF16)
    intr("hb3", [4096, HSEQ], BF16)

    ch = Chain()
    cx = Ctx(nc, ch)
    make_common_consts(cx)
    selA = cx.sbuf("g_selA", [128, 1], F32)
    selB = cx.sbuf("g_selB", [128, 1], F32)
    ch.dma_n("sync", 2, lambda p: [
        p.dma_start(out=selA[:], in_=T["selA"][:, :]),
        p.dma_start(out=selB[:], in_=T["selB"][:, :]),
    ])

    cx.prefix = "a1_"
    emit_a1f(cx, {"x": T["x"], "wqt": T["a1_wqt"], "q": T["q"], "k": T["k"],
                  "v": T["v"], "kvtm": T["kvtm"]})
    cx.prefix = "a2_"
    emit_a2f(cx, {"q": T["q"], "k": T["k"], "kvtm": T["kvtm"],
                  "o": T["o_att"], "selA": selA, "selB": selB})
    cx.prefix = "a3_"
    emit_projf(cx, {"inp": T["o_att"], "x": T["x"], "wt": T["a3_wot"],
                    "y": T["x1"]})

    cx.prefix = "hx0_"
    emit_halo_fill(cx, "hx0_", T["x1"], T["halo0"], 1025, selA, selB,
                   T["agh"])
    cx.prefix = "f0_"
    emit_l0ff(cx, {"xs": T["x1"], "halo": T["halo0"],
                   "wqkv": T["f0_wqkv"], "qsc": T["f0_qsc"],
                   "ksc": T["f0_ksc"], "wo": T["f0_wo"], "lg": T["f0_lg"],
                   "lb": T["f0_lb"], "w1pt": T["f0_w1pt"],
                   "b1p": T["f0_b1p"], "w2pt": T["f0_w2pt"],
                   "b2": T["f0_b2"], "prot": T["f_prot"], "tqc": T["f_tqc"],
                   "tqs": T["f_tqs"], "tkc": T["f_tkc"], "tks": T["f_tks"],
                   "amask": T["f_amask"], "bmask": T["f_bmask"],
                   "ffnkeep": T["ffnkeep"], "y": T["inner1"]},
              True, False)
    cx.prefix = "hx1_"
    emit_halo_fill(cx, "hx1_", T["inner1"], T["halo1"], 1025, selA, selB,
                   T["agh"])
    cx.prefix = "f1_"
    emit_l0ff(cx, {"xs": T["inner1"], "halo": T["halo1"],
                   "wqkv": T["f1_wqkv"], "qsc": T["f1_qsc"],
                   "ksc": T["f1_ksc"], "wo": T["f1_wo"], "lg": T["f1_lg"],
                   "lb": T["f1_lb"], "w1pt": T["f1_w1pt"],
                   "b1p": T["f1_b1p"], "w2pt": T["f1_w2pt"],
                   "b2": T["f1_b2"], "prot": T["f_prot"], "tqc": T["f_tqc"],
                   "tqs": T["f_tqs"], "tkc": T["f_tkc"], "tks": T["f_tks"],
                   "amask": T["f_amask"], "bmask": T["f_bmask"],
                   "ffnkeep": T["ffnkeep"], "xo": T["x1"], "y": T["x2"]},
              False, True)

    cx.prefix = "hxr_"
    emit_halo_fill(cx, "hxr_", T["x2"], T["haloR"], 1, selA, selB,
                   T["aghR"])
    cx.prefix = "r1_"
    emit_r1f(cx, {"x": T["x2"], "halo1": T["haloR"], "k": T["kR"],
                  "v": T["vR"], "sr": T["srR"], "wkt": T["r_wkt"],
                  "wvt": T["r_wvt"], "wrt": T["r_wrt"], "mixk": T["r_mixk"],
                  "mixv": T["r_mixv"], "mixr": T["r_mixr"]})
    cx.prefix = "wk_"
    emit_wkvf(cx, {"k": T["kR"], "v": T["vR"], "wkv": T["wkvo"],
                   "rw": T["r_w"], "ru": T["r_u"], "selA": selA,
                   "selB": selB})
    cx.prefix = "r3_"
    emit_r3f(cx, {"wkv": T["wkvo"], "sr": T["srR"], "x2": T["x2"],
                  "wot": T["r_wot"], "y": T["x3"]})

    cx.prefix = "n1_"
    emit_n1f(cx, {"x": T["x3"], "w1t": T["n_w1t"], "w2t": T["n_w2t"],
                  "s1": T["n_s1"], "s2": T["n_s2"], "h": T["hb1"]})
    cx.prefix = "n2_"
    emit_nhf(cx, {"hi": T["hb1"], "wt": T["n_h1wt"], "ss": T["n_h1s"],
                  "ho": T["hb2"]})
    cx.prefix = "n3_"
    emit_nhf(cx, {"hi": T["hb2"], "wt": T["n_h2wt"], "ss": T["n_h2s"],
                  "ho": T["hb3"]})
    cx.prefix = "n4_"
    emit_n4f(cx, {"hi": T["hb3"], "w3t": T["n_w3t"], "s3": T["n_s3"],
                  "x3": T["x3"], "y": T["y"]})

    ch.build_with_clear(nc)
    cx.close()
    return nc


def emit_r3f(cx, T):
    ch = cx.ch
    mk = cx.mark()
    wk, sr, x2, wot, y = T["wkv"], T["sr"], T["x2"], T["wot"], T["y"]
    wt = cx.sbuf("wt", [128, NCHUNK, TT], F32)
    st = cx.sbuf("st", [128, NCHUNK, TT], F32)
    x2t = cx.sbuf("x2t", [128, NCHUNK, TT], F32)
    rb = cx.sbuf("rb", [128, NCHUNK, TT], BF16)
    wos = cx.sbuf("wos", [128, NCHUNK, D_MODEL], BF16)
    yt = cx.sbuf("yt", [128, NCHUNK, TT], F32)
    yb = cx.sbuf("yb", [128, NCHUNK, TT], BF16)
    sq = cx.sbuf("sq", [128, NCHUNK, TT], BF16)
    rows = cx.sbuf("rows", [1, 4 * TT], F32)
    outt = cx.sbuf("outt", [128, NCHUNK, TT], F32)
    psm = [cx.psum(f"psm{i}", [128, TT]) for i in range(2)]
    psA = cx.psum("psA", [128, TT])
    psB = cx.psum("psB", [128, TT])
    psA2 = cx.psum("psA2", [128, TT])
    psB2 = cx.psum("psB2", [128, TT])

    ch.dma_n("gpsimd", 1, lambda p: [p.dma_start(out=wos[:], in_=fmview(wot))])
    for t in range(NTILE):
        t0 = t * TT
        ch.dma_n("sync", 3, lambda p, t0=t0: [
            p.dma_start(out=wt[:], in_=fmview(wk)[:, :, t0:t0 + TT]),
            p.dma_start(out=st[:], in_=fmview(sr)[:, :, t0:t0 + TT]),
            p.dma_start(out=x2t[:], in_=fmview(x2)[:, :, t0:t0 + TT]),
        ])

        def e_rb(p):
            for c in range(NCHUNK):
                p.tensor_tensor(rb[:, c, :], wt[:, c, :], st[:, c, :],
                                ALU.mult)

        ch.comp("vector", e_rb, relax=True)
        s_in = ch.snap()
        s_mm, s_ev = {}, {}
        for o in range(NCHUNK):
            j = o % 2

            def e_mm(p, o=o, j=j):
                for c in range(NCHUNK):
                    p.matmul(psm[j][:, :], wos[:, c, o * 128:(o + 1) * 128],
                             rb[:, c, :], start=(c == 0),
                             stop=(c == NCHUNK - 1))

            ch.comp("tensor", e_mm, relax=True,
                    dep=Chain.dep_max(s_in, s_ev.get(o - 2, s_in)))
            s_mm[o] = ch.snap()

            def e_ev(p, o=o, j=j):
                p.tensor_copy(yt[:, o, :], psm[j][:])
                p.tensor_copy(yb[:, o, :], psm[j][:])

            ch.comp("vector", e_ev, relax=True, dep=s_mm[o])
            s_ev[o] = ch.snap()
        emit_ln(cx, yt, TT, sq, yb, psA, psB, psA2, psB2, rows, need_xb=False)
        emit_ln_apply(cx, yt, TT, psA2, psB2, outt, add_to=x2t)
        ch.dma_n("sync", 1, lambda p, t0=t0: [
            p.dma_start(out=fmview(y)[:, :, t0:t0 + TT], in_=outt[:])])
    cx.release(mk)


def _fused_maps(inp):
    w = inp
    x = np.asarray(inp["L0_attn_x"], np.float32)
    x_fm = _c(x.transpose(0, 2, 1))
    QK = 8.0
    shared = {
        "a1_wqt": _c(w["L0_attn_W_qkv"].T.astype(np.float32)),
        "a3_wot": _c(w["L0_attn_W_out"].T.astype(np.float32)),
        "f_tqc": None, "f_tqs": None, "f_tkc": None, "f_tks": None,
        "r_wkt": _c(w["L1_attn_Wk"].T.astype(np.float32)),
        "r_wvt": _c(w["L1_attn_Wv"].T.astype(np.float32)),
        "r_wrt": _c(w["L1_attn_Wr"].T.astype(np.float32)),
        "r_wot": _c(w["L1_attn_Wo"].T.astype(np.float32)),
        "r_mixk": _scale_tiles(w["L1_attn_mix_k"]),
        "r_mixv": _scale_tiles(w["L1_attn_mix_v"]),
        "r_mixr": _scale_tiles(w["L1_attn_mix_r"]),
        "r_w": _scale_tiles(w["L1_attn_w"]),
        "r_u": _scale_tiles(w["L1_attn_u"]),
        "n_w1t": _c(w["L1_ffn_w1_w"].T.astype(np.float32)),
        "n_w2t": _c(w["L1_ffn_w2_w"].T.astype(np.float32)),
        "n_s1": _scale_tiles(w["L1_ffn_w1_s"]),
        "n_s2": _scale_tiles(w["L1_ffn_w2_s"]),
        "n_h1wt": _c(w["L1_ffn_h1_w"].T.astype(np.float32)
                     .reshape(4096, 32, 128).transpose(1, 0, 2)),
        "n_h1s": _scale_tiles(w["L1_ffn_h1_s"]),
        "n_h2wt": _c(w["L1_ffn_h2_w"].T.astype(np.float32)
                     .reshape(4096, 32, 128).transpose(1, 0, 2)),
        "n_h2s": _scale_tiles(w["L1_ffn_h2_s"]),
        "n_w3t": _c(w["L1_ffn_w3_w"].T.astype(np.float32)),
        "n_s3": _scale_tiles(w["L1_ffn_w3_s"]),
    }
    for li in range(2):
        wqkv = w["L0_ffn_wqkv"][li].astype(np.float64)
        w1 = w["L0_ffn_w1"][li].astype(np.float64)
        b1 = w["L0_ffn_b1"][li].astype(np.float64)
        w2 = w["L0_ffn_w2"][li].astype(np.float64)
        w1p = np.zeros((1024, 5632))
        w1p[:, 0:2730] = w1[:, 0:2730]
        w1p[:, 2816:5546] = w1[:, 2730:]
        b1p = np.zeros(5632)
        b1p[0:2730] = b1[0:2730]
        b1p[2816:5546] = b1[2730:]
        w2p = np.zeros((2816, 1024))
        w2p[0:2730] = w2
        shared[f"f{li}_wqkv"] = _l0f_tile_w(wqkv, 12)
        shared[f"f{li}_qsc"] = _c(np.tile(w["L0_ffn_q_scale"][li] * QK, 2)
                                  .reshape(128, 1).astype(np.float32))
        shared[f"f{li}_ksc"] = _c(np.tile(w["L0_ffn_k_scale"][li], 2)
                                  .reshape(128, 1).astype(np.float32))
        shared[f"f{li}_wo"] = _c(w["L0_ffn_wo"][li].astype(np.float32))
        shared[f"f{li}_lg"] = _scale_tiles(w["L0_ffn_ln_g"][li])
        shared[f"f{li}_lb"] = _scale_tiles(w["L0_ffn_ln_b"][li])
        shared[f"f{li}_w1pt"] = _l0f_tile_w(w1p, 44)
        shared[f"f{li}_b1p"] = _c(b1p.reshape(44, 128).T.astype(np.float32))
        shared[f"f{li}_w2pt"] = _l0f_tile_w(w2p, 8)
        shared[f"f{li}_b2"] = _scale_tiles(w["L0_ffn_b2"][li])
    consts = [_l0f_consts(0), _l0f_consts(1)]
    maps = []
    for core in range(8):
        b, hh = core // 2, core % 2
        tqc, tqs, tkc, tks, am, bm, prot = consts[hh]
        m = dict(shared)
        m["f_tqc"], m["f_tqs"], m["f_tkc"], m["f_tks"] = tqc, tqs, tkc, tks
        m["f_amask"] = am
        m["f_bmask"] = bm
        m["f_prot"] = prot
        m["x"] = _c(x_fm[b, :, hh * HSEQ:(hh + 1) * HSEQ])
        m["selA"] = np.full((128, 1), 1.0 if hh == 1 else 0.0, np.float32)
        m["selB"] = np.zeros((128, 1), np.float32)
        m["ffnkeep"] = np.full((128, 1), 0.0 if hh == 0 else 1.0, np.float32)
        maps.append(m)
    return maps


def kernel_fused(inp):
    nc = build_fused()
    maps = _fused_maps(inp)
    res = run_bass_kernel_spmd(nc, maps, core_ids=list(range(8)))
    x = np.asarray(inp["L0_attn_x"], np.float32)
    out = np.empty((BATCH, D_MODEL, SEQ), np.float32)
    for core in range(8):
        b, hh = core // 2, core % 2
        out[b, :, hh * HSEQ:(hh + 1) * HSEQ] = res.results[core]["y"]
    return _c(out.transpose(0, 2, 1))


# version 10
# speedup vs baseline: 1.1159x; 1.0658x over previous
import numpy as np
import ml_dtypes
import concourse.bass as bass
import concourse.mybir as mybir
from concourse.bass_utils import run_bass_kernel_spmd

F32 = mybir.dt.float32
BF16 = mybir.dt.bfloat16
AF = mybir.ActivationFunctionType
ALU = mybir.AluOpType
AX = mybir.AxisListType

D_MODEL = 1024
NCHUNK = 8
BATCH = 4
SEQ = 8192
HSEQ = SEQ // 2
TT = 512
NTILE = HSEQ // TT

bfdt = ml_dtypes.bfloat16


class _DummyInstr:
    def then_inc(self, *a, **k):
        return self


class _CountProxy:

    def __init__(self):
        self.n = 0

    def __getattr__(self, name):
        if name.startswith("_"):
            raise AttributeError(name)
        d = _DummyInstr()

        def f(*a, **k):
            self.n += 1
            return d

        return f


ENG_IDS = {"sync": 0, "scalar": 1, "vector": 2, "tensor": 3, "gpsimd": 4}
SNAP_DMA = 5
SNAP_CC = 9
NSNAP = 10


class Chain:

    ENGINES = ("sync", "scalar", "vector", "tensor", "gpsimd")

    def __init__(self):
        self.q = {e: [] for e in self.ENGINES}
        self.cnt = [0] * NSNAP
        self.dma_phase = {"sync": 0, "gpsimd": 0}

    def snap(self):
        return tuple(self.cnt)

    @staticmethod
    def dep_max(*deps):
        return tuple(max(d[i] for d in deps) for i in range(NSNAP))

    def _emit_waits(self, p, sems, dep, eng_i, include_same=True):
        for e in range(5):
            v = dep[e]
            if v <= 0:
                continue
            if e == eng_i and not include_same:
                continue
            p.wait_ge(sems[e], v)
        for k in range(4):
            if dep[SNAP_DMA + k] > 0:
                p.wait_ge(sems[5 + k], 16 * dep[SNAP_DMA + k])
        if dep[SNAP_CC] > 0:
            p.wait_ge(sems[9], dep[SNAP_CC])

    def comp(self, eng, emit, relax=False, dep=None):
        cp = _CountProxy()
        emit(cp)
        n = cp.n
        assert n > 0
        ei = ENG_IDS[eng]
        st0 = tuple(dep) if dep is not None else self.snap()
        base_e = self.cnt[ei]
        self.cnt[ei] += 1 if relax else n
        ch = self

        def run(p, sems):
            insts = []

            class _Cap:
                def __getattr__(self, name):
                    if name.startswith("_"):
                        raise AttributeError(name)
                    meth = getattr(p, name)

                    def f(*a, **k):
                        if not relax:
                            if insts:
                                p.wait_ge(sems[ei], base_e + len(insts))
                            elif base_e > 0:
                                p.wait_ge(sems[ei], base_e)
                        elif not insts and base_e > 0:
                            p.wait_ge(sems[ei], base_e)
                        if not insts:
                            ch._emit_waits(p, sems, st0, ei,
                                           include_same=False)
                        inst = meth(*a, **k)
                        if not relax:
                            inst.then_inc(sems[ei], 1)
                        insts.append(inst)
                        return inst

                    return f

            emit(_Cap())
            assert len(insts) == n
            if relax:
                insts[-1].then_inc(sems[ei], 1)

        self.q[eng].append(run)

    def dma_n(self, eng, n, emit_list, dep=None):
        ei = ENG_IDS[eng]
        st0 = tuple(dep) if dep is not None else self.snap()
        qn = "gpsimd" if eng == "gpsimd" else "sync"
        par = self.dma_phase[qn] % 2
        self.dma_phase[qn] += 1
        didx = 5 + (2 if qn == "gpsimd" else 0) + par
        prior_same = self.cnt[SNAP_DMA + (2 if qn == "gpsimd" else 0) + par]
        self.cnt[SNAP_DMA + (2 if qn == "gpsimd" else 0) + par] += n
        ch = self

        def run(p, sems):
            if prior_same > 0:
                p.wait_ge(sems[didx], 16 * prior_same)
            ch._emit_waits(p, sems, st0, ei, include_same=True)
            insts = emit_list(p)
            assert len(insts) == n, f"declared {n} dmas, emitted {len(insts)}"
            for i in insts:
                i.then_inc(sems[didx], 16)

        self.q[eng].append(run)

    def collective(self, emit):
        st0 = self.snap()
        self.cnt[SNAP_CC] += 1
        ch = self

        def run(p, sems):
            ch._emit_waits(p, sems, st0, ENG_IDS["gpsimd"],
                           include_same=True)
            inst = emit(p)
            inst.then_inc(sems[9], 1)

        self.q["gpsimd"].append(run)

    def build_with_clear(self, nc):
        totals = self.snap()
        fin_engs = [e for e in ("sync", "scalar", "vector", "tensor")
                    if self.q[e]]
        nfin = len(fin_engs)
        ch = self

        def fin(eng):
            ei = ENG_IDS[eng]

            def run(p, sems):
                ch._emit_waits(p, sems, totals, ei, include_same=True)
                p.nop().then_inc(sems[ei], 1)

            self.q[eng].append(run)

        for e in fin_engs:
            fin(e)

        def gfin(p, sems):
            for e in ("sync", "scalar", "vector", "tensor"):
                ei = ENG_IDS[e]
                v = totals[ei] + (1 if e in fin_engs else 0)
                if v > 0:
                    p.wait_ge(sems[ei], v)
            for k in range(4):
                if totals[SNAP_DMA + k] > 0:
                    p.wait_ge(sems[5 + k], 16 * totals[SNAP_DMA + k])
            if totals[SNAP_CC] > 0:
                p.wait_ge(sems[9], totals[SNAP_CC])
            if totals[ENG_IDS["gpsimd"]] > 0:
                p.wait_ge(sems[ENG_IDS["gpsimd"]],
                          totals[ENG_IDS["gpsimd"]])
            nums = sorted(s.num for s in sems)
            p.dma_reset(range(nums[0], nums[-1] + 1))
            for s in sems:
                p.sem_clear(s)

        self.q["gpsimd"].append(gfin)
        self.build(nc)

    def build(self, nc):
        with (
            nc.semaphore("c_sy") as s0,
            nc.semaphore("c_sc") as s1,
            nc.semaphore("c_ve") as s2,
            nc.semaphore("c_te") as s3,
            nc.semaphore("c_gp") as s4,
            nc.semaphore("dsyA") as s5,
            nc.semaphore("dsyB") as s6,
            nc.semaphore("dgpA") as s7,
            nc.semaphore("dgpB") as s8,
            nc.semaphore("dcc") as s9,
            nc.Block() as block,
        ):
            ch = self
            sems = (s0, s1, s2, s3, s4, s5, s6, s7, s8, s9)

            @block.sync
            def _(p):
                for fn in ch.q["sync"]:
                    fn(p, sems)

            @block.scalar
            def _(p):
                for fn in ch.q["scalar"]:
                    fn(p, sems)

            @block.vector
            def _(p):
                for fn in ch.q["vector"]:
                    fn(p, sems)

            @block.tensor
            def _(p):
                for fn in ch.q["tensor"]:
                    fn(p, sems)

            @block.gpsimd
            def _(p):
                for fn in ch.q["gpsimd"]:
                    fn(p, sems)


class Ctx:

    def __init__(self, nc, ch):
        self.nc = nc
        self.ch = ch
        self.exits = []
        self.prefix = ""

    def sbuf(self, name, shape, dtype):
        t = self.nc.sbuf_tensor(self.prefix + name, shape, dtype)
        h = t.__enter__()
        self.exits.append(t)
        return h

    def psum(self, name, shape, dtype=F32):
        t = self.nc.psum_tensor(self.prefix + name, shape, dtype)
        h = t.__enter__()
        self.exits.append(t)
        return h

    def mark(self):
        return len(self.exits)

    def release(self, mark):
        while len(self.exits) > mark:
            self.exits.pop().__exit__(None, None, None)

    def close(self):
        self.release(0)


def make_common_consts(cx):
    cx.ones_col_bf = cx.sbuf("ones_col_bf", [128, 1], BF16)
    cx.ones_row_f = cx.sbuf("ones_row_f", [1, 128], F32)

    def emit(p):
        p.memset(cx.ones_col_bf[:], 1.0)
        return p.memset(cx.ones_row_f[:], 1.0)

    cx.ch.comp("vector", emit)


def fmview(ap, nch=NCHUNK):
    return ap.rearrange("(c p) t -> p c t", p=128)


def emit_ln(cx, xt, T, sq, xb, psA, psB, psA2, psB2, rows, need_xb=True):
    ch = cx.ch
    ch.comp("scalar",
            lambda p: p.activation(sq[:, :, 0:T], xt[:, :, 0:T], AF.Square))
    if need_xb:
        ch.comp("vector",
                lambda p: p.tensor_copy(xb[:, :, 0:T], xt[:, :, 0:T]))

    def e_mm(p):
        for c in range(NCHUNK):
            p.matmul(psA[0:1, 0:T], cx.ones_col_bf[:], xb[:, c, 0:T],
                     start=(c == 0), stop=(c == NCHUNK - 1))
        last = None
        for c in range(NCHUNK):
            last = p.matmul(psB[0:1, 0:T], cx.ones_col_bf[:], sq[:, c, 0:T],
                            start=(c == 0), stop=(c == NCHUNK - 1))
        return last

    ch.comp("tensor", e_mm, relax=True)

    mean = rows[0:1, 0:T]
    m2 = rows[0:1, T:2 * T]
    varp = rows[0:1, 2 * T:3 * T]
    amb = rows[0:1, 3 * T:4 * T]

    def e_rows(p):
        p.tensor_scalar_mul(mean, psA[0:1, 0:T], 1.0 / D_MODEL)
        p.tensor_tensor(m2, mean, mean, ALU.mult)
        p.tensor_scalar_mul(varp, psB[0:1, 0:T], 1.0 / D_MODEL)
        p.tensor_tensor(varp, varp, m2, ALU.subtract)
        p.tensor_scalar_add(varp, varp, 1e-5)
        return p.reciprocal(varp, varp)

    ch.comp("vector", e_rows)
    ch.comp("scalar", lambda p: p.activation(varp, varp, AF.Sqrt))
    ch.comp("vector",
            lambda p: p.scalar_tensor_tensor(amb, mean, -1.0, varp, ALU.mult,
                                             ALU.mult))

    def e_bc(p):
        p.matmul(psA2[:, 0:T], cx.ones_row_f[:], varp, start=True, stop=True)
        return p.matmul(psB2[:, 0:T], cx.ones_row_f[:], amb, start=True,
                        stop=True)

    ch.comp("tensor", e_bc, relax=True)


def emit_ln_apply(cx, xt, T, psA2, psB2, out, tmp=None, add_to=None):
    ch = cx.ch
    if add_to is None:
        ch.comp("vector", lambda p: [p.tensor_tensor(
            out[:, c, 0:T], xt[:, c, 0:T], psA2[:, 0:T], ALU.mult)
            for c in range(NCHUNK)], relax=True)
        ch.comp("vector", lambda p: [p.scalar_tensor_tensor(
            out[:, c, 0:T], out[:, c, 0:T], 1.0, psB2[:, 0:T], ALU.mult,
            ALU.add) for c in range(NCHUNK)], relax=True)
    else:
        ch.comp("vector", lambda p: [p.tensor_tensor(
            out[:, c, 0:T], xt[:, c, 0:T], psA2[:, 0:T], ALU.mult)
            for c in range(NCHUNK)], relax=True)
        ch.comp("vector", lambda p: [p.scalar_tensor_tensor(
            out[:, c, 0:T], out[:, c, 0:T], 1.0, psB2[:, 0:T], ALU.mult,
            ALU.add) for c in range(NCHUNK)], relax=True)
        ch.comp("vector", lambda p: [p.tensor_tensor(
            out[:, c, 0:T], out[:, c, 0:T], add_to[:, c, 0:T], ALU.add)
            for c in range(NCHUNK)], relax=True)


def _c(a):
    return np.ascontiguousarray(a)


def _scale_tiles(s):
    n = s.shape[0]
    return _c(s.reshape(n // 128, 128).T.astype(np.float32))


def kernel(**inp):
    raise NotImplementedError("assembled later")


def _l0f_tile_w(wmat, nout):
    Kd = wmat.shape[0]
    return _c(wmat.reshape(Kd // 128, 128, nout, 128).transpose(2, 1, 0, 3)
              .astype(np.float32))


def _l0f_consts(hh):
    W, DH = 512, 64
    t0g = hh * HSEQ
    pos = np.maximum(np.arange(t0g - 1024, t0g + HSEQ), 0).astype(np.float64)
    half_idx = np.arange(0, DH, 2, dtype=np.float64)
    inv_freq = 1.0 / 10000.0 ** (half_idx / DH)
    freqs = pos[:, None] * inv_freq[None, :]
    freqs = np.concatenate([freqs, freqs], axis=-1)
    base = (half_idx + 0.4 * DH) / (1.4 * DH)
    power = (pos - SEQ // 2) / 256.0
    scale = base[None, :] ** power[:, None]
    scale = np.concatenate([scale, scale], axis=-1)
    cos, sin = np.cos(freqs), np.sin(freqs)
    tqc = (cos * scale).T
    tqs = (sin * scale).T
    tkc = (cos / scale).T
    tks = (sin / scale).T

    def tile2(tt):
        return _c(np.concatenate([tt, tt], axis=0).astype(np.float32))

    iL = np.arange(W)[None, :]
    pj = np.arange(128)[:, None]
    am = np.zeros((2, 4, 128, 1024), np.float32)
    bm = np.zeros((2, 8, 128, W), np.float32)
    for it in range(4):
        iloc = it * 128 + np.arange(128)[:, None]
        jprev = np.arange(W)[None, :]
        jown = np.arange(W)[None, :]
        am[1, it, :, 0:W] = np.where(iloc <= jprev, 0.0, -1e30)
        am[1, it, :, W:] = np.where(iloc >= jown, 0.0, -1e30)
    for jj in range(4):
        jloc = jj * 128 + pj
        bm[1, jj] = (iL <= jloc).astype(np.float32)
        bm[1, 4 + jj] = (iL >= jloc).astype(np.float32)
    am[0] = am[1]
    bm[0] = bm[1]
    if hh == 0:
        am[0, :, :, 0:W] = -1e30
        bm[0, 0:4] = 0.0
    P = np.zeros((64, 64), np.float64)
    for d in range(32):
        P[d, d + 32] = -1.0
        P[d + 32, d] = 1.0
    PT = P.T
    prot = np.zeros((128, 128), np.float32)
    prot[0:64, 0:64] = PT
    prot[64:128, 64:128] = PT
    return (tile2(tqc), tile2(tqs), tile2(tkc), tile2(tks), _c(am), _c(bm),
            prot)


def kernel(**inp):
    return kernel_fused(inp)


L0F_H = 1025
L0F_W = 512
L0F_NW = 10


REPL_PAIRS = [[0, 1], [2, 3], [4, 5], [6, 7]]


def emit_pair_exchange(cx, pref, src_ap, agout_ap, rows, cols, selA, selB,
                       write_fn, tile_cols=None):
    ch = cx.ch
    nc = cx.nc
    mk = cx.mark()
    tc = cols if tile_cols is None else tile_cols
    blkA = cx.sbuf(pref + "blkA", [128, tc], F32)
    blkB = cx.sbuf(pref + "blkB", [128, tc], F32)
    cmb = cx.sbuf(pref + "cmb", [128, tc], F32)
    ch.collective(lambda p: p.collective_compute(
        "AllGather", ALU.bypass, replica_groups=REPL_PAIRS,
        ins=[src_ap], outs=[agout_ap]))
    slow = cols < 64
    rg = min(128, rows)
    for r0 in range(0, rows, rg):
        for c0 in range(0, cols, tc):
            cw = min(tc, cols - c0)
            ch.dma_n("sync", 2, lambda p, r0=r0, c0=c0, cw=cw: [
                p.dma_start(out=blkA[0:rg, 0:cw],
                            in_=agout_ap[r0:r0 + rg, c0:c0 + cw],
                            allow_slow_non_contiguous=slow),
                p.dma_start(out=blkB[0:rg, 0:cw],
                            in_=agout_ap[rows + r0:rows + r0 + rg,
                                         c0:c0 + cw],
                            allow_slow_non_contiguous=slow),
            ])

            def e_cmb(p, cw=cw):
                p.tensor_scalar_mul(cmb[0:rg, 0:cw], blkA[0:rg, 0:cw],
                                    selA[0:rg, 0:1])
                p.scalar_tensor_tensor(cmb[0:rg, 0:cw], blkB[0:rg, 0:cw],
                                       selB[0:rg, 0:1], cmb[0:rg, 0:cw],
                                       ALU.mult, ALU.add)

            ch.comp("vector", e_cmb)
            write_fn(ch, cmb, r0, c0, cw, slow)
    cx.release(mk)


def emit_halo_fill(cx, pref, stream_ap, halo_ap, hcols, selA, selB, tmp_ag):
    nc = cx.nc
    src = nc.dram_tensor(pref + "agsrc", [D_MODEL, hcols], F32,
                         kind="Internal").ap()
    ch = cx.ch
    mk = cx.mark()
    bt = cx.sbuf(pref + "bt", [128, NCHUNK, hcols], F32)
    slow = hcols < 64
    ch.dma_n("sync", 1, lambda p: [
        p.dma_start(out=bt[:], in_=fmview(stream_ap)[:, :,
                    HSEQ - hcols:HSEQ], allow_slow_non_contiguous=slow)])
    ch.dma_n("sync", 1, lambda p: [
        p.dma_start(out=fmview(src)[:, :, 0:hcols], in_=bt[:],
                    allow_slow_non_contiguous=slow)])
    cx.release(mk)

    def wr(ch2, cmb, r0, c0, cw, slow2):
        ch2.dma_n("sync", 1, lambda p: [
            p.dma_start(out=halo_ap[r0:r0 + 128, c0:c0 + cw],
                        in_=cmb[:, 0:cw], allow_slow_non_contiguous=slow2)])

    emit_pair_exchange(cx, pref, src, tmp_ag, D_MODEL, hcols, selA, selB, wr,
                       tile_cols=min(hcols, 1025))


def emit_a1f(cx, T):
    ch = cx.ch
    mk = cx.mark()
    x, wqt = T["x"], T["wqt"]
    qo, ko, vo, kvtm = T["q"], T["k"], T["v"], T["kvtm"]
    xt = cx.sbuf("xt", [128, NCHUNK, TT], F32)
    sq = cx.sbuf("sq", [128, NCHUNK, TT], BF16)
    xb = cx.sbuf("xb", [128, NCHUNK, TT], BF16)
    ub = cx.sbuf("ub", [128, NCHUNK, TT], BF16)
    wqs = cx.sbuf("wqs", [128, NCHUNK, 3072], BF16)
    rows = cx.sbuf("rows", [1, 4 * TT], F32)
    t0b = [cx.sbuf(f"t0b{i}", [128, TT], F32) for i in range(2)]
    t1b = [cx.sbuf(f"t1b{i}", [128, TT], F32) for i in range(2)]
    evt = [cx.sbuf(f"evt{i}", [128, TT], F32) for i in range(2)]
    evb = [cx.sbuf(f"evb{i}", [128, TT], BF16) for i in range(2)]
    psA = cx.psum("psA", [128, TT])
    psB = cx.psum("psB", [128, TT])
    psA2 = cx.psum("psA2", [128, TT])
    psB2 = cx.psum("psB2", [128, TT])
    psm = [cx.psum(f"psm{i}", [128, TT]) for i in range(2)]

    ch.dma_n("gpsimd", 1, lambda p: [p.dma_start(out=wqs[:], in_=fmview(wqt))])
    outs = (qo, ko, vo)
    for t in range(NTILE):
        t0 = t * TT
        ch.dma_n("sync", 1, lambda p, t0=t0: [
            p.dma_start(out=xt[:], in_=fmview(x)[:, :, t0:t0 + TT])])
        emit_ln(cx, xt, TT, sq, xb, psA, psB, psA2, psB2, rows)
        emit_ln_apply(cx, xt, TT, psA2, psB2, ub)
        s_in = ch.snap()
        s_mm, s_ev, s_st = {}, {}, {}
        for o in range(24):
            j = o % 2
            kind = o // 8

            def e_mm(p, o=o, j=j):
                for c in range(NCHUNK):
                    p.matmul(psm[j][:, :], wqs[:, c, o * 128:(o + 1) * 128],
                             ub[:, c, :], start=(c == 0),
                             stop=(c == NCHUNK - 1))

            ch.comp("tensor", e_mm, relax=True,
                    dep=Chain.dep_max(s_in, s_ev.get(o - 2, s_in)))
            s_mm[o] = ch.snap()
            if kind < 2:
                def e_elu(p, j=j):
                    p.tensor_scalar_min(t0b[j][:], psm[j][:], 0.0)
                    p.tensor_scalar_max(t1b[j][:], psm[j][:], 0.0)

                ch.comp("vector", e_elu, relax=True,
                        dep=Chain.dep_max(s_mm[o], s_st.get(o - 2, s_in)))
                sv = ch.snap()
                ch.comp("scalar",
                        lambda p, j=j: p.activation(t0b[j][:], t0b[j][:],
                                                    AF.Exp), dep=sv)
                sa_ = ch.snap()

                def e_sum(p, kind=kind, j=j):
                    p.tensor_tensor(evt[j][:], t0b[j][:], t1b[j][:], ALU.add)
                    if kind == 0:
                        p.tensor_scalar_mul(evt[j][:], evt[j][:], 0.125)

                ch.comp("vector", e_sum, dep=sa_)
            else:
                ch.comp("vector",
                        lambda p, j=j: p.tensor_copy(evt[j][:], psm[j][:]),
                        dep=Chain.dep_max(s_mm[o], s_st.get(o - 2, s_in)))
            s_ev[o] = ch.snap()
            ch.dma_n("sync", 1, lambda p, o=o, t0=t0, kind=kind, j=j: [
                p.dma_start(out=outs[kind][(o % 8) * 128:(o % 8 + 1) * 128,
                            t0:t0 + TT], in_=evt[j][:])], dep=s_ev[o])
            s_st[o] = ch.snap()
        s_mmk, s_evk, s_stk = {}, {}, {}
        for s in range(4):
            stile = t * 4 + s
            for cb in range(4):
                i = s * 4 + cb
                j = i % 2

                def e_kv(p, s=s, cb=cb, j=j):
                    for c in range(NCHUNK):
                        p.matmul(psm[j][:, :],
                                 ub[:, c, s * 128:(s + 1) * 128],
                                 wqs[:, c, 1024 + cb * 512:1024 +
                                     (cb + 1) * 512],
                                 start=(c == 0), stop=(c == NCHUNK - 1))

                ch.comp("tensor", e_kv, relax=True,
                        dep=Chain.dep_max(s_in, s_evk.get(i - 2, s_in),
                                          s_ev.get(23, s_in)))
                s_mmk[i] = ch.snap()
                if cb < 2:
                    def e_elu2(p, j=j):
                        p.tensor_scalar_min(t0b[j][:], psm[j][:], 0.0)
                        p.tensor_scalar_max(t1b[j][:], psm[j][:], 0.0)

                    ch.comp("vector", e_elu2, relax=True,
                            dep=Chain.dep_max(s_mmk[i],
                                              s_stk.get(i - 2, s_in)))
                    sv = ch.snap()
                    ch.comp("scalar",
                            lambda p, j=j: p.activation(
                                t0b[j][:], t0b[j][:], AF.Exp), dep=sv)
                    sa_ = ch.snap()
                    ch.comp("vector", lambda p, j=j: p.tensor_tensor(
                        evb[j][:], t0b[j][:], t1b[j][:], ALU.add), dep=sa_)
                else:
                    ch.comp("vector", lambda p, j=j: p.tensor_copy(
                        evb[j][:], psm[j][:]),
                            dep=Chain.dep_max(s_mmk[i],
                                              s_stk.get(i - 2, s_in)))
                s_evk[i] = ch.snap()
                ch.dma_n("sync", 1, lambda p, stile=stile, cb=cb, j=j: [
                    p.dma_start(out=kvtm[stile, :,
                                cb * 512:(cb + 1) * 512],
                                in_=evb[j][:])], dep=s_evk[i])
                s_stk[i] = ch.snap()
    cx.release(mk)


def emit_a2f(cx, T):
    CH = 256
    NCHL = HSEQ // CH
    ch = cx.ch
    nc = cx.nc
    mk = cx.mark()
    qd, kd, kvtm, oo = T["q"], T["k"], T["kvtm"], T["o"]
    selA, selB = T["selA"], T["selB"]
    agS_in = nc.dram_tensor("a2_agin", [64, 16 * 65], F32,
                            kind="Internal").ap()
    agS_out = nc.dram_tensor("a2_agout", [128, 16 * 65], F32,
                             kind="Internal").ap()
    qhd = qd.rearrange("(h d) t -> d h t", d=64)
    khd = kd.rearrange("(h d) t -> d h t", d=64)
    ohd = oo.rearrange("(h d) t -> d h t", d=64)
    qc = cx.sbuf("qc", [64, 16, CH], BF16)
    kc = cx.sbuf("kc", [64, 16, CH], BF16)
    kvsb = cx.sbuf("kvsb", [128, 2, 2048], BF16)
    atT = [cx.sbuf(f"atT{i}", [128, CH], BF16) for i in range(2)]
    maskb = [cx.sbuf(f"maskb{i}", [128, CH], BF16) for i in range(2)]
    maskf = cx.sbuf("maskf", [128, CH], F32)
    Schk = cx.sbuf("Schk", [64, 16, NCHL, 65], F32)
    Stot = cx.sbuf("Stot", [64, 16, 65], F32)
    Srun = cx.sbuf("Srun", [64, 16, 65], F32)
    Srb = cx.sbuf("Srb", [64, 16, 65], BF16)
    dd = cx.sbuf("dd", [1, CH], F32)
    bc = cx.sbuf("bc", [64, CH], F32)
    carc = cx.sbuf("carc", [64, CH], F32)
    carD = cx.sbuf("carD", [1, CH], F32)
    numt = cx.sbuf("numt", [64, CH], F32)
    ofm = cx.sbuf("ofm", [64, 16, CH], F32)
    ones_row64 = cx.sbuf("ones_row64", [1, 64], F32)
    ps_t = [cx.psum(f"ps_t{i}", [128, CH]) for i in range(2)]
    ps_num = cx.psum("ps_num", [64, CH])
    ps_den = cx.psum("ps_den", [1, CH])
    ps_bc = cx.psum("ps_bc", [64, CH])
    ps_S = cx.psum("ps_S", [64, 128])
    ps_car = cx.psum("ps_car", [64, CH])
    ps_carD = cx.psum("ps_carD", [1, CH])

    def e_consts(p):
        p.memset(ones_row64[:], 1.0)
        p.memset(maskf[:], 0.0)
        p.memset(Stot[:], 0.0)

    ch.comp("vector", e_consts)
    ch.comp("gpsimd", lambda p: p.affine_select(
        maskf[:], maskf[:], [[-1, CH]], ALU.is_ge, 1.0, base=-1,
        channel_multiplier=1))

    def e_mb(p):
        p.tensor_copy(maskb[0][:], maskf[:])
        p.memset(maskf[:], 0.0)

    ch.comp("vector", e_mb)
    ch.comp("gpsimd", lambda p: p.affine_select(
        maskf[:], maskf[:], [[-1, CH]], ALU.is_ge, 1.0, base=127,
        channel_multiplier=1))
    ch.comp("vector", lambda p: p.tensor_copy(maskb[1][:], maskf[:]))

    for i in range(NCHL):
        ch.dma_n("sync", 2, lambda p, i=i: [
            p.dma_start(out=kvsb[:, 0, :], in_=kvtm[2 * i]),
            p.dma_start(out=kvsb[:, 1, :], in_=kvtm[2 * i + 1]),
        ])
        for h in range(16):
            def e_S(p, h=h):
                for m in range(2):
                    p.matmul(ps_S[0:64, 0:64], kvsb[:, m, h * 64:h * 64 + 64],
                             kvsb[:, m, 1024 + h * 64:1024 + h * 64 + 64],
                             start=(m == 0), stop=(m == 1))
                for m in range(2):
                    p.matmul(ps_S[0:64, 64:65],
                             kvsb[:, m, h * 64:h * 64 + 64],
                             cx.ones_col_bf[:], start=(m == 0), stop=(m == 1))

            ch.comp("tensor", e_S, relax=True)

            def e_acc(p, i=i, h=h):
                p.tensor_copy(Schk[:, h, i, :], ps_S[0:64, 0:65])
                p.tensor_tensor(Stot[:, h, :], Stot[:, h, :],
                                ps_S[0:64, 0:65], ALU.add)

            ch.comp("vector", e_acc, relax=True)

    mkx = cx.mark()
    sbt = cx.sbuf("sbt", [64, 16 * 65], F32)
    ch.comp("vector", lambda p: p.tensor_copy(
        sbt[:], Stot[:].rearrange("p c e -> p (c e)")))
    ch.dma_n("sync", 1, lambda p: [p.dma_start(out=agS_in[:, :], in_=sbt[:])])

    def wrS(ch2, cmb, r0, c0, cw, slow2):
        def e(p):
            p.tensor_copy(Srun[:].rearrange("p c e -> p (c e)")[:,
                          c0:c0 + cw], cmb[0:64, 0:cw])

        ch2.comp("vector", e)

    emit_pair_exchange(cx, "a2x", agS_in, agS_out, 64, 16 * 65, selA, selB,
                       wrS)
    cx.release(mkx)
    ch.comp("vector", lambda p: p.tensor_copy(Srb[:], Srun[:]))

    for i in range(NCHL):
        i0 = i * CH
        ch.dma_n("sync", 2, lambda p, i=i: [
            p.dma_start(out=kvsb[:, 0, :], in_=kvtm[2 * i]),
            p.dma_start(out=kvsb[:, 1, :], in_=kvtm[2 * i + 1]),
        ])
        ch.dma_n("gpsimd", 2, lambda p, i0=i0: [
            p.dma_start(out=qc[:], in_=qhd[:, :, i0:i0 + CH]),
            p.dma_start(out=kc[:], in_=khd[:, :, i0:i0 + CH]),
        ])
        for h in range(16):
            def e_attnT(p, h=h):
                p.matmul(ps_t[0][:], kc[:, h, 0:128], qc[:, h, :],
                         start=True, stop=True)
                p.matmul(ps_t[1][:], kc[:, h, 128:256], qc[:, h, :],
                         start=True, stop=True)

            ch.comp("tensor", e_attnT, relax=True)

            def e_mask(p):
                p.tensor_tensor(atT[0][:], ps_t[0][:], maskb[0][:], ALU.mult)
                p.tensor_tensor(atT[1][:], ps_t[1][:], maskb[1][:], ALU.mult)

            ch.comp("vector", e_mask, relax=True)

            def e_num(p, h=h):
                for m in range(2):
                    p.matmul(ps_num[:, :],
                             kvsb[:, m, 1024 + h * 64:1024 + h * 64 + 64],
                             atT[m][:], start=(m == 0), stop=(m == 1))
                for m in range(2):
                    p.matmul(ps_den[:, :], cx.ones_col_bf[:], atT[m][:],
                             start=(m == 0), stop=(m == 1))
                p.matmul(ps_car[:, :], Srb[:, h, 0:64], qc[:, h, :],
                         start=True, stop=True)
                p.matmul(ps_carD[:, :], Srb[:, h, 64:65], qc[:, h, :],
                         start=True, stop=True)

            ch.comp("tensor", e_num, relax=True)

            def e_div(p):
                p.tensor_copy(carc[:, :], ps_car[:, :])
                p.tensor_copy(carD[:, :], ps_carD[:, :])
                p.tensor_tensor(dd[:], ps_den[:, :], carD[:, :], ALU.add)
                p.tensor_scalar_max(dd[:], dd[:], 1e-6)
                p.reciprocal(dd[:], dd[:])
                p.tensor_tensor(numt[:], ps_num[:, :], carc[:, :], ALU.add)

            ch.comp("vector", e_div)
            ch.comp("tensor", lambda p: p.matmul(
                ps_bc[:, :], ones_row64[:], dd[:], start=True, stop=True), relax=True)

            def e_out(p, h=h):
                p.tensor_copy(bc[:], ps_bc[:, :])
                p.tensor_tensor(ofm[:, h, :], numt[:], bc[:], ALU.mult)

            ch.comp("vector", e_out)

        def e_Supd(p, i=i):
            p.tensor_tensor(Srun[:, :, :], Srun[:, :, :], Schk[:, :, i, :],
                            ALU.add)
            p.tensor_copy(Srb[:], Srun[:])

        ch.comp("vector", e_Supd)
        ch.dma_n("sync", 1, lambda p, i0=i0: [
            p.dma_start(out=ohd[:, :, i0:i0 + CH], in_=ofm[:])])
    cx.release(mk)


def emit_projf(cx, T):
    ch = cx.ch
    mk = cx.mark()
    ii, xx, wt, y = T["inp"], T["x"], T["wt"], T["y"]
    it = cx.sbuf("it", [128, NCHUNK, TT], BF16)
    xt = cx.sbuf("xt", [128, NCHUNK, TT], F32)
    ws = cx.sbuf("ws", [128, NCHUNK, D_MODEL], BF16)
    yt = cx.sbuf("yt", [128, NCHUNK, TT], F32)
    yb = cx.sbuf("yb", [128, NCHUNK, TT], BF16)
    sq = cx.sbuf("sq", [128, NCHUNK, TT], BF16)
    rows = cx.sbuf("rows", [1, 4 * TT], F32)
    outt = cx.sbuf("outt", [128, NCHUNK, TT], F32)
    psm = [cx.psum(f"psm{i}", [128, TT]) for i in range(2)]
    psA = cx.psum("psA", [128, TT])
    psB = cx.psum("psB", [128, TT])
    psA2 = cx.psum("psA2", [128, TT])
    psB2 = cx.psum("psB2", [128, TT])

    ch.dma_n("gpsimd", 1, lambda p: [p.dma_start(out=ws[:], in_=fmview(wt))])
    for t in range(NTILE):
        t0 = t * TT
        ch.dma_n("gpsimd", 1, lambda p, t0=t0: [
            p.dma_start(out=it[:], in_=fmview(ii)[:, :, t0:t0 + TT])])
        ch.dma_n("sync", 1, lambda p, t0=t0: [
            p.dma_start(out=xt[:], in_=fmview(xx)[:, :, t0:t0 + TT])])
        s_in = ch.snap()
        s_mm, s_ev = {}, {}
        for o in range(NCHUNK):
            j = o % 2

            def e_mm(p, o=o, j=j):
                for c in range(NCHUNK):
                    p.matmul(psm[j][:, :], ws[:, c, o * 128:(o + 1) * 128],
                             it[:, c, :], start=(c == 0),
                             stop=(c == NCHUNK - 1))

            ch.comp("tensor", e_mm, relax=True,
                    dep=Chain.dep_max(s_in, s_ev.get(o - 2, s_in)))
            s_mm[o] = ch.snap()

            def e_ev(p, o=o, j=j):
                p.tensor_copy(yt[:, o, :], psm[j][:])
                p.tensor_copy(yb[:, o, :], psm[j][:])

            ch.comp("vector", e_ev, relax=True, dep=s_mm[o])
            s_ev[o] = ch.snap()
        emit_ln(cx, yt, TT, sq, yb, psA, psB, psA2, psB2, rows, need_xb=False)
        emit_ln_apply(cx, yt, TT, psA2, psB2, outt, add_to=xt)
        ch.dma_n("sync", 1, lambda p, t0=t0: [
            p.dma_start(out=fmview(y)[:, :, t0:t0 + TT], in_=outt[:])])
    cx.release(mk)


def emit_r1f(cx, T):
    ch = cx.ch
    mk = cx.mark()
    x, halo1 = T["x"], T["halo1"]
    ko, vo, sro = T["k"], T["v"], T["sr"]
    wkt, wvt, wrt = T["wkt"], T["wvt"], T["wrt"]
    mixk, mixv, mixr = T["mixk"], T["mixv"], T["mixr"]
    xt = cx.sbuf("xt", [128, NCHUNK, TT], F32)
    xh = cx.sbuf("xh", [128, NCHUNK, 1], F32)
    ut = cx.sbuf("ut", [128, NCHUNK, TT], F32)
    ucol = cx.sbuf("ucol", [128, NCHUNK, 1], F32)
    dcol = cx.sbuf("dcol", [128, NCHUNK, TT], F32)
    sq = cx.sbuf("sq", [128, NCHUNK, TT], BF16)
    xb = cx.sbuf("xb", [128, NCHUNK, TT], BF16)
    mixs = [cx.sbuf(f"mix{i}", [128, NCHUNK], F32) for i in range(3)]
    wss = [cx.sbuf(f"w{i}", [128, NCHUNK, D_MODEL], BF16) for i in range(3)]
    mxb = [cx.sbuf(f"mxb{i}", [128, NCHUNK, TT], BF16) for i in range(3)]
    evt = [cx.sbuf(f"evt{i}", [128, TT], F32) for i in range(2)]
    rows = cx.sbuf("rows", [1, 4 * TT], F32)
    psA = cx.psum("psA", [128, TT])
    psB = cx.psum("psB", [128, TT])
    psA2 = cx.psum("psA2", [128, TT])
    psB2 = cx.psum("psB2", [128, TT])
    psm = [cx.psum(f"psm{i}", [128, TT]) for i in range(2)]

    ch.dma_n("gpsimd", 6, lambda p: [
        p.dma_start(out=wss[0][:], in_=fmview(wkt)),
        p.dma_start(out=wss[1][:], in_=fmview(wvt)),
        p.dma_start(out=wss[2][:], in_=fmview(wrt)),
        p.dma_start(out=mixs[0][:], in_=mixk[:, :]),
        p.dma_start(out=mixs[1][:], in_=mixv[:, :]),
        p.dma_start(out=mixs[2][:], in_=mixr[:, :]),
    ])
    ch.dma_n("sync", 1,
             lambda p: [p.dma_start(out=xh[:], in_=fmview(halo1)[:, :, 0:1],
                                    allow_slow_non_contiguous=True)])
    emit_ln(cx, xh, 1, sq, xb, psA, psB, psA2, psB2, rows)
    emit_ln_apply(cx, xh, 1, psA2, psB2, ucol)

    outs = (ko, vo, sro)
    for t in range(NTILE):
        t0 = t * TT
        ch.dma_n("sync", 1, lambda p, t0=t0: [
            p.dma_start(out=xt[:], in_=fmview(x)[:, :, t0:t0 + TT])])
        emit_ln(cx, xt, TT, sq, xb, psA, psB, psA2, psB2, rows)
        emit_ln_apply(cx, xt, TT, psA2, psB2, ut)

        def e_d(p):
            for c in range(NCHUNK):
                p.tensor_tensor(dcol[:, c, 0:1], ut[:, c, 0:1],
                                ucol[:, c, 0:1], ALU.subtract)
                p.tensor_tensor(dcol[:, c, 1:TT], ut[:, c, 1:TT],
                                ut[:, c, 0:TT - 1], ALU.subtract)

        ch.comp("vector", e_d, relax=True)

        for mi in range(3):
            def e_mix(p, mi=mi):
                for c in range(NCHUNK):
                    p.scalar_tensor_tensor(mxb[mi][:, c, 0:1],
                                           dcol[:, c, 0:1],
                                           mixs[mi][:, c:c + 1],
                                           ucol[:, c, 0:1], ALU.mult, ALU.add)
                    p.scalar_tensor_tensor(
                        mxb[mi][:, c, 1:TT], dcol[:, c, 1:TT],
                        mixs[mi][:, c:c + 1], ut[:, c, 0:TT - 1], ALU.mult,
                        ALU.add)

            ch.comp("vector", e_mix, relax=True)
        s_in = ch.snap()
        s_mm, s_ev, s_st = {}, {}, {}
        for mi in range(3):
            for o in range(NCHUNK):
                i = mi * NCHUNK + o
                j = i % 2

                def e_mm(p, mi=mi, o=o, j=j):
                    for c in range(NCHUNK):
                        p.matmul(psm[j][:, :],
                                 wss[mi][:, c, o * 128:(o + 1) * 128],
                                 mxb[mi][:, c, :], start=(c == 0),
                                 stop=(c == NCHUNK - 1))

                ch.comp("tensor", e_mm, relax=True,
                        dep=Chain.dep_max(s_in, s_ev.get(i - 2, s_in)))
                s_mm[i] = ch.snap()
                dd_ = Chain.dep_max(s_mm[i], s_st.get(i - 2, s_in))
                if mi < 2:
                    ch.comp("vector",
                            lambda p, j=j: p.tensor_copy(evt[j][:],
                                                         psm[j][:]),
                            dep=dd_)
                else:
                    ch.comp("scalar",
                            lambda p, j=j: p.activation(evt[j][:], psm[j][:],
                                                        AF.Sigmoid), dep=dd_)
                s_ev[i] = ch.snap()
                ch.dma_n("sync", 1, lambda p, mi=mi, o=o, t0=t0, j=j: [
                    p.dma_start(out=outs[mi][o * 128:(o + 1) * 128,
                                t0:t0 + TT], in_=evt[j][:])], dep=s_ev[i])
                s_st[i] = ch.snap()

        def e_carry(p):
            return p.tensor_copy(ucol[:, :, 0:1], ut[:, :, TT - 1:TT])

        ch.comp("vector", e_carry)
    cx.release(mk)


def emit_wkvf(cx, T):
    ch = cx.ch
    nc = cx.nc
    mk = cx.mark()
    kd, vd, oo = T["k"], T["v"], T["wkv"]
    wv, uv = T["rw"], T["ru"]
    selA, selB = T["selA"], T["selB"]
    agin = nc.dram_tensor("wkv_agin", [128, 16], F32, kind="Internal").ap()
    agout = nc.dram_tensor("wkv_agout", [256, 16], F32, kind="Internal").ap()
    kb = cx.sbuf("kb", [128, HSEQ], F32)
    vb = cx.sbuf("vb", [128, HSEQ], F32)
    ek = cx.sbuf("ek", [128, HSEQ], F32)
    ekv = cx.sbuf("ekv", [128, HSEQ], F32)
    dbc = cx.sbuf("dbc", [128, HSEQ], F32)
    ns = cx.sbuf("ns", [128, HSEQ], F32)
    ds = cx.sbuf("ds", [128, HSEQ], F32)
    zb = cx.sbuf("zb", [128, HSEQ], F32)
    pw = cx.sbuf("pw", [128, HSEQ], F32)
    wd = cx.sbuf("wd", [128, NCHUNK], F32)
    eu = cx.sbuf("eu", [128, NCHUNK], F32)
    lasts = cx.sbuf("lasts", [128, 16], F32)
    carry = cx.sbuf("carry", [128, 16], F32)

    ch.dma_n("sync", 2, lambda p: [
        p.dma_start(out=wd[:], in_=wv[:, :]),
        p.dma_start(out=eu[:], in_=uv[:, :]),
    ])
    ch.comp("scalar", lambda p: p.activation(wd[:], wd[:], AF.Exp))
    ch.comp("vector", lambda p: p.tensor_scalar_mul(wd[:], wd[:], -1.0))
    ch.comp("scalar", lambda p: p.activation(wd[:], wd[:], AF.Exp))
    ch.comp("scalar", lambda p: p.activation(eu[:], eu[:], AF.Exp))
    ch.comp("vector", lambda p: p.memset(zb[:], 0.0))

    def scan_pass(c, final):
        ch.dma_n("sync", 2, lambda p, c=c: [
            p.dma_start(out=kb[:], in_=fmview(kd)[:, c, :]),
            p.dma_start(out=vb[:], in_=fmview(vd)[:, c, :]),
        ])
        ch.comp("scalar", lambda p: p.activation(ek[:], kb[:], AF.Exp))

        def e_pre(p, c=c):
            p.tensor_tensor(ekv[:], ek[:], vb[:], ALU.mult)
            p.memset(dbc[:], 1.0)
            p.tensor_scalar_mul(dbc[:], dbc[:], wd[:, c:c + 1])
            p.tensor_tensor_scan(ns[:], dbc[:], ekv[:], 0.0, ALU.mult,
                                 ALU.add)
            p.tensor_tensor_scan(ds[:], dbc[:], ek[:], 0.0, ALU.mult,
                                 ALU.add)

        ch.comp("vector", e_pre)
        if not final:
            def e_last(p, c=c):
                p.tensor_copy(lasts[:, 2 * c:2 * c + 1],
                              ns[:, HSEQ - 1:HSEQ])
                p.tensor_copy(lasts[:, 2 * c + 1:2 * c + 2],
                              ds[:, HSEQ - 1:HSEQ])

            ch.comp("vector", e_last, relax=True)
            return

        def e_fin(p, c=c):
            p.tensor_tensor_scan(pw[:], dbc[:], zb[:], 1.0, ALU.mult, ALU.add)
            p.scalar_tensor_tensor(ns[:], pw[:], carry[:, 2 * c:2 * c + 1],
                                   ns[:], ALU.mult, ALU.add)
            p.scalar_tensor_tensor(ds[:], pw[:], carry[:, 2 * c + 1:2 * c + 2],
                                   ds[:], ALU.mult, ALU.add)
            p.scalar_tensor_tensor(kb[:, 0:1], ekv[:, 0:1], eu[:, c:c + 1],
                                   carry[:, 2 * c:2 * c + 1], ALU.mult,
                                   ALU.add)
            p.scalar_tensor_tensor(vb[:, 0:1], ek[:, 0:1], eu[:, c:c + 1],
                                   carry[:, 2 * c + 1:2 * c + 2], ALU.mult,
                                   ALU.add)
            p.scalar_tensor_tensor(kb[:, 1:HSEQ], ekv[:, 1:HSEQ],
                                   eu[:, c:c + 1], ns[:, 0:HSEQ - 1],
                                   ALU.mult, ALU.add)
            p.scalar_tensor_tensor(vb[:, 1:HSEQ], ek[:, 1:HSEQ],
                                   eu[:, c:c + 1], ds[:, 0:HSEQ - 1],
                                   ALU.mult, ALU.add)
            p.reciprocal(vb[:], vb[:])
            p.tensor_tensor(ek[:], kb[:], vb[:], ALU.mult)

        ch.comp("vector", e_fin)
        ch.dma_n("sync", 1, lambda p, c=c: [
            p.dma_start(out=fmview(oo)[:, c, :], in_=ek[:])])

    for c in range(NCHUNK):
        scan_pass(c, False)
    ch.dma_n("sync", 1, lambda p: [p.dma_start(out=agin[:, :], in_=lasts[:])])

    def wrC(ch2, cmb, r0, c0, cw, slow2):
        ch2.comp("vector",
                 lambda p: p.tensor_copy(carry[:, c0:c0 + cw],
                                         cmb[:, 0:cw]))

    emit_pair_exchange(cx, "wkvx", agin, agout, 128, 16, selA, selB, wrC)
    for c in range(NCHUNK):
        scan_pass(c, True)
    cx.release(mk)


def emit_l0ff(cx, T, ln_in, fin_out):
    W = L0F_W
    ch = cx.ch
    nc = cx.nc
    mk = cx.mark()
    xs = T["xs"]
    wqkv = T["wqkv"]
    qsc = T["qsc"]
    ksc = T["ksc"]
    wo = T["wo"]
    lg = T["lg"]
    lb = T["lb"]
    w1pt = T["w1pt"]
    b1p = T["b1p"]
    w2pt = T["w2pt"]
    b2 = T["b2"]
    prot = T["prot"]
    tqc = T["tqc"]
    tqs = T["tqs"]
    tkc = T["tkc"]
    tks = T["tks"]
    amask = T["amask"]
    bmask = T["bmask"]
    ffnkeep = T["ffnkeep"]
    halo = T["halo"]
    if fin_out:
        xo = T["xo"]
    y = T["y"]

    xt = cx.sbuf("xt", [128, NCHUNK, W], F32)
    ut = cx.sbuf("ut", [128, NCHUNK, W], F32)
    xm = cx.sbuf("xm", [128, NCHUNK, W], F32)
    hs = cx.sbuf("hs", [128, NCHUNK, W], F32)
    ucar = cx.sbuf("ucar", [128, NCHUNK, 1], F32)
    xmcar = cx.sbuf("xmcar", [128, NCHUNK, 1], F32)
    xh1 = cx.sbuf("xh1", [128, NCHUNK, 1], F32)
    rows = cx.sbuf("rows", [1, 4 * W], F32)
    sq = cx.sbuf("sq", [128, NCHUNK, W], BF16)
    xb = cx.sbuf("xb", [128, NCHUNK, W], BF16)
    wqs = [cx.sbuf(f"wqs{i}", [128, NCHUNK, 128], BF16) for i in range(2)]
    wos = cx.sbuf("wos", [128, 4, D_MODEL], BF16)
    w1s = [cx.sbuf(f"w1s{i}", [128, NCHUNK, 128], BF16) for i in range(4)]
    w2s = [cx.sbuf(f"w2s{i}", [128, 22, 128], BF16) for i in range(2)]
    qscs = cx.sbuf("qscs", [128, 1], F32)
    kscs = cx.sbuf("kscs", [128, 1], F32)
    lgs = cx.sbuf("lgs", [128, NCHUNK], F32)
    lbs = cx.sbuf("lbs", [128, NCHUNK], F32)
    b1s = cx.sbuf("b1s", [128, 44], F32)
    b2s = cx.sbuf("b2s", [128, NCHUNK], F32)
    fks = cx.sbuf("fks", [128, 1], F32)
    t4 = [cx.sbuf(f"t4_{i}", [128, W], F32) for i in range(4)]
    ams = cx.sbuf("ams", [128, 4, 1024], BF16)
    bms = cx.sbuf("bms", [128, 8, W], BF16)
    ident = cx.sbuf("ident", [128, 128], F32)
    blkdiag = cx.sbuf("blkdiag", [128, 128], BF16)
    blkrot = cx.sbuf("blkrot", [128, 128], BF16)
    onesq = cx.sbuf("onesq", [128, 128], BF16)
    ones64 = cx.sbuf("ones64", [1, 64], F32)
    qr = cx.sbuf("qr", [128, 4, W], BF16)
    kr = cx.sbuf("kr", [128, 4, 2 * W], BF16)
    vat = cx.sbuf("vat", [128, 8, 8, 65], BF16)
    evq = cx.sbuf("evq", [128, W], F32)
    qhat = cx.sbuf("qhat", [128, W], F32)
    rstb = cx.sbuf("rstb", [128, W], F32)
    rt1 = cx.sbuf("rt1", [128, W], F32)
    sqs = cx.sbuf("sqs", [128, W], BF16)
    nmrow = cx.sbuf("nmrow", [128, W], BF16)
    nmcol = cx.sbuf("nmcol", [128, 1], F32)
    n0 = cx.sbuf("n0", [128, 1], F32)
    atT = cx.sbuf("atT", [128, W], BF16)
    dd = cx.sbuf("dd", [1, W], F32)
    bc = cx.sbuf("bc", [64, W], F32)
    ow = cx.sbuf("ow", [128, 4, W], BF16)
    hgl = cx.sbuf("hgl", [128, 22, W], BF16)
    sa = [evq, rstb]
    gg = [qhat, rt1]
    P0 = cx.psum("P0", [128, 1024])
    P1 = cx.psum("P1", [128, W])
    P2 = cx.psum("P2", [128, W])
    P3 = cx.psum("P3", [128, W])
    P4 = cx.psum("P4", [128, W])
    P5 = cx.psum("P5", [128, W])

    ch.dma_n("gpsimd", 6, lambda p: [
        p.dma_start(out=wos[:], in_=wo.rearrange("(c p) o -> p c o", p=128)),
        p.dma_start(out=ams[:], in_=amask.rearrange("s i p j -> p (s i) j")[:, 4:8, :]),
        p.dma_start(out=bms[:], in_=bmask.rearrange("s r p j -> p (s r) j")[:, 8:16, :]),
        p.dma_start(out=qscs[:], in_=qsc[:, :]),
        p.dma_start(out=kscs[:], in_=ksc[:, :]),
        p.dma_start(out=blkrot[:], in_=prot[:, :]),
    ])
    ch.dma_n("sync", 6, lambda p: [
        p.dma_start(out=lgs[:], in_=lg[:, :]),
        p.dma_start(out=lbs[:], in_=lb[:, :]),
        p.dma_start(out=b1s[:], in_=b1p[:, :]),
        p.dma_start(out=b2s[:], in_=b2[:, :]),
        p.dma_start(out=fks[:], in_=ffnkeep[:, :]),
        p.dma_start(out=xh1[:], in_=fmview(halo)[:, :, 0:1],
                    allow_slow_non_contiguous=True),
    ])

    def e_consts(p):
        p.memset(ident[:], 0.0)
        p.memset(blkdiag[:], 0.0)
        p.memset(blkdiag[0:64, 0:64], 1.0)
        p.memset(blkdiag[64:128, 64:128], 1.0)
        p.memset(onesq[:], 1.0)
        p.memset(ones64[:], 1.0)
        p.memset(kr[:], 0.0)
        p.memset(vat[:], 0.0)

    ch.comp("vector", e_consts)
    ch.comp("gpsimd", lambda p: p.affine_select(
        ident[:], ident[:], [[-1, 128]], ALU.not_equal, 1.0, base=0,
        channel_multiplier=1))

    if ln_in:
        emit_ln(cx, xh1, 1, sq, xb, P2, P3, P1, P5, rows)
        emit_ln_apply(cx, xh1, 1, P1, P5, ucar)
    else:
        ch.comp("vector", lambda p: p.tensor_copy(ucar[:], xh1[:]))

    for w in range(L0F_NW):
        if w < 2:
            hsrc = fmview(halo)[:, :, 1 + w * W:1 + (w + 1) * W]
        else:
            hsrc = fmview(xs)[:, :, (w - 2) * W:(w - 1) * W]
        ch.dma_n("sync", 1, lambda p, hsrc=hsrc: [
            p.dma_start(out=xt[:], in_=hsrc)])
        if ln_in:
            emit_ln(cx, xt, W, sq, xb, P2, P3, P1, P5, rows)
            emit_ln_apply(cx, xt, W, P1, P5, ut)
        else:
            ch.comp("vector", lambda p: p.tensor_copy(ut[:], xt[:]))

        def e_shift(p):
            for c in range(NCHUNK):
                if c < 4:
                    p.tensor_copy(xb[:, c, :], ut[:, c, :])
                else:
                    p.tensor_copy(xb[:, c, 0:1], ucar[:, c, 0:1])
                    p.tensor_copy(xb[:, c, 1:W], ut[:, c, 0:W - 1])

        ch.comp("vector", e_shift, relax=True)
        ch.comp("vector",
                lambda p: p.tensor_copy(ucar[:], ut[:, :, W - 1:W]))

        def e_ring(p):
            p.tensor_copy(kr[:, :, 0:W], kr[:, :, W:2 * W])
            p.tensor_copy(vat[:, 0:4, :, :], vat[:, 4:8, :, :])

        ch.comp("vector", e_ring, relax=True)
        ch.dma_n("sync", 4, lambda p, w=w: [
            p.dma_start(out=t4[0][:], in_=tqc[:, w * W:(w + 1) * W]),
            p.dma_start(out=t4[1][:], in_=tqs[:, w * W:(w + 1) * W]),
            p.dma_start(out=t4[2][:], in_=tkc[:, w * W:(w + 1) * W]),
            p.dma_start(out=t4[3][:], in_=tks[:, w * W:(w + 1) * W]),
        ])

        for o in range(8):
            kind = o // 4
            pt = o % 4
            wb = wqs[o % 2]
            ch.dma_n("gpsimd", 1,
                     lambda p, o=o, wb=wb: [p.dma_start(out=wb[:],
                                                        in_=wqkv[o])])

            def e_mm(p, wb=wb):
                for c in range(NCHUNK):
                    p.matmul(P4[:, :], wb[:, c, :], xb[:, c, :],
                             start=(c == 0), stop=(c == NCHUNK - 1))

            ch.comp("tensor", e_mm, relax=True)
            ch.comp("vector", lambda p: p.tensor_copy(evq[:], P4[:, :]))
            ch.comp("scalar",
                    lambda p: p.activation(sqs[:], evq[:], AF.Square))
            ch.comp("tensor", lambda p: p.matmul(
                P1[:, :], blkdiag[:], sqs[:], start=True, stop=True), relax=True)

            def e_rst(p):
                p.tensor_scalar_max(rstb[:], P1[:, :], 1e-24)
                p.reciprocal(rstb[:], rstb[:])

            ch.comp("vector", e_rst)
            ch.comp("scalar",
                    lambda p: p.activation(rstb[:], rstb[:], AF.Sqrt))

            def e_hat(p, kind=kind):
                scs = qscs if kind == 0 else kscs
                p.tensor_tensor(qhat[:], evq[:], rstb[:], ALU.mult)
                p.tensor_scalar_mul(qhat[:], qhat[:], scs[:])
                p.tensor_copy(sqs[:], qhat[:])

            ch.comp("vector", e_hat)
            ch.comp("tensor", lambda p: p.matmul(
                P2[:, :], blkrot[:], sqs[:], start=True, stop=True), relax=True)

            def e_rot(p, kind=kind, pt=pt):
                tc, ts_ = (t4[0], t4[1]) if kind == 0 else (t4[2], t4[3])
                p.tensor_tensor(rt1[:], qhat[:], tc[:], ALU.mult)
                p.tensor_tensor(qhat[:], P2[:, :], ts_[:], ALU.mult)
                if kind == 0:
                    p.tensor_tensor(qr[:, pt, :], rt1[:], qhat[:], ALU.add)
                else:
                    p.tensor_tensor(kr[:, pt, W:2 * W], rt1[:], qhat[:],
                                    ALU.add)

            ch.comp("vector", e_rot)

        for o in range(8, 12):
            wb = wqs[o % 2]
            ch.dma_n("gpsimd", 1,
                     lambda p, o=o, wb=wb: [p.dma_start(out=wb[:],
                                                        in_=wqkv[o])])

            def e_mm(p, wb=wb):
                for c in range(NCHUNK):
                    p.matmul(P4[:, :], wb[:, c, :], xb[:, c, :],
                             start=(c == 0), stop=(c == NCHUNK - 1))

            ch.comp("tensor", e_mm, relax=True)
            ch.comp("vector", lambda p: p.tensor_copy(evq[:], P4[:, :]))
            for hh in range(2):
                h = (o - 8) * 2 + hh
                rb = hh * 64
                for jj in range(4):
                    def e_tr(p, rb=rb, jj=jj):
                        return p.transpose(
                            P3[0:128, 0:64],
                            evq[rb:rb + 64, jj * 128:(jj + 1) * 128],
                            ident[rb:rb + 64, rb:rb + 64])

                    ch.comp("tensor", e_tr, relax=True)

                    def e_vat(p, h=h, jj=jj):
                        p.tensor_copy(vat[:, 4 + jj, h, 0:64],
                                      P3[0:128, 0:64])
                        p.memset(vat[:, 4 + jj, h, 64:65], 1.0)

                    ch.comp("vector", e_vat, relax=True)

        if w == 0:
            continue

        mset = 0 if w == 2 else 1
        if w == 2:
            ch.dma_n("gpsimd", 1, lambda p: [p.dma_start(
                out=ams[:],
                in_=amask.rearrange("s i p j -> p (s i) j")[:, 0:4, :])])
            ch.dma_n("gpsimd", 1, lambda p: [p.dma_start(
                out=bms[:],
                in_=bmask.rearrange("s r p j -> p (s r) j")[:, 0:8, :])])
        if w == 3:
            ch.dma_n("gpsimd", 1, lambda p: [p.dma_start(
                out=ams[:],
                in_=amask.rearrange("s i p j -> p (s i) j")[:, 4:8, :])])
            ch.dma_n("gpsimd", 1, lambda p: [p.dma_start(
                out=bms[:],
                in_=bmask.rearrange("s r p j -> p (s r) j")[:, 8:16, :])])
        for h in range(8):
            pt, rb = h // 2, (h % 2) * 64
            for it in range(4):
                def e_sim(p, pt=pt, rb=rb, it=it):
                    p.matmul(
                        P0[:, 0:W],
                        qr[rb:rb + 64, pt, it * 128:(it + 1) * 128],
                        kr[rb:rb + 64, pt, 0:W], start=True, stop=True)
                    return p.matmul(
                        P0[:, W:2 * W],
                        qr[rb:rb + 64, pt, it * 128:(it + 1) * 128],
                        kr[rb:rb + 64, pt, W:2 * W], start=True, stop=True)

                ch.comp("tensor", e_sim, relax=True)

                def e_am(p, it=it, mset=mset):
                    p.tensor_tensor(rt1[:], P0[:, 0:W],
                                    ams[:, it, 0:W], ALU.add)
                    p.reduce_max(n0[:], rt1[:], axis=AX.X, negate=True)
                    p.tensor_tensor(rt1[:], P0[:, W:2 * W],
                                    ams[:, it, W:2 * W], ALU.add)
                    p.reduce_max(nmcol[:], rt1[:], axis=AX.X, negate=True)
                    p.tensor_tensor(nmcol[:], nmcol[:], n0[:], ALU.min)

                ch.comp("vector", e_am)
                ch.comp("tensor", lambda p: p.matmul(
                    P3[0:1, 0:128], nmcol[:], ident[:], start=True,
                    stop=True), relax=True)

                def e_nm(p, it=it):
                    p.tensor_copy(nmrow[0:1, it * 128:(it + 1) * 128],
                                  P3[0:1, 0:128])
                    p.tensor_copy(nmrow[64:65, it * 128:(it + 1) * 128],
                                  P3[0:1, 0:128])

                ch.comp("vector", e_nm, relax=True)

            for js in range(8):
                def e_simT(p, pt=pt, rb=rb, js=js):
                    p.matmul(P1[:, :],
                             kr[rb:rb + 64, pt, js * 128:(js + 1) * 128],
                             qr[rb:rb + 64, pt, :], start=True, stop=False)
                    p.matmul(P1[:, :], onesq[rb:rb + 1, :],
                             nmrow[rb:rb + 1, :], start=False, stop=True)

                ch.comp("tensor", e_simT, relax=True)
                ch.comp("scalar",
                        lambda p: p.activation(atT[:], P1[:, :], AF.Exp))

                def e_bm(p, js=js, mset=mset):
                    r, jj = js // 4, js % 4
                    p.tensor_tensor(atT[:], atT[:],
                                    bms[:, r * 4 + jj, :], ALU.mult)

                ch.comp("vector", e_bm)

                def e_av(p, h=h, js=js):
                    return p.matmul(P2[0:65, :], vat[:, js, h, :], atT[:],
                                    start=(js == 0), stop=(js == 7))

                ch.comp("tensor", e_av, relax=True)

            ch.comp("vector", lambda p: p.reciprocal(dd[:], P2[64:65, :]))
            ch.comp("tensor", lambda p: p.matmul(
                P5[0:64, :], ones64[:], dd[:], start=True, stop=True), relax=True)

            def e_o(p, h=h):
                p.tensor_copy(bc[:], P5[0:64, :])
                p.tensor_tensor(ow[(h % 2) * 64:(h % 2) * 64 + 64,
                                h // 2, :], P2[0:64, :], bc[:], ALU.mult)

            ch.comp("vector", e_o)

        for o in range(NCHUNK):
            def e_wo(p, o=o):
                for c in range(4):
                    p.matmul(P4[:, :], wos[:, c, o * 128:(o + 1) * 128],
                             ow[:, c, :], start=(c == 0), stop=(c == 3))

            ch.comp("tensor", e_wo, relax=True)
            ch.comp("vector", lambda p, o=o: p.tensor_tensor(
                xm[:, o, :], ut[:, o, :], P4[:, :], ALU.add))

        if w == 1:
            ch.comp("vector",
                    lambda p: p.tensor_copy(xmcar[:], xm[:, :, W - 1:W]))
            continue

        if w == 2:
            def e_fk(p):
                for c in range(NCHUNK):
                    p.tensor_scalar_mul(xmcar[:, c, 0:1], xmcar[:, c, 0:1],
                                        fks[:])

            ch.comp("vector", e_fk, relax=True)

        def e_shift2(p):
            for c in range(NCHUNK):
                if c < 4:
                    p.tensor_copy(hs[:, c, :], xm[:, c, :])
                else:
                    p.tensor_copy(hs[:, c, 0:1], xmcar[:, c, 0:1])
                    p.tensor_copy(hs[:, c, 1:W], xm[:, c, 0:W - 1])

        ch.comp("vector", e_shift2, relax=True)
        ch.comp("vector",
                lambda p: p.tensor_copy(xmcar[:], xm[:, :, W - 1:W]))
        emit_ln(cx, hs, W, sq, xb, P2, P3, P1, P5, rows)

        def e_ln2(p):
            for c in range(NCHUNK):
                p.tensor_tensor(rt1[:], hs[:, c, :], P1[:, :], ALU.mult)
                p.scalar_tensor_tensor(rt1[:], rt1[:], 1.0, P5[:, :],
                                       ALU.mult, ALU.add)
                p.tensor_scalar(sq[:, c, :], rt1[:], lgs[:, c:c + 1],
                                lbs[:, c:c + 1], ALU.mult, ALU.add)

        ch.comp("vector", e_ln2)

        s_u2 = ch.snap()
        s_w1ld, s_w1a, s_w1g, s_sa, s_gg, s_hg = {}, {}, {}, {}, {}, {}
        for j in range(22):
            jj = j % 2
            ch.dma_n("gpsimd", 2, lambda p, j=j, jj=jj: [
                p.dma_start(out=w1s[2 * jj][:], in_=w1pt[j]),
                p.dma_start(out=w1s[2 * jj + 1][:], in_=w1pt[22 + j]),
            ], dep=Chain.dep_max(s_u2, s_w1g.get(j - 2, s_u2)))
            s_w1ld[j] = ch.snap()

            def e_w1a(p, jj=jj):
                for c in range(NCHUNK):
                    p.matmul(P4[:, :] if jj == 0 else P2[:, :],
                             w1s[2 * jj][:, c, :], sq[:, c, :],
                             start=(c == 0), stop=(c == NCHUNK - 1))

            ch.comp("tensor", e_w1a, relax=True,
                    dep=Chain.dep_max(s_w1ld[j], s_sa.get(j - 2, s_u2)))
            s_w1a[j] = ch.snap()

            def e_w1g(p, jj=jj):
                for c in range(NCHUNK):
                    p.matmul(P5[:, :] if jj == 0 else P3[:, :],
                             w1s[2 * jj + 1][:, c, :], sq[:, c, :],
                             start=(c == 0), stop=(c == NCHUNK - 1))

            ch.comp("tensor", e_w1g, relax=True,
                    dep=Chain.dep_max(s_w1ld[j], s_gg.get(j - 2, s_u2)))
            s_w1g[j] = ch.snap()
            ch.comp("vector", lambda p, j=j, jj=jj: p.tensor_scalar_add(
                sa[jj][:], (P4 if jj == 0 else P2)[:, :], b1s[:, j:j + 1]),
                    dep=Chain.dep_max(s_w1a[j], s_hg.get(j - 2, s_u2)))
            s_sa[j] = ch.snap()
            ch.comp("scalar", lambda p, j=j, jj=jj: p.activation(
                gg[jj][:], (P5 if jj == 0 else P3)[:, :], AF.Gelu,
                bias=b1s[:, 22 + j:23 + j], scale=1.0),
                    dep=Chain.dep_max(s_w1g[j], s_hg.get(j - 2, s_u2)))
            s_gg[j] = ch.snap()
            ch.comp("vector", lambda p, j=j, jj=jj: p.tensor_tensor(
                hgl[:, j, :], sa[jj][:], gg[jj][:], ALU.mult),
                    dep=Chain.dep_max(s_sa[j], s_gg[j]))
            s_hg[j] = ch.snap()

        s_h = ch.snap()
        s_w2ld, s_w2, s_res = {}, {}, {}
        for o in range(NCHUNK):
            jj = o % 2
            ch.dma_n("gpsimd", 1,
                     lambda p, o=o, jj=jj: [p.dma_start(out=w2s[jj][:],
                                                        in_=w2pt[o])],
                     dep=Chain.dep_max(s_h, s_w2.get(o - 2, s_h)))
            s_w2ld[o] = ch.snap()

            def e_w2(p, jj=jj):
                for c in range(22):
                    p.matmul((P4 if jj == 0 else P2)[:, :], w2s[jj][:, c, :],
                             hgl[:, c, :], start=(c == 0), stop=(c == 21))

            ch.comp("tensor", e_w2, relax=True,
                    dep=Chain.dep_max(s_w2ld[o], s_res.get(o - 2, s_h)))
            s_w2[o] = ch.snap()

            def e_res(p, o=o, jj=jj):
                p.tensor_scalar_add(rt1[:], (P4 if jj == 0 else P2)[:, :],
                                    b2s[:, o:o + 1])
                p.tensor_tensor(xm[:, o, :], xm[:, o, :], rt1[:], ALU.add)

            ch.comp("vector", e_res, dep=s_w2[o])
            s_res[o] = ch.snap()

        t0 = (w - 2) * W
        if fin_out:
            ch.dma_n("sync", 1, lambda p, t0=t0: [
                p.dma_start(out=hs[:], in_=fmview(xo)[:, :, t0:t0 + W])])
            emit_ln(cx, xm, W, sq, xb, P2, P3, P1, P5, rows)
            emit_ln_apply(cx, xm, W, P1, P5, xm, tmp=rt1, add_to=hs)

        ch.dma_n("sync", 1, lambda p, t0=t0: [
            p.dma_start(out=fmview(y)[:, :, t0:t0 + W], in_=xm[:])])

    cx.release(mk)




def emit_n1f(cx, T):
    ch = cx.ch
    mk = cx.mark()
    x, w1t, w2t, s1, s2, h = (T["x"], T["w1t"], T["w2t"], T["s1"], T["s2"],
                              T["h"])
    xt = cx.sbuf("xt", [128, NCHUNK, TT], F32)
    sq = cx.sbuf("sq", [128, NCHUNK, TT], BF16)
    xb = cx.sbuf("xb", [128, NCHUNK, TT], BF16)
    ub = cx.sbuf("ub", [128, NCHUNK, TT], BF16)
    w1s = cx.sbuf("w1s", [128, NCHUNK, 4096], BF16)
    w2s = cx.sbuf("w2s", [128, NCHUNK, 4096], BF16)
    s1s = cx.sbuf("s1s", [128, 32], F32)
    s2s = cx.sbuf("s2s", [128, 32], F32)
    rows = cx.sbuf("rows", [1, 4 * TT], F32)
    sa = [cx.sbuf(f"sa{i}", [128, TT], F32) for i in range(2)]
    sb2 = [cx.sbuf(f"sb2{i}", [128, TT], F32) for i in range(2)]
    hb = [cx.sbuf(f"hb{i}", [128, TT], BF16) for i in range(2)]
    psA = cx.psum("psA", [128, TT])
    psB = cx.psum("psB", [128, TT])
    psA2 = cx.psum("psA2", [128, TT])
    psB2 = cx.psum("psB2", [128, TT])
    psm = [cx.psum(f"psm{i}", [128, TT]) for i in range(2)]
    psm2 = [cx.psum(f"psm2{i}", [128, TT]) for i in range(2)]

    ch.dma_n("gpsimd", 4, lambda p: [
        p.dma_start(out=w1s[:], in_=fmview(w1t)),
        p.dma_start(out=w2s[:], in_=fmview(w2t)),
        p.dma_start(out=s1s[:], in_=s1[:, :]),
        p.dma_start(out=s2s[:], in_=s2[:, :]),
    ])
    for t in range(NTILE):
        t0 = t * TT
        ch.dma_n("sync", 1, lambda p, t0=t0: [
            p.dma_start(out=xt[:], in_=fmview(x)[:, :, t0:t0 + TT])])
        emit_ln(cx, xt, TT, sq, xb, psA, psB, psA2, psB2, rows)
        emit_ln_apply(cx, xt, TT, psA2, psB2, ub)
        s_ub = ch.snap()
        s_mma, s_mmb, s_act, s_mul, s_st = {}, {}, {}, {}, {}
        for o in range(32):
            j = o % 2

            def e_mma(p, o=o, j=j):
                for c in range(NCHUNK):
                    p.matmul(psm[j][:, :], w1s[:, c, o * 128:(o + 1) * 128],
                             ub[:, c, :], start=(c == 0),
                             stop=(c == NCHUNK - 1))

            ch.comp("tensor", e_mma, relax=True,
                    dep=Chain.dep_max(s_ub, s_act.get(o - 2, s_ub)))
            s_mma[o] = ch.snap()

            def e_mmb(p, o=o, j=j):
                for c in range(NCHUNK):
                    p.matmul(psm2[j][:, :], w2s[:, c, o * 128:(o + 1) * 128],
                             ub[:, c, :], start=(c == 0),
                             stop=(c == NCHUNK - 1))

            ch.comp("tensor", e_mmb, relax=True,
                    dep=Chain.dep_max(s_ub, s_mul.get(o - 2, s_ub)))
            s_mmb[o] = ch.snap()
            ch.comp("scalar", lambda p, o=o, j=j: p.activation(
                sa[j][:], psm[j][:], AF.Silu, scale=s1s[:, o:o + 1]),
                    relax=True,
                    dep=Chain.dep_max(s_mma[o], s_mul.get(o - 2, s_ub)))
            s_act[o] = ch.snap()

            def e_mul(p, o=o, j=j):
                p.tensor_scalar_mul(sb2[j][:], psm2[j][:], s2s[:, o:o + 1])
                p.tensor_tensor(hb[j][:], sa[j][:], sb2[j][:], ALU.mult)

            ch.comp("vector", e_mul,
                    dep=Chain.dep_max(s_mmb[o], s_act[o],
                                      s_st.get(o - 2, s_ub)))
            s_mul[o] = ch.snap()
            ch.dma_n("sync", 1, lambda p, o=o, t0=t0, j=j: [
                p.dma_start(out=h[o * 128:(o + 1) * 128, t0:t0 + TT],
                            in_=hb[j][:])], dep=s_mul[o])
            s_st[o] = ch.snap()
    cx.release(mk)


def emit_nhf(cx, T):
    ch = cx.ch
    mk = cx.mark()
    hi, wt, ss, ho = T["hi"], T["wt"], T["ss"], T["ho"]
    ht = [cx.sbuf(f"ht{i}", [128, 32, TT], BF16) for i in range(2)]
    ws = [cx.sbuf(f"ws{i}", [128, 32, 128], BF16) for i in range(3)]
    sss = cx.sbuf("sss", [128, 32], F32)
    ob = [cx.sbuf(f"ob{i}", [128, TT], BF16) for i in range(2)]
    psm = [cx.psum(f"psm{i}", [128, TT]) for i in range(2)]

    ch.dma_n("gpsimd", 1, lambda p: [p.dma_start(out=sss[:], in_=ss[:, :])])
    z = ch.snap()
    s_ld = {}
    s_wld = {}
    s_mm = {}
    s_act = {}
    s_st = {}
    idx = lambda t, o: t * 32 + o
    for t in range(NTILE):
        t0 = t * TT
        ch.dma_n("sync", 1, lambda p, t0=t0, t=t: [
            p.dma_start(out=ht[t % 2][:],
                        in_=fmview(hi, 32)[:, :, t0:t0 + TT])],
                  dep=s_mm.get(idx(t - 2, 31), z))
        s_ld[t] = ch.snap()
        for o in range(32):
            i = idx(t, o)
            wb = ws[i % 3]
            ch.dma_n("gpsimd", 1, lambda p, o=o, wb=wb: [
                p.dma_start(out=wb[:],
                            in_=wt[o].rearrange("(c p) m -> p c m", p=128))],
                      dep=s_mm.get(i - 3, z))
            s_wld[i] = ch.snap()
            ps = psm[i % 2]

            def e_mm(p, wb=wb, ps=ps, t=t):
                for c in range(32):
                    p.matmul(ps[:, :], wb[:, c, :], ht[t % 2][:, c, :],
                             start=(c == 0), stop=(c == 31))

            ch.comp("tensor", e_mm, relax=True,
                    dep=Chain.dep_max(s_wld[i], s_ld[t],
                                      s_act.get(i - 2, z)))
            s_mm[i] = ch.snap()
            ch.comp("scalar", lambda p, o=o, ps=ps, ob2=ob[i % 2]:
                    p.activation(ob2[:], ps[:], AF.Gelu,
                                 scale=sss[:, o:o + 1]),
                    relax=True,
                    dep=Chain.dep_max(s_mm[i], s_st.get(i - 2, z)))
            s_act[i] = ch.snap()
            ch.dma_n("sync", 1, lambda p, o=o, t0=t0, ob2=ob[i % 2]: [
                p.dma_start(out=ho[o * 128:(o + 1) * 128, t0:t0 + TT],
                            in_=ob2[:])], dep=s_act[i])
            s_st[i] = ch.snap()
    cx.release(mk)


def emit_n4f(cx, T):
    ch = cx.ch
    mk = cx.mark()
    hi, w3t, s3, x3, y = T["hi"], T["w3t"], T["s3"], T["x3"], T["y"]
    ht = cx.sbuf("ht", [128, 32, TT], BF16)
    x3t = cx.sbuf("x3t", [128, NCHUNK, TT], F32)
    w3s = cx.sbuf("w3s", [128, 32, D_MODEL], BF16)
    s3s = cx.sbuf("s3s", [128, 8], F32)
    yt = cx.sbuf("yt", [128, NCHUNK, TT], F32)
    yb = cx.sbuf("yb", [128, NCHUNK, TT], BF16)
    sq = cx.sbuf("sq", [128, NCHUNK, TT], BF16)
    rows = cx.sbuf("rows", [1, 4 * TT], F32)
    outt = cx.sbuf("outt", [128, NCHUNK, TT], F32)
    psm = [cx.psum(f"psm{i}", [128, TT]) for i in range(2)]
    psA = cx.psum("psA", [128, TT])
    psB = cx.psum("psB", [128, TT])
    psA2 = cx.psum("psA2", [128, TT])
    psB2 = cx.psum("psB2", [128, TT])

    ch.dma_n("gpsimd", 2, lambda p: [
        p.dma_start(out=w3s[:], in_=fmview(w3t, 32)),
        p.dma_start(out=s3s[:], in_=s3[:, :]),
    ])
    for t in range(NTILE):
        t0 = t * TT
        ch.dma_n("sync", 2, lambda p, t0=t0: [
            p.dma_start(out=ht[:], in_=fmview(hi, 32)[:, :, t0:t0 + TT]),
            p.dma_start(out=x3t[:], in_=fmview(x3)[:, :, t0:t0 + TT]),
        ])
        s_in = ch.snap()
        s_mm, s_ev = {}, {}
        for o in range(NCHUNK):
            j = o % 2

            def e_mm(p, o=o, j=j):
                for c in range(32):
                    p.matmul(psm[j][:, :], w3s[:, c, o * 128:(o + 1) * 128],
                             ht[:, c, :], start=(c == 0), stop=(c == 31))

            ch.comp("tensor", e_mm, relax=True,
                    dep=Chain.dep_max(s_in, s_ev.get(o - 2, s_in)))
            s_mm[o] = ch.snap()

            def e_ev(p, o=o, j=j):
                p.tensor_scalar_mul(yt[:, o, :], psm[j][:], s3s[:, o:o + 1])
                p.tensor_scalar_mul(yb[:, o, :], psm[j][:], s3s[:, o:o + 1])

            ch.comp("vector", e_ev, relax=True, dep=s_mm[o])
            s_ev[o] = ch.snap()
        emit_ln(cx, yt, TT, sq, yb, psA, psB, psA2, psB2, rows, need_xb=False)
        emit_ln_apply(cx, yt, TT, psA2, psB2, outt, add_to=x3t)
        ch.dma_n("sync", 1, lambda p, t0=t0: [
            p.dma_start(out=fmview(y)[:, :, t0:t0 + TT], in_=outt[:])])
    cx.release(mk)


def build_fused():
    nc = bass.Bass("TRN2", target_bir_lowering=False, debug=False)
    E = "ExternalInput"
    d = nc.dram_tensor
    T = {}

    def ein(name, shape):
        T[name] = d(name, shape, F32, kind=E).ap()

    def intr(name, shape, dt=F32):
        T[name] = d(name, shape, dt, kind="Internal").ap()

    ein("x", [D_MODEL, HSEQ])
    T["y"] = d("y", [D_MODEL, HSEQ], F32, kind="ExternalOutput").ap()
    ein("selA", [128, 1])
    ein("selB", [128, 1])
    ein("ffnkeep", [128, 1])
    ein("a1_wqt", [D_MODEL, 3072])
    ein("a3_wot", [D_MODEL, D_MODEL])
    intr("q", [D_MODEL, HSEQ])
    intr("k", [D_MODEL, HSEQ])
    intr("v", [D_MODEL, HSEQ])
    intr("kvtm", [32, 128, 2048], BF16)
    intr("o_att", [D_MODEL, HSEQ])
    intr("x1", [D_MODEL, HSEQ])
    for li in range(2):
        ein(f"f{li}_wqkv", [12, 128, NCHUNK, 128])
        ein(f"f{li}_qsc", [128, 1])
        ein(f"f{li}_ksc", [128, 1])
        ein(f"f{li}_wo", [512, D_MODEL])
        ein(f"f{li}_lg", [128, NCHUNK])
        ein(f"f{li}_lb", [128, NCHUNK])
        ein(f"f{li}_w1pt", [44, 128, NCHUNK, 128])
        ein(f"f{li}_b1p", [128, 44])
        ein(f"f{li}_w2pt", [8, 128, 22, 128])
        ein(f"f{li}_b2", [128, NCHUNK])
    ein("f_prot", [128, 128])
    ein("f_tqc", [128, L0F_NW * L0F_W])
    ein("f_tqs", [128, L0F_NW * L0F_W])
    ein("f_tkc", [128, L0F_NW * L0F_W])
    ein("f_tks", [128, L0F_NW * L0F_W])
    ein("f_amask", [2, 4, 128, 1024])
    ein("f_bmask", [2, 8, 128, L0F_W])
    intr("halo0", [D_MODEL, 1025])
    intr("inner1", [D_MODEL, HSEQ])
    intr("halo1", [D_MODEL, 1025])
    intr("x2", [D_MODEL, HSEQ])
    intr("agh", [2 * D_MODEL, 1025])
    intr("aghR", [2 * D_MODEL, 1])
    ein("r_wkt", [D_MODEL, D_MODEL])
    ein("r_wvt", [D_MODEL, D_MODEL])
    ein("r_wrt", [D_MODEL, D_MODEL])
    ein("r_wot", [D_MODEL, D_MODEL])
    ein("r_mixk", [128, NCHUNK])
    ein("r_mixv", [128, NCHUNK])
    ein("r_mixr", [128, NCHUNK])
    ein("r_w", [128, NCHUNK])
    ein("r_u", [128, NCHUNK])
    intr("haloR", [D_MODEL, 1])
    intr("kR", [D_MODEL, HSEQ])
    intr("vR", [D_MODEL, HSEQ])
    intr("srR", [D_MODEL, HSEQ])
    intr("wkvo", [D_MODEL, HSEQ])
    intr("x3", [D_MODEL, HSEQ])
    ein("n_w1t", [D_MODEL, 4096])
    ein("n_w2t", [D_MODEL, 4096])
    ein("n_s1", [128, 32])
    ein("n_s2", [128, 32])
    ein("n_h1wt", [32, 4096, 128])
    ein("n_h1s", [128, 32])
    ein("n_h2wt", [32, 4096, 128])
    ein("n_h2s", [128, 32])
    ein("n_w3t", [4096, D_MODEL])
    ein("n_s3", [128, 8])
    intr("hb1", [4096, HSEQ], BF16)
    intr("hb2", [4096, HSEQ], BF16)
    intr("hb3", [4096, HSEQ], BF16)

    ch = Chain()
    cx = Ctx(nc, ch)
    make_common_consts(cx)
    selA = cx.sbuf("g_selA", [128, 1], F32)
    selB = cx.sbuf("g_selB", [128, 1], F32)
    ch.dma_n("sync", 2, lambda p: [
        p.dma_start(out=selA[:], in_=T["selA"][:, :]),
        p.dma_start(out=selB[:], in_=T["selB"][:, :]),
    ])

    cx.prefix = "a1_"
    emit_a1f(cx, {"x": T["x"], "wqt": T["a1_wqt"], "q": T["q"], "k": T["k"],
                  "v": T["v"], "kvtm": T["kvtm"]})
    cx.prefix = "a2_"
    emit_a2f(cx, {"q": T["q"], "k": T["k"], "kvtm": T["kvtm"],
                  "o": T["o_att"], "selA": selA, "selB": selB})
    cx.prefix = "a3_"
    emit_projf(cx, {"inp": T["o_att"], "x": T["x"], "wt": T["a3_wot"],
                    "y": T["x1"]})

    cx.prefix = "hx0_"
    emit_halo_fill(cx, "hx0_", T["x1"], T["halo0"], 1025, selA, selB,
                   T["agh"])
    cx.prefix = "f0_"
    emit_l0ff(cx, {"xs": T["x1"], "halo": T["halo0"],
                   "wqkv": T["f0_wqkv"], "qsc": T["f0_qsc"],
                   "ksc": T["f0_ksc"], "wo": T["f0_wo"], "lg": T["f0_lg"],
                   "lb": T["f0_lb"], "w1pt": T["f0_w1pt"],
                   "b1p": T["f0_b1p"], "w2pt": T["f0_w2pt"],
                   "b2": T["f0_b2"], "prot": T["f_prot"], "tqc": T["f_tqc"],
                   "tqs": T["f_tqs"], "tkc": T["f_tkc"], "tks": T["f_tks"],
                   "amask": T["f_amask"], "bmask": T["f_bmask"],
                   "ffnkeep": T["ffnkeep"], "y": T["inner1"]},
              True, False)
    cx.prefix = "hx1_"
    emit_halo_fill(cx, "hx1_", T["inner1"], T["halo1"], 1025, selA, selB,
                   T["agh"])
    cx.prefix = "f1_"
    emit_l0ff(cx, {"xs": T["inner1"], "halo": T["halo1"],
                   "wqkv": T["f1_wqkv"], "qsc": T["f1_qsc"],
                   "ksc": T["f1_ksc"], "wo": T["f1_wo"], "lg": T["f1_lg"],
                   "lb": T["f1_lb"], "w1pt": T["f1_w1pt"],
                   "b1p": T["f1_b1p"], "w2pt": T["f1_w2pt"],
                   "b2": T["f1_b2"], "prot": T["f_prot"], "tqc": T["f_tqc"],
                   "tqs": T["f_tqs"], "tkc": T["f_tkc"], "tks": T["f_tks"],
                   "amask": T["f_amask"], "bmask": T["f_bmask"],
                   "ffnkeep": T["ffnkeep"], "xo": T["x1"], "y": T["x2"]},
              False, True)

    cx.prefix = "hxr_"
    emit_halo_fill(cx, "hxr_", T["x2"], T["haloR"], 1, selA, selB,
                   T["aghR"])
    cx.prefix = "r1_"
    emit_r1f(cx, {"x": T["x2"], "halo1": T["haloR"], "k": T["kR"],
                  "v": T["vR"], "sr": T["srR"], "wkt": T["r_wkt"],
                  "wvt": T["r_wvt"], "wrt": T["r_wrt"], "mixk": T["r_mixk"],
                  "mixv": T["r_mixv"], "mixr": T["r_mixr"]})
    cx.prefix = "wk_"
    emit_wkvf(cx, {"k": T["kR"], "v": T["vR"], "wkv": T["wkvo"],
                   "rw": T["r_w"], "ru": T["r_u"], "selA": selA,
                   "selB": selB})
    cx.prefix = "r3_"
    emit_r3f(cx, {"wkv": T["wkvo"], "sr": T["srR"], "x2": T["x2"],
                  "wot": T["r_wot"], "y": T["x3"]})

    cx.prefix = "n1_"
    emit_n1f(cx, {"x": T["x3"], "w1t": T["n_w1t"], "w2t": T["n_w2t"],
                  "s1": T["n_s1"], "s2": T["n_s2"], "h": T["hb1"]})
    cx.prefix = "n2_"
    emit_nhf(cx, {"hi": T["hb1"], "wt": T["n_h1wt"], "ss": T["n_h1s"],
                  "ho": T["hb2"]})
    cx.prefix = "n3_"
    emit_nhf(cx, {"hi": T["hb2"], "wt": T["n_h2wt"], "ss": T["n_h2s"],
                  "ho": T["hb3"]})
    cx.prefix = "n4_"
    emit_n4f(cx, {"hi": T["hb3"], "w3t": T["n_w3t"], "s3": T["n_s3"],
                  "x3": T["x3"], "y": T["y"]})

    ch.build_with_clear(nc)
    cx.close()
    return nc


def emit_r3f(cx, T):
    ch = cx.ch
    mk = cx.mark()
    wk, sr, x2, wot, y = T["wkv"], T["sr"], T["x2"], T["wot"], T["y"]
    wt = cx.sbuf("wt", [128, NCHUNK, TT], F32)
    st = cx.sbuf("st", [128, NCHUNK, TT], F32)
    x2t = cx.sbuf("x2t", [128, NCHUNK, TT], F32)
    rb = cx.sbuf("rb", [128, NCHUNK, TT], BF16)
    wos = cx.sbuf("wos", [128, NCHUNK, D_MODEL], BF16)
    yt = cx.sbuf("yt", [128, NCHUNK, TT], F32)
    yb = cx.sbuf("yb", [128, NCHUNK, TT], BF16)
    sq = cx.sbuf("sq", [128, NCHUNK, TT], BF16)
    rows = cx.sbuf("rows", [1, 4 * TT], F32)
    outt = cx.sbuf("outt", [128, NCHUNK, TT], F32)
    psm = [cx.psum(f"psm{i}", [128, TT]) for i in range(2)]
    psA = cx.psum("psA", [128, TT])
    psB = cx.psum("psB", [128, TT])
    psA2 = cx.psum("psA2", [128, TT])
    psB2 = cx.psum("psB2", [128, TT])

    ch.dma_n("gpsimd", 1, lambda p: [p.dma_start(out=wos[:], in_=fmview(wot))])
    for t in range(NTILE):
        t0 = t * TT
        ch.dma_n("sync", 3, lambda p, t0=t0: [
            p.dma_start(out=wt[:], in_=fmview(wk)[:, :, t0:t0 + TT]),
            p.dma_start(out=st[:], in_=fmview(sr)[:, :, t0:t0 + TT]),
            p.dma_start(out=x2t[:], in_=fmview(x2)[:, :, t0:t0 + TT]),
        ])

        def e_rb(p):
            for c in range(NCHUNK):
                p.tensor_tensor(rb[:, c, :], wt[:, c, :], st[:, c, :],
                                ALU.mult)

        ch.comp("vector", e_rb, relax=True)
        s_in = ch.snap()
        s_mm, s_ev = {}, {}
        for o in range(NCHUNK):
            j = o % 2

            def e_mm(p, o=o, j=j):
                for c in range(NCHUNK):
                    p.matmul(psm[j][:, :], wos[:, c, o * 128:(o + 1) * 128],
                             rb[:, c, :], start=(c == 0),
                             stop=(c == NCHUNK - 1))

            ch.comp("tensor", e_mm, relax=True,
                    dep=Chain.dep_max(s_in, s_ev.get(o - 2, s_in)))
            s_mm[o] = ch.snap()

            def e_ev(p, o=o, j=j):
                p.tensor_copy(yt[:, o, :], psm[j][:])
                p.tensor_copy(yb[:, o, :], psm[j][:])

            ch.comp("vector", e_ev, relax=True, dep=s_mm[o])
            s_ev[o] = ch.snap()
        emit_ln(cx, yt, TT, sq, yb, psA, psB, psA2, psB2, rows, need_xb=False)
        emit_ln_apply(cx, yt, TT, psA2, psB2, outt, add_to=x2t)
        ch.dma_n("sync", 1, lambda p, t0=t0: [
            p.dma_start(out=fmview(y)[:, :, t0:t0 + TT], in_=outt[:])])
    cx.release(mk)


def _fused_maps(inp):
    w = inp
    x = np.asarray(inp["L0_attn_x"], np.float32)
    x_fm = _c(x.transpose(0, 2, 1))
    QK = 8.0
    shared = {
        "a1_wqt": _c(w["L0_attn_W_qkv"].T.astype(np.float32)),
        "a3_wot": _c(w["L0_attn_W_out"].T.astype(np.float32)),
        "f_tqc": None, "f_tqs": None, "f_tkc": None, "f_tks": None,
        "r_wkt": _c(w["L1_attn_Wk"].T.astype(np.float32)),
        "r_wvt": _c(w["L1_attn_Wv"].T.astype(np.float32)),
        "r_wrt": _c(w["L1_attn_Wr"].T.astype(np.float32)),
        "r_wot": _c(w["L1_attn_Wo"].T.astype(np.float32)),
        "r_mixk": _scale_tiles(w["L1_attn_mix_k"]),
        "r_mixv": _scale_tiles(w["L1_attn_mix_v"]),
        "r_mixr": _scale_tiles(w["L1_attn_mix_r"]),
        "r_w": _scale_tiles(w["L1_attn_w"]),
        "r_u": _scale_tiles(w["L1_attn_u"]),
        "n_w1t": _c(w["L1_ffn_w1_w"].T.astype(np.float32)),
        "n_w2t": _c(w["L1_ffn_w2_w"].T.astype(np.float32)),
        "n_s1": _scale_tiles(w["L1_ffn_w1_s"]),
        "n_s2": _scale_tiles(w["L1_ffn_w2_s"]),
        "n_h1wt": _c(w["L1_ffn_h1_w"].T.astype(np.float32)
                     .reshape(4096, 32, 128).transpose(1, 0, 2)),
        "n_h1s": _scale_tiles(w["L1_ffn_h1_s"]),
        "n_h2wt": _c(w["L1_ffn_h2_w"].T.astype(np.float32)
                     .reshape(4096, 32, 128).transpose(1, 0, 2)),
        "n_h2s": _scale_tiles(w["L1_ffn_h2_s"]),
        "n_w3t": _c(w["L1_ffn_w3_w"].T.astype(np.float32)),
        "n_s3": _scale_tiles(w["L1_ffn_w3_s"]),
    }
    for li in range(2):
        wqkv = w["L0_ffn_wqkv"][li].astype(np.float64)
        w1 = w["L0_ffn_w1"][li].astype(np.float64)
        b1 = w["L0_ffn_b1"][li].astype(np.float64)
        w2 = w["L0_ffn_w2"][li].astype(np.float64)
        w1p = np.zeros((1024, 5632))
        w1p[:, 0:2730] = w1[:, 0:2730]
        w1p[:, 2816:5546] = w1[:, 2730:]
        b1p = np.zeros(5632)
        b1p[0:2730] = b1[0:2730]
        b1p[2816:5546] = b1[2730:]
        w2p = np.zeros((2816, 1024))
        w2p[0:2730] = w2
        shared[f"f{li}_wqkv"] = _l0f_tile_w(wqkv, 12)
        shared[f"f{li}_qsc"] = _c(np.tile(w["L0_ffn_q_scale"][li] * QK, 2)
                                  .reshape(128, 1).astype(np.float32))
        shared[f"f{li}_ksc"] = _c(np.tile(w["L0_ffn_k_scale"][li], 2)
                                  .reshape(128, 1).astype(np.float32))
        shared[f"f{li}_wo"] = _c(w["L0_ffn_wo"][li].astype(np.float32))
        shared[f"f{li}_lg"] = _scale_tiles(w["L0_ffn_ln_g"][li])
        shared[f"f{li}_lb"] = _scale_tiles(w["L0_ffn_ln_b"][li])
        shared[f"f{li}_w1pt"] = _l0f_tile_w(w1p, 44)
        shared[f"f{li}_b1p"] = _c(b1p.reshape(44, 128).T.astype(np.float32))
        shared[f"f{li}_w2pt"] = _l0f_tile_w(w2p, 8)
        shared[f"f{li}_b2"] = _scale_tiles(w["L0_ffn_b2"][li])
    consts = [_l0f_consts(0), _l0f_consts(1)]
    maps = []
    for core in range(8):
        b, hh = core // 2, core % 2
        tqc, tqs, tkc, tks, am, bm, prot = consts[hh]
        m = dict(shared)
        m["f_tqc"], m["f_tqs"], m["f_tkc"], m["f_tks"] = tqc, tqs, tkc, tks
        m["f_amask"] = am
        m["f_bmask"] = bm
        m["f_prot"] = prot
        m["x"] = _c(x_fm[b, :, hh * HSEQ:(hh + 1) * HSEQ])
        m["selA"] = np.full((128, 1), 1.0 if hh == 1 else 0.0, np.float32)
        m["selB"] = np.zeros((128, 1), np.float32)
        m["ffnkeep"] = np.full((128, 1), 0.0 if hh == 0 else 1.0, np.float32)
        maps.append(m)
    return maps


def kernel_fused(inp):
    nc = build_fused()
    maps = _fused_maps(inp)
    res = run_bass_kernel_spmd(nc, maps, core_ids=list(range(8)))
    x = np.asarray(inp["L0_attn_x"], np.float32)
    out = np.empty((BATCH, D_MODEL, SEQ), np.float32)
    for core in range(8):
        b, hh = core // 2, core % 2
        out[b, :, hh * HSEQ:(hh + 1) * HSEQ] = res.results[core]["y"]
    return _c(out.transpose(0, 2, 1))


# version 11
# speedup vs baseline: 1.1716x; 1.0500x over previous
import numpy as np
import ml_dtypes
import concourse.bass as bass
import concourse.mybir as mybir
from concourse.bass_utils import run_bass_kernel_spmd

F32 = mybir.dt.float32
BF16 = mybir.dt.bfloat16
AF = mybir.ActivationFunctionType
ALU = mybir.AluOpType
AX = mybir.AxisListType

D_MODEL = 1024
NCHUNK = 8
BATCH = 4
SEQ = 8192
HSEQ = SEQ // 2
TT = 512
NTILE = HSEQ // TT

bfdt = ml_dtypes.bfloat16


class _DummyInstr:
    def then_inc(self, *a, **k):
        return self


class _CountProxy:

    def __init__(self):
        self.n = 0

    def __getattr__(self, name):
        if name.startswith("_"):
            raise AttributeError(name)
        d = _DummyInstr()

        def f(*a, **k):
            self.n += 1
            return d

        return f


ENG_IDS = {"sync": 0, "scalar": 1, "vector": 2, "tensor": 3, "gpsimd": 4}
SNAP_DMA = 5
SNAP_CC = 9
NSNAP = 10


class Chain:

    ENGINES = ("sync", "scalar", "vector", "tensor", "gpsimd")

    def __init__(self):
        self.q = {e: [] for e in self.ENGINES}
        self.cnt = [0] * NSNAP
        self.dma_phase = {"sync": 0, "gpsimd": 0}

    def snap(self):
        return tuple(self.cnt)

    @staticmethod
    def dep_max(*deps):
        return tuple(max(d[i] for d in deps) for i in range(NSNAP))

    def _emit_waits(self, p, sems, dep, eng_i, include_same=True):
        for e in range(5):
            v = dep[e]
            if v <= 0:
                continue
            if e == eng_i and not include_same:
                continue
            p.wait_ge(sems[e], v)
        for k in range(4):
            if dep[SNAP_DMA + k] > 0:
                p.wait_ge(sems[5 + k], 16 * dep[SNAP_DMA + k])
        if dep[SNAP_CC] > 0:
            p.wait_ge(sems[9], dep[SNAP_CC])

    def comp(self, eng, emit, relax=False, dep=None):
        cp = _CountProxy()
        emit(cp)
        n = cp.n
        assert n > 0
        ei = ENG_IDS[eng]
        st0 = tuple(dep) if dep is not None else self.snap()
        base_e = self.cnt[ei]
        self.cnt[ei] += 1 if relax else n
        ch = self

        def run(p, sems):
            insts = []

            class _Cap:
                def __getattr__(self, name):
                    if name.startswith("_"):
                        raise AttributeError(name)
                    meth = getattr(p, name)

                    def f(*a, **k):
                        if not relax:
                            if insts:
                                p.wait_ge(sems[ei], base_e + len(insts))
                            elif base_e > 0:
                                p.wait_ge(sems[ei], base_e)
                        elif not insts and base_e > 0:
                            p.wait_ge(sems[ei], base_e)
                        if not insts:
                            ch._emit_waits(p, sems, st0, ei,
                                           include_same=False)
                        inst = meth(*a, **k)
                        if not relax:
                            inst.then_inc(sems[ei], 1)
                        insts.append(inst)
                        return inst

                    return f

            emit(_Cap())
            assert len(insts) == n
            if relax:
                insts[-1].then_inc(sems[ei], 1)

        self.q[eng].append(run)

    def dma_n(self, eng, n, emit_list, dep=None):
        ei = ENG_IDS[eng]
        st0 = tuple(dep) if dep is not None else self.snap()
        qn = "gpsimd" if eng == "gpsimd" else "sync"
        par = self.dma_phase[qn] % 2
        self.dma_phase[qn] += 1
        didx = 5 + (2 if qn == "gpsimd" else 0) + par
        prior_same = self.cnt[SNAP_DMA + (2 if qn == "gpsimd" else 0) + par]
        self.cnt[SNAP_DMA + (2 if qn == "gpsimd" else 0) + par] += n
        ch = self

        def run(p, sems):
            if prior_same > 0:
                p.wait_ge(sems[didx], 16 * prior_same)
            ch._emit_waits(p, sems, st0, ei, include_same=True)
            insts = emit_list(p)
            assert len(insts) == n, f"declared {n} dmas, emitted {len(insts)}"
            for i in insts:
                i.then_inc(sems[didx], 16)

        self.q[eng].append(run)

    def collective(self, emit):
        st0 = self.snap()
        self.cnt[SNAP_CC] += 1
        ch = self

        def run(p, sems):
            ch._emit_waits(p, sems, st0, ENG_IDS["gpsimd"],
                           include_same=True)
            inst = emit(p)
            inst.then_inc(sems[9], 1)

        self.q["gpsimd"].append(run)

    def build_with_clear(self, nc):
        totals = self.snap()
        fin_engs = [e for e in ("sync", "scalar", "vector", "tensor")
                    if self.q[e]]
        nfin = len(fin_engs)
        ch = self

        def fin(eng):
            ei = ENG_IDS[eng]

            def run(p, sems):
                ch._emit_waits(p, sems, totals, ei, include_same=True)
                p.nop().then_inc(sems[ei], 1)

            self.q[eng].append(run)

        for e in fin_engs:
            fin(e)

        def gfin(p, sems):
            for e in ("sync", "scalar", "vector", "tensor"):
                ei = ENG_IDS[e]
                v = totals[ei] + (1 if e in fin_engs else 0)
                if v > 0:
                    p.wait_ge(sems[ei], v)
            for k in range(4):
                if totals[SNAP_DMA + k] > 0:
                    p.wait_ge(sems[5 + k], 16 * totals[SNAP_DMA + k])
            if totals[SNAP_CC] > 0:
                p.wait_ge(sems[9], totals[SNAP_CC])
            if totals[ENG_IDS["gpsimd"]] > 0:
                p.wait_ge(sems[ENG_IDS["gpsimd"]],
                          totals[ENG_IDS["gpsimd"]])
            nums = sorted(s.num for s in sems)
            p.dma_reset(range(nums[0], nums[-1] + 1))
            for s in sems:
                p.sem_clear(s)

        self.q["gpsimd"].append(gfin)
        self.build(nc)

    def build(self, nc):
        with (
            nc.semaphore("c_sy") as s0,
            nc.semaphore("c_sc") as s1,
            nc.semaphore("c_ve") as s2,
            nc.semaphore("c_te") as s3,
            nc.semaphore("c_gp") as s4,
            nc.semaphore("dsyA") as s5,
            nc.semaphore("dsyB") as s6,
            nc.semaphore("dgpA") as s7,
            nc.semaphore("dgpB") as s8,
            nc.semaphore("dcc") as s9,
            nc.Block() as block,
        ):
            ch = self
            sems = (s0, s1, s2, s3, s4, s5, s6, s7, s8, s9)

            @block.sync
            def _(p):
                for fn in ch.q["sync"]:
                    fn(p, sems)

            @block.scalar
            def _(p):
                for fn in ch.q["scalar"]:
                    fn(p, sems)

            @block.vector
            def _(p):
                for fn in ch.q["vector"]:
                    fn(p, sems)

            @block.tensor
            def _(p):
                for fn in ch.q["tensor"]:
                    fn(p, sems)

            @block.gpsimd
            def _(p):
                for fn in ch.q["gpsimd"]:
                    fn(p, sems)


class Ctx:

    def __init__(self, nc, ch):
        self.nc = nc
        self.ch = ch
        self.exits = []
        self.prefix = ""

    def sbuf(self, name, shape, dtype):
        t = self.nc.sbuf_tensor(self.prefix + name, shape, dtype)
        h = t.__enter__()
        self.exits.append(t)
        return h

    def psum(self, name, shape, dtype=F32):
        t = self.nc.psum_tensor(self.prefix + name, shape, dtype)
        h = t.__enter__()
        self.exits.append(t)
        return h

    def mark(self):
        return len(self.exits)

    def release(self, mark):
        while len(self.exits) > mark:
            self.exits.pop().__exit__(None, None, None)

    def close(self):
        self.release(0)


def make_common_consts(cx):
    cx.ones_col_bf = cx.sbuf("ones_col_bf", [128, 1], BF16)
    cx.ones_row_f = cx.sbuf("ones_row_f", [1, 128], F32)

    def emit(p):
        p.memset(cx.ones_col_bf[:], 1.0)
        return p.memset(cx.ones_row_f[:], 1.0)

    cx.ch.comp("vector", emit)


def fmview(ap, nch=NCHUNK):
    return ap.rearrange("(c p) t -> p c t", p=128)


def emit_ln(cx, xt, T, sq, xb, psA, psB, psA2, psB2, rows, need_xb=True):
    ch = cx.ch
    ch.comp("scalar",
            lambda p: p.activation(sq[:, :, 0:T], xt[:, :, 0:T], AF.Square))
    if need_xb:
        ch.comp("vector",
                lambda p: p.tensor_copy(xb[:, :, 0:T], xt[:, :, 0:T]))

    def e_mm(p):
        for c in range(NCHUNK):
            p.matmul(psA[0:1, 0:T], cx.ones_col_bf[:], xb[:, c, 0:T],
                     start=(c == 0), stop=(c == NCHUNK - 1))
        last = None
        for c in range(NCHUNK):
            last = p.matmul(psB[0:1, 0:T], cx.ones_col_bf[:], sq[:, c, 0:T],
                            start=(c == 0), stop=(c == NCHUNK - 1))
        return last

    ch.comp("tensor", e_mm, relax=True)

    mean = rows[0:1, 0:T]
    m2 = rows[0:1, T:2 * T]
    varp = rows[0:1, 2 * T:3 * T]
    amb = rows[0:1, 3 * T:4 * T]

    def e_rows(p):
        p.tensor_scalar_mul(mean, psA[0:1, 0:T], 1.0 / D_MODEL)
        p.tensor_tensor(m2, mean, mean, ALU.mult)
        p.tensor_scalar_mul(varp, psB[0:1, 0:T], 1.0 / D_MODEL)
        p.tensor_tensor(varp, varp, m2, ALU.subtract)
        p.tensor_scalar_add(varp, varp, 1e-5)
        return p.reciprocal(varp, varp)

    ch.comp("vector", e_rows)
    ch.comp("scalar", lambda p: p.activation(varp, varp, AF.Sqrt))
    ch.comp("vector",
            lambda p: p.scalar_tensor_tensor(amb, mean, -1.0, varp, ALU.mult,
                                             ALU.mult))

    def e_bc(p):
        p.matmul(psA2[:, 0:T], cx.ones_row_f[:], varp, start=True, stop=True)
        return p.matmul(psB2[:, 0:T], cx.ones_row_f[:], amb, start=True,
                        stop=True)

    ch.comp("tensor", e_bc, relax=True)


def emit_ln_apply(cx, xt, T, psA2, psB2, out, tmp=None, add_to=None):
    ch = cx.ch
    if add_to is None:
        ch.comp("vector", lambda p: [p.tensor_tensor(
            out[:, c, 0:T], xt[:, c, 0:T], psA2[:, 0:T], ALU.mult)
            for c in range(NCHUNK)], relax=True)
        ch.comp("vector", lambda p: [p.scalar_tensor_tensor(
            out[:, c, 0:T], out[:, c, 0:T], 1.0, psB2[:, 0:T], ALU.mult,
            ALU.add) for c in range(NCHUNK)], relax=True)
    else:
        ch.comp("vector", lambda p: [p.tensor_tensor(
            out[:, c, 0:T], xt[:, c, 0:T], psA2[:, 0:T], ALU.mult)
            for c in range(NCHUNK)], relax=True)
        ch.comp("vector", lambda p: [p.scalar_tensor_tensor(
            out[:, c, 0:T], out[:, c, 0:T], 1.0, psB2[:, 0:T], ALU.mult,
            ALU.add) for c in range(NCHUNK)], relax=True)
        ch.comp("vector", lambda p: [p.tensor_tensor(
            out[:, c, 0:T], out[:, c, 0:T], add_to[:, c, 0:T], ALU.add)
            for c in range(NCHUNK)], relax=True)


def _c(a):
    return np.ascontiguousarray(a)


def _scale_tiles(s):
    n = s.shape[0]
    return _c(s.reshape(n // 128, 128).T.astype(np.float32))


def kernel(**inp):
    raise NotImplementedError("assembled later")


def _l0f_tile_w(wmat, nout):
    Kd = wmat.shape[0]
    return _c(wmat.reshape(Kd // 128, 128, nout, 128).transpose(2, 1, 0, 3)
              .astype(np.float32))


def _l0f_consts(hh):
    W, DH = 512, 64
    t0g = hh * HSEQ
    pos = np.maximum(np.arange(t0g - 1024, t0g + HSEQ), 0).astype(np.float64)
    half_idx = np.arange(0, DH, 2, dtype=np.float64)
    inv_freq = 1.0 / 10000.0 ** (half_idx / DH)
    freqs = pos[:, None] * inv_freq[None, :]
    freqs = np.concatenate([freqs, freqs], axis=-1)
    base = (half_idx + 0.4 * DH) / (1.4 * DH)
    power = (pos - SEQ // 2) / 256.0
    scale = base[None, :] ** power[:, None]
    scale = np.concatenate([scale, scale], axis=-1)
    cos, sin = np.cos(freqs), np.sin(freqs)
    tqc = (cos * scale).T
    tqs = (sin * scale).T
    tkc = (cos / scale).T
    tks = (sin / scale).T

    def tile2(tt):
        return _c(np.concatenate([tt, tt], axis=0).astype(np.float32))

    iL = np.arange(W)[None, :]
    pj = np.arange(128)[:, None]
    am = np.zeros((2, 4, 128, 1024), np.float32)
    bm = np.zeros((2, 8, 128, W), np.float32)
    for it in range(4):
        iloc = it * 128 + np.arange(128)[:, None]
        jprev = np.arange(W)[None, :]
        jown = np.arange(W)[None, :]
        am[1, it, :, 0:W] = np.where(iloc <= jprev, 0.0, -1e30)
        am[1, it, :, W:] = np.where(iloc >= jown, 0.0, -1e30)
    for jj in range(4):
        jloc = jj * 128 + pj
        bm[1, jj] = (iL <= jloc).astype(np.float32)
        bm[1, 4 + jj] = (iL >= jloc).astype(np.float32)
    am[0] = am[1]
    bm[0] = bm[1]
    if hh == 0:
        am[0, :, :, 0:W] = -1e30
        bm[0, 0:4] = 0.0
    P = np.zeros((64, 64), np.float64)
    for d in range(32):
        P[d, d + 32] = -1.0
        P[d + 32, d] = 1.0
    PT = P.T
    prot = np.zeros((128, 128), np.float32)
    prot[0:64, 0:64] = PT
    prot[64:128, 64:128] = PT
    return (tile2(tqc), tile2(tqs), tile2(tkc), tile2(tks), _c(am), _c(bm),
            prot)


def kernel(**inp):
    return kernel_fused(inp)


L0F_H = 1025
L0F_W = 512
L0F_NW = 10


REPL_PAIRS = [[0, 1], [2, 3], [4, 5], [6, 7]]


def emit_pair_exchange(cx, pref, src_ap, agout_ap, rows, cols, selA, selB,
                       write_fn, tile_cols=None):
    ch = cx.ch
    nc = cx.nc
    mk = cx.mark()
    tc = cols if tile_cols is None else tile_cols
    blkA = cx.sbuf(pref + "blkA", [128, tc], F32)
    blkB = cx.sbuf(pref + "blkB", [128, tc], F32)
    cmb = cx.sbuf(pref + "cmb", [128, tc], F32)
    ch.collective(lambda p: p.collective_compute(
        "AllGather", ALU.bypass, replica_groups=REPL_PAIRS,
        ins=[src_ap], outs=[agout_ap]))
    slow = cols < 64
    rg = min(128, rows)
    for r0 in range(0, rows, rg):
        for c0 in range(0, cols, tc):
            cw = min(tc, cols - c0)
            ch.dma_n("sync", 2, lambda p, r0=r0, c0=c0, cw=cw: [
                p.dma_start(out=blkA[0:rg, 0:cw],
                            in_=agout_ap[r0:r0 + rg, c0:c0 + cw],
                            allow_slow_non_contiguous=slow),
                p.dma_start(out=blkB[0:rg, 0:cw],
                            in_=agout_ap[rows + r0:rows + r0 + rg,
                                         c0:c0 + cw],
                            allow_slow_non_contiguous=slow),
            ])

            def e_cmb(p, cw=cw):
                p.tensor_scalar_mul(cmb[0:rg, 0:cw], blkA[0:rg, 0:cw],
                                    selA[0:rg, 0:1])
                p.scalar_tensor_tensor(cmb[0:rg, 0:cw], blkB[0:rg, 0:cw],
                                       selB[0:rg, 0:1], cmb[0:rg, 0:cw],
                                       ALU.mult, ALU.add)

            ch.comp("vector", e_cmb)
            write_fn(ch, cmb, r0, c0, cw, slow)
    cx.release(mk)


def emit_halo_fill(cx, pref, stream_ap, halo_ap, hcols, selA, selB, tmp_ag):
    nc = cx.nc
    src = nc.dram_tensor(pref + "agsrc", [D_MODEL, hcols], F32,
                         kind="Internal").ap()
    ch = cx.ch
    mk = cx.mark()
    bt = cx.sbuf(pref + "bt", [128, NCHUNK, hcols], F32)
    slow = hcols < 64
    ch.dma_n("sync", 1, lambda p: [
        p.dma_start(out=bt[:], in_=fmview(stream_ap)[:, :,
                    HSEQ - hcols:HSEQ], allow_slow_non_contiguous=slow)])
    ch.dma_n("sync", 1, lambda p: [
        p.dma_start(out=fmview(src)[:, :, 0:hcols], in_=bt[:],
                    allow_slow_non_contiguous=slow)])
    cx.release(mk)

    def wr(ch2, cmb, r0, c0, cw, slow2):
        ch2.dma_n("sync", 1, lambda p: [
            p.dma_start(out=halo_ap[r0:r0 + 128, c0:c0 + cw],
                        in_=cmb[:, 0:cw], allow_slow_non_contiguous=slow2)])

    emit_pair_exchange(cx, pref, src, tmp_ag, D_MODEL, hcols, selA, selB, wr,
                       tile_cols=min(hcols, 1025))


def emit_a1f(cx, T):
    ch = cx.ch
    mk = cx.mark()
    x, wqt = T["x"], T["wqt"]
    qo, ko, vo, kvtm = T["q"], T["k"], T["v"], T["kvtm"]
    xt = cx.sbuf("xt", [128, NCHUNK, TT], F32)
    sq = cx.sbuf("sq", [128, NCHUNK, TT], BF16)
    xb = cx.sbuf("xb", [128, NCHUNK, TT], BF16)
    ub = cx.sbuf("ub", [128, NCHUNK, TT], BF16)
    wqs = cx.sbuf("wqs", [128, NCHUNK, 3072], BF16)
    rows = cx.sbuf("rows", [1, 4 * TT], F32)
    t0b = [cx.sbuf(f"t0b{i}", [128, TT], F32) for i in range(2)]
    t1b = [cx.sbuf(f"t1b{i}", [128, TT], F32) for i in range(2)]
    evt = [cx.sbuf(f"evt{i}", [128, TT], F32) for i in range(2)]
    evb = [cx.sbuf(f"evb{i}", [128, TT], BF16) for i in range(2)]
    psA = cx.psum("psA", [128, TT])
    psB = cx.psum("psB", [128, TT])
    psA2 = cx.psum("psA2", [128, TT])
    psB2 = cx.psum("psB2", [128, TT])
    psm = [cx.psum(f"psm{i}", [128, TT]) for i in range(2)]

    ch.dma_n("gpsimd", 1, lambda p: [p.dma_start(out=wqs[:], in_=fmview(wqt))])
    outs = (qo, ko, vo)
    for t in range(NTILE):
        t0 = t * TT
        ch.dma_n("sync", 1, lambda p, t0=t0: [
            p.dma_start(out=xt[:], in_=fmview(x)[:, :, t0:t0 + TT])])
        emit_ln(cx, xt, TT, sq, xb, psA, psB, psA2, psB2, rows)
        emit_ln_apply(cx, xt, TT, psA2, psB2, ub)
        s_in = ch.snap()
        s_mm, s_ev, s_st = {}, {}, {}
        for o in range(24):
            j = o % 2
            kind = o // 8

            def e_mm(p, o=o, j=j):
                for c in range(NCHUNK):
                    p.matmul(psm[j][:, :], wqs[:, c, o * 128:(o + 1) * 128],
                             ub[:, c, :], start=(c == 0),
                             stop=(c == NCHUNK - 1))

            ch.comp("tensor", e_mm, relax=True,
                    dep=Chain.dep_max(s_in, s_ev.get(o - 2, s_in)))
            s_mm[o] = ch.snap()
            if kind < 2:
                def e_elu(p, j=j):
                    p.tensor_scalar_min(t0b[j][:], psm[j][:], 0.0)
                    p.tensor_scalar_max(t1b[j][:], psm[j][:], 0.0)

                ch.comp("vector", e_elu, relax=True,
                        dep=Chain.dep_max(s_mm[o], s_st.get(o - 2, s_in)))
                sv = ch.snap()
                ch.comp("scalar",
                        lambda p, j=j: p.activation(t0b[j][:], t0b[j][:],
                                                    AF.Exp), dep=sv)
                sa_ = ch.snap()

                def e_sum(p, kind=kind, j=j):
                    p.tensor_tensor(evt[j][:], t0b[j][:], t1b[j][:], ALU.add)
                    if kind == 0:
                        p.tensor_scalar_mul(evt[j][:], evt[j][:], 0.125)

                ch.comp("vector", e_sum, dep=sa_)
            else:
                ch.comp("vector",
                        lambda p, j=j: p.tensor_copy(evt[j][:], psm[j][:]),
                        dep=Chain.dep_max(s_mm[o], s_st.get(o - 2, s_in)))
            s_ev[o] = ch.snap()
            ch.dma_n("sync", 1, lambda p, o=o, t0=t0, kind=kind, j=j: [
                p.dma_start(out=outs[kind][(o % 8) * 128:(o % 8 + 1) * 128,
                            t0:t0 + TT], in_=evt[j][:])], dep=s_ev[o])
            s_st[o] = ch.snap()
        s_mmk, s_evk, s_stk = {}, {}, {}
        for s in range(4):
            stile = t * 4 + s
            for cb in range(4):
                i = s * 4 + cb
                j = i % 2

                def e_kv(p, s=s, cb=cb, j=j):
                    for c in range(NCHUNK):
                        p.matmul(psm[j][:, :],
                                 ub[:, c, s * 128:(s + 1) * 128],
                                 wqs[:, c, 1024 + cb * 512:1024 +
                                     (cb + 1) * 512],
                                 start=(c == 0), stop=(c == NCHUNK - 1))

                ch.comp("tensor", e_kv, relax=True,
                        dep=Chain.dep_max(s_in, s_evk.get(i - 2, s_in),
                                          s_ev.get(23, s_in)))
                s_mmk[i] = ch.snap()
                if cb < 2:
                    def e_elu2(p, j=j):
                        p.tensor_scalar_min(t0b[j][:], psm[j][:], 0.0)
                        p.tensor_scalar_max(t1b[j][:], psm[j][:], 0.0)

                    ch.comp("vector", e_elu2, relax=True,
                            dep=Chain.dep_max(s_mmk[i],
                                              s_stk.get(i - 2, s_in)))
                    sv = ch.snap()
                    ch.comp("scalar",
                            lambda p, j=j: p.activation(
                                t0b[j][:], t0b[j][:], AF.Exp), dep=sv)
                    sa_ = ch.snap()
                    ch.comp("vector", lambda p, j=j: p.tensor_tensor(
                        evb[j][:], t0b[j][:], t1b[j][:], ALU.add), dep=sa_)
                else:
                    ch.comp("vector", lambda p, j=j: p.tensor_copy(
                        evb[j][:], psm[j][:]),
                            dep=Chain.dep_max(s_mmk[i],
                                              s_stk.get(i - 2, s_in)))
                s_evk[i] = ch.snap()
                ch.dma_n("sync", 1, lambda p, stile=stile, cb=cb, j=j: [
                    p.dma_start(out=kvtm[stile, :,
                                cb * 512:(cb + 1) * 512],
                                in_=evb[j][:])], dep=s_evk[i])
                s_stk[i] = ch.snap()
    cx.release(mk)


def emit_a2f(cx, T):
    CH = 256
    NCHL = HSEQ // CH
    ch = cx.ch
    nc = cx.nc
    mk = cx.mark()
    qd, kd, kvtm, oo = T["q"], T["k"], T["kvtm"], T["o"]
    selA, selB = T["selA"], T["selB"]
    agS_in = nc.dram_tensor("a2_agin", [64, 16 * 65], F32,
                            kind="Internal").ap()
    agS_out = nc.dram_tensor("a2_agout", [128, 16 * 65], F32,
                             kind="Internal").ap()
    qhd = qd.rearrange("(h d) t -> d h t", d=64)
    khd = kd.rearrange("(h d) t -> d h t", d=64)
    ohd = oo.rearrange("(h d) t -> d h t", d=64)
    qc = cx.sbuf("qc", [64, 16, CH], BF16)
    kc = cx.sbuf("kc", [64, 16, CH], BF16)
    kvsb = cx.sbuf("kvsb", [128, 2, 2048], BF16)
    atT = [cx.sbuf(f"atT{i}", [128, CH], BF16) for i in range(2)]
    maskb = [cx.sbuf(f"maskb{i}", [128, CH], BF16) for i in range(2)]
    maskf = cx.sbuf("maskf", [128, CH], F32)
    Schk = cx.sbuf("Schk", [64, 16, NCHL, 65], F32)
    Stot = cx.sbuf("Stot", [64, 16, 65], F32)
    Srun = cx.sbuf("Srun", [64, 16, 65], F32)
    Srb = cx.sbuf("Srb", [64, 16, 65], BF16)
    dd = cx.sbuf("dd", [1, CH], F32)
    bc = cx.sbuf("bc", [64, CH], F32)
    carc = cx.sbuf("carc", [64, CH], F32)
    carD = cx.sbuf("carD", [1, CH], F32)
    numt = cx.sbuf("numt", [64, CH], F32)
    ofm = cx.sbuf("ofm", [64, 16, CH], F32)
    ones_row64 = cx.sbuf("ones_row64", [1, 64], F32)
    ps_t = [cx.psum(f"ps_t{i}", [128, CH]) for i in range(2)]
    ps_num = cx.psum("ps_num", [64, CH])
    ps_den = cx.psum("ps_den", [1, CH])
    ps_bc = cx.psum("ps_bc", [64, CH])
    ps_S = cx.psum("ps_S", [64, 128])
    ps_car = cx.psum("ps_car", [64, CH])
    ps_carD = cx.psum("ps_carD", [1, CH])

    def e_consts(p):
        p.memset(ones_row64[:], 1.0)
        p.memset(maskf[:], 0.0)
        p.memset(Stot[:], 0.0)

    ch.comp("vector", e_consts)
    ch.comp("gpsimd", lambda p: p.affine_select(
        maskf[:], maskf[:], [[-1, CH]], ALU.is_ge, 1.0, base=-1,
        channel_multiplier=1))

    def e_mb(p):
        p.tensor_copy(maskb[0][:], maskf[:])
        p.memset(maskf[:], 0.0)

    ch.comp("vector", e_mb)
    ch.comp("gpsimd", lambda p: p.affine_select(
        maskf[:], maskf[:], [[-1, CH]], ALU.is_ge, 1.0, base=127,
        channel_multiplier=1))
    ch.comp("vector", lambda p: p.tensor_copy(maskb[1][:], maskf[:]))

    for i in range(NCHL):
        ch.dma_n("sync", 2, lambda p, i=i: [
            p.dma_start(out=kvsb[:, 0, :], in_=kvtm[2 * i]),
            p.dma_start(out=kvsb[:, 1, :], in_=kvtm[2 * i + 1]),
        ])
        for h in range(16):
            def e_S(p, h=h):
                for m in range(2):
                    p.matmul(ps_S[0:64, 0:64], kvsb[:, m, h * 64:h * 64 + 64],
                             kvsb[:, m, 1024 + h * 64:1024 + h * 64 + 64],
                             start=(m == 0), stop=(m == 1))
                for m in range(2):
                    p.matmul(ps_S[0:64, 64:65],
                             kvsb[:, m, h * 64:h * 64 + 64],
                             cx.ones_col_bf[:], start=(m == 0), stop=(m == 1))

            ch.comp("tensor", e_S, relax=True)

            def e_acc(p, i=i, h=h):
                p.tensor_copy(Schk[:, h, i, :], ps_S[0:64, 0:65])
                p.tensor_tensor(Stot[:, h, :], Stot[:, h, :],
                                ps_S[0:64, 0:65], ALU.add)

            ch.comp("vector", e_acc, relax=True)

    mkx = cx.mark()
    sbt = cx.sbuf("sbt", [64, 16 * 65], F32)
    ch.comp("vector", lambda p: p.tensor_copy(
        sbt[:], Stot[:].rearrange("p c e -> p (c e)")))
    ch.dma_n("sync", 1, lambda p: [p.dma_start(out=agS_in[:, :], in_=sbt[:])])

    def wrS(ch2, cmb, r0, c0, cw, slow2):
        def e(p):
            p.tensor_copy(Srun[:].rearrange("p c e -> p (c e)")[:,
                          c0:c0 + cw], cmb[0:64, 0:cw])

        ch2.comp("vector", e)

    emit_pair_exchange(cx, "a2x", agS_in, agS_out, 64, 16 * 65, selA, selB,
                       wrS)
    cx.release(mkx)
    ch.comp("vector", lambda p: p.tensor_copy(Srb[:], Srun[:]))

    for i in range(NCHL):
        i0 = i * CH
        ch.dma_n("sync", 2, lambda p, i=i: [
            p.dma_start(out=kvsb[:, 0, :], in_=kvtm[2 * i]),
            p.dma_start(out=kvsb[:, 1, :], in_=kvtm[2 * i + 1]),
        ])
        ch.dma_n("gpsimd", 2, lambda p, i0=i0: [
            p.dma_start(out=qc[:], in_=qhd[:, :, i0:i0 + CH]),
            p.dma_start(out=kc[:], in_=khd[:, :, i0:i0 + CH]),
        ])
        for h in range(16):
            def e_attnT(p, h=h):
                p.matmul(ps_t[0][:], kc[:, h, 0:128], qc[:, h, :],
                         start=True, stop=True)
                p.matmul(ps_t[1][:], kc[:, h, 128:256], qc[:, h, :],
                         start=True, stop=True)

            ch.comp("tensor", e_attnT, relax=True)

            def e_mask(p):
                p.tensor_tensor(atT[0][:], ps_t[0][:], maskb[0][:], ALU.mult)
                p.tensor_tensor(atT[1][:], ps_t[1][:], maskb[1][:], ALU.mult)

            ch.comp("vector", e_mask, relax=True)

            def e_num(p, h=h):
                for m in range(2):
                    p.matmul(ps_num[:, :],
                             kvsb[:, m, 1024 + h * 64:1024 + h * 64 + 64],
                             atT[m][:], start=(m == 0), stop=(m == 1))
                for m in range(2):
                    p.matmul(ps_den[:, :], cx.ones_col_bf[:], atT[m][:],
                             start=(m == 0), stop=(m == 1))
                p.matmul(ps_car[:, :], Srb[:, h, 0:64], qc[:, h, :],
                         start=True, stop=True)
                p.matmul(ps_carD[:, :], Srb[:, h, 64:65], qc[:, h, :],
                         start=True, stop=True)

            ch.comp("tensor", e_num, relax=True)

            def e_div(p):
                p.tensor_copy(carc[:, :], ps_car[:, :])
                p.tensor_copy(carD[:, :], ps_carD[:, :])
                p.tensor_tensor(dd[:], ps_den[:, :], carD[:, :], ALU.add)
                p.tensor_scalar_max(dd[:], dd[:], 1e-6)
                p.reciprocal(dd[:], dd[:])
                p.tensor_tensor(numt[:], ps_num[:, :], carc[:, :], ALU.add)

            ch.comp("vector", e_div)
            ch.comp("tensor", lambda p: p.matmul(
                ps_bc[:, :], ones_row64[:], dd[:], start=True, stop=True), relax=True)

            def e_out(p, h=h):
                p.tensor_copy(bc[:], ps_bc[:, :])
                p.tensor_tensor(ofm[:, h, :], numt[:], bc[:], ALU.mult)

            ch.comp("vector", e_out)

        def e_Supd(p, i=i):
            p.tensor_tensor(Srun[:, :, :], Srun[:, :, :], Schk[:, :, i, :],
                            ALU.add)
            p.tensor_copy(Srb[:], Srun[:])

        ch.comp("vector", e_Supd)
        ch.dma_n("sync", 1, lambda p, i0=i0: [
            p.dma_start(out=ohd[:, :, i0:i0 + CH], in_=ofm[:])])
    cx.release(mk)


def emit_projf(cx, T):
    ch = cx.ch
    mk = cx.mark()
    ii, xx, wt, y = T["inp"], T["x"], T["wt"], T["y"]
    it = cx.sbuf("it", [128, NCHUNK, TT], BF16)
    xt = cx.sbuf("xt", [128, NCHUNK, TT], F32)
    ws = cx.sbuf("ws", [128, NCHUNK, D_MODEL], BF16)
    yt = cx.sbuf("yt", [128, NCHUNK, TT], F32)
    yb = cx.sbuf("yb", [128, NCHUNK, TT], BF16)
    sq = cx.sbuf("sq", [128, NCHUNK, TT], BF16)
    rows = cx.sbuf("rows", [1, 4 * TT], F32)
    outt = cx.sbuf("outt", [128, NCHUNK, TT], F32)
    psm = [cx.psum(f"psm{i}", [128, TT]) for i in range(2)]
    psA = cx.psum("psA", [128, TT])
    psB = cx.psum("psB", [128, TT])
    psA2 = cx.psum("psA2", [128, TT])
    psB2 = cx.psum("psB2", [128, TT])

    ch.dma_n("gpsimd", 1, lambda p: [p.dma_start(out=ws[:], in_=fmview(wt))])
    for t in range(NTILE):
        t0 = t * TT
        ch.dma_n("gpsimd", 1, lambda p, t0=t0: [
            p.dma_start(out=it[:], in_=fmview(ii)[:, :, t0:t0 + TT])])
        ch.dma_n("sync", 1, lambda p, t0=t0: [
            p.dma_start(out=xt[:], in_=fmview(xx)[:, :, t0:t0 + TT])])
        s_in = ch.snap()
        s_mm, s_ev = {}, {}
        for o in range(NCHUNK):
            j = o % 2

            def e_mm(p, o=o, j=j):
                for c in range(NCHUNK):
                    p.matmul(psm[j][:, :], ws[:, c, o * 128:(o + 1) * 128],
                             it[:, c, :], start=(c == 0),
                             stop=(c == NCHUNK - 1))

            ch.comp("tensor", e_mm, relax=True,
                    dep=Chain.dep_max(s_in, s_ev.get(o - 2, s_in)))
            s_mm[o] = ch.snap()

            def e_ev(p, o=o, j=j):
                p.tensor_copy(yt[:, o, :], psm[j][:])
                p.tensor_copy(yb[:, o, :], psm[j][:])

            ch.comp("vector", e_ev, relax=True, dep=s_mm[o])
            s_ev[o] = ch.snap()
        emit_ln(cx, yt, TT, sq, yb, psA, psB, psA2, psB2, rows, need_xb=False)
        emit_ln_apply(cx, yt, TT, psA2, psB2, outt, add_to=xt)
        ch.dma_n("sync", 1, lambda p, t0=t0: [
            p.dma_start(out=fmview(y)[:, :, t0:t0 + TT], in_=outt[:])])
    cx.release(mk)


def emit_r1f(cx, T):
    ch = cx.ch
    mk = cx.mark()
    x, halo1 = T["x"], T["halo1"]
    ko, vo, sro = T["k"], T["v"], T["sr"]
    wkt, wvt, wrt = T["wkt"], T["wvt"], T["wrt"]
    mixk, mixv, mixr = T["mixk"], T["mixv"], T["mixr"]
    xt = cx.sbuf("xt", [128, NCHUNK, TT], F32)
    xh = cx.sbuf("xh", [128, NCHUNK, 1], F32)
    ut = cx.sbuf("ut", [128, NCHUNK, TT], F32)
    ucol = cx.sbuf("ucol", [128, NCHUNK, 1], F32)
    dcol = cx.sbuf("dcol", [128, NCHUNK, TT], F32)
    sq = cx.sbuf("sq", [128, NCHUNK, TT], BF16)
    xb = cx.sbuf("xb", [128, NCHUNK, TT], BF16)
    mixs = [cx.sbuf(f"mix{i}", [128, NCHUNK], F32) for i in range(3)]
    wss = [cx.sbuf(f"w{i}", [128, NCHUNK, D_MODEL], BF16) for i in range(3)]
    mxb = [cx.sbuf(f"mxb{i}", [128, NCHUNK, TT], BF16) for i in range(3)]
    evt = [cx.sbuf(f"evt{i}", [128, TT], F32) for i in range(2)]
    rows = cx.sbuf("rows", [1, 4 * TT], F32)
    psA = cx.psum("psA", [128, TT])
    psB = cx.psum("psB", [128, TT])
    psA2 = cx.psum("psA2", [128, TT])
    psB2 = cx.psum("psB2", [128, TT])
    psm = [cx.psum(f"psm{i}", [128, TT]) for i in range(2)]

    ch.dma_n("gpsimd", 6, lambda p: [
        p.dma_start(out=wss[0][:], in_=fmview(wkt)),
        p.dma_start(out=wss[1][:], in_=fmview(wvt)),
        p.dma_start(out=wss[2][:], in_=fmview(wrt)),
        p.dma_start(out=mixs[0][:], in_=mixk[:, :]),
        p.dma_start(out=mixs[1][:], in_=mixv[:, :]),
        p.dma_start(out=mixs[2][:], in_=mixr[:, :]),
    ])
    ch.dma_n("sync", 1,
             lambda p: [p.dma_start(out=xh[:], in_=fmview(halo1)[:, :, 0:1],
                                    allow_slow_non_contiguous=True)])
    emit_ln(cx, xh, 1, sq, xb, psA, psB, psA2, psB2, rows)
    emit_ln_apply(cx, xh, 1, psA2, psB2, ucol)

    outs = (ko, vo, sro)
    for t in range(NTILE):
        t0 = t * TT
        ch.dma_n("sync", 1, lambda p, t0=t0: [
            p.dma_start(out=xt[:], in_=fmview(x)[:, :, t0:t0 + TT])])
        emit_ln(cx, xt, TT, sq, xb, psA, psB, psA2, psB2, rows)
        emit_ln_apply(cx, xt, TT, psA2, psB2, ut)

        def e_d(p):
            for c in range(NCHUNK):
                p.tensor_tensor(dcol[:, c, 0:1], ut[:, c, 0:1],
                                ucol[:, c, 0:1], ALU.subtract)
                p.tensor_tensor(dcol[:, c, 1:TT], ut[:, c, 1:TT],
                                ut[:, c, 0:TT - 1], ALU.subtract)

        ch.comp("vector", e_d, relax=True)

        for mi in range(3):
            def e_mix(p, mi=mi):
                for c in range(NCHUNK):
                    p.scalar_tensor_tensor(mxb[mi][:, c, 0:1],
                                           dcol[:, c, 0:1],
                                           mixs[mi][:, c:c + 1],
                                           ucol[:, c, 0:1], ALU.mult, ALU.add)
                    p.scalar_tensor_tensor(
                        mxb[mi][:, c, 1:TT], dcol[:, c, 1:TT],
                        mixs[mi][:, c:c + 1], ut[:, c, 0:TT - 1], ALU.mult,
                        ALU.add)

            ch.comp("vector", e_mix, relax=True)
        s_in = ch.snap()
        s_mm, s_ev, s_st = {}, {}, {}
        for mi in range(3):
            for o in range(NCHUNK):
                i = mi * NCHUNK + o
                j = i % 2

                def e_mm(p, mi=mi, o=o, j=j):
                    for c in range(NCHUNK):
                        p.matmul(psm[j][:, :],
                                 wss[mi][:, c, o * 128:(o + 1) * 128],
                                 mxb[mi][:, c, :], start=(c == 0),
                                 stop=(c == NCHUNK - 1))

                ch.comp("tensor", e_mm, relax=True,
                        dep=Chain.dep_max(s_in, s_ev.get(i - 2, s_in)))
                s_mm[i] = ch.snap()
                dd_ = Chain.dep_max(s_mm[i], s_st.get(i - 2, s_in))
                if mi < 2:
                    ch.comp("vector",
                            lambda p, j=j: p.tensor_copy(evt[j][:],
                                                         psm[j][:]),
                            dep=dd_)
                else:
                    ch.comp("scalar",
                            lambda p, j=j: p.activation(evt[j][:], psm[j][:],
                                                        AF.Sigmoid), dep=dd_)
                s_ev[i] = ch.snap()
                ch.dma_n("sync", 1, lambda p, mi=mi, o=o, t0=t0, j=j: [
                    p.dma_start(out=outs[mi][o * 128:(o + 1) * 128,
                                t0:t0 + TT], in_=evt[j][:])], dep=s_ev[i])
                s_st[i] = ch.snap()

        def e_carry(p):
            return p.tensor_copy(ucol[:, :, 0:1], ut[:, :, TT - 1:TT])

        ch.comp("vector", e_carry)
    cx.release(mk)


def emit_wkvf(cx, T):
    ch = cx.ch
    nc = cx.nc
    mk = cx.mark()
    kd, vd, oo = T["k"], T["v"], T["wkv"]
    wv, uv = T["rw"], T["ru"]
    selA, selB = T["selA"], T["selB"]
    agin = nc.dram_tensor("wkv_agin", [128, 16], F32, kind="Internal").ap()
    agout = nc.dram_tensor("wkv_agout", [256, 16], F32, kind="Internal").ap()
    kb = cx.sbuf("kb", [128, HSEQ], F32)
    vb = cx.sbuf("vb", [128, HSEQ], F32)
    ek = cx.sbuf("ek", [128, HSEQ], F32)
    ekv = cx.sbuf("ekv", [128, HSEQ], F32)
    dbc = cx.sbuf("dbc", [128, HSEQ], F32)
    ns = cx.sbuf("ns", [128, HSEQ], F32)
    ds = cx.sbuf("ds", [128, HSEQ], F32)
    zb = cx.sbuf("zb", [128, HSEQ], F32)
    pw = cx.sbuf("pw", [128, HSEQ], F32)
    wd = cx.sbuf("wd", [128, NCHUNK], F32)
    eu = cx.sbuf("eu", [128, NCHUNK], F32)
    lasts = cx.sbuf("lasts", [128, 16], F32)
    carry = cx.sbuf("carry", [128, 16], F32)

    ch.dma_n("sync", 2, lambda p: [
        p.dma_start(out=wd[:], in_=wv[:, :]),
        p.dma_start(out=eu[:], in_=uv[:, :]),
    ])
    ch.comp("scalar", lambda p: p.activation(wd[:], wd[:], AF.Exp))
    ch.comp("vector", lambda p: p.tensor_scalar_mul(wd[:], wd[:], -1.0))
    ch.comp("scalar", lambda p: p.activation(wd[:], wd[:], AF.Exp))
    ch.comp("scalar", lambda p: p.activation(eu[:], eu[:], AF.Exp))
    ch.comp("vector", lambda p: p.memset(zb[:], 0.0))

    def scan_pass(c, final):
        ch.dma_n("sync", 2, lambda p, c=c: [
            p.dma_start(out=kb[:], in_=fmview(kd)[:, c, :]),
            p.dma_start(out=vb[:], in_=fmview(vd)[:, c, :]),
        ])
        ch.comp("scalar", lambda p: p.activation(ek[:], kb[:], AF.Exp))

        def e_pre(p, c=c):
            p.tensor_tensor(ekv[:], ek[:], vb[:], ALU.mult)
            p.memset(dbc[:], 1.0)
            p.tensor_scalar_mul(dbc[:], dbc[:], wd[:, c:c + 1])
            p.tensor_tensor_scan(ns[:], dbc[:], ekv[:], 0.0, ALU.mult,
                                 ALU.add)
            p.tensor_tensor_scan(ds[:], dbc[:], ek[:], 0.0, ALU.mult,
                                 ALU.add)

        ch.comp("vector", e_pre)
        if not final:
            def e_last(p, c=c):
                p.tensor_copy(lasts[:, 2 * c:2 * c + 1],
                              ns[:, HSEQ - 1:HSEQ])
                p.tensor_copy(lasts[:, 2 * c + 1:2 * c + 2],
                              ds[:, HSEQ - 1:HSEQ])

            ch.comp("vector", e_last, relax=True)
            return

        def e_fin(p, c=c):
            p.tensor_tensor_scan(pw[:], dbc[:], zb[:], 1.0, ALU.mult, ALU.add)
            p.scalar_tensor_tensor(ns[:], pw[:], carry[:, 2 * c:2 * c + 1],
                                   ns[:], ALU.mult, ALU.add)
            p.scalar_tensor_tensor(ds[:], pw[:], carry[:, 2 * c + 1:2 * c + 2],
                                   ds[:], ALU.mult, ALU.add)
            p.scalar_tensor_tensor(kb[:, 0:1], ekv[:, 0:1], eu[:, c:c + 1],
                                   carry[:, 2 * c:2 * c + 1], ALU.mult,
                                   ALU.add)
            p.scalar_tensor_tensor(vb[:, 0:1], ek[:, 0:1], eu[:, c:c + 1],
                                   carry[:, 2 * c + 1:2 * c + 2], ALU.mult,
                                   ALU.add)
            p.scalar_tensor_tensor(kb[:, 1:HSEQ], ekv[:, 1:HSEQ],
                                   eu[:, c:c + 1], ns[:, 0:HSEQ - 1],
                                   ALU.mult, ALU.add)
            p.scalar_tensor_tensor(vb[:, 1:HSEQ], ek[:, 1:HSEQ],
                                   eu[:, c:c + 1], ds[:, 0:HSEQ - 1],
                                   ALU.mult, ALU.add)
            p.reciprocal(vb[:], vb[:])
            p.tensor_tensor(ek[:], kb[:], vb[:], ALU.mult)

        ch.comp("vector", e_fin)
        ch.dma_n("sync", 1, lambda p, c=c: [
            p.dma_start(out=fmview(oo)[:, c, :], in_=ek[:])])

    for c in range(NCHUNK):
        scan_pass(c, False)
    ch.dma_n("sync", 1, lambda p: [p.dma_start(out=agin[:, :], in_=lasts[:])])

    def wrC(ch2, cmb, r0, c0, cw, slow2):
        ch2.comp("vector",
                 lambda p: p.tensor_copy(carry[:, c0:c0 + cw],
                                         cmb[:, 0:cw]))

    emit_pair_exchange(cx, "wkvx", agin, agout, 128, 16, selA, selB, wrC)
    for c in range(NCHUNK):
        scan_pass(c, True)
    cx.release(mk)


def emit_l0ff(cx, T, ln_in, fin_out):
    W = L0F_W
    ch = cx.ch
    nc = cx.nc
    mk = cx.mark()
    xs = T["xs"]
    wqkv = T["wqkv"]
    qsc = T["qsc"]
    ksc = T["ksc"]
    wo = T["wo"]
    lg = T["lg"]
    lb = T["lb"]
    w1pt = T["w1pt"]
    b1p = T["b1p"]
    w2pt = T["w2pt"]
    b2 = T["b2"]
    prot = T["prot"]
    tqc = T["tqc"]
    tqs = T["tqs"]
    tkc = T["tkc"]
    tks = T["tks"]
    amask = T["amask"]
    bmask = T["bmask"]
    ffnkeep = T["ffnkeep"]
    halo = T["halo"]
    if fin_out:
        xo = T["xo"]
    y = T["y"]

    xt = cx.sbuf("xt", [128, NCHUNK, W], F32)
    ut = cx.sbuf("ut", [128, NCHUNK, W], F32)
    xm = cx.sbuf("xm", [128, NCHUNK, W], F32)
    hs = cx.sbuf("hs", [128, NCHUNK, W], F32)
    ucar = cx.sbuf("ucar", [128, NCHUNK, 1], F32)
    xmcar = cx.sbuf("xmcar", [128, NCHUNK, 1], F32)
    xh1 = cx.sbuf("xh1", [128, NCHUNK, 1], F32)
    rows = cx.sbuf("rows", [1, 4 * W], F32)
    sq = cx.sbuf("sq", [128, NCHUNK, W], BF16)
    xb = cx.sbuf("xb", [128, NCHUNK, W], BF16)
    wqs = [cx.sbuf(f"wqs{i}", [128, NCHUNK, 128], BF16) for i in range(2)]
    wos = cx.sbuf("wos", [128, 4, D_MODEL], BF16)
    w1s = [cx.sbuf(f"w1s{i}", [128, NCHUNK, 128], BF16) for i in range(4)]
    w2s = [cx.sbuf(f"w2s{i}", [128, 22, 128], BF16) for i in range(2)]
    qscs = cx.sbuf("qscs", [128, 1], F32)
    kscs = cx.sbuf("kscs", [128, 1], F32)
    lgs = cx.sbuf("lgs", [128, NCHUNK], F32)
    lbs = cx.sbuf("lbs", [128, NCHUNK], F32)
    b1s = cx.sbuf("b1s", [128, 44], F32)
    b2s = cx.sbuf("b2s", [128, NCHUNK], F32)
    fks = cx.sbuf("fks", [128, 1], F32)
    t4 = [cx.sbuf(f"t4_{i}", [128, W], F32) for i in range(4)]
    ams = cx.sbuf("ams", [128, 4, 1024], BF16)
    bms = cx.sbuf("bms", [128, 8, W], BF16)
    ident = cx.sbuf("ident", [128, 128], F32)
    blkdiag = cx.sbuf("blkdiag", [128, 128], BF16)
    blkrot = cx.sbuf("blkrot", [128, 128], BF16)
    onesq = cx.sbuf("onesq", [128, 128], BF16)
    ones64 = cx.sbuf("ones64", [1, 64], F32)
    qr = cx.sbuf("qr", [128, 4, W], BF16)
    kr = cx.sbuf("kr", [128, 4, 2 * W], BF16)
    vat = cx.sbuf("vat", [128, 8, 8, 65], BF16)
    evq = cx.sbuf("evq", [128, W], F32)
    qhat = cx.sbuf("qhat", [128, W], F32)
    rstb = cx.sbuf("rstb", [128, W], F32)
    rt1 = cx.sbuf("rt1", [128, W], F32)
    sqs = cx.sbuf("sqs", [128, W], BF16)
    nmrow = cx.sbuf("nmrow", [128, W], BF16)
    nmcol = [cx.sbuf(f"nmcol{i}", [128, 1], F32) for i in range(4)]
    n0 = cx.sbuf("n0", [128, 1], F32)
    atT = [cx.sbuf(f"atT{i}", [128, W], BF16) for i in range(2)]
    dd = cx.sbuf("dd", [1, W], F32)
    bc = cx.sbuf("bc", [64, W], F32)
    ow = cx.sbuf("ow", [128, 4, W], BF16)
    hgl = cx.sbuf("hgl", [128, 22, W], BF16)
    sa = [evq, rstb]
    gg = [qhat, rt1]
    P0 = cx.psum("P0", [128, 1024])
    P1 = cx.psum("P1", [128, W])
    P2 = cx.psum("P2", [128, W])
    P3 = cx.psum("P3", [128, W])
    P4 = cx.psum("P4", [128, W])
    P5 = cx.psum("P5", [128, W])

    ch.dma_n("gpsimd", 6, lambda p: [
        p.dma_start(out=wos[:], in_=wo.rearrange("(c p) o -> p c o", p=128)),
        p.dma_start(out=ams[:], in_=amask.rearrange("s i p j -> p (s i) j")[:, 4:8, :]),
        p.dma_start(out=bms[:], in_=bmask.rearrange("s r p j -> p (s r) j")[:, 8:16, :]),
        p.dma_start(out=qscs[:], in_=qsc[:, :]),
        p.dma_start(out=kscs[:], in_=ksc[:, :]),
        p.dma_start(out=blkrot[:], in_=prot[:, :]),
    ])
    ch.dma_n("sync", 6, lambda p: [
        p.dma_start(out=lgs[:], in_=lg[:, :]),
        p.dma_start(out=lbs[:], in_=lb[:, :]),
        p.dma_start(out=b1s[:], in_=b1p[:, :]),
        p.dma_start(out=b2s[:], in_=b2[:, :]),
        p.dma_start(out=fks[:], in_=ffnkeep[:, :]),
        p.dma_start(out=xh1[:], in_=fmview(halo)[:, :, 0:1],
                    allow_slow_non_contiguous=True),
    ])

    def e_consts(p):
        p.memset(ident[:], 0.0)
        p.memset(blkdiag[:], 0.0)
        p.memset(blkdiag[0:64, 0:64], 1.0)
        p.memset(blkdiag[64:128, 64:128], 1.0)
        p.memset(onesq[:], 1.0)
        p.memset(ones64[:], 1.0)
        p.memset(kr[:], 0.0)
        p.memset(vat[:], 0.0)

    ch.comp("vector", e_consts)
    ch.comp("gpsimd", lambda p: p.affine_select(
        ident[:], ident[:], [[-1, 128]], ALU.not_equal, 1.0, base=0,
        channel_multiplier=1))

    if ln_in:
        emit_ln(cx, xh1, 1, sq, xb, P2, P3, P1, P5, rows)
        emit_ln_apply(cx, xh1, 1, P1, P5, ucar)
    else:
        ch.comp("vector", lambda p: p.tensor_copy(ucar[:], xh1[:]))

    for w in range(L0F_NW):
        if w < 2:
            hsrc = fmview(halo)[:, :, 1 + w * W:1 + (w + 1) * W]
        else:
            hsrc = fmview(xs)[:, :, (w - 2) * W:(w - 1) * W]
        ch.dma_n("sync", 1, lambda p, hsrc=hsrc: [
            p.dma_start(out=xt[:], in_=hsrc)])
        if ln_in:
            emit_ln(cx, xt, W, sq, xb, P2, P3, P1, P5, rows)
            emit_ln_apply(cx, xt, W, P1, P5, ut)
        else:
            ch.comp("vector", lambda p: p.tensor_copy(ut[:], xt[:]))

        def e_shift(p):
            for c in range(NCHUNK):
                if c < 4:
                    p.tensor_copy(xb[:, c, :], ut[:, c, :])
                else:
                    p.tensor_copy(xb[:, c, 0:1], ucar[:, c, 0:1])
                    p.tensor_copy(xb[:, c, 1:W], ut[:, c, 0:W - 1])

        ch.comp("vector", e_shift, relax=True)
        ch.comp("vector",
                lambda p: p.tensor_copy(ucar[:], ut[:, :, W - 1:W]))

        def e_ring(p):
            p.tensor_copy(kr[:, :, 0:W], kr[:, :, W:2 * W])
            p.tensor_copy(vat[:, 0:4, :, :], vat[:, 4:8, :, :])

        ch.comp("vector", e_ring, relax=True)
        ch.dma_n("sync", 4, lambda p, w=w: [
            p.dma_start(out=t4[0][:], in_=tqc[:, w * W:(w + 1) * W]),
            p.dma_start(out=t4[1][:], in_=tqs[:, w * W:(w + 1) * W]),
            p.dma_start(out=t4[2][:], in_=tkc[:, w * W:(w + 1) * W]),
            p.dma_start(out=t4[3][:], in_=tks[:, w * W:(w + 1) * W]),
        ])

        for o in range(8):
            kind = o // 4
            pt = o % 4
            wb = wqs[o % 2]
            ch.dma_n("gpsimd", 1,
                     lambda p, o=o, wb=wb: [p.dma_start(out=wb[:],
                                                        in_=wqkv[o])])

            def e_mm(p, wb=wb):
                for c in range(NCHUNK):
                    p.matmul(P4[:, :], wb[:, c, :], xb[:, c, :],
                             start=(c == 0), stop=(c == NCHUNK - 1))

            ch.comp("tensor", e_mm, relax=True)
            ch.comp("vector", lambda p: p.tensor_copy(evq[:], P4[:, :]))
            ch.comp("scalar",
                    lambda p: p.activation(sqs[:], evq[:], AF.Square))
            ch.comp("tensor", lambda p: p.matmul(
                P1[:, :], blkdiag[:], sqs[:], start=True, stop=True), relax=True)

            def e_rst(p):
                p.tensor_scalar_max(rstb[:], P1[:, :], 1e-24)
                p.reciprocal(rstb[:], rstb[:])

            ch.comp("vector", e_rst)
            ch.comp("scalar",
                    lambda p: p.activation(rstb[:], rstb[:], AF.Sqrt))

            def e_hat(p, kind=kind):
                scs = qscs if kind == 0 else kscs
                p.tensor_tensor(qhat[:], evq[:], rstb[:], ALU.mult)
                p.tensor_scalar_mul(qhat[:], qhat[:], scs[:])
                p.tensor_copy(sqs[:], qhat[:])

            ch.comp("vector", e_hat)
            ch.comp("tensor", lambda p: p.matmul(
                P2[:, :], blkrot[:], sqs[:], start=True, stop=True), relax=True)

            def e_rot(p, kind=kind, pt=pt):
                tc, ts_ = (t4[0], t4[1]) if kind == 0 else (t4[2], t4[3])
                p.tensor_tensor(rt1[:], qhat[:], tc[:], ALU.mult)
                p.tensor_tensor(qhat[:], P2[:, :], ts_[:], ALU.mult)
                if kind == 0:
                    p.tensor_tensor(qr[:, pt, :], rt1[:], qhat[:], ALU.add)
                else:
                    p.tensor_tensor(kr[:, pt, W:2 * W], rt1[:], qhat[:],
                                    ALU.add)

            ch.comp("vector", e_rot)

        for o in range(8, 12):
            wb = wqs[o % 2]
            ch.dma_n("gpsimd", 1,
                     lambda p, o=o, wb=wb: [p.dma_start(out=wb[:],
                                                        in_=wqkv[o])])

            def e_mm(p, wb=wb):
                for c in range(NCHUNK):
                    p.matmul(P4[:, :], wb[:, c, :], xb[:, c, :],
                             start=(c == 0), stop=(c == NCHUNK - 1))

            ch.comp("tensor", e_mm, relax=True)
            ch.comp("vector", lambda p: p.tensor_copy(evq[:], P4[:, :]))
            for hh in range(2):
                h = (o - 8) * 2 + hh
                rb = hh * 64
                for jj in range(4):
                    def e_tr(p, rb=rb, jj=jj):
                        return p.transpose(
                            P3[0:128, 0:64],
                            evq[rb:rb + 64, jj * 128:(jj + 1) * 128],
                            ident[rb:rb + 64, rb:rb + 64])

                    ch.comp("tensor", e_tr, relax=True)

                    def e_vat(p, h=h, jj=jj):
                        p.tensor_copy(vat[:, 4 + jj, h, 0:64],
                                      P3[0:128, 0:64])
                        p.memset(vat[:, 4 + jj, h, 64:65], 1.0)

                    ch.comp("vector", e_vat, relax=True)

        if w == 0:
            continue

        mset = 0 if w == 2 else 1
        if w == 2:
            ch.dma_n("gpsimd", 1, lambda p: [p.dma_start(
                out=ams[:],
                in_=amask.rearrange("s i p j -> p (s i) j")[:, 0:4, :])])
            ch.dma_n("gpsimd", 1, lambda p: [p.dma_start(
                out=bms[:],
                in_=bmask.rearrange("s r p j -> p (s r) j")[:, 0:8, :])])
        if w == 3:
            ch.dma_n("gpsimd", 1, lambda p: [p.dma_start(
                out=ams[:],
                in_=amask.rearrange("s i p j -> p (s i) j")[:, 4:8, :])])
            ch.dma_n("gpsimd", 1, lambda p: [p.dma_start(
                out=bms[:],
                in_=bmask.rearrange("s r p j -> p (s r) j")[:, 8:16, :])])
        s_w0 = ch.snap()
        s_prev_pB = s_w0
        for h in range(8):
            pt, rb = h // 2, (h % 2) * 64
            s_sim, s_am, s_nmm, s_nm = {}, {}, {}, {}
            for it in range(4):
                def e_sim(p, pt=pt, rb=rb, it=it):
                    p.matmul(
                        P0[:, 0:W],
                        qr[rb:rb + 64, pt, it * 128:(it + 1) * 128],
                        kr[rb:rb + 64, pt, 0:W], start=True, stop=True)
                    p.matmul(
                        P0[:, W:2 * W],
                        qr[rb:rb + 64, pt, it * 128:(it + 1) * 128],
                        kr[rb:rb + 64, pt, W:2 * W], start=True, stop=True)

                ch.comp("tensor", e_sim, relax=True,
                        dep=Chain.dep_max(s_w0, s_am.get(it - 1, s_prev_pB)))
                s_sim[it] = ch.snap()

                def e_am(p, it=it):
                    p.tensor_tensor(rt1[:], P0[:, 0:W],
                                    ams[:, it, 0:W], ALU.add)
                    p.reduce_max(n0[:], rt1[:], axis=AX.X, negate=True)
                    p.tensor_tensor(rt1[:], P0[:, W:2 * W],
                                    ams[:, it, W:2 * W], ALU.add)
                    p.reduce_max(nmcol[it][:], rt1[:], axis=AX.X,
                                 negate=True)
                    p.tensor_tensor(nmcol[it][:], nmcol[it][:], n0[:],
                                    ALU.min)

                ch.comp("vector", e_am, dep=s_sim[it])
                s_am[it] = ch.snap()
            for it in range(4):
                ch.comp("tensor", lambda p, it=it: p.matmul(
                    P3[0:1, it * 128:(it + 1) * 128], nmcol[it][:],
                    ident[:], start=True, stop=True), relax=True,
                        dep=s_am[it])
                s_nmm[it] = ch.snap()

            def e_nm(p):
                for it in range(4):
                    p.tensor_copy(nmrow[0:1, it * 128:(it + 1) * 128],
                                  P3[0:1, it * 128:(it + 1) * 128])
                    p.tensor_copy(nmrow[64:65, it * 128:(it + 1) * 128],
                                  P3[0:1, it * 128:(it + 1) * 128])

            ch.comp("vector", e_nm, relax=True,
                    dep=Chain.dep_max(s_nmm[3], s_prev_pB))
            s_nmrow = ch.snap()

            s_simT, s_exp, s_bm, s_av = {}, {}, {}, {}

            def emit_simT(js):
                ps = P1 if js % 2 == 0 else P0

                def e_simT(p, pt=pt, rb=rb, js=js, ps=ps):
                    p.matmul(ps[:, 0:W],
                             kr[rb:rb + 64, pt, js * 128:(js + 1) * 128],
                             qr[rb:rb + 64, pt, :], start=True, stop=False)
                    p.matmul(ps[:, 0:W], onesq[rb:rb + 1, :],
                             nmrow[rb:rb + 1, :], start=False, stop=True)

                ch.comp("tensor", e_simT, relax=True,
                        dep=Chain.dep_max(s_nmrow,
                                          s_exp.get(js - 2, s_nmrow)))
                s_simT[js] = ch.snap()

            def emit_eba(js):
                ps = P1 if js % 2 == 0 else P0
                aj = js % 2
                ch.comp("scalar", lambda p, aj=aj, ps=ps: p.activation(
                    atT[aj][:], ps[:, 0:W], AF.Exp),
                        dep=Chain.dep_max(s_simT[js],
                                          s_av.get(js - 2, s_nmrow)))
                s_exp[js] = ch.snap()

                def e_bm(p, js=js, aj=aj):
                    r, jj = js // 4, js % 4
                    p.tensor_tensor(atT[aj][:], atT[aj][:],
                                    bms[:, r * 4 + jj, :], ALU.mult)

                ch.comp("vector", e_bm, dep=s_exp[js])
                s_bm[js] = ch.snap()

                def e_av(p, h=h, js=js, aj=aj):
                    return p.matmul(P2[0:65, :], vat[:, js, h, :],
                                    atT[aj][:], start=(js == 0),
                                    stop=(js == 7))

                ch.comp("tensor", e_av, relax=True, dep=s_bm[js])
                s_av[js] = ch.snap()

            emit_simT(0)
            for js in range(1, 8):
                emit_simT(js)
                emit_eba(js - 1)
            emit_eba(7)
            s_prev_pB = Chain.dep_max(s_exp[7], s_av[7])

            ch.comp("vector", lambda p: p.reciprocal(dd[:], P2[64:65, :]),
                    dep=s_av[7])
            s_dd = ch.snap()
            ch.comp("tensor", lambda p: p.matmul(
                P5[0:64, :], ones64[:], dd[:], start=True, stop=True),
                    relax=True, dep=s_dd)
            s_bcm = ch.snap()

            def e_o(p, h=h):
                p.tensor_copy(bc[:], P5[0:64, :])
                p.tensor_tensor(ow[(h % 2) * 64:(h % 2) * 64 + 64,
                                h // 2, :], P2[0:64, :], bc[:], ALU.mult)

            ch.comp("vector", e_o, dep=s_bcm)
            s_prev_pB = Chain.dep_max(s_prev_pB, ch.snap())

        for o in range(NCHUNK):
            def e_wo(p, o=o):
                for c in range(4):
                    p.matmul(P4[:, :], wos[:, c, o * 128:(o + 1) * 128],
                             ow[:, c, :], start=(c == 0), stop=(c == 3))

            ch.comp("tensor", e_wo, relax=True)
            ch.comp("vector", lambda p, o=o: p.tensor_tensor(
                xm[:, o, :], ut[:, o, :], P4[:, :], ALU.add))

        if w == 1:
            ch.comp("vector",
                    lambda p: p.tensor_copy(xmcar[:], xm[:, :, W - 1:W]))
            continue

        if w == 2:
            def e_fk(p):
                for c in range(NCHUNK):
                    p.tensor_scalar_mul(xmcar[:, c, 0:1], xmcar[:, c, 0:1],
                                        fks[:])

            ch.comp("vector", e_fk, relax=True)

        def e_shift2(p):
            for c in range(NCHUNK):
                if c < 4:
                    p.tensor_copy(hs[:, c, :], xm[:, c, :])
                else:
                    p.tensor_copy(hs[:, c, 0:1], xmcar[:, c, 0:1])
                    p.tensor_copy(hs[:, c, 1:W], xm[:, c, 0:W - 1])

        ch.comp("vector", e_shift2, relax=True)
        ch.comp("vector",
                lambda p: p.tensor_copy(xmcar[:], xm[:, :, W - 1:W]))
        emit_ln(cx, hs, W, sq, xb, P2, P3, P1, P5, rows)

        def e_ln2(p):
            for c in range(NCHUNK):
                p.tensor_tensor(rt1[:], hs[:, c, :], P1[:, :], ALU.mult)
                p.scalar_tensor_tensor(rt1[:], rt1[:], 1.0, P5[:, :],
                                       ALU.mult, ALU.add)
                p.tensor_scalar(sq[:, c, :], rt1[:], lgs[:, c:c + 1],
                                lbs[:, c:c + 1], ALU.mult, ALU.add)

        ch.comp("vector", e_ln2)

        s_u2 = ch.snap()
        s_w1ld, s_w1a, s_w1g, s_sa, s_gg, s_hg = {}, {}, {}, {}, {}, {}
        for j in range(22):
            jj = j % 2
            ch.dma_n("gpsimd", 2, lambda p, j=j, jj=jj: [
                p.dma_start(out=w1s[2 * jj][:], in_=w1pt[j]),
                p.dma_start(out=w1s[2 * jj + 1][:], in_=w1pt[22 + j]),
            ], dep=Chain.dep_max(s_u2, s_w1g.get(j - 2, s_u2)))
            s_w1ld[j] = ch.snap()

            def e_w1a(p, jj=jj):
                for c in range(NCHUNK):
                    p.matmul(P4[:, :] if jj == 0 else P2[:, :],
                             w1s[2 * jj][:, c, :], sq[:, c, :],
                             start=(c == 0), stop=(c == NCHUNK - 1))

            ch.comp("tensor", e_w1a, relax=True,
                    dep=Chain.dep_max(s_w1ld[j], s_sa.get(j - 2, s_u2)))
            s_w1a[j] = ch.snap()

            def e_w1g(p, jj=jj):
                for c in range(NCHUNK):
                    p.matmul(P5[:, :] if jj == 0 else P3[:, :],
                             w1s[2 * jj + 1][:, c, :], sq[:, c, :],
                             start=(c == 0), stop=(c == NCHUNK - 1))

            ch.comp("tensor", e_w1g, relax=True,
                    dep=Chain.dep_max(s_w1ld[j], s_gg.get(j - 2, s_u2)))
            s_w1g[j] = ch.snap()
            ch.comp("vector", lambda p, j=j, jj=jj: p.tensor_scalar_add(
                sa[jj][:], (P4 if jj == 0 else P2)[:, :], b1s[:, j:j + 1]),
                    dep=Chain.dep_max(s_w1a[j], s_hg.get(j - 2, s_u2)))
            s_sa[j] = ch.snap()
            ch.comp("scalar", lambda p, j=j, jj=jj: p.activation(
                gg[jj][:], (P5 if jj == 0 else P3)[:, :], AF.Gelu,
                bias=b1s[:, 22 + j:23 + j], scale=1.0),
                    dep=Chain.dep_max(s_w1g[j], s_hg.get(j - 2, s_u2)))
            s_gg[j] = ch.snap()
            ch.comp("vector", lambda p, j=j, jj=jj: p.tensor_tensor(
                hgl[:, j, :], sa[jj][:], gg[jj][:], ALU.mult),
                    dep=Chain.dep_max(s_sa[j], s_gg[j]))
            s_hg[j] = ch.snap()

        s_h = ch.snap()
        s_w2ld, s_w2, s_res = {}, {}, {}
        for o in range(NCHUNK):
            jj = o % 2
            ch.dma_n("gpsimd", 1,
                     lambda p, o=o, jj=jj: [p.dma_start(out=w2s[jj][:],
                                                        in_=w2pt[o])],
                     dep=Chain.dep_max(s_h, s_w2.get(o - 2, s_h)))
            s_w2ld[o] = ch.snap()

            def e_w2(p, jj=jj):
                for c in range(22):
                    p.matmul((P4 if jj == 0 else P2)[:, :], w2s[jj][:, c, :],
                             hgl[:, c, :], start=(c == 0), stop=(c == 21))

            ch.comp("tensor", e_w2, relax=True,
                    dep=Chain.dep_max(s_w2ld[o], s_res.get(o - 2, s_h)))
            s_w2[o] = ch.snap()

            def e_res(p, o=o, jj=jj):
                p.tensor_scalar_add(rt1[:], (P4 if jj == 0 else P2)[:, :],
                                    b2s[:, o:o + 1])
                p.tensor_tensor(xm[:, o, :], xm[:, o, :], rt1[:], ALU.add)

            ch.comp("vector", e_res, dep=s_w2[o])
            s_res[o] = ch.snap()

        t0 = (w - 2) * W
        if fin_out:
            ch.dma_n("sync", 1, lambda p, t0=t0: [
                p.dma_start(out=hs[:], in_=fmview(xo)[:, :, t0:t0 + W])])
            emit_ln(cx, xm, W, sq, xb, P2, P3, P1, P5, rows)
            emit_ln_apply(cx, xm, W, P1, P5, xm, tmp=rt1, add_to=hs)

        ch.dma_n("sync", 1, lambda p, t0=t0: [
            p.dma_start(out=fmview(y)[:, :, t0:t0 + W], in_=xm[:])])

    cx.release(mk)




def emit_n1f(cx, T):
    ch = cx.ch
    mk = cx.mark()
    x, w1t, w2t, s1, s2, h = (T["x"], T["w1t"], T["w2t"], T["s1"], T["s2"],
                              T["h"])
    xt = cx.sbuf("xt", [128, NCHUNK, TT], F32)
    sq = cx.sbuf("sq", [128, NCHUNK, TT], BF16)
    xb = cx.sbuf("xb", [128, NCHUNK, TT], BF16)
    ub = cx.sbuf("ub", [128, NCHUNK, TT], BF16)
    w1s = cx.sbuf("w1s", [128, NCHUNK, 4096], BF16)
    w2s = cx.sbuf("w2s", [128, NCHUNK, 4096], BF16)
    s1s = cx.sbuf("s1s", [128, 32], F32)
    s2s = cx.sbuf("s2s", [128, 32], F32)
    rows = cx.sbuf("rows", [1, 4 * TT], F32)
    sa = [cx.sbuf(f"sa{i}", [128, TT], F32) for i in range(2)]
    sb2 = [cx.sbuf(f"sb2{i}", [128, TT], F32) for i in range(2)]
    hb = [cx.sbuf(f"hb{i}", [128, TT], BF16) for i in range(2)]
    psA = cx.psum("psA", [128, TT])
    psB = cx.psum("psB", [128, TT])
    psA2 = cx.psum("psA2", [128, TT])
    psB2 = cx.psum("psB2", [128, TT])
    psm = [cx.psum(f"psm{i}", [128, TT]) for i in range(2)]
    psm2 = [cx.psum(f"psm2{i}", [128, TT]) for i in range(2)]

    ch.dma_n("gpsimd", 4, lambda p: [
        p.dma_start(out=w1s[:], in_=fmview(w1t)),
        p.dma_start(out=w2s[:], in_=fmview(w2t)),
        p.dma_start(out=s1s[:], in_=s1[:, :]),
        p.dma_start(out=s2s[:], in_=s2[:, :]),
    ])
    for t in range(NTILE):
        t0 = t * TT
        ch.dma_n("sync", 1, lambda p, t0=t0: [
            p.dma_start(out=xt[:], in_=fmview(x)[:, :, t0:t0 + TT])])
        emit_ln(cx, xt, TT, sq, xb, psA, psB, psA2, psB2, rows)
        emit_ln_apply(cx, xt, TT, psA2, psB2, ub)
        s_ub = ch.snap()
        s_mma, s_mmb, s_act, s_mul, s_st = {}, {}, {}, {}, {}
        for o in range(32):
            j = o % 2

            def e_mma(p, o=o, j=j):
                for c in range(NCHUNK):
                    p.matmul(psm[j][:, :], w1s[:, c, o * 128:(o + 1) * 128],
                             ub[:, c, :], start=(c == 0),
                             stop=(c == NCHUNK - 1))

            ch.comp("tensor", e_mma, relax=True,
                    dep=Chain.dep_max(s_ub, s_act.get(o - 2, s_ub)))
            s_mma[o] = ch.snap()

            def e_mmb(p, o=o, j=j):
                for c in range(NCHUNK):
                    p.matmul(psm2[j][:, :], w2s[:, c, o * 128:(o + 1) * 128],
                             ub[:, c, :], start=(c == 0),
                             stop=(c == NCHUNK - 1))

            ch.comp("tensor", e_mmb, relax=True,
                    dep=Chain.dep_max(s_ub, s_mul.get(o - 2, s_ub)))
            s_mmb[o] = ch.snap()
            ch.comp("scalar", lambda p, o=o, j=j: p.activation(
                sa[j][:], psm[j][:], AF.Silu, scale=s1s[:, o:o + 1]),
                    relax=True,
                    dep=Chain.dep_max(s_mma[o], s_mul.get(o - 2, s_ub)))
            s_act[o] = ch.snap()

            def e_mul(p, o=o, j=j):
                p.tensor_scalar_mul(sb2[j][:], psm2[j][:], s2s[:, o:o + 1])
                p.tensor_tensor(hb[j][:], sa[j][:], sb2[j][:], ALU.mult)

            ch.comp("vector", e_mul,
                    dep=Chain.dep_max(s_mmb[o], s_act[o],
                                      s_st.get(o - 2, s_ub)))
            s_mul[o] = ch.snap()
            ch.dma_n("sync", 1, lambda p, o=o, t0=t0, j=j: [
                p.dma_start(out=h[o * 128:(o + 1) * 128, t0:t0 + TT],
                            in_=hb[j][:])], dep=s_mul[o])
            s_st[o] = ch.snap()
    cx.release(mk)


def emit_nhf(cx, T):
    ch = cx.ch
    mk = cx.mark()
    hi, wt, ss, ho = T["hi"], T["wt"], T["ss"], T["ho"]
    ht = [cx.sbuf(f"ht{i}", [128, 32, TT], BF16) for i in range(2)]
    ws = [cx.sbuf(f"ws{i}", [128, 32, 128], BF16) for i in range(3)]
    sss = cx.sbuf("sss", [128, 32], F32)
    ob = [cx.sbuf(f"ob{i}", [128, TT], BF16) for i in range(2)]
    psm = [cx.psum(f"psm{i}", [128, TT]) for i in range(2)]

    ch.dma_n("gpsimd", 1, lambda p: [p.dma_start(out=sss[:], in_=ss[:, :])])
    z = ch.snap()
    s_ld = {}
    s_wld = {}
    s_mm = {}
    s_act = {}
    s_st = {}
    idx = lambda t, o: t * 32 + o
    for t in range(NTILE):
        t0 = t * TT
        ch.dma_n("sync", 1, lambda p, t0=t0, t=t: [
            p.dma_start(out=ht[t % 2][:],
                        in_=fmview(hi, 32)[:, :, t0:t0 + TT])],
                  dep=s_mm.get(idx(t - 2, 31), z))
        s_ld[t] = ch.snap()
        for o in range(32):
            i = idx(t, o)
            wb = ws[i % 3]
            ch.dma_n("gpsimd", 1, lambda p, o=o, wb=wb: [
                p.dma_start(out=wb[:],
                            in_=wt[o].rearrange("(c p) m -> p c m", p=128))],
                      dep=s_mm.get(i - 3, z))
            s_wld[i] = ch.snap()
            ps = psm[i % 2]

            def e_mm(p, wb=wb, ps=ps, t=t):
                for c in range(32):
                    p.matmul(ps[:, :], wb[:, c, :], ht[t % 2][:, c, :],
                             start=(c == 0), stop=(c == 31))

            ch.comp("tensor", e_mm, relax=True,
                    dep=Chain.dep_max(s_wld[i], s_ld[t],
                                      s_act.get(i - 2, z)))
            s_mm[i] = ch.snap()
            ch.comp("scalar", lambda p, o=o, ps=ps, ob2=ob[i % 2]:
                    p.activation(ob2[:], ps[:], AF.Gelu,
                                 scale=sss[:, o:o + 1]),
                    relax=True,
                    dep=Chain.dep_max(s_mm[i], s_st.get(i - 2, z)))
            s_act[i] = ch.snap()
            ch.dma_n("sync", 1, lambda p, o=o, t0=t0, ob2=ob[i % 2]: [
                p.dma_start(out=ho[o * 128:(o + 1) * 128, t0:t0 + TT],
                            in_=ob2[:])], dep=s_act[i])
            s_st[i] = ch.snap()
    cx.release(mk)


def emit_n4f(cx, T):
    ch = cx.ch
    mk = cx.mark()
    hi, w3t, s3, x3, y = T["hi"], T["w3t"], T["s3"], T["x3"], T["y"]
    ht = cx.sbuf("ht", [128, 32, TT], BF16)
    x3t = cx.sbuf("x3t", [128, NCHUNK, TT], F32)
    w3s = cx.sbuf("w3s", [128, 32, D_MODEL], BF16)
    s3s = cx.sbuf("s3s", [128, 8], F32)
    yt = cx.sbuf("yt", [128, NCHUNK, TT], F32)
    yb = cx.sbuf("yb", [128, NCHUNK, TT], BF16)
    sq = cx.sbuf("sq", [128, NCHUNK, TT], BF16)
    rows = cx.sbuf("rows", [1, 4 * TT], F32)
    outt = cx.sbuf("outt", [128, NCHUNK, TT], F32)
    psm = [cx.psum(f"psm{i}", [128, TT]) for i in range(2)]
    psA = cx.psum("psA", [128, TT])
    psB = cx.psum("psB", [128, TT])
    psA2 = cx.psum("psA2", [128, TT])
    psB2 = cx.psum("psB2", [128, TT])

    ch.dma_n("gpsimd", 2, lambda p: [
        p.dma_start(out=w3s[:], in_=fmview(w3t, 32)),
        p.dma_start(out=s3s[:], in_=s3[:, :]),
    ])
    for t in range(NTILE):
        t0 = t * TT
        ch.dma_n("sync", 2, lambda p, t0=t0: [
            p.dma_start(out=ht[:], in_=fmview(hi, 32)[:, :, t0:t0 + TT]),
            p.dma_start(out=x3t[:], in_=fmview(x3)[:, :, t0:t0 + TT]),
        ])
        s_in = ch.snap()
        s_mm, s_ev = {}, {}
        for o in range(NCHUNK):
            j = o % 2

            def e_mm(p, o=o, j=j):
                for c in range(32):
                    p.matmul(psm[j][:, :], w3s[:, c, o * 128:(o + 1) * 128],
                             ht[:, c, :], start=(c == 0), stop=(c == 31))

            ch.comp("tensor", e_mm, relax=True,
                    dep=Chain.dep_max(s_in, s_ev.get(o - 2, s_in)))
            s_mm[o] = ch.snap()

            def e_ev(p, o=o, j=j):
                p.tensor_scalar_mul(yt[:, o, :], psm[j][:], s3s[:, o:o + 1])
                p.tensor_scalar_mul(yb[:, o, :], psm[j][:], s3s[:, o:o + 1])

            ch.comp("vector", e_ev, relax=True, dep=s_mm[o])
            s_ev[o] = ch.snap()
        emit_ln(cx, yt, TT, sq, yb, psA, psB, psA2, psB2, rows, need_xb=False)
        emit_ln_apply(cx, yt, TT, psA2, psB2, outt, add_to=x3t)
        ch.dma_n("sync", 1, lambda p, t0=t0: [
            p.dma_start(out=fmview(y)[:, :, t0:t0 + TT], in_=outt[:])])
    cx.release(mk)


def build_fused():
    nc = bass.Bass("TRN2", target_bir_lowering=False, debug=False)
    E = "ExternalInput"
    d = nc.dram_tensor
    T = {}

    def ein(name, shape):
        T[name] = d(name, shape, F32, kind=E).ap()

    def intr(name, shape, dt=F32):
        T[name] = d(name, shape, dt, kind="Internal").ap()

    ein("x", [D_MODEL, HSEQ])
    T["y"] = d("y", [D_MODEL, HSEQ], F32, kind="ExternalOutput").ap()
    ein("selA", [128, 1])
    ein("selB", [128, 1])
    ein("ffnkeep", [128, 1])
    ein("a1_wqt", [D_MODEL, 3072])
    ein("a3_wot", [D_MODEL, D_MODEL])
    intr("q", [D_MODEL, HSEQ])
    intr("k", [D_MODEL, HSEQ])
    intr("v", [D_MODEL, HSEQ])
    intr("kvtm", [32, 128, 2048], BF16)
    intr("o_att", [D_MODEL, HSEQ])
    intr("x1", [D_MODEL, HSEQ])
    for li in range(2):
        ein(f"f{li}_wqkv", [12, 128, NCHUNK, 128])
        ein(f"f{li}_qsc", [128, 1])
        ein(f"f{li}_ksc", [128, 1])
        ein(f"f{li}_wo", [512, D_MODEL])
        ein(f"f{li}_lg", [128, NCHUNK])
        ein(f"f{li}_lb", [128, NCHUNK])
        ein(f"f{li}_w1pt", [44, 128, NCHUNK, 128])
        ein(f"f{li}_b1p", [128, 44])
        ein(f"f{li}_w2pt", [8, 128, 22, 128])
        ein(f"f{li}_b2", [128, NCHUNK])
    ein("f_prot", [128, 128])
    ein("f_tqc", [128, L0F_NW * L0F_W])
    ein("f_tqs", [128, L0F_NW * L0F_W])
    ein("f_tkc", [128, L0F_NW * L0F_W])
    ein("f_tks", [128, L0F_NW * L0F_W])
    ein("f_amask", [2, 4, 128, 1024])
    ein("f_bmask", [2, 8, 128, L0F_W])
    intr("halo0", [D_MODEL, 1025])
    intr("inner1", [D_MODEL, HSEQ])
    intr("halo1", [D_MODEL, 1025])
    intr("x2", [D_MODEL, HSEQ])
    intr("agh", [2 * D_MODEL, 1025])
    intr("aghR", [2 * D_MODEL, 1])
    ein("r_wkt", [D_MODEL, D_MODEL])
    ein("r_wvt", [D_MODEL, D_MODEL])
    ein("r_wrt", [D_MODEL, D_MODEL])
    ein("r_wot", [D_MODEL, D_MODEL])
    ein("r_mixk", [128, NCHUNK])
    ein("r_mixv", [128, NCHUNK])
    ein("r_mixr", [128, NCHUNK])
    ein("r_w", [128, NCHUNK])
    ein("r_u", [128, NCHUNK])
    intr("haloR", [D_MODEL, 1])
    intr("kR", [D_MODEL, HSEQ])
    intr("vR", [D_MODEL, HSEQ])
    intr("srR", [D_MODEL, HSEQ])
    intr("wkvo", [D_MODEL, HSEQ])
    intr("x3", [D_MODEL, HSEQ])
    ein("n_w1t", [D_MODEL, 4096])
    ein("n_w2t", [D_MODEL, 4096])
    ein("n_s1", [128, 32])
    ein("n_s2", [128, 32])
    ein("n_h1wt", [32, 4096, 128])
    ein("n_h1s", [128, 32])
    ein("n_h2wt", [32, 4096, 128])
    ein("n_h2s", [128, 32])
    ein("n_w3t", [4096, D_MODEL])
    ein("n_s3", [128, 8])
    intr("hb1", [4096, HSEQ], BF16)
    intr("hb2", [4096, HSEQ], BF16)
    intr("hb3", [4096, HSEQ], BF16)

    ch = Chain()
    cx = Ctx(nc, ch)
    make_common_consts(cx)
    selA = cx.sbuf("g_selA", [128, 1], F32)
    selB = cx.sbuf("g_selB", [128, 1], F32)
    ch.dma_n("sync", 2, lambda p: [
        p.dma_start(out=selA[:], in_=T["selA"][:, :]),
        p.dma_start(out=selB[:], in_=T["selB"][:, :]),
    ])

    cx.prefix = "a1_"
    emit_a1f(cx, {"x": T["x"], "wqt": T["a1_wqt"], "q": T["q"], "k": T["k"],
                  "v": T["v"], "kvtm": T["kvtm"]})
    cx.prefix = "a2_"
    emit_a2f(cx, {"q": T["q"], "k": T["k"], "kvtm": T["kvtm"],
                  "o": T["o_att"], "selA": selA, "selB": selB})
    cx.prefix = "a3_"
    emit_projf(cx, {"inp": T["o_att"], "x": T["x"], "wt": T["a3_wot"],
                    "y": T["x1"]})

    cx.prefix = "hx0_"
    emit_halo_fill(cx, "hx0_", T["x1"], T["halo0"], 1025, selA, selB,
                   T["agh"])
    cx.prefix = "f0_"
    emit_l0ff(cx, {"xs": T["x1"], "halo": T["halo0"],
                   "wqkv": T["f0_wqkv"], "qsc": T["f0_qsc"],
                   "ksc": T["f0_ksc"], "wo": T["f0_wo"], "lg": T["f0_lg"],
                   "lb": T["f0_lb"], "w1pt": T["f0_w1pt"],
                   "b1p": T["f0_b1p"], "w2pt": T["f0_w2pt"],
                   "b2": T["f0_b2"], "prot": T["f_prot"], "tqc": T["f_tqc"],
                   "tqs": T["f_tqs"], "tkc": T["f_tkc"], "tks": T["f_tks"],
                   "amask": T["f_amask"], "bmask": T["f_bmask"],
                   "ffnkeep": T["ffnkeep"], "y": T["inner1"]},
              True, False)
    cx.prefix = "hx1_"
    emit_halo_fill(cx, "hx1_", T["inner1"], T["halo1"], 1025, selA, selB,
                   T["agh"])
    cx.prefix = "f1_"
    emit_l0ff(cx, {"xs": T["inner1"], "halo": T["halo1"],
                   "wqkv": T["f1_wqkv"], "qsc": T["f1_qsc"],
                   "ksc": T["f1_ksc"], "wo": T["f1_wo"], "lg": T["f1_lg"],
                   "lb": T["f1_lb"], "w1pt": T["f1_w1pt"],
                   "b1p": T["f1_b1p"], "w2pt": T["f1_w2pt"],
                   "b2": T["f1_b2"], "prot": T["f_prot"], "tqc": T["f_tqc"],
                   "tqs": T["f_tqs"], "tkc": T["f_tkc"], "tks": T["f_tks"],
                   "amask": T["f_amask"], "bmask": T["f_bmask"],
                   "ffnkeep": T["ffnkeep"], "xo": T["x1"], "y": T["x2"]},
              False, True)

    cx.prefix = "hxr_"
    emit_halo_fill(cx, "hxr_", T["x2"], T["haloR"], 1, selA, selB,
                   T["aghR"])
    cx.prefix = "r1_"
    emit_r1f(cx, {"x": T["x2"], "halo1": T["haloR"], "k": T["kR"],
                  "v": T["vR"], "sr": T["srR"], "wkt": T["r_wkt"],
                  "wvt": T["r_wvt"], "wrt": T["r_wrt"], "mixk": T["r_mixk"],
                  "mixv": T["r_mixv"], "mixr": T["r_mixr"]})
    cx.prefix = "wk_"
    emit_wkvf(cx, {"k": T["kR"], "v": T["vR"], "wkv": T["wkvo"],
                   "rw": T["r_w"], "ru": T["r_u"], "selA": selA,
                   "selB": selB})
    cx.prefix = "r3_"
    emit_r3f(cx, {"wkv": T["wkvo"], "sr": T["srR"], "x2": T["x2"],
                  "wot": T["r_wot"], "y": T["x3"]})

    cx.prefix = "n1_"
    emit_n1f(cx, {"x": T["x3"], "w1t": T["n_w1t"], "w2t": T["n_w2t"],
                  "s1": T["n_s1"], "s2": T["n_s2"], "h": T["hb1"]})
    cx.prefix = "n2_"
    emit_nhf(cx, {"hi": T["hb1"], "wt": T["n_h1wt"], "ss": T["n_h1s"],
                  "ho": T["hb2"]})
    cx.prefix = "n3_"
    emit_nhf(cx, {"hi": T["hb2"], "wt": T["n_h2wt"], "ss": T["n_h2s"],
                  "ho": T["hb3"]})
    cx.prefix = "n4_"
    emit_n4f(cx, {"hi": T["hb3"], "w3t": T["n_w3t"], "s3": T["n_s3"],
                  "x3": T["x3"], "y": T["y"]})

    ch.build_with_clear(nc)
    cx.close()
    return nc


def emit_r3f(cx, T):
    ch = cx.ch
    mk = cx.mark()
    wk, sr, x2, wot, y = T["wkv"], T["sr"], T["x2"], T["wot"], T["y"]
    wt = cx.sbuf("wt", [128, NCHUNK, TT], F32)
    st = cx.sbuf("st", [128, NCHUNK, TT], F32)
    x2t = cx.sbuf("x2t", [128, NCHUNK, TT], F32)
    rb = cx.sbuf("rb", [128, NCHUNK, TT], BF16)
    wos = cx.sbuf("wos", [128, NCHUNK, D_MODEL], BF16)
    yt = cx.sbuf("yt", [128, NCHUNK, TT], F32)
    yb = cx.sbuf("yb", [128, NCHUNK, TT], BF16)
    sq = cx.sbuf("sq", [128, NCHUNK, TT], BF16)
    rows = cx.sbuf("rows", [1, 4 * TT], F32)
    outt = cx.sbuf("outt", [128, NCHUNK, TT], F32)
    psm = [cx.psum(f"psm{i}", [128, TT]) for i in range(2)]
    psA = cx.psum("psA", [128, TT])
    psB = cx.psum("psB", [128, TT])
    psA2 = cx.psum("psA2", [128, TT])
    psB2 = cx.psum("psB2", [128, TT])

    ch.dma_n("gpsimd", 1, lambda p: [p.dma_start(out=wos[:], in_=fmview(wot))])
    for t in range(NTILE):
        t0 = t * TT
        ch.dma_n("sync", 3, lambda p, t0=t0: [
            p.dma_start(out=wt[:], in_=fmview(wk)[:, :, t0:t0 + TT]),
            p.dma_start(out=st[:], in_=fmview(sr)[:, :, t0:t0 + TT]),
            p.dma_start(out=x2t[:], in_=fmview(x2)[:, :, t0:t0 + TT]),
        ])

        def e_rb(p):
            for c in range(NCHUNK):
                p.tensor_tensor(rb[:, c, :], wt[:, c, :], st[:, c, :],
                                ALU.mult)

        ch.comp("vector", e_rb, relax=True)
        s_in = ch.snap()
        s_mm, s_ev = {}, {}
        for o in range(NCHUNK):
            j = o % 2

            def e_mm(p, o=o, j=j):
                for c in range(NCHUNK):
                    p.matmul(psm[j][:, :], wos[:, c, o * 128:(o + 1) * 128],
                             rb[:, c, :], start=(c == 0),
                             stop=(c == NCHUNK - 1))

            ch.comp("tensor", e_mm, relax=True,
                    dep=Chain.dep_max(s_in, s_ev.get(o - 2, s_in)))
            s_mm[o] = ch.snap()

            def e_ev(p, o=o, j=j):
                p.tensor_copy(yt[:, o, :], psm[j][:])
                p.tensor_copy(yb[:, o, :], psm[j][:])

            ch.comp("vector", e_ev, relax=True, dep=s_mm[o])
            s_ev[o] = ch.snap()
        emit_ln(cx, yt, TT, sq, yb, psA, psB, psA2, psB2, rows, need_xb=False)
        emit_ln_apply(cx, yt, TT, psA2, psB2, outt, add_to=x2t)
        ch.dma_n("sync", 1, lambda p, t0=t0: [
            p.dma_start(out=fmview(y)[:, :, t0:t0 + TT], in_=outt[:])])
    cx.release(mk)


def _fused_maps(inp):
    w = inp
    x = np.asarray(inp["L0_attn_x"], np.float32)
    x_fm = _c(x.transpose(0, 2, 1))
    QK = 8.0
    shared = {
        "a1_wqt": _c(w["L0_attn_W_qkv"].T.astype(np.float32)),
        "a3_wot": _c(w["L0_attn_W_out"].T.astype(np.float32)),
        "f_tqc": None, "f_tqs": None, "f_tkc": None, "f_tks": None,
        "r_wkt": _c(w["L1_attn_Wk"].T.astype(np.float32)),
        "r_wvt": _c(w["L1_attn_Wv"].T.astype(np.float32)),
        "r_wrt": _c(w["L1_attn_Wr"].T.astype(np.float32)),
        "r_wot": _c(w["L1_attn_Wo"].T.astype(np.float32)),
        "r_mixk": _scale_tiles(w["L1_attn_mix_k"]),
        "r_mixv": _scale_tiles(w["L1_attn_mix_v"]),
        "r_mixr": _scale_tiles(w["L1_attn_mix_r"]),
        "r_w": _scale_tiles(w["L1_attn_w"]),
        "r_u": _scale_tiles(w["L1_attn_u"]),
        "n_w1t": _c(w["L1_ffn_w1_w"].T.astype(np.float32)),
        "n_w2t": _c(w["L1_ffn_w2_w"].T.astype(np.float32)),
        "n_s1": _scale_tiles(w["L1_ffn_w1_s"]),
        "n_s2": _scale_tiles(w["L1_ffn_w2_s"]),
        "n_h1wt": _c(w["L1_ffn_h1_w"].T.astype(np.float32)
                     .reshape(4096, 32, 128).transpose(1, 0, 2)),
        "n_h1s": _scale_tiles(w["L1_ffn_h1_s"]),
        "n_h2wt": _c(w["L1_ffn_h2_w"].T.astype(np.float32)
                     .reshape(4096, 32, 128).transpose(1, 0, 2)),
        "n_h2s": _scale_tiles(w["L1_ffn_h2_s"]),
        "n_w3t": _c(w["L1_ffn_w3_w"].T.astype(np.float32)),
        "n_s3": _scale_tiles(w["L1_ffn_w3_s"]),
    }
    for li in range(2):
        wqkv = w["L0_ffn_wqkv"][li].astype(np.float64)
        w1 = w["L0_ffn_w1"][li].astype(np.float64)
        b1 = w["L0_ffn_b1"][li].astype(np.float64)
        w2 = w["L0_ffn_w2"][li].astype(np.float64)
        w1p = np.zeros((1024, 5632))
        w1p[:, 0:2730] = w1[:, 0:2730]
        w1p[:, 2816:5546] = w1[:, 2730:]
        b1p = np.zeros(5632)
        b1p[0:2730] = b1[0:2730]
        b1p[2816:5546] = b1[2730:]
        w2p = np.zeros((2816, 1024))
        w2p[0:2730] = w2
        shared[f"f{li}_wqkv"] = _l0f_tile_w(wqkv, 12)
        shared[f"f{li}_qsc"] = _c(np.tile(w["L0_ffn_q_scale"][li] * QK, 2)
                                  .reshape(128, 1).astype(np.float32))
        shared[f"f{li}_ksc"] = _c(np.tile(w["L0_ffn_k_scale"][li], 2)
                                  .reshape(128, 1).astype(np.float32))
        shared[f"f{li}_wo"] = _c(w["L0_ffn_wo"][li].astype(np.float32))
        shared[f"f{li}_lg"] = _scale_tiles(w["L0_ffn_ln_g"][li])
        shared[f"f{li}_lb"] = _scale_tiles(w["L0_ffn_ln_b"][li])
        shared[f"f{li}_w1pt"] = _l0f_tile_w(w1p, 44)
        shared[f"f{li}_b1p"] = _c(b1p.reshape(44, 128).T.astype(np.float32))
        shared[f"f{li}_w2pt"] = _l0f_tile_w(w2p, 8)
        shared[f"f{li}_b2"] = _scale_tiles(w["L0_ffn_b2"][li])
    consts = [_l0f_consts(0), _l0f_consts(1)]
    maps = []
    for core in range(8):
        b, hh = core // 2, core % 2
        tqc, tqs, tkc, tks, am, bm, prot = consts[hh]
        m = dict(shared)
        m["f_tqc"], m["f_tqs"], m["f_tkc"], m["f_tks"] = tqc, tqs, tkc, tks
        m["f_amask"] = am
        m["f_bmask"] = bm
        m["f_prot"] = prot
        m["x"] = _c(x_fm[b, :, hh * HSEQ:(hh + 1) * HSEQ])
        m["selA"] = np.full((128, 1), 1.0 if hh == 1 else 0.0, np.float32)
        m["selB"] = np.zeros((128, 1), np.float32)
        m["ffnkeep"] = np.full((128, 1), 0.0 if hh == 0 else 1.0, np.float32)
        maps.append(m)
    return maps


def kernel_fused(inp):
    nc = build_fused()
    maps = _fused_maps(inp)
    res = run_bass_kernel_spmd(nc, maps, core_ids=list(range(8)))
    x = np.asarray(inp["L0_attn_x"], np.float32)
    out = np.empty((BATCH, D_MODEL, SEQ), np.float32)
    for core in range(8):
        b, hh = core // 2, core % 2
        out[b, :, hh * HSEQ:(hh + 1) * HSEQ] = res.results[core]["y"]
    return _c(out.transpose(0, 2, 1))
